# version 06
# baseline (speedup 1.0000x reference)
import math
from contextlib import ExitStack

import numpy as np
import concourse.bass as bass
import concourse.mybir as mybir
from concourse.bass_utils import run_bass_kernel_spmd

F32 = mybir.dt.float32
BF16 = mybir.dt.bfloat16
I32 = mybir.dt.int32
AF = mybir.ActivationFunctionType
ALU = mybir.AluOpType

D = 4096
NH = 16
NKV = 4
HD = 64
LAT = NH * HD
KVD = NKV * HD
QKV = LAT + 2 * KVD
NCH = D // 128
EPS = 1e-5
NEG_BIG = -3.0e38

def _split(x):
    a = float(np.float32(x))
    b = float(np.float32(x - a))
    return a, b

LN1E4_S1, LN1E4_S2 = _split(-math.log(10000.0) / 32.0)
PI1, _PI2 = _split(math.pi)
TWOPI1, TWOPI2 = _split(2.0 * math.pi)
INV2PI1 = float(np.float32(1.0 / TWOPI1))
HPI1, HPI2 = _split(math.pi / 2.0)


class Bld:

    ENG = ("T", "V", "A", "S", "G")

    def __init__(self, nc, es):
        self.nc = nc
        self.es = es
        self.eng = {"T": nc.tensor, "V": nc.vector, "A": nc.scalar,
                    "S": nc.sync, "G": nc.gpsimd}
        self.sems = {}
        self.counts = {}
        self.wmark = {e: {} for e in self.ENG}
        for e in self.ENG:
            self._mksem("p" + e)
        self.rings = {}
        for q, nslots in (("S", 8), ("G", 8)):
            slots = []
            for i in range(nslots):
                nm = f"dq{q}{i}"
                self._mksem(nm)
                slots.append(nm)
            self.rings[q] = {"slots": slots, "next": 0, "intok": {}}

    def _mksem(self, name):
        self.sems[name] = self.es.enter_context(self.nc.semaphore(name))
        self.counts[name] = 0

    def wait(self, e, tok):
        if tok is None:
            return
        sem, val = tok
        if self.wmark[e].get(sem, 0) >= val:
            return
        self.eng[e].wait_ge(self.sems[sem], val)
        self.wmark[e][sem] = val

    def waits(self, e, toks):
        for t in toks:
            self.wait(e, t)

    def tok(self, e):
        return ("p" + e, self.counts["p" + e])

    def inc(self, e, instr):
        sem = "p" + e
        self.counts[sem] += 1
        instr.then_inc(self.sems[sem], 1)
        return (sem, self.counts[sem])

    def dma(self, q, pairs, deps=()):
        ring = self.rings[q]
        slot = ring["slots"][ring["next"]]
        ring["next"] = (ring["next"] + 1) % len(ring["slots"])
        if self.counts[slot]:
            self.wait(q, (slot, self.counts[slot]))
        self.waits(q, deps)
        for p in pairs:
            kw = p[2] if len(p) > 2 else {}
            self.eng[q].dma_start(p[0], p[1], **kw).then_inc(self.sems[slot], 16)
            self.counts[slot] += 16
        return (slot, self.counts[slot])

    def barrier(self, engines="TVASG"):
        for e in engines:
            self.wait(e, ("p" + e, self.counts["p" + e]))
        toks = [self.inc(e, self.eng[e].nop()) for e in engines]
        for e in engines:
            for t in toks:
                self.wait(e, t)

    def drain_dma(self):
        for q, ring in self.rings.items():
            for slot in ring["slots"]:
                if self.counts[slot]:
                    self.wait(q, (slot, self.counts[slot]))


def host_stage(inp):
    perm = np.asarray(inp["L0_ffn_perm"])
    iperm = np.argsort(perm)
    x = np.asarray(inp["L0_attn_x"])
    B, T, _ = x.shape

    shared = {}
    shared["wqkvT"] = np.ascontiguousarray(np.asarray(inp["L0_attn_w_qkv"]).T[perm])
    shared["woT"] = np.ascontiguousarray(np.asarray(inp["L0_attn_w_o"])[perm].T)

    qc1 = np.asarray(inp["L0_attn_qc1_w"])
    qc1bd = np.zeros((3, 8, 128, 128), dtype=np.float32)
    for tap in range(3):
        for c in range(8):
            for gg in range(2):
                qc1bd[tap, c, 64 * gg:64 * gg + 64, 64 * gg:64 * gg + 64] = \
                    qc1[128 * c + 64 * gg:128 * c + 64 * gg + 64, :, tap].T
    shared["qc1bd"] = qc1bd
    shared["qc2T"] = np.ascontiguousarray(np.asarray(inp["L0_attn_qc2_w"]).transpose(2, 1, 0))
    kc1 = np.asarray(inp["L0_attn_kc1_w"])
    kc1bd = np.zeros((3, 2, 128, 128), dtype=np.float32)
    for tap in range(3):
        for c in range(2):
            for gg in range(2):
                kc1bd[tap, c, 64 * gg:64 * gg + 64, 64 * gg:64 * gg + 64] = \
                    kc1[128 * c + 64 * gg:128 * c + 64 * gg + 64, :, tap].T
    shared["kc1bd"] = kc1bd
    shared["kc2T"] = np.ascontiguousarray(np.asarray(inp["L0_attn_kc2_w"]).transpose(2, 1, 0))

    shared["qc1b"] = np.ascontiguousarray(np.asarray(inp["L0_attn_qc1_b"]).reshape(8, 128).T)
    shared["qc2b"] = np.ascontiguousarray(np.asarray(inp["L0_attn_qc2_b"]).reshape(8, 128).T)
    shared["kc1b"] = np.ascontiguousarray(np.asarray(inp["L0_attn_kc1_b"]).reshape(2, 128).T)
    shared["kc2b"] = np.ascontiguousarray(np.asarray(inp["L0_attn_kc2_b"]).reshape(2, 128).T)
    shared["ktemp"] = np.asarray(inp["L0_attn_key_temp"]).reshape(1, 1).astype(np.float32)
    shared["up_w"] = np.ascontiguousarray(np.asarray(inp["L0_ffn_up"]))
    shared["down_w"] = np.ascontiguousarray(np.asarray(inp["L0_ffn_down"]))

    shared["eye128"] = np.eye(128, dtype=np.float32)
    kk = np.arange(128)[:, None]
    qq = np.tile(np.arange(128)[None, :], (1, 4)).reshape(1, 512)
    shared["maskT"] = np.where(kk > qq, np.float32(NEG_BIG), np.float32(0.0)).astype(np.float32)
    dl = np.arange(128) % 64
    shared["msk16"] = (dl[:, None] == np.arange(64)[None, :]).astype(np.float32) / 32.0
    shared["msk4"] = (dl[:, None] == np.arange(64)[None, :]).astype(np.float32) / 8.0
    sel16 = np.zeros((8, 128, 16), dtype=np.float32)
    bc16 = np.zeros((8, 16, 128), dtype=np.float32)
    for c in range(8):
        for p in range(128):
            h = 2 * c + p // 64
            sel16[c, p, h] = 1.0
            bc16[c, h, p] = 1.0
    pvec = np.arange(128)
    ivec = np.zeros((128, 2), dtype=np.float32)
    ivec[:, 0] = (pvec % 64) // 2
    ivec[:, 1] = np.where(pvec % 2 == 0, -1.0, 1.0)
    shared["ivec"] = ivec
    shared["sel16"] = np.ascontiguousarray(sel16.transpose(1, 0, 2)).reshape(128, 8 * 16)
    shared["bc16"] = np.ascontiguousarray(bc16.transpose(1, 0, 2)).reshape(16, 8 * 128)
    sel4 = np.zeros((2, 128, 4), dtype=np.float32)
    bc4 = np.zeros((2, 4, 128), dtype=np.float32)
    for c in range(2):
        for p in range(128):
            h = 2 * c + p // 64
            sel4[c, p, h] = 1.0
            bc4[c, h, p] = 1.0
    shared["sel4"] = np.ascontiguousarray(sel4.transpose(1, 0, 2)).reshape(128, 2 * 4)
    bc4p = np.zeros((36, 2 * 128), dtype=np.float32)
    bc4p[32:36] = np.ascontiguousarray(bc4.transpose(1, 0, 2)).reshape(4, 2 * 128)
    shared["bc4"] = bc4p

    in_maps = []
    for b in range(B):
        m = dict(shared)
        m["x_in"] = np.ascontiguousarray(x[b].T[perm])
        in_maps.append(m)
    return in_maps, perm, iperm, B, T


def assemble_output(results, perm, B, T):
    out = np.empty((B, T, D), dtype=np.float32)
    for b in range(B):
        y = np.asarray(results[b]["y_out"])
        out[b][:, perm] = y.T
    return out


def build_program(T, upto="p3"):
    nc = bass.Bass("TRN2", target_bir_lowering=False)
    es = ExitStack()

    dr = {}
    dr["x_in"] = nc.dram_tensor("x_in", [D, T], F32, kind="ExternalInput")
    dr["wqkvT"] = nc.dram_tensor("wqkvT", [D, QKV], F32, kind="ExternalInput")
    dr["woT"] = nc.dram_tensor("woT", [LAT, D], F32, kind="ExternalInput")
    dr["qc1bd"] = nc.dram_tensor("qc1bd", [3, 8, 128, 128], F32, kind="ExternalInput")
    dr["qc2T"] = nc.dram_tensor("qc2T", [3, 1024, 1024], F32, kind="ExternalInput")
    dr["kc1bd"] = nc.dram_tensor("kc1bd", [3, 2, 128, 128], F32, kind="ExternalInput")
    dr["kc2T"] = nc.dram_tensor("kc2T", [3, 256, 256], F32, kind="ExternalInput")
    dr["qc1b"] = nc.dram_tensor("qc1b", [128, 8], F32, kind="ExternalInput")
    dr["qc2b"] = nc.dram_tensor("qc2b", [128, 8], F32, kind="ExternalInput")
    dr["kc1b"] = nc.dram_tensor("kc1b", [128, 2], F32, kind="ExternalInput")
    dr["kc2b"] = nc.dram_tensor("kc2b", [128, 2], F32, kind="ExternalInput")
    dr["ktemp"] = nc.dram_tensor("ktemp", [1, 1], F32, kind="ExternalInput")
    dr["up_w"] = nc.dram_tensor("up_w", [16, 256, 1024], F32, kind="ExternalInput")
    dr["down_w"] = nc.dram_tensor("down_w", [16, 1024, 256], F32, kind="ExternalInput")
    dr["eye128"] = nc.dram_tensor("eye128", [128, 128], F32, kind="ExternalInput")
    dr["maskT"] = nc.dram_tensor("maskT", [128, 512], F32, kind="ExternalInput")
    dr["msk16"] = nc.dram_tensor("msk16", [128, 64], F32, kind="ExternalInput")
    dr["msk4"] = nc.dram_tensor("msk4", [128, 64], F32, kind="ExternalInput")
    dr["ivec"] = nc.dram_tensor("ivec", [128, 2], F32, kind="ExternalInput")
    dr["sel16"] = nc.dram_tensor("sel16", [128, 8 * 16], F32, kind="ExternalInput")
    dr["bc16"] = nc.dram_tensor("bc16", [16, 8 * 128], F32, kind="ExternalInput")
    dr["sel4"] = nc.dram_tensor("sel4", [128, 2 * 4], F32, kind="ExternalInput")
    dr["bc4"] = nc.dram_tensor("bc4", [36, 2 * 128], F32, kind="ExternalInput")
    dr["y_out"] = nc.dram_tensor("y_out", [D, T], F32, kind="ExternalOutput")
    dr["x1s"] = nc.dram_tensor("x1s", [D, T], F32)
    dr["qrawD"] = nc.dram_tensor("qrawD", [128, 8 * (T + 2)], BF16)
    dr["krawD"] = nc.dram_tensor("krawD", [128, 2 * (T + 2)], BF16)
    dr["rowscr"] = nc.dram_tensor("rowscr", [1, max(T, 2048)], F32)

    b = Bld(nc, es)
    blk = es.enter_context(nc.Block())

    PT = T + 2
    sb = {}
    def alloc(name, shape, dtype):
        sb[name] = es.enter_context(nc.sbuf_tensor(name, shape, dtype))
        return sb[name]

    es_qm = ExitStack()
    sb["qmkm"] = es_qm.enter_context(nc.sbuf_tensor("qmkm", [128, T], F32, side="right"))
    es_raw = ExitStack()
    sb["vfm"] = es_raw.enter_context(nc.sbuf_tensor("vfm", [128, 2 * T], BF16, side="right"))
    sb["zrow"] = es_raw.enter_context(nc.sbuf_tensor("zrow", [128, 16], BF16, side="right"))
    alloc("rows", [128, T], F32)
    def m1row(t0, TN):
        return sb["rows"][0:1, t0:t0 + TN]
    alloc("rlb", [128, 2 * 512], BF16)
    def rlbs(i, TN):
        return sb["rlb"][0:1, i * 512:i * 512 + TN]
    alloc("r3c3b", [128, 2 * T], BF16)
    def r3row(t0, TN):
        return sb["r3c3b"][0:1, t0:t0 + TN]
    def c3row(t0, TN):
        return sb["r3c3b"][0:1, T + t0:T + t0 + TN]
    alloc("eyeb", [128, 128], BF16)
    alloc("msk16b", [128, 64], BF16)
    alloc("msk4b", [128, 64], BF16)
    alloc("sel16b", [128, 8 * 16], BF16)
    alloc("bc16b", [16, 8 * 128], BF16)
    alloc("sel4b", [128, 2 * 4], BF16)
    alloc("bc4b", [36, 2 * 128], BF16)
    alloc("biascols", [128, 20], F32)
    alloc("onescol", [128, 1], BF16)
    alloc("onesrow", [128, 128], BF16)
    alloc("negm", [128, 1], F32)
    alloc("cc", [128, 8], F32)
    alloc("ktcol", [128, 1], F32)
    alloc("smallf", [128, 8 * 512], F32)
    def sfs(i, TN):
        return sb["smallf"][0:1, i * 512:i * 512 + TN]

    ps = []
    for i in range(7):
        ps.append(es.enter_context(nc.psum_tensor(f"pb{i}", [128, 512], F32)))
    psb = es.enter_context(nc.psum_tensor("pbT", [128, 1024], BF16))

    tok_c = b.dma("S", [
        (sb["biascols"][:, 0:8], dr["qc1b"][:, :]),
        (sb["biascols"][:, 8:16], dr["qc2b"][:, :]),
        (sb["biascols"][:, 16:18], dr["kc1b"][:, :]),
        (sb["biascols"][:, 18:20], dr["kc2b"][:, :]),
        (sb["smallf"][0:1, 7 * 512:7 * 512 + 1], dr["ktemp"][:, :]),
        (sb["cc"][:, 5:6], dr["ivec"][:, 0:1], dict(allow_slow_non_contiguous=True)),
        (sb["cc"][:, 6:7], dr["ivec"][:, 1:2], dict(allow_slow_non_contiguous=True)),
    ])
    tok_cb = b.dma("G", [
        (sb["eyeb"][:, :], dr["eye128"][:, :]),
        (sb["msk16b"][:, :], dr["msk16"][:, :]),
        (sb["msk4b"][:, :], dr["msk4"][:, :]),
        (sb["sel16b"][:, :], dr["sel16"][:, :]),
        (sb["bc16b"][:, :], dr["bc16"][:, :]),
        (sb["sel4b"][:, :], dr["sel4"][:, :]),
        (sb["bc4b"][:, :], dr["bc4"][:, :]),
    ])
    b.wait("V", tok_c)
    tv = b.inc("V", nc.vector.memset(sb["onescol"][:, :], 1.0))
    tv = b.inc("V", nc.vector.memset(sb["onesrow"][:, :], 1.0))
    tcc = b.inc("V", nc.vector.memset(sb["cc"][:, 0:1], EPS))
    tcc = b.inc("V", nc.vector.memset(sb["cc"][:, 1:2], HPI1))
    b.waits("A", [tcc])
    ta = b.inc("A", nc.scalar.activation(sb["smallf"][0:1, 7 * 512 + 1:7 * 512 + 2],
                                         sb["smallf"][0:1, 7 * 512:7 * 512 + 1],
                                         AF.Abs, scale=0.125))
    b.wait("A", ta)
    ta = b.inc("A", nc.scalar.activation(sb["smallf"][0:1, 7 * 512 + 2:7 * 512 + 3],
                                         sb["smallf"][0:1, 7 * 512 + 1:7 * 512 + 2],
                                         AF.Copy, scale=-1.0))
    tok_r = b.dma("S", [(dr["rowscr"][0:1, 0:2],
                         sb["smallf"][0:1, 7 * 512:7 * 512 + 3:2],
                         dict(allow_slow_non_contiguous=True))], deps=[ta])
    tok_r2 = b.dma("S", [
        (sb["negm"][:, 0:1], bass.AP(dr["rowscr"], 1, [[0, 128], [1, 1]])),
        (sb["ktcol"][32:36, 0:1], bass.AP(dr["rowscr"], 0, [[0, 4], [1, 1]])),
    ], deps=[tok_r])

    b.wait("V", tok_cb)
    tv0 = b.inc("V", nc.vector.memset(sb["zrow"][:, :], 0.0))
    tvv = b.inc("V", nc.vector.memset(sb["vfm"][:, T:T + 1], 0.0))
    tok_zp = b.dma("S", [
        (dr["qrawD"].ap().rearrange("p (c t) -> p c t", c=8)[:, :, 0:2],
         sb["zrow"][:, 0:16].rearrange("p (c t) -> p c t", c=8)),
        (dr["krawD"].ap().rearrange("p (c t) -> p c t", c=2)[:, :, 0:2],
         sb["zrow"][:, 0:4].rearrange("p (c t) -> p c t", c=2)),
    ], deps=[tv0])

    if upto == "p0":
        b.drain_dma()
        b.barrier()
        es_raw.close()
        es_qm.close()
        es.close()
        return nc
    TN1 = min(256, T)
    nt1 = T // TN1
    es1 = ExitStack()
    def alloc1(name, shape, dtype):
        sb[name] = es1.enter_context(nc.sbuf_tensor(name, shape, dtype, side="right"))
        return sb[name]
    alloc1("wqkvb", [128, NCH * QKV], BF16)
    alloc1("x32", [128, (NCH // 2) * TN1], F32)
    alloc1("xb", [128, NCH * TN1], BF16)
    alloc1("xsq", [128, 4 * TN1], BF16)
    alloc1("r1s", [128, 2 * TN1], F32)
    alloc1("uevac", [128, 2 * TN1], F32)
    alloc1("stg", [128, 3 * TN1], BF16)
    alloc1("wsumb", [1, QKV], BF16)

    wq_toks = []
    for g in range(4):
        prs = []
        for kc in range(8 * g, 8 * g + 8):
            prs.append((sb["wqkvb"][:, kc * QKV:(kc + 1) * QKV],
                        dr["wqkvT"][128 * kc:128 * kc + 128, :]))
        wq_toks.append(b.dma("G", prs))

    b.wait("T", tv)
    for seg in range(3):
        for kc in range(NCH):
            b.wait("T", wq_toks[kc // 8])
            ins = nc.tensor.matmul(ps[6][0:1, 0:512], sb["onescol"][:, :],
                                   sb["wqkvb"][:, kc * QKV + seg * 512: kc * QKV + (seg + 1) * 512],
                                   start=(kc == 0), stop=(kc == NCH - 1))
        tt = b.inc("T", ins)
        b.wait("V", tt)
        tv = b.inc("V", nc.vector.tensor_copy(sb["wsumb"][0:1, seg * 512:(seg + 1) * 512],
                                              ps[6][0:1, 0:512]))
        b.wait("T", tv)

    if upto == "p1a_w":
        b.drain_dma()
        b.barrier()
        es1.close()
        es_raw.close()
        es_qm.close()
        es.close()
        return nc

    def ln_rows(TN, ps_sx, ps_sx2, m_ap, rb_r_ap, rb_c_ap, dep=None, rbdep=None):
        b.wait("A", dep)
        if ps_sx is not None:
            ta = b.inc("A", nc.scalar.activation(m_ap, ps_sx, AF.Copy, scale=1.0 / D))
            b.wait("A", ta)
        ta = b.inc("A", nc.scalar.activation(sfs(0, TN), ps_sx2, AF.Copy, scale=1.0 / D))
        ta2 = b.inc("A", nc.scalar.activation(sfs(1, TN), m_ap, AF.Square))
        b.waits("V", [ta, ta2, dep])
        tv = b.inc("V", nc.vector.tensor_sub(sfs(0, TN), sfs(0, TN), sfs(1, TN)))
        b.wait("A", tv)
        ta = b.inc("A", nc.scalar.activation(sfs(2, TN), sfs(0, TN), AF.Sqrt,
                                             bias=sb["cc"][0:1, 0:1]))
        b.wait("V", ta)
        tv = b.inc("V", nc.vector.reciprocal(sfs(2, TN), sfs(2, TN)))
        b.wait("V", tv)
        tv = b.inc("V", nc.vector.tensor_mul(sfs(3, TN), m_ap, sfs(2, TN)))
        b.waits("V", [tv, rbdep])
        tv = b.inc("V", nc.vector.tensor_copy(rb_r_ap, sfs(2, TN)))
        tv = b.inc("V", nc.vector.tensor_copy(rb_c_ap, sfs(3, TN)))
        return tv

    cast_tok = [None, None]
    last_evac = {}

    for t in wq_toks:
        b.wait("T", t)

    HN = NCH // 2
    mm_tok = None
    for ti in range(nt1):
        t0 = ti * TN1
        for hf in range(2):
            prs = []
            for kc in range(HN * hf, HN * (hf + 1)):
                prs.append((sb["x32"][:, (kc - HN * hf) * TN1:(kc - HN * hf + 1) * TN1],
                            dr["x_in"][128 * kc:128 * kc + 128, t0:t0 + TN1]))
            x_tok = b.dma("S", prs, deps=[d for d in (cast_tok[1 - hf],) if d])
            b.wait("V", x_tok)
            if mm_tok is not None:
                b.wait("V", mm_tok)
            cast_tok[hf] = b.inc("V", nc.vector.tensor_copy(
                sb["xb"][:, hf * HN * TN1:(hf + 1) * HN * TN1], sb["x32"][:, :]))

        b.waits("A", [cast_tok[0], cast_tok[1]])
        if ti > 0:
            b.wait("T", last_evac["rows_done"])
        tsq = None
        for gq in range(8):
            if tsq is not None:
                b.wait("A", tsq)
            ta = b.inc("A", nc.scalar.activation(
                sb["xsq"][:, :], sb["xb"][:, gq * 4 * TN1:(gq + 1) * 4 * TN1], AF.Square))
            b.wait("T", ta)
            b.waits("T", [cast_tok[0], cast_tok[1]])
            for c4 in range(4):
                kc = gq * 4 + c4
                ins = nc.tensor.matmul(ps[5][0:1, 0:TN1], sb["onescol"][:, :],
                                       sb["xb"][:, kc * TN1:(kc + 1) * TN1],
                                       start=(kc == 0), stop=False)
                ins = nc.tensor.matmul(ps[5][0:1, TN1:2 * TN1], sb["onescol"][:, :],
                                       sb["xsq"][:, c4 * TN1:(c4 + 1) * TN1],
                                       start=False, stop=(kc == NCH - 1))
            tsq = b.inc("T", ins)
        stats_tok = tsq
        if upto == "p1a_s":
            break
        b.wait("A", stats_tok)
        b.wait("V", stats_tok)
        rows_tok = ln_rows(TN1, ps[5][0:1, 0:TN1], ps[5][0:1, TN1:2 * TN1],
                           m1row(t0, TN1), rlbs(0, TN1), rlbs(1, TN1),
                           dep=last_evac.get("rows_done"),
                           rbdep=last_evac.get("rlb_used"))
        last_evac["rows_done"] = rows_tok

        if upto == "p1a_r":
            break
        b.wait("T", rows_tok)
        tt = b.inc("T", nc.tensor.matmul(ps[4][:, 0:TN1], sb["onesrow"][0:1, :],
                                         rlbs(0, TN1), start=True, stop=True))
        b.wait("V", tt)
        if last_evac.get("r1s_free") is not None:
            b.wait("V", last_evac["r1s_free"])
        par = ti % 2
        r1s_tok = b.inc("V", nc.vector.tensor_copy(sb["r1s"][:, par * TN1:(par + 1) * TN1],
                                                   ps[4][:, 0:TN1]))
        b.wait("T", r1s_tok)

        if upto == "p1a_b":
            break
        for oc in range(12):
            pq = ps[oc % 2]
            pc = ps[2 + oc % 2]
            if ("pq", oc % 2) in last_evac:
                b.wait("T", last_evac[("pq", oc % 2)])
            for kc in range(NCH):
                ins = nc.tensor.matmul(pq[:, 0:TN1],
                                       sb["wqkvb"][:, kc * QKV + oc * 128: kc * QKV + (oc + 1) * 128],
                                       sb["xb"][:, kc * TN1:(kc + 1) * TN1],
                                       start=(kc == 0), stop=(kc == NCH - 1))
            mm_tok = b.inc("T", ins)
            if ("pc", oc % 2) in last_evac:
                b.wait("T", last_evac[("pc", oc % 2)])
            tt2 = b.inc("T", nc.tensor.matmul(pc[:, 0:TN1],
                                              sb["wsumb"][0:1, oc * 128:(oc + 1) * 128],
                                              rlbs(1, TN1),
                                              start=True, stop=True))
            last_evac["rlb_used"] = tt2
            b.waits("V", [mm_tok, tt2, r1s_tok])
            epar = oc % 2
            if ("ue", epar) in last_evac:
                b.wait("V", last_evac[("ue", epar)])
            tv1 = b.inc("V", nc.vector.tensor_mul(
                sb["uevac"][:, epar * TN1:(epar + 1) * TN1],
                sb["r1s"][:, par * TN1:(par + 1) * TN1], pq[:, 0:TN1]))
            last_evac[("pq", oc % 2)] = tv1
            b.wait("V", tv1)
            if oc < 10:
                ss = oc % 3
                b.waits("V", [last_evac.get(("stgd", ss)), last_evac.get(("stgm", ss))])
                dst = sb["stg"][:, ss * TN1:(ss + 1) * TN1]
            elif oc == 10:
                dst = sb["vfm"][:, t0:t0 + TN1]
            else:
                hi = min(t0 + 1 + TN1, T)
                dst = sb["vfm"][:, T + t0 + 1: T + hi]
            if oc == 11:
                w = hi - (t0 + 1)
                tv2 = b.inc("V", nc.vector.tensor_sub(
                    dst, sb["uevac"][:, epar * TN1: epar * TN1 + w], pc[:, 0:w]))
            else:
                tv2 = b.inc("V", nc.vector.tensor_sub(
                    dst, sb["uevac"][:, epar * TN1:(epar + 1) * TN1], pc[:, 0:TN1]))
            last_evac[("pc", oc % 2)] = tv2
            last_evac[("ue", epar)] = tv2
            if oc < 10:
                if oc < 8:
                    ddst = dr["qrawD"][:, oc * PT + 2 + t0: oc * PT + 2 + t0 + TN1]
                else:
                    ddst = dr["krawD"][:, (oc - 8) * PT + 2 + t0: (oc - 8) * PT + 2 + t0 + TN1]
                last_evac[("stgd", ss)] = b.dma("S", [(ddst, dst)], deps=[tv2])
                b.wait("T", tv2)
                if oc < 8:
                    tmn = b.inc("T", nc.tensor.matmul(ps[4][0:64, 0:TN1], sb["msk16b"][:, :],
                                                      dst, start=(oc == 0), stop=(oc == 7)))
                else:
                    tmn = b.inc("T", nc.tensor.matmul(ps[6][0:64, 0:TN1], sb["msk4b"][:, :],
                                                      dst, start=(oc == 8), stop=(oc == 9)))
                last_evac[("stgm", ss)] = tmn
                if oc == 7:
                    tqm = tmn
                if oc == 9:
                    tkm = tmn
        if upto == "p1a_q":
            break
        b.waits("V", [tqm, tkm])
        tv = b.inc("V", nc.vector.tensor_copy(sb["qmkm"][0:64, t0:t0 + TN1], ps[4][0:64, 0:TN1]))
        b.wait("V", tv)
        tv = b.inc("V", nc.vector.tensor_copy(sb["qmkm"][64:128, t0:t0 + TN1], ps[6][0:64, 0:TN1]))
        b.wait("T", tv)
        last_evac["r1s_free"] = last_evac[("pq", 1)]

    b.drain_dma()
    b.barrier()
    es1.close()
    if upto.startswith("p1a"):
        b.drain_dma()
        b.barrier()
        es.close()
        return nc

    TC1 = min(512, T)
    TN2 = min(256, T)
    nt2 = T // TN2
    NJ = T // 128
    VW = 65
    es_qkv = ExitStack()
    sb["vtm"] = es_qkv.enter_context(nc.sbuf_tensor("vtm", [128, 4 * (NJ * VW)], BF16))
    sb["q4a"] = es_qkv.enter_context(nc.sbuf_tensor("q4a", [128, 4 * T], BF16))
    sb["q4b"] = es_qkv.enter_context(nc.sbuf_tensor("q4b", [128, 4 * T], BF16))
    sb["kfin"] = es_qkv.enter_context(nc.sbuf_tensor("kfin", [128, 2 * T], BF16))
    es_tabs = ExitStack()
    sb["tabs"] = es_tabs.enter_context(nc.sbuf_tensor("tabs", [128, 2 * T], BF16))
    es_tab = ExitStack()
    sb["tabscr"] = es_tab.enter_context(nc.sbuf_tensor("tabscr", [128, 4 * T], F32, side="right"))
    sb["kint"] = es_tab.enter_context(nc.sbuf_tensor("kint", [128, T], I32, side="right"))

    def ts_op(out, in0, s1, s2, op0, op1=ALU.bypass, deps=(), eng="V"):
        b.waits(eng, deps)
        return b.inc(eng, nc.vector.tensor_scalar(out, in0, s1, s2, op0, op1))

    A_TB, A_ANG, A_S = (slice(0, T), slice(T, 2 * T), slice(2 * T, 3 * T))
    SC2 = slice(3 * T, 4 * T)
    tg = b.inc("G", nc.gpsimd.iota(sb["tabscr"][0:1, 0:T], [[1, T]],
                                   channel_multiplier=0,
                                   allow_small_or_imprecise_dtypes=True))
    tok_tr = b.dma("S", [(dr["rowscr"][0:1, 0:T], sb["tabscr"][0:1, 0:T])], deps=[tg, tok_r2])
    tok_tb = b.dma("S", [(sb["tabscr"][:, A_TB], bass.AP(dr["rowscr"], 0, [[0, 128], [1, T]]))],
                   deps=[tok_tr])
    b.waits("A", [tok_c])
    t_e1 = b.inc("A", nc.scalar.activation(sb["cc"][:, 7:8], sb["cc"][:, 5:6], AF.Exp, scale=LN1E4_S1))
    t_e2 = b.inc("A", nc.scalar.activation(sb["cc"][:, 4:5], sb["cc"][:, 5:6], AF.Exp, scale=LN1E4_S2))
    b.waits("V", [t_e1, t_e2])
    t_inv = b.inc("V", nc.vector.tensor_mul(sb["cc"][:, 4:5], sb["cc"][:, 7:8], sb["cc"][:, 4:5]))
    b.wait("V", tok_tb)
    t_ang = ts_op(sb["tabscr"][:, A_ANG], sb["tabscr"][:, A_TB], sb["cc"][:, 4:5], None,
                  ALU.mult, deps=[t_inv])

    HPIC = float(np.float32(math.pi - PI1))
    def build_sintab(src_sl, dst_sl, shift, dep):
        W0, SC = A_TB, A_S
        dd = dep
        if shift:
            dd = ts_op(sb["tabscr"][:, SC], sb["tabscr"][:, src_sl], HPI1, HPI2,
                       ALU.add, ALU.add, deps=[dd])
            srcs = SC
        else:
            srcs = src_sl
        d0 = ts_op(sb["tabscr"][:, W0], sb["tabscr"][:, srcs], INV2PI1, None,
                   ALU.mult, deps=[dd])
        b.waits("V", [d0])
        d1 = b.inc("V", nc.vector.tensor_copy(sb["kint"][:, :], sb["tabscr"][:, W0]))
        b.waits("V", [d1])
        d2 = b.inc("V", nc.vector.tensor_copy(dst_sl, sb["kint"][:, :]))
        b.waits("V", [d2])
        d3 = b.inc("V", nc.vector.tensor_sub(sb["tabscr"][:, W0], sb["tabscr"][:, W0],
                                             dst_sl))
        d4 = ts_op(sb["tabscr"][:, SC2], sb["tabscr"][:, W0], 0.5, None, ALU.is_ge,
                   deps=[d3])
        b.waits("V", [d4])
        d5 = b.inc("V", nc.vector.tensor_add(dst_sl, dst_sl, sb["tabscr"][:, SC2]))
        d6 = ts_op(sb["tabscr"][:, SC2], sb["tabscr"][:, W0], -0.5, None, ALU.is_lt,
                   deps=[d5])
        b.waits("V", [d6])
        d7 = b.inc("V", nc.vector.tensor_sub(dst_sl, dst_sl, sb["tabscr"][:, SC2]))
        d8 = ts_op(sb["tabscr"][:, W0], sb["tabscr"][:, srcs], 0.5, None, ALU.mult,
                   deps=[d7])
        b.waits("V", [d8])
        d9 = b.inc("V", nc.vector.scalar_tensor_tensor(
            sb["tabscr"][:, W0], dst_sl, -PI1, sb["tabscr"][:, W0], ALU.mult, ALU.add))
        b.waits("V", [d9])
        da = b.inc("V", nc.vector.scalar_tensor_tensor(
            sb["tabscr"][:, W0], dst_sl, -HPIC, sb["tabscr"][:, W0], ALU.mult, ALU.add))
        b.waits("A", [da])
        db = b.inc("A", nc.scalar.activation(sb["tabscr"][:, SC2], sb["tabscr"][:, W0],
                                             AF.Abs))
        dc = b.inc("A", nc.scalar.activation(dst_sl, sb["tabscr"][:, W0], AF.Sin))
        dd2 = ts_op(sb["tabscr"][:, SC2], sb["tabscr"][:, SC2], HPI2, None,
                    ALU.subtract, deps=[db, dc], eng="V")
        b.waits("A", [dd2])
        de = b.inc("A", nc.scalar.activation(sb["tabscr"][:, SC2], sb["tabscr"][:, SC2],
                                             AF.Sin, scale=-1.0, bias=sb["cc"][:, 1:2]))
        b.waits("V", [dc, de])
        df = b.inc("V", nc.vector.scalar_tensor_tensor(
            dst_sl, dst_sl, 2.0, sb["tabscr"][:, SC2], ALU.mult, ALU.mult))
        return df

    t_sn = build_sintab(A_ANG, sb["tabs"][:, T:2 * T], False, t_ang)
    t_sn = ts_op(sb["tabs"][:, T:2 * T], sb["tabs"][:, T:2 * T], sb["cc"][:, 6:7], None,
                 ALU.mult, deps=[t_sn])
    t_cs = build_sintab(A_ANG, sb["tabs"][:, 0:T], True, t_sn)
    b.drain_dma()
    b.barrier()
    es_tab.close()
    if upto == "p1b_t":
        b.drain_dma()
        b.barrier()
        es_tabs.close()
        es_qkv.close()
        es_raw.close()
        es_qm.close()
        es.close()
        return nc

    es2 = ExitStack()
    sb["c1q"] = es2.enter_context(nc.sbuf_tensor("c1q", [128, 8 * PT], BF16))
    sb["c1k"] = es2.enter_context(nc.sbuf_tensor("c1k", [128, 2 * PT], BF16))
    es_c1 = ExitStack()
    sb["wc1q"] = es_c1.enter_context(nc.sbuf_tensor("wc1q", [128, 24 * 128], BF16, side="right"))
    sb["wc1k"] = es_c1.enter_context(nc.sbuf_tensor("wc1k", [128, 6 * 128], BF16, side="right"))

    tok_wc1 = b.dma("G", [
        (sb["wc1q"][:, :].rearrange("p (t c b) -> p t c b", t=3, c=8),
         dr["qc1bd"].ap().transpose([2, 0, 1, 3])),
        (sb["wc1k"][:, :].rearrange("p (t c b) -> p t c b", t=3, c=2),
         dr["kc1bd"].ap().transpose([2, 0, 1, 3]))])

    tzp = b.inc("V", nc.vector.memset(sb["c1q"].ap().rearrange("p (c t) -> p c t", c=8)[:, :, 0:2], 0.0))
    tzp = b.inc("V", nc.vector.memset(sb["c1k"].ap().rearrange("p (c t) -> p c t", c=2)[:, :, 0:2], 0.0))
    tvo = b.inc("V", nc.vector.memset(
        sb["vtm"].ap().rearrange("p (g j w) -> p g j w", g=4, j=NJ)[:, :, :, 64:65], 1.0))

    sb["c1wq"] = es_c1.enter_context(nc.sbuf_tensor("c1wq", [128, 2 * 8 * (TC1 + 2)], BF16, side="right"))
    sb["c1wk"] = es_c1.enter_context(nc.sbuf_tensor("c1wk", [128, 2 * 2 * (TC1 + 2)], BF16, side="right"))
    c1q_c = sb["c1q"].ap().rearrange("p (c t) -> p c t", c=8)
    c1k_c = sb["c1k"].ap().rearrange("p (c t) -> p c t", c=2)
    pv_guard = {}
    WQ = 8 * (TC1 + 2)
    WK = 2 * (TC1 + 2)
    for ti in range(T // TC1):
        t0 = ti * TC1
        wsl = ti % 2
        prs = [(sb["c1wq"][:, wsl * WQ + c * (TC1 + 2): wsl * WQ + (c + 1) * (TC1 + 2)],
                dr["qrawD"][:, c * PT + t0: c * PT + t0 + TC1 + 2]) for c in range(8)]
        prs += [(sb["c1wk"][:, wsl * WK + c * (TC1 + 2): wsl * WK + (c + 1) * (TC1 + 2)],
                 dr["krawD"][:, c * PT + t0: c * PT + t0 + TC1 + 2]) for c in range(2)]
        win_tok = b.dma("S", prs, deps=[tok_zp, pv_guard.get(("win", wsl))])
        for side, nch_s, wsb, wbuf, woff, dst_c, bias_off in (
                ("q", 8, "wc1q", "c1wq", wsl * WQ, c1q_c, 0),
                ("k", 2, "wc1k", "c1wk", wsl * WK, c1k_c, 16)):
            for c in range(nch_s):
                bank = ps[c % 2]
                if ("c1", c % 2) in pv_guard:
                    b.wait("T", pv_guard[("c1", c % 2)])
                b.wait("T", tok_wc1)
                b.wait("T", win_tok)
                for tap in range(3):
                    ins = nc.tensor.matmul(
                        bank[:, 0:TC1],
                        sb[wsb][:, (tap * nch_s + c) * 128:(tap * nch_s + c + 1) * 128],
                        sb[wbuf][:, woff + c * (TC1 + 2) + tap: woff + c * (TC1 + 2) + tap + TC1],
                        start=(tap == 0), stop=(tap == 2))
                tmm = b.inc("T", ins)
                b.waits("V", [tmm, tzp])
                ta = b.inc("V", nc.vector.tensor_scalar(
                    dst_c[:, c, 2 + t0: 2 + t0 + TC1], bank[:, 0:TC1],
                    sb["biascols"][:, bias_off + c: bias_off + c + 1], None, ALU.add))
                pv_guard[("c1", c % 2)] = ta
        pv_guard[("win", wsl)] = tmm
    for g in range(4):
        bpv = 64 * (g % 2)
        for j in range(NJ):
            if ("vt",) in pv_guard:
                b.wait("T", pv_guard[("vt",)])
            tt = b.inc("T", nc.tensor.transpose(
                psb[:, 0:64],
                sb["vfm"][bpv:bpv + 64, (g // 2) * T + 128 * j: (g // 2) * T + 128 * (j + 1)],
                sb["eyeb"][bpv:bpv + 64, bpv:bpv + 64]))
            b.wait("V", tt)
            tv = b.inc("V", nc.vector.tensor_copy(
                sb["vtm"][:, g * NJ * VW + j * VW: g * NJ * VW + j * VW + 64],
                psb[:, 0:64]))
            pv_guard[("vt",)] = tv
    b.drain_dma()
    b.barrier()
    es_c1.close()
    es_raw.close()
    if upto == "p1b_c1":
        b.barrier()
        es2.close()
        es_tabs.close()
        es_qkv.close()
        es_qm.close()
        es.close()
        return nc

    es3 = ExitStack()
    for nm, shp, dt_ in (("wc2q", [128, 24 * 1024], BF16), ("wc2k", [128, 6 * 256], BF16),
                         ("qpost", [128, 8 * TN2], BF16), ("kpost", [128, 2 * TN2], BF16),
                         ("sqb", [128, 2 * TN2], BF16), ("mm2b", [128, 2 * TN2], BF16),
                         ("rqf", [128, TN2], F32), ("rqb", [128, TN2], BF16),
                         ("rsw", [128, TN2], BF16), ("rt1", [128, TN2], F32),
                         ("rt2", [128, TN2], F32)):
        sb[nm] = es3.enter_context(nc.sbuf_tensor(nm, shp, dt_, side="right"))
    wq_toks2 = []
    for tap in range(3):
        prs = [(sb["wc2q"][:, (tap * 8 + kc) * 1024:(tap * 8 + kc + 1) * 1024],
                dr["qc2T"][tap, 128 * kc:128 * kc + 128, :]) for kc in range(8)]
        wq_toks2.append(b.dma("G", prs))
    tok_wc2k = b.dma("G", [(sb["wc2k"][:, (tap * 2 + kc) * 256:(tap * 2 + kc + 1) * 256],
                            dr["kc2T"][tap, 128 * kc:128 * kc + 128, :])
                           for tap in range(3) for kc in range(2)])

    PAIRSWAP = []
    for i in range(16):
        PAIRSWAP += [2 * i + 1, 2 * i]

    guard = {}
    for ti in range(nt2):
        t0 = ti * TN2
        mdeps = [guard.get("mm2b")]
        b.waits("V", [d for d in mdeps if d])
        tv = b.inc("V", nc.vector.tensor_copy(sb["mm2b"][0:64, 0:TN2], sb["qmkm"][64:128, t0:t0 + TN2]))
        tv = b.inc("V", nc.vector.tensor_copy(sb["mm2b"][64:128, 0:TN2], sb["qmkm"][64:128, t0:t0 + TN2]))
        tv = b.inc("V", nc.vector.tensor_copy(sb["mm2b"][0:64, TN2:2 * TN2], sb["qmkm"][0:64, t0:t0 + TN2]))
        t_mm2b = b.inc("V", nc.vector.tensor_copy(sb["mm2b"][64:128, TN2:2 * TN2], sb["qmkm"][0:64, t0:t0 + TN2]))

        for side, nch_s, wsb, src_c, nkc, bias_off, post_sl, dst_post, ps_ssq, selb, selw in (
                ("q", 8, "wc2q", c1q_c, 8, 8, slice(0, TN2), "qpost", ps[3], "sel16b", 16),
                ("k", 2, "wc2k", c1k_c, 2, 18, slice(TN2, 2 * TN2), "kpost", ps[4], "sel4b", 4)):
            for oc in range(nch_s):
                bank = ps[oc % 2]
                if ("c2", oc % 2) in guard:
                    b.wait("T", guard[("c2", oc % 2)])
                for tok in (wq_toks2 if side == "q" else [tok_wc2k]):
                    b.wait("T", tok)
                first = True
                for tap in range(3):
                    for kc in range(nkc):
                        if side == "q":
                            wsl = sb[wsb][:, (tap * 8 + kc) * 1024 + oc * 128:
                                          (tap * 8 + kc) * 1024 + (oc + 1) * 128]
                        else:
                            wsl = sb[wsb][:, (tap * 2 + kc) * 256 + oc * 128:
                                          (tap * 2 + kc) * 256 + (oc + 1) * 128]
                        ins = nc.tensor.matmul(
                            bank[:, 0:TN2], wsl, src_c[:, kc, t0 + tap: t0 + tap + TN2],
                            start=first, stop=(tap == 2 and kc == nkc - 1))
                        first = False
                tmm = b.inc("T", ins)
                b.waits("V", [tmm, t_mm2b, guard.get(("rope", side, oc))])
                tpost = b.inc("V", nc.vector.scalar_tensor_tensor(
                    sb[dst_post][:, oc * TN2:(oc + 1) * TN2], bank[:, 0:TN2],
                    sb["biascols"][:, bias_off + oc: bias_off + oc + 1],
                    sb["mm2b"][:, post_sl], ALU.add, ALU.add))
                guard[("c2", oc % 2)] = tpost
                guard[("post", side, oc)] = tpost
                b.waits("A", [tpost, guard.get(("sq", oc % 2))])
                tsq = b.inc("A", nc.scalar.activation(
                    sb["sqb"][:, (oc % 2) * TN2:(oc % 2 + 1) * TN2],
                    sb[dst_post][:, oc * TN2:(oc + 1) * TN2], AF.Square))
                b.waits("T", [tsq, guard.get("ssq_read")])
                ins = nc.tensor.matmul(
                    ps_ssq[0:selw, 0:TN2], sb[selb][:, oc * selw:(oc + 1) * selw],
                    sb["sqb"][:, (oc % 2) * TN2:(oc % 2 + 1) * TN2],
                    start=(oc == 0), stop=(oc == nch_s - 1))
                tssq = b.inc("T", ins)
                guard[("sq", oc % 2)] = tssq
            guard[("ssq_done", side)] = tssq

        b.waits("A", [guard[("ssq_done", "q")], guard[("ssq_done", "k")], guard.get("rq_used")])
        ta = b.inc("A", nc.scalar.activation(sb["rqf"][0:16, :], ps[3][0:16, 0:TN2], AF.Sqrt))
        ta2 = b.inc("A", nc.scalar.activation(sb["rqf"][32:36, :], ps[4][0:4, 0:TN2], AF.Sqrt))
        b.waits("V", [ta, ta2])
        tvq = b.inc("V", nc.vector.reciprocal(sb["rqf"][0:16, :], sb["rqf"][0:16, :]))
        tvk = b.inc("V", nc.vector.reciprocal(sb["rqf"][32:36, :], sb["rqf"][32:36, :]))
        b.waits("V", [tvq, tvk, tok_r2])
        tvk = b.inc("V", nc.vector.tensor_scalar(sb["rqf"][32:36, :], sb["rqf"][32:36, :],
                                                 sb["ktcol"][32:36, 0:1], None, ALU.mult))
        b.waits("V", [tvk, guard.get("rqb_used")])
        tvq = b.inc("V", nc.vector.tensor_copy(sb["rqb"][0:16, :], sb["rqf"][0:16, :]))
        tvk = b.inc("V", nc.vector.tensor_copy(sb["rqb"][32:36, :], sb["rqf"][32:36, :]))
        guard["ssq_read"] = tvk
        guard["rqb_ready"] = tvk

        for side, nch_s, dst_post in (("q", 8, "qpost"), ("k", 2, "kpost")):
            for oc in range(nch_s):
                post_ap = sb[dst_post][:, oc * TN2:(oc + 1) * TN2]
                bankR = ps[5 + oc % 2]
                if ("bR", oc % 2) in guard:
                    b.wait("T", guard[("bR", oc % 2)])
                b.wait("T", guard["rqb_ready"])
                if side == "q":
                    tbc = b.inc("T", nc.tensor.matmul(
                        bankR[:, 0:TN2], sb["bc16b"][:, oc * 128:(oc + 1) * 128],
                        sb["rqb"][0:16, :], start=True, stop=True))
                else:
                    tbc = b.inc("T", nc.tensor.matmul(
                        bankR[:, 0:TN2], sb["bc4b"][32:36, oc * 128:(oc + 1) * 128],
                        sb["rqb"][32:36, :], start=True, stop=True))
                deps = [guard.get(("rope", side, oc)), guard.get("rope_bufs")]
                b.waits("V", [d for d in deps if d] + [guard[("post", side, oc)]])
                s1 = b.inc("V", nc.vector.stream_shuffle(sb["rsw"][:, :], post_ap, PAIRSWAP))
                s2 = b.inc("V", nc.vector.tensor_mul(sb["rt1"][:, :], post_ap,
                                                     sb["tabs"][:, t0:t0 + TN2]))
                b.waits("V", [s1])
                s3 = b.inc("V", nc.vector.tensor_mul(sb["rt2"][:, :], sb["rsw"][:, :],
                                                     sb["tabs"][:, T + t0:T + t0 + TN2]))
                b.waits("V", [s2, s3])
                s4 = b.inc("V", nc.vector.tensor_add(sb["rt1"][:, :], sb["rt1"][:, :],
                                                     sb["rt2"][:, :]))
                b.waits("V", [s4, tbc])
                if side == "q":
                    for half in range(2):
                        h = 2 * oc + half
                        g, l = h // 4, h % 4
                        q4t = sb["q4a"] if g < 2 else sb["q4b"]
                        bpq = 64 * (g % 2)
                        for j in range(TN2 // 128):
                            qt = t0 // 128 + j
                            sv = b.inc("V", nc.vector.tensor_mul(
                                q4t[bpq:bpq + 64, 512 * qt + 128 * l: 512 * qt + 128 * (l + 1)],
                                sb["rt1"][64 * half:64 * half + 64, 128 * j:128 * (j + 1)],
                                bankR[64 * half:64 * half + 64, 128 * j:128 * (j + 1)]))
                else:
                    sv = b.inc("V", nc.vector.tensor_mul(
                        sb["kfin"][:, oc * T + t0: oc * T + t0 + TN2],
                        sb["rt1"][:, :], bankR[:, 0:TN2]))
                guard[("rope", side, oc)] = sv
                guard[("bR", oc % 2)] = sv
                guard["rope_bufs"] = sv
                guard["rqb_used"] = tbc
        guard["mm2b"] = tpost
    b.drain_dma()
    b.barrier()
    es3.close()
    es_qm.close()
    es2.close()
    es_tabs.close()
    if upto == "p1b":
        b.drain_dma()
        b.barrier()
        es_qkv.close()
        es.close()
        return nc

    es_oan = ExitStack()
    sb["oan"] = es_oan.enter_context(nc.sbuf_tensor("oan", [128, 8 * T], BF16, side="right"))
    es_p2a = ExitStack()
    sb["pt"] = es_p2a.enter_context(nc.sbuf_tensor("pt", [128, 2 * 512], BF16, side="right"))
    sb["o4"] = es_p2a.enter_context(nc.sbuf_tensor("o4", [128, 2 * 512], F32, side="right"))
    sb["lrb"] = es_p2a.enter_context(nc.sbuf_tensor("lrb", [128, 2 * 512], BF16, side="right"))
    sb["maskTs"] = es_p2a.enter_context(nc.sbuf_tensor("maskTs", [128, 512], F32, side="right"))
    tok_mT = b.dma("S", [(sb["maskTs"][:, :], dr["maskT"][:, :])])
    b.wait("V", tok_mT)
    NQT = T // 128
    ag = {}
    b.waits("A", [tok_r2])
    for i in range(NQT):
        for g in range(4):
            gp = 64 * (g % 2)
            q4t = sb["q4a"] if g < 2 else sb["q4b"]
            gsl = 2 * (i * 4 + g)
            psO = ps[2 + g % 2]
            b.wait("T", ag.get(("psO", g % 2)))
            for j in range(i + 1):
                psS = ps[j % 2]
                b.wait("T", ag.get(("psS", j % 2)))
                tmm = b.inc("T", nc.tensor.matmul(
                    psS[:, 0:512],
                    sb["kfin"][gp:gp + 64, (g // 2) * T + 128 * j: (g // 2) * T + 128 * (j + 1)],
                    q4t[gp:gp + 64, 512 * i: 512 * (i + 1)],
                    start=True, stop=True))
                expdep = tmm
                if j == i:
                    b.waits("V", [tmm])
                    expdep = b.inc("V", nc.vector.tensor_add(psS[:, 0:512], psS[:, 0:512],
                                                             sb["maskTs"][:, :]))
                b.waits("A", [expdep, ag.get(("pt", j % 2))])
                texp = b.inc("A", nc.scalar.activation(
                    sb["pt"][:, (j % 2) * 512:(j % 2 + 1) * 512], psS[:, 0:512],
                    AF.Exp, scale=0.125, bias=sb["negm"][:, 0:1]))
                ag[("psS", j % 2)] = texp
                b.waits("T", [texp])
                tpv = b.inc("T", nc.tensor.matmul(
                    psO[0:65, 0:512],
                    sb["vtm"][:, g * NJ * VW + j * VW: g * NJ * VW + (j + 1) * VW],
                    sb["pt"][:, (j % 2) * 512:(j % 2 + 1) * 512],
                    start=(j == 0), stop=(j == i)))
                ag[("pt", j % 2)] = tpv
            osl = (g % 2) * 512
            b.waits("A", [tpv, ag.get(("o4", g % 2))])
            tev = b.inc("A", nc.scalar.activation(sb["o4"][0:65, osl:osl + 512],
                                                  psO[0:65, 0:512], AF.Copy))
            ag[("psO", g % 2)] = tev
            b.waits("V", [tev])
            trc = b.inc("V", nc.vector.reciprocal(sb["o4"][64:65, osl:osl + 512],
                                                  sb["o4"][64:65, osl:osl + 512]))
            b.waits("V", [trc, ag.get(("lrb", g % 2))])
            tcb = b.inc("V", nc.vector.tensor_copy(sb["lrb"][64:65, osl:osl + 512],
                                                   sb["o4"][64:65, osl:osl + 512]))
            b.waits("T", [tcb, ag.get(("psR", g % 2))])
            tbr = b.inc("T", nc.tensor.matmul(ps[4 + g % 2][0:64, 0:512],
                                              sb["onesrow"][64:65, 0:64],
                                              sb["lrb"][64:65, osl:osl + 512],
                                              start=True, stop=True))
            b.waits("V", [tbr])
            for l in range(4):
                h = 4 * g + l
                tm = b.inc("V", nc.vector.tensor_mul(
                    sb["oan"][64 * (l % 2):64 * (l % 2) + 64,
                              (2 * g + l // 2) * T + 128 * i: (2 * g + l // 2) * T + 128 * (i + 1)],
                    sb["o4"][0:64, osl + 128 * l: osl + 128 * (l + 1)],
                    ps[4 + g % 2][0:64, 128 * l: 128 * (l + 1)]))
            ag[("psR", g % 2)] = tm
            ag[("o4", g % 2)] = tm
            ag[("lrb", g % 2)] = tbr
    b.drain_dma()
    b.barrier()
    es_p2a.close()
    es_qkv.close()
    if upto == "p2a":
        b.drain_dma()
        b.barrier()
        es_oan.close()
        es.close()
        return nc

    TNB = min(512, T)
    ntb = T // TNB
    es_p2bl = ExitStack()
    sb["woTb"] = es_p2bl.enter_context(nc.sbuf_tensor("woTb", [128, 8 * D], BF16))
    sb["ztile"] = es_p2bl.enter_context(nc.sbuf_tensor("ztile", [128, NCH * TNB], BF16))
    es_p2bt = ExitStack()
    for nm, shp, dt_ in (("xc", [128, 3 * TNB], F32), ("x1b", [128, 2 * TNB], F32),
                         ("zsq", [128, TNB], BF16), ("x1q", [128, TNB], BF16),
                         ("za", [128, TNB], F32), ("wocsb", [128, 8], BF16),
                         ("wocsf", [128, 8], F32)):
        sb[nm] = es_p2bt.enter_context(nc.sbuf_tensor(nm, shp, dt_, side="right"))
    wo_toks = []
    for g4 in range(2):
        prs = [(sb["woTb"][:, kc * D:(kc + 1) * D], dr["woT"][128 * kc:128 * kc + 128, :])
               for kc in range(4 * g4, 4 * g4 + 4)]
        wo_toks.append(b.dma("G", prs))
    for kc in range(8):
        b.wait("V", wo_toks[kc // 4])
        tw = b.inc("V", nc.vector.reduce_sum(sb["wocsf"][:, kc:kc + 1],
                                             sb["woTb"][:, kc * D:(kc + 1) * D],
                                             mybir.AxisListType.X))
    b.wait("V", tw)
    twb = b.inc("V", nc.vector.tensor_copy(sb["wocsb"][:, :], sb["wocsf"][:, :]))

    bg = {}
    for ti in range(ntb):
        t0 = ti * TNB
        b.waits("T", [twb, bg.get("sz_read")])
        for kc in range(8):
            ins = nc.tensor.matmul(ps[5][0:1, 0:TNB], sb["wocsb"][:, kc:kc + 1],
                                   sb["oan"][:, kc * T + t0: kc * T + t0 + TNB],
                                   start=(kc == 0), stop=(kc == 7))
        tsz = b.inc("T", ins)
        for oc in range(NCH):
            bank = ps[oc % 2]
            b.wait("T", bg.get(("pz", oc % 2)))
            for kc in range(8):
                ins = nc.tensor.matmul(
                    bank[:, 0:TNB],
                    sb["woTb"][:, kc * D + oc * 128: kc * D + (oc + 1) * 128],
                    sb["oan"][:, kc * T + t0: kc * T + t0 + TNB],
                    start=(kc == 0), stop=(kc == 7))
            tmm = b.inc("T", ins)
            b.waits("A", [tmm, bg.get(("zt", oc))])
            tze = b.inc("A", nc.scalar.activation(sb["ztile"][:, oc * TNB:(oc + 1) * TNB],
                                                  bank[:, 0:TNB], AF.Copy))
            b.waits("A", [bg.get("zsq_read")])
            tzq = b.inc("A", nc.scalar.activation(sb["zsq"][:, :], bank[:, 0:TNB], AF.Square))
            bg[("pz", oc % 2)] = tzq
            b.waits("T", [tzq])
            ins = nc.tensor.matmul(ps[4][0:1, 0:TNB], sb["onescol"][:, :], sb["zsq"][:, :],
                                   start=(oc == 0), stop=(oc == NCH - 1))
            tq2 = b.inc("T", ins)
            bg["zsq_read"] = tq2
            bg[("ze", oc)] = tze
        b.waits("A", [tsz, tq2])
        b.waits("V", [tsz, tq2])
        rt2b = ln_rows(TNB, ps[5][0:1, 0:TNB], ps[4][0:1, 0:TNB],
                       sfs(6, TNB), rlbs(0, TNB), rlbs(1, TNB),
                       dep=bg.get("rows_dep"), rbdep=bg.get("rlb_used"))
        bg["rows_dep"] = rt2b
        bg["sz_read"] = rt2b
        b.waits("T", [rt2b, bg.get("bc2_read")])
        tb1 = b.inc("T", nc.tensor.matmul(ps[2][:, 0:TNB], sb["onesrow"][0:1, :],
                                          rlbs(0, TNB), start=True, stop=True))
        tb2 = b.inc("T", nc.tensor.matmul(ps[3][:, 0:TNB], sb["onesrow"][0:1, :],
                                          rlbs(1, TNB), start=True, stop=True))
        bg["rlb_used"] = tb2
        xtoks = {}
        for oc in range(3):
            xtoks[oc] = b.dma("S", [(sb["xc"][:, (oc % 3) * TNB:(oc % 3 + 1) * TNB],
                                     dr["x_in"][128 * oc:128 * (oc + 1), t0:t0 + TNB])],
                              deps=[bg.get(("xc", oc % 3))])
        for oc in range(NCH):
            b.waits("V", [tb1, tb2, bg.get(("ze", oc)), bg.get("za_free")])
            tu = b.inc("V", nc.vector.tensor_mul(sb["za"][:, :],
                                                 sb["ztile"][:, oc * TNB:(oc + 1) * TNB],
                                                 ps[2][:, 0:TNB]))
            b.waits("V", [tu])
            tu = b.inc("V", nc.vector.tensor_sub(sb["za"][:, :], sb["za"][:, :], ps[3][:, 0:TNB]))
            x1sl = oc % 2
            b.waits("V", [tu, xtoks[oc], bg.get(("x1b", x1sl)), bg.get(("x1ba", x1sl))])
            tx1 = b.inc("V", nc.vector.tensor_add(sb["x1b"][:, x1sl * TNB:(x1sl + 1) * TNB],
                                                  sb["za"][:, :],
                                                  sb["xc"][:, (oc % 3) * TNB:(oc % 3 + 1) * TNB]))
            bg["za_free"] = tx1
            bg[("xc", oc % 3)] = tx1
            if oc + 3 < NCH:
                xtoks[oc + 3] = b.dma("S", [(sb["xc"][:, ((oc + 3) % 3) * TNB:((oc + 3) % 3 + 1) * TNB],
                                             dr["x_in"][128 * (oc + 3):128 * (oc + 4), t0:t0 + TNB])],
                                      deps=[bg.get(("xc", (oc + 3) % 3))])
            tws = b.dma("S", [(dr["x1s"][128 * oc:128 * (oc + 1), t0:t0 + TNB],
                               sb["x1b"][:, x1sl * TNB:(x1sl + 1) * TNB])], deps=[tx1])
            bg[("x1b", x1sl)] = tws
            b.waits("A", [tx1, bg.get("x1q_read")])
            txq = b.inc("A", nc.scalar.activation(sb["x1q"][:, :],
                                                  sb["x1b"][:, x1sl * TNB:(x1sl + 1) * TNB],
                                                  AF.Square))
            bg[("x1ba", x1sl)] = txq
            b.waits("T", [txq])
            ins = nc.tensor.matmul(ps[6][0:1, 0:TNB], sb["onescol"][:, :], sb["x1q"][:, :],
                                   start=(oc == 0), stop=(oc == NCH - 1))
            tx2 = b.inc("T", ins)
            bg["x1q_read"] = tx2
        bg["bc2_read"] = tx1
        b.waits("A", [tx2])
        b.waits("V", [tx2])
        rt3 = ln_rows(TNB, None, ps[6][0:1, 0:TNB],
                      m1row(t0, TNB), r3row(t0, TNB), c3row(t0, TNB),
                      dep=bg.get("rows_dep"))
        bg["rows_dep"] = rt3
    b.drain_dma()
    b.barrier()
    es_p2bt.close()
    es_oan.close()
    es_p2bl.close()
    if upto == "p2b":
        b.drain_dma()
        b.barrier()
        es.close()
        return nc

    TN3 = min(256, T)
    nt3 = T // TN3
    es_p3l = ExitStack()
    sb["upb"] = es_p3l.enter_context(nc.sbuf_tensor("upb", [128, 32 * 1024], BF16))
    sb["downb"] = es_p3l.enter_context(nc.sbuf_tensor("downb", [128, 128 * 256], BF16))
    es_p3r = ExitStack()
    for nm, shp, dt_ in (("x1r", [128, 6 * TN3], F32), ("ftile", [128, NCH * TN3], BF16),
                         ("hidb", [128, 2 * 8 * TN3], BF16), ("yh", [128, 2 * 2 * TN3], BF16),
                         ("ysc", [128, TN3], F32), ("fsq", [128, 2 * TN3], BF16),
                         ("outb", [128, 2 * TN3], F32)):
        sb[nm] = es_p3r.enter_context(nc.sbuf_tensor(nm, shp, dt_, side="right"))
    up_toks = []
    for h4 in range(4):
        prs = []
        for h in range(4 * h4, 4 * h4 + 4):
            for kc in range(2):
                prs.append((sb["upb"][:, (2 * h + kc) * 1024:(2 * h + kc + 1) * 1024],
                            dr["up_w"][h, 128 * kc:128 * (kc + 1), :]))
        up_toks.append(b.dma("G", prs))
    dn_toks = []
    for h4 in range(4):
        prs = []
        for h in range(4 * h4, 4 * h4 + 4):
            for mc in range(8):
                prs.append((sb["downb"][:, (8 * h + mc) * 256:(8 * h + mc + 1) * 256],
                            dr["down_w"][h, 128 * mc:128 * (mc + 1), :]))
        dn_toks.append(b.dma("G", prs))

    fg = {}
    out_toks = []
    for ti in range(nt3):
        t0 = ti * TN3
        xr = {"tok": {}, "cons": fg.get("xr_cons", {})}
        def x1_load(li):
            sl = li % 6
            c = li % 32
            xr["tok"][li] = b.dma("S", [(sb["x1r"][:, sl * TN3:(sl + 1) * TN3],
                                         dr["x1s"][128 * c:128 * (c + 1), t0:t0 + TN3])],
                                  deps=[d for d in (xr["cons"].get(sl),) if d])
        def x1_slot(li):
            sl = li % 6
            return sb["x1r"][:, sl * TN3:(sl + 1) * TN3]
        def x1_done(li, tok):
            xr["cons"][li % 6] = tok
            if li + 6 < 64:
                x1_load(li + 6)
        for li in range(6):
            x1_load(li)
        b.waits("T", [fg.get("bc3_read")])
        tb3 = b.inc("T", nc.tensor.matmul(ps[4][:, 0:TN3], sb["onesrow"][0:1, :],
                                          r3row(t0, TN3), start=True, stop=False))
        tb3 = b.inc("T", nc.tensor.matmul(ps[4][:, TN3:2 * TN3], sb["onesrow"][0:1, :],
                                          c3row(t0, TN3), start=False, stop=True))
        for h in range(NH):
            ysl = h % 2
            for ci, c in enumerate((2 * h, 2 * h + 1)):
                b.waits("V", [tb3, xr["tok"][c], fg.get("ysc_free"), fg.get(("yh", ysl))])
                tyu = b.inc("V", nc.vector.tensor_mul(sb["ysc"][:, :],
                                                      x1_slot(c), ps[4][:, 0:TN3]))
                b.waits("V", [tyu])
                ty = b.inc("V", nc.vector.tensor_sub(
                    sb["yh"][:, (2 * ysl + ci) * TN3:(2 * ysl + ci + 1) * TN3],
                    sb["ysc"][:, :], ps[4][:, TN3:2 * TN3]))
                fg["ysc_free"] = ty
                x1_done(c, tyu)
            hsl = h % 2
            b.wait("T", up_toks[h // 4])
            b.wait("T", ty)
            for mc in range(8):
                bank = ps[mc // 2 % 2]
                half = mc % 2
                if half == 0:
                    b.wait("T", fg.get(("hid", mc // 2 % 2)))
                for kc in range(2):
                    ins = nc.tensor.matmul(
                        bank[:, half * TN3:(half + 1) * TN3],
                        sb["upb"][:, (2 * h + kc) * 1024 + mc * 128:(2 * h + kc) * 1024 + (mc + 1) * 128],
                        sb["yh"][:, (2 * ysl + kc) * TN3:(2 * ysl + kc + 1) * TN3],
                        start=(half == 0 and kc == 0), stop=(half == 1 and kc == 1))
                if half == 1:
                    tup = b.inc("T", ins)
                    b.waits("A", [tup, fg.get(("hidb", hsl, mc // 2))])
                    tg_ = b.inc("A", nc.scalar.activation(
                        sb["hidb"][:, hsl * 8 * TN3 + (mc - 1) * TN3: hsl * 8 * TN3 + (mc + 1) * TN3],
                        bank[:, 0:2 * TN3], AF.Gelu))
                    fg[("hid", mc // 2 % 2)] = tg_
                    fg[("gel", mc // 2)] = tg_
            fg[("yh", ysl)] = tup
            b.wait("T", dn_toks[h // 4])
            b.wait("T", fg.get("psF_read"))
            first = True
            for dc in range(2):
                for mc in range(8):
                    b.wait("T", fg[("gel", mc // 2)])
                    ins = nc.tensor.matmul(
                        ps[2][:, dc * TN3:(dc + 1) * TN3],
                        sb["downb"][:, (8 * h + mc) * 256 + dc * 128:(8 * h + mc) * 256 + (dc + 1) * 128],
                        sb["hidb"][:, hsl * 8 * TN3 + mc * TN3: hsl * 8 * TN3 + (mc + 1) * TN3],
                        start=first, stop=(dc == 1 and mc == 7))
                    first = False
            tdn = b.inc("T", ins)
            fg[("hidb", hsl, 0)] = tdn
            fg[("hidb", hsl, 1)] = tdn
            fg[("hidb", hsl, 2)] = tdn
            fg[("hidb", hsl, 3)] = tdn
            b.waits("A", [tdn, fg.get(("ft", h))])
            tfe = b.inc("A", nc.scalar.activation(sb["ftile"][:, 2 * h * TN3:(2 * h + 2) * TN3],
                                                  ps[2][:, 0:2 * TN3], AF.Copy))
            b.waits("A", [fg.get("fsq_read")])
            tfq = b.inc("A", nc.scalar.activation(sb["fsq"][:, :], ps[2][:, 0:2 * TN3], AF.Square))
            fg["psF_read"] = tfq
            fg[("fe", h)] = tfe
            b.waits("T", [tfe, tfq])
            ins = nc.tensor.matmul(ps[5][0:1, 0:2 * TN3], sb["onescol"][:, :],
                                   sb["ftile"][:, 2 * h * TN3:(2 * h + 2) * TN3],
                                   start=(h == 0), stop=(h == NH - 1))
            tS1 = b.inc("T", ins)
            ins = nc.tensor.matmul(ps[6][0:1, 0:2 * TN3], sb["onescol"][:, :], sb["fsq"][:, :],
                                   start=(h == 0), stop=(h == NH - 1))
            tS2 = b.inc("T", ins)
            fg["fsq_read"] = tS2
        fg["x1t_read"] = tyu
        b.waits("A", [tS1, tS2, fg.get("rows_dep4")])
        SF4, SF5 = 4 * 512, 5 * 512
        th1 = b.inc("A", nc.scalar.activation(sb["smallf"][0:1, SF4:SF4 + 2 * TN3],
                                              ps[5][0:1, 0:2 * TN3], AF.Copy))
        th2 = b.inc("A", nc.scalar.activation(sb["smallf"][0:1, SF5:SF5 + 2 * TN3],
                                              ps[6][0:1, 0:2 * TN3], AF.Copy))
        b.waits("V", [th1, th2, fg.get("rows_dep4")])
        th3 = b.inc("V", nc.vector.tensor_add(sb["smallf"][0:1, SF4:SF4 + TN3],
                                              sb["smallf"][0:1, SF4:SF4 + TN3],
                                              sb["smallf"][0:1, SF4 + TN3:SF4 + 2 * TN3]))
        th4 = b.inc("V", nc.vector.tensor_add(sb["smallf"][0:1, SF5:SF5 + TN3],
                                              sb["smallf"][0:1, SF5:SF5 + TN3],
                                              sb["smallf"][0:1, SF5 + TN3:SF5 + 2 * TN3]))
        b.waits("A", [th3, th4])
        b.waits("V", [th3, th4])
        rt4 = ln_rows(TN3, sb["smallf"][0:1, SF4:SF4 + TN3], sb["smallf"][0:1, SF5:SF5 + TN3],
                      sfs(6, TN3), rlbs(0, TN3), rlbs(1, TN3),
                      dep=fg.get("rows_dep4"), rbdep=fg.get("rlb_used"))
        fg["rows_dep4"] = rt4
        fg["sums_read"] = rt4
        b.waits("T", [rt4, fg.get("bc4_read")])
        tb4 = b.inc("T", nc.tensor.matmul(ps[3][:, 0:TN3], sb["onesrow"][0:1, :],
                                          rlbs(0, TN3), start=True, stop=False))
        tb4 = b.inc("T", nc.tensor.matmul(ps[3][:, TN3:2 * TN3], sb["onesrow"][0:1, :],
                                          rlbs(1, TN3), start=False, stop=True))
        fg["rlb_used"] = tb4
        for oc in range(NCH):
            osl2 = oc % 2
            b.waits("V", [tb4, fg[("fe", oc // 2)], fg.get("ysc_free"), fg.get(("outb", osl2))])
            tu = b.inc("V", nc.vector.tensor_mul(sb["ysc"][:, :],
                                                 sb["ftile"][:, oc * TN3:(oc + 1) * TN3],
                                                 ps[3][:, 0:TN3]))
            b.waits("V", [tu])
            tu = b.inc("V", nc.vector.tensor_sub(sb["ysc"][:, :], sb["ysc"][:, :],
                                                 ps[3][:, TN3:2 * TN3]))
            b.waits("V", [tu, xr["tok"][32 + oc]])
            to = b.inc("V", nc.vector.tensor_add(sb["outb"][:, osl2 * TN3:(osl2 + 1) * TN3],
                                                 sb["ysc"][:, :], x1_slot(32 + oc)))
            fg["ysc_free"] = to
            x1_done(32 + oc, to)
            tso = b.dma("S", [(dr["y_out"][128 * oc:128 * (oc + 1), t0:t0 + TN3],
                               sb["outb"][:, osl2 * TN3:(osl2 + 1) * TN3])], deps=[to])
            fg[("outb", osl2)] = tso
            out_toks.append(tso)
        fg["xr_cons"] = xr["cons"]
        fg["bc4_read"] = tu
        fg["bc3_read"] = ty
    b.drain_dma()
    b.barrier()
    es_p3r.close()
    es_p3l.close()
    es.close()
    return nc


def kernel(**inputs):
    in_maps, perm, iperm, B, T = host_stage(inputs)
    nc = build_program(T)
    res = run_bass_kernel_spmd(nc, in_maps, core_ids=list(range(B)))
    return assemble_output(res.results, perm, B, T)


# version 07
# speedup vs baseline: 1.1227x; 1.1227x over previous
import math
from contextlib import ExitStack

import numpy as np
import concourse.bass as bass
import concourse.mybir as mybir
from concourse.bass_utils import run_bass_kernel_spmd

F32 = mybir.dt.float32
BF16 = mybir.dt.bfloat16
I32 = mybir.dt.int32
AF = mybir.ActivationFunctionType
ALU = mybir.AluOpType

D = 4096
NH = 16
NKV = 4
HD = 64
LAT = NH * HD
KVD = NKV * HD
QKV = LAT + 2 * KVD
NCH = D // 128
EPS = 1e-5
NEG_BIG = -3.0e38

def _split(x):
    a = float(np.float32(x))
    b = float(np.float32(x - a))
    return a, b

LN1E4_S1, LN1E4_S2 = _split(-math.log(10000.0) / 32.0)
PI1, _PI2 = _split(math.pi)
TWOPI1, TWOPI2 = _split(2.0 * math.pi)
INV2PI1 = float(np.float32(1.0 / TWOPI1))
HPI1, HPI2 = _split(math.pi / 2.0)


class Bld:

    ENG = ("T", "V", "A", "S", "G")

    def __init__(self, nc, es):
        self.nc = nc
        self.es = es
        self.eng = {"T": nc.tensor, "V": nc.vector, "A": nc.scalar,
                    "S": nc.sync, "G": nc.gpsimd}
        self.sems = {}
        self.counts = {}
        self.wmark = {e: {} for e in self.ENG}
        for e in self.ENG:
            self._mksem("p" + e)
        self.rings = {}
        for q, nslots in (("S", 8), ("G", 8)):
            slots = []
            for i in range(nslots):
                nm = f"dq{q}{i}"
                self._mksem(nm)
                slots.append(nm)
            self.rings[q] = {"slots": slots, "next": 0, "intok": {}}

    def _mksem(self, name):
        self.sems[name] = self.es.enter_context(self.nc.semaphore(name))
        self.counts[name] = 0

    def wait(self, e, tok):
        if tok is None:
            return
        sem, val = tok
        if self.wmark[e].get(sem, 0) >= val:
            return
        self.eng[e].wait_ge(self.sems[sem], val)
        self.wmark[e][sem] = val

    def waits(self, e, toks):
        for t in toks:
            self.wait(e, t)

    def tok(self, e):
        return ("p" + e, self.counts["p" + e])

    def inc(self, e, instr):
        sem = "p" + e
        self.counts[sem] += 1
        instr.then_inc(self.sems[sem], 1)
        return (sem, self.counts[sem])

    def dma(self, q, pairs, deps=()):
        ring = self.rings[q]
        slot = ring["slots"][ring["next"]]
        ring["next"] = (ring["next"] + 1) % len(ring["slots"])
        if self.counts[slot]:
            self.wait(q, (slot, self.counts[slot]))
        self.waits(q, deps)
        for p in pairs:
            kw = p[2] if len(p) > 2 else {}
            self.eng[q].dma_start(p[0], p[1], **kw).then_inc(self.sems[slot], 16)
            self.counts[slot] += 16
        return (slot, self.counts[slot])

    def barrier(self, engines="TVASG"):
        for e in engines:
            self.wait(e, ("p" + e, self.counts["p" + e]))
        toks = [self.inc(e, self.eng[e].nop()) for e in engines]
        for e in engines:
            for t in toks:
                self.wait(e, t)

    def drain_dma(self):
        for q, ring in self.rings.items():
            for slot in ring["slots"]:
                if self.counts[slot]:
                    self.wait(q, (slot, self.counts[slot]))


def host_stage(inp):
    perm = np.asarray(inp["L0_ffn_perm"])
    iperm = np.argsort(perm)
    x = np.asarray(inp["L0_attn_x"])
    B, T, _ = x.shape

    shared = {}
    shared["wqkvT"] = np.ascontiguousarray(np.asarray(inp["L0_attn_w_qkv"]).T[perm])
    shared["woT"] = np.ascontiguousarray(np.asarray(inp["L0_attn_w_o"])[perm].T)

    qc1 = np.asarray(inp["L0_attn_qc1_w"])
    qc1bd = np.zeros((3, 8, 128, 128), dtype=np.float32)
    for tap in range(3):
        for c in range(8):
            for gg in range(2):
                qc1bd[tap, c, 64 * gg:64 * gg + 64, 64 * gg:64 * gg + 64] = \
                    qc1[128 * c + 64 * gg:128 * c + 64 * gg + 64, :, tap].T
    shared["qc1bd"] = qc1bd
    shared["qc2T"] = np.ascontiguousarray(np.asarray(inp["L0_attn_qc2_w"]).transpose(2, 1, 0))
    kc1 = np.asarray(inp["L0_attn_kc1_w"])
    kc1bd = np.zeros((3, 2, 128, 128), dtype=np.float32)
    for tap in range(3):
        for c in range(2):
            for gg in range(2):
                kc1bd[tap, c, 64 * gg:64 * gg + 64, 64 * gg:64 * gg + 64] = \
                    kc1[128 * c + 64 * gg:128 * c + 64 * gg + 64, :, tap].T
    shared["kc1bd"] = kc1bd
    shared["kc2T"] = np.ascontiguousarray(np.asarray(inp["L0_attn_kc2_w"]).transpose(2, 1, 0))

    shared["qc1b"] = np.ascontiguousarray(np.asarray(inp["L0_attn_qc1_b"]).reshape(8, 128).T)
    shared["qc2b"] = np.ascontiguousarray(np.asarray(inp["L0_attn_qc2_b"]).reshape(8, 128).T)
    shared["kc1b"] = np.ascontiguousarray(np.asarray(inp["L0_attn_kc1_b"]).reshape(2, 128).T)
    shared["kc2b"] = np.ascontiguousarray(np.asarray(inp["L0_attn_kc2_b"]).reshape(2, 128).T)
    shared["ktemp"] = np.asarray(inp["L0_attn_key_temp"]).reshape(1, 1).astype(np.float32)
    shared["up_w"] = np.ascontiguousarray(np.asarray(inp["L0_ffn_up"]))
    shared["down_w"] = np.ascontiguousarray(np.asarray(inp["L0_ffn_down"]))

    shared["eye128"] = np.eye(128, dtype=np.float32)
    kk = np.arange(128)[:, None]
    qq = np.tile(np.arange(128)[None, :], (1, 4)).reshape(1, 512)
    shared["maskT"] = np.where(kk > qq, np.float32(NEG_BIG), np.float32(0.0)).astype(np.float32)
    dl = np.arange(128) % 64
    shared["msk16"] = (dl[:, None] == np.arange(64)[None, :]).astype(np.float32) / 32.0
    shared["msk4"] = (dl[:, None] == np.arange(64)[None, :]).astype(np.float32) / 8.0
    sel16 = np.zeros((8, 128, 16), dtype=np.float32)
    bc16 = np.zeros((8, 16, 128), dtype=np.float32)
    for c in range(8):
        for p in range(128):
            h = 2 * c + p // 64
            sel16[c, p, h] = 1.0
            bc16[c, h, p] = 1.0
    pvec = np.arange(128)
    ivec = np.zeros((128, 2), dtype=np.float32)
    ivec[:, 0] = (pvec % 64) // 2
    ivec[:, 1] = np.where(pvec % 2 == 0, -1.0, 1.0)
    shared["ivec"] = ivec
    shared["sel16"] = np.ascontiguousarray(sel16.transpose(1, 0, 2)).reshape(128, 8 * 16)
    shared["bc16"] = np.ascontiguousarray(bc16.transpose(1, 0, 2)).reshape(16, 8 * 128)
    sel4 = np.zeros((2, 128, 4), dtype=np.float32)
    bc4 = np.zeros((2, 4, 128), dtype=np.float32)
    for c in range(2):
        for p in range(128):
            h = 2 * c + p // 64
            sel4[c, p, h] = 1.0
            bc4[c, h, p] = 1.0
    shared["sel4"] = np.ascontiguousarray(sel4.transpose(1, 0, 2)).reshape(128, 2 * 4)
    bc4p = np.zeros((36, 2 * 128), dtype=np.float32)
    bc4p[32:36] = np.ascontiguousarray(bc4.transpose(1, 0, 2)).reshape(4, 2 * 128)
    shared["bc4"] = bc4p

    in_maps = []
    for b in range(B):
        m = dict(shared)
        m["x_in"] = np.ascontiguousarray(x[b].T[perm])
        in_maps.append(m)
    return in_maps, perm, iperm, B, T


def assemble_output(results, perm, B, T):
    out = np.empty((B, T, D), dtype=np.float32)
    for b in range(B):
        y = np.asarray(results[b]["y_out"])
        out[b][:, perm] = y.T
    return out


def build_program(T, upto="p3"):
    nc = bass.Bass("TRN2", target_bir_lowering=False)
    es = ExitStack()

    dr = {}
    dr["x_in"] = nc.dram_tensor("x_in", [D, T], F32, kind="ExternalInput")
    dr["wqkvT"] = nc.dram_tensor("wqkvT", [D, QKV], F32, kind="ExternalInput")
    dr["woT"] = nc.dram_tensor("woT", [LAT, D], F32, kind="ExternalInput")
    dr["qc1bd"] = nc.dram_tensor("qc1bd", [3, 8, 128, 128], F32, kind="ExternalInput")
    dr["qc2T"] = nc.dram_tensor("qc2T", [3, 1024, 1024], F32, kind="ExternalInput")
    dr["kc1bd"] = nc.dram_tensor("kc1bd", [3, 2, 128, 128], F32, kind="ExternalInput")
    dr["kc2T"] = nc.dram_tensor("kc2T", [3, 256, 256], F32, kind="ExternalInput")
    dr["qc1b"] = nc.dram_tensor("qc1b", [128, 8], F32, kind="ExternalInput")
    dr["qc2b"] = nc.dram_tensor("qc2b", [128, 8], F32, kind="ExternalInput")
    dr["kc1b"] = nc.dram_tensor("kc1b", [128, 2], F32, kind="ExternalInput")
    dr["kc2b"] = nc.dram_tensor("kc2b", [128, 2], F32, kind="ExternalInput")
    dr["ktemp"] = nc.dram_tensor("ktemp", [1, 1], F32, kind="ExternalInput")
    dr["up_w"] = nc.dram_tensor("up_w", [16, 256, 1024], F32, kind="ExternalInput")
    dr["down_w"] = nc.dram_tensor("down_w", [16, 1024, 256], F32, kind="ExternalInput")
    dr["eye128"] = nc.dram_tensor("eye128", [128, 128], F32, kind="ExternalInput")
    dr["maskT"] = nc.dram_tensor("maskT", [128, 512], F32, kind="ExternalInput")
    dr["msk16"] = nc.dram_tensor("msk16", [128, 64], F32, kind="ExternalInput")
    dr["msk4"] = nc.dram_tensor("msk4", [128, 64], F32, kind="ExternalInput")
    dr["ivec"] = nc.dram_tensor("ivec", [128, 2], F32, kind="ExternalInput")
    dr["sel16"] = nc.dram_tensor("sel16", [128, 8 * 16], F32, kind="ExternalInput")
    dr["bc16"] = nc.dram_tensor("bc16", [16, 8 * 128], F32, kind="ExternalInput")
    dr["sel4"] = nc.dram_tensor("sel4", [128, 2 * 4], F32, kind="ExternalInput")
    dr["bc4"] = nc.dram_tensor("bc4", [36, 2 * 128], F32, kind="ExternalInput")
    dr["y_out"] = nc.dram_tensor("y_out", [D, T], F32, kind="ExternalOutput")
    dr["x1s"] = nc.dram_tensor("x1s", [D, T], F32)
    dr["qrawD"] = nc.dram_tensor("qrawD", [128, 8 * (T + 2)], BF16)
    dr["krawD"] = nc.dram_tensor("krawD", [128, 2 * (T + 2)], BF16)
    dr["rowscr"] = nc.dram_tensor("rowscr", [1, max(T, 2048)], F32)

    b = Bld(nc, es)
    blk = es.enter_context(nc.Block())

    PT = T + 2
    sb = {}
    def alloc(name, shape, dtype):
        sb[name] = es.enter_context(nc.sbuf_tensor(name, shape, dtype))
        return sb[name]

    es_qm = ExitStack()
    sb["qmkm"] = es_qm.enter_context(nc.sbuf_tensor("qmkm", [128, T], F32, side="right"))
    es_raw = ExitStack()
    sb["vfm"] = es_raw.enter_context(nc.sbuf_tensor("vfm", [128, 2 * T], BF16, side="right"))
    sb["zrow"] = es_raw.enter_context(nc.sbuf_tensor("zrow", [128, 16], BF16, side="right"))
    alloc("rows", [128, T], F32)
    def m1row(t0, TN):
        return sb["rows"][0:1, t0:t0 + TN]
    alloc("rlb", [128, 2 * 512], BF16)
    def rlbs(i, TN):
        return sb["rlb"][0:1, i * 512:i * 512 + TN]
    alloc("r3c3b", [128, 2 * T], BF16)
    def r3row(t0, TN):
        return sb["r3c3b"][0:1, t0:t0 + TN]
    def c3row(t0, TN):
        return sb["r3c3b"][0:1, T + t0:T + t0 + TN]
    alloc("eyeb", [128, 128], BF16)
    alloc("msk16b", [128, 64], BF16)
    alloc("msk4b", [128, 64], BF16)
    alloc("sel16b", [128, 8 * 16], BF16)
    alloc("bc16b", [16, 8 * 128], BF16)
    alloc("sel4b", [128, 2 * 4], BF16)
    alloc("bc4b", [36, 2 * 128], BF16)
    alloc("biascols", [128, 20], F32)
    alloc("onescol", [128, 1], BF16)
    alloc("onesrow", [128, 128], BF16)
    alloc("negm", [128, 1], F32)
    alloc("cc", [128, 8], F32)
    alloc("ktcol", [128, 1], F32)
    alloc("smallf", [128, 8 * 512], F32)
    def sfs(i, TN):
        return sb["smallf"][0:1, i * 512:i * 512 + TN]

    ps = []
    for i in range(7):
        ps.append(es.enter_context(nc.psum_tensor(f"pb{i}", [128, 512], F32)))
    psb = es.enter_context(nc.psum_tensor("pbT", [128, 1024], BF16))

    tok_c = b.dma("S", [
        (sb["biascols"][:, 0:8], dr["qc1b"][:, :]),
        (sb["biascols"][:, 8:16], dr["qc2b"][:, :]),
        (sb["biascols"][:, 16:18], dr["kc1b"][:, :]),
        (sb["biascols"][:, 18:20], dr["kc2b"][:, :]),
        (sb["smallf"][0:1, 7 * 512:7 * 512 + 1], dr["ktemp"][:, :]),
        (sb["cc"][:, 5:6], dr["ivec"][:, 0:1], dict(allow_slow_non_contiguous=True)),
        (sb["cc"][:, 6:7], dr["ivec"][:, 1:2], dict(allow_slow_non_contiguous=True)),
    ])
    tok_cb = b.dma("G", [
        (sb["eyeb"][:, :], dr["eye128"][:, :]),
        (sb["msk16b"][:, :], dr["msk16"][:, :]),
        (sb["msk4b"][:, :], dr["msk4"][:, :]),
        (sb["sel16b"][:, :], dr["sel16"][:, :]),
        (sb["bc16b"][:, :], dr["bc16"][:, :]),
        (sb["sel4b"][:, :], dr["sel4"][:, :]),
        (sb["bc4b"][:, :], dr["bc4"][:, :]),
    ])
    b.wait("V", tok_c)
    tv = b.inc("V", nc.vector.memset(sb["onescol"][:, :], 1.0))
    tv = b.inc("V", nc.vector.memset(sb["onesrow"][:, :], 1.0))
    tcc = b.inc("V", nc.vector.memset(sb["cc"][:, 0:1], EPS))
    tcc = b.inc("V", nc.vector.memset(sb["cc"][:, 1:2], HPI1))
    b.waits("A", [tcc])
    ta = b.inc("A", nc.scalar.activation(sb["smallf"][0:1, 7 * 512 + 1:7 * 512 + 2],
                                         sb["smallf"][0:1, 7 * 512:7 * 512 + 1],
                                         AF.Abs, scale=0.125))
    b.wait("A", ta)
    ta = b.inc("A", nc.scalar.activation(sb["smallf"][0:1, 7 * 512 + 2:7 * 512 + 3],
                                         sb["smallf"][0:1, 7 * 512 + 1:7 * 512 + 2],
                                         AF.Copy, scale=-1.0))
    tok_r = b.dma("S", [(dr["rowscr"][0:1, 0:2],
                         sb["smallf"][0:1, 7 * 512:7 * 512 + 3:2],
                         dict(allow_slow_non_contiguous=True))], deps=[ta])
    tok_r2 = b.dma("S", [
        (sb["negm"][:, 0:1], bass.AP(dr["rowscr"], 1, [[0, 128], [1, 1]])),
        (sb["ktcol"][32:36, 0:1], bass.AP(dr["rowscr"], 0, [[0, 4], [1, 1]])),
    ], deps=[tok_r])

    b.wait("V", tok_cb)
    tv0 = b.inc("V", nc.vector.memset(sb["zrow"][:, :], 0.0))
    tvv = b.inc("V", nc.vector.memset(sb["vfm"][:, T:T + 1], 0.0))
    tok_zp = b.dma("S", [
        (dr["qrawD"].ap().rearrange("p (c t) -> p c t", c=8)[:, :, 0:2],
         sb["zrow"][:, 0:16].rearrange("p (c t) -> p c t", c=8)),
        (dr["krawD"].ap().rearrange("p (c t) -> p c t", c=2)[:, :, 0:2],
         sb["zrow"][:, 0:4].rearrange("p (c t) -> p c t", c=2)),
    ], deps=[tv0])

    if upto == "p0":
        b.drain_dma()
        b.barrier()
        es_raw.close()
        es_qm.close()
        es.close()
        return nc
    TN1 = min(256, T)
    nt1 = T // TN1
    es1 = ExitStack()
    def alloc1(name, shape, dtype):
        sb[name] = es1.enter_context(nc.sbuf_tensor(name, shape, dtype, side="right"))
        return sb[name]
    alloc1("wqkvb", [128, NCH * QKV], BF16)
    alloc1("x32", [128, (NCH // 2) * TN1], F32)
    alloc1("xb", [128, NCH * TN1], BF16)
    alloc1("xsq", [128, 4 * TN1], BF16)
    alloc1("r1s", [128, 2 * TN1], F32)
    alloc1("uevac", [128, 2 * TN1], F32)
    alloc1("stg", [128, 3 * TN1], BF16)
    alloc1("wsumb", [1, QKV], BF16)

    wq_toks = []
    for g in range(4):
        prs = []
        for kc in range(8 * g, 8 * g + 8):
            prs.append((sb["wqkvb"][:, kc * QKV:(kc + 1) * QKV],
                        dr["wqkvT"][128 * kc:128 * kc + 128, :]))
        wq_toks.append(b.dma("G", prs))

    b.wait("T", tv)
    for seg in range(3):
        for kc in range(NCH):
            b.wait("T", wq_toks[kc // 8])
            ins = nc.tensor.matmul(ps[6][0:1, 0:512], sb["onescol"][:, :],
                                   sb["wqkvb"][:, kc * QKV + seg * 512: kc * QKV + (seg + 1) * 512],
                                   start=(kc == 0), stop=(kc == NCH - 1))
        tt = b.inc("T", ins)
        b.wait("V", tt)
        tv = b.inc("V", nc.vector.tensor_copy(sb["wsumb"][0:1, seg * 512:(seg + 1) * 512],
                                              ps[6][0:1, 0:512]))
        b.wait("T", tv)

    if upto == "p1a_w":
        b.drain_dma()
        b.barrier()
        es1.close()
        es_raw.close()
        es_qm.close()
        es.close()
        return nc

    def ln_rows(TN, ps_sx, ps_sx2, m_ap, rb_r_ap, rb_c_ap, dep=None, rbdep=None):
        b.wait("A", dep)
        if ps_sx is not None:
            ta = b.inc("A", nc.scalar.activation(m_ap, ps_sx, AF.Copy, scale=1.0 / D))
            b.wait("A", ta)
        ta = b.inc("A", nc.scalar.activation(sfs(0, TN), ps_sx2, AF.Copy, scale=1.0 / D))
        ta2 = b.inc("A", nc.scalar.activation(sfs(1, TN), m_ap, AF.Square))
        b.waits("V", [ta, ta2, dep])
        tv = b.inc("V", nc.vector.tensor_sub(sfs(0, TN), sfs(0, TN), sfs(1, TN)))
        b.wait("A", tv)
        ta = b.inc("A", nc.scalar.activation(sfs(2, TN), sfs(0, TN), AF.Sqrt,
                                             bias=sb["cc"][0:1, 0:1]))
        b.wait("V", ta)
        tv = b.inc("V", nc.vector.reciprocal(sfs(2, TN), sfs(2, TN)))
        b.wait("V", tv)
        tv = b.inc("V", nc.vector.tensor_mul(sfs(3, TN), m_ap, sfs(2, TN)))
        b.waits("V", [tv, rbdep])
        tv = b.inc("V", nc.vector.tensor_copy(rb_r_ap, sfs(2, TN)))
        tv = b.inc("V", nc.vector.tensor_copy(rb_c_ap, sfs(3, TN)))
        return tv

    cast_tok = [None, None]
    last_evac = {}

    for t in wq_toks:
        b.wait("T", t)

    HN = NCH // 2
    mm_tok = None
    for ti in range(nt1):
        t0 = ti * TN1
        for hf in range(2):
            prs = []
            for kc in range(HN * hf, HN * (hf + 1)):
                prs.append((sb["x32"][:, (kc - HN * hf) * TN1:(kc - HN * hf + 1) * TN1],
                            dr["x_in"][128 * kc:128 * kc + 128, t0:t0 + TN1]))
            x_tok = b.dma("S", prs, deps=[d for d in (cast_tok[1 - hf],) if d])
            b.wait("V", x_tok)
            if mm_tok is not None:
                b.wait("V", mm_tok)
            cast_tok[hf] = b.inc("V", nc.vector.tensor_copy(
                sb["xb"][:, hf * HN * TN1:(hf + 1) * HN * TN1], sb["x32"][:, :]))

        b.waits("A", [cast_tok[0], cast_tok[1]])
        if ti > 0:
            b.wait("T", last_evac["rows_done"])
        tsq = None
        for gq in range(8):
            if tsq is not None:
                b.wait("A", tsq)
            ta = b.inc("A", nc.scalar.activation(
                sb["xsq"][:, :], sb["xb"][:, gq * 4 * TN1:(gq + 1) * 4 * TN1], AF.Square))
            b.wait("T", ta)
            b.waits("T", [cast_tok[0], cast_tok[1]])
            for c4 in range(4):
                kc = gq * 4 + c4
                ins = nc.tensor.matmul(ps[5][0:1, 0:TN1], sb["onescol"][:, :],
                                       sb["xb"][:, kc * TN1:(kc + 1) * TN1],
                                       start=(kc == 0), stop=False)
                ins = nc.tensor.matmul(ps[5][0:1, TN1:2 * TN1], sb["onescol"][:, :],
                                       sb["xsq"][:, c4 * TN1:(c4 + 1) * TN1],
                                       start=False, stop=(kc == NCH - 1))
            tsq = b.inc("T", ins)
        stats_tok = tsq
        if upto == "p1a_s":
            break
        b.wait("A", stats_tok)
        b.wait("V", stats_tok)
        rows_tok = ln_rows(TN1, ps[5][0:1, 0:TN1], ps[5][0:1, TN1:2 * TN1],
                           m1row(t0, TN1), rlbs(0, TN1), rlbs(1, TN1),
                           dep=last_evac.get("rows_done"),
                           rbdep=last_evac.get("rlb_used"))
        last_evac["rows_done"] = rows_tok

        if upto == "p1a_r":
            break
        b.wait("T", rows_tok)
        tt = b.inc("T", nc.tensor.matmul(ps[4][:, 0:TN1], sb["onesrow"][0:1, :],
                                         rlbs(0, TN1), start=True, stop=True))
        b.wait("V", tt)
        if last_evac.get("r1s_free") is not None:
            b.wait("V", last_evac["r1s_free"])
        par = ti % 2
        r1s_tok = b.inc("V", nc.vector.tensor_copy(sb["r1s"][:, par * TN1:(par + 1) * TN1],
                                                   ps[4][:, 0:TN1]))
        b.wait("T", r1s_tok)

        if upto == "p1a_b":
            break
        for oc in range(12):
            pq = ps[oc % 2]
            pc = ps[2 + oc % 2]
            if ("pq", oc % 2) in last_evac:
                b.wait("T", last_evac[("pq", oc % 2)])
            for kc in range(NCH):
                ins = nc.tensor.matmul(pq[:, 0:TN1],
                                       sb["wqkvb"][:, kc * QKV + oc * 128: kc * QKV + (oc + 1) * 128],
                                       sb["xb"][:, kc * TN1:(kc + 1) * TN1],
                                       start=(kc == 0), stop=(kc == NCH - 1))
            mm_tok = b.inc("T", ins)
            if ("pc", oc % 2) in last_evac:
                b.wait("T", last_evac[("pc", oc % 2)])
            tt2 = b.inc("T", nc.tensor.matmul(pc[:, 0:TN1],
                                              sb["wsumb"][0:1, oc * 128:(oc + 1) * 128],
                                              rlbs(1, TN1),
                                              start=True, stop=True))
            last_evac["rlb_used"] = tt2
            b.waits("V", [mm_tok, tt2, r1s_tok])
            epar = oc % 2
            if ("ue", epar) in last_evac:
                b.wait("V", last_evac[("ue", epar)])
            tv1 = b.inc("V", nc.vector.tensor_mul(
                sb["uevac"][:, epar * TN1:(epar + 1) * TN1],
                sb["r1s"][:, par * TN1:(par + 1) * TN1], pq[:, 0:TN1]))
            last_evac[("pq", oc % 2)] = tv1
            b.wait("V", tv1)
            if oc < 10:
                ss = oc % 3
                b.waits("V", [last_evac.get(("stgd", ss)), last_evac.get(("stgm", ss))])
                dst = sb["stg"][:, ss * TN1:(ss + 1) * TN1]
            elif oc == 10:
                dst = sb["vfm"][:, t0:t0 + TN1]
            else:
                hi = min(t0 + 1 + TN1, T)
                dst = sb["vfm"][:, T + t0 + 1: T + hi]
            if oc == 11:
                w = hi - (t0 + 1)
                tv2 = b.inc("V", nc.vector.tensor_sub(
                    dst, sb["uevac"][:, epar * TN1: epar * TN1 + w], pc[:, 0:w]))
            else:
                tv2 = b.inc("V", nc.vector.tensor_sub(
                    dst, sb["uevac"][:, epar * TN1:(epar + 1) * TN1], pc[:, 0:TN1]))
            last_evac[("pc", oc % 2)] = tv2
            last_evac[("ue", epar)] = tv2
            if oc < 10:
                if oc < 8:
                    ddst = dr["qrawD"][:, oc * PT + 2 + t0: oc * PT + 2 + t0 + TN1]
                else:
                    ddst = dr["krawD"][:, (oc - 8) * PT + 2 + t0: (oc - 8) * PT + 2 + t0 + TN1]
                last_evac[("stgd", ss)] = b.dma("S", [(ddst, dst)], deps=[tv2])
                b.wait("T", tv2)
                if oc < 8:
                    tmn = b.inc("T", nc.tensor.matmul(ps[4][0:64, 0:TN1], sb["msk16b"][:, :],
                                                      dst, start=(oc == 0), stop=(oc == 7)))
                else:
                    tmn = b.inc("T", nc.tensor.matmul(ps[6][0:64, 0:TN1], sb["msk4b"][:, :],
                                                      dst, start=(oc == 8), stop=(oc == 9)))
                last_evac[("stgm", ss)] = tmn
                if oc == 7:
                    tqm = tmn
                if oc == 9:
                    tkm = tmn
        if upto == "p1a_q":
            break
        b.waits("V", [tqm, tkm])
        tv = b.inc("V", nc.vector.tensor_copy(sb["qmkm"][0:64, t0:t0 + TN1], ps[4][0:64, 0:TN1]))
        b.wait("V", tv)
        tv = b.inc("V", nc.vector.tensor_copy(sb["qmkm"][64:128, t0:t0 + TN1], ps[6][0:64, 0:TN1]))
        b.wait("T", tv)
        last_evac["r1s_free"] = last_evac[("pq", 1)]

    b.drain_dma()
    b.barrier()
    es1.close()
    if upto.startswith("p1a"):
        b.drain_dma()
        b.barrier()
        es.close()
        return nc

    TC1 = min(512, T)
    TN2 = min(256, T)
    nt2 = T // TN2
    NJ = T // 128
    VW = 65
    es_qkv = ExitStack()
    sb["vtm"] = es_qkv.enter_context(nc.sbuf_tensor("vtm", [128, 4 * (NJ * VW)], BF16))
    sb["q4a"] = es_qkv.enter_context(nc.sbuf_tensor("q4a", [128, 4 * T], BF16))
    sb["q4b"] = es_qkv.enter_context(nc.sbuf_tensor("q4b", [128, 4 * T], BF16))
    sb["kfin"] = es_qkv.enter_context(nc.sbuf_tensor("kfin", [128, 2 * T], BF16))
    es_tabs = ExitStack()
    sb["tabs"] = es_tabs.enter_context(nc.sbuf_tensor("tabs", [128, 2 * T], BF16))
    es_tab = ExitStack()
    sb["tabscr"] = es_tab.enter_context(nc.sbuf_tensor("tabscr", [128, 4 * T], F32, side="right"))
    sb["kint"] = es_tab.enter_context(nc.sbuf_tensor("kint", [128, T], I32, side="right"))

    def ts_op(out, in0, s1, s2, op0, op1=ALU.bypass, deps=(), eng="V"):
        b.waits(eng, deps)
        return b.inc(eng, nc.vector.tensor_scalar(out, in0, s1, s2, op0, op1))

    A_TB, A_ANG, A_S = (slice(0, T), slice(T, 2 * T), slice(2 * T, 3 * T))
    SC2 = slice(3 * T, 4 * T)
    tg = b.inc("G", nc.gpsimd.iota(sb["tabscr"][0:1, 0:T], [[1, T]],
                                   channel_multiplier=0,
                                   allow_small_or_imprecise_dtypes=True))
    tok_tr = b.dma("S", [(dr["rowscr"][0:1, 0:T], sb["tabscr"][0:1, 0:T])], deps=[tg, tok_r2])
    tok_tb = b.dma("S", [(sb["tabscr"][:, A_TB], bass.AP(dr["rowscr"], 0, [[0, 128], [1, T]]))],
                   deps=[tok_tr])
    b.waits("A", [tok_c])
    t_e1 = b.inc("A", nc.scalar.activation(sb["cc"][:, 7:8], sb["cc"][:, 5:6], AF.Exp, scale=LN1E4_S1))
    t_e2 = b.inc("A", nc.scalar.activation(sb["cc"][:, 4:5], sb["cc"][:, 5:6], AF.Exp, scale=LN1E4_S2))
    b.waits("V", [t_e1, t_e2])
    t_inv = b.inc("V", nc.vector.tensor_mul(sb["cc"][:, 4:5], sb["cc"][:, 7:8], sb["cc"][:, 4:5]))
    b.wait("V", tok_tb)
    t_ang = ts_op(sb["tabscr"][:, A_ANG], sb["tabscr"][:, A_TB], sb["cc"][:, 4:5], None,
                  ALU.mult, deps=[t_inv])

    HPIC = float(np.float32(math.pi - PI1))
    def build_sintab(src_sl, dst_sl, shift, dep):
        W0, SC = A_TB, A_S
        dd = dep
        if shift:
            dd = ts_op(sb["tabscr"][:, SC], sb["tabscr"][:, src_sl], HPI1, HPI2,
                       ALU.add, ALU.add, deps=[dd])
            srcs = SC
        else:
            srcs = src_sl
        d0 = ts_op(sb["tabscr"][:, W0], sb["tabscr"][:, srcs], INV2PI1, None,
                   ALU.mult, deps=[dd])
        b.waits("V", [d0])
        d1 = b.inc("V", nc.vector.tensor_copy(sb["kint"][:, :], sb["tabscr"][:, W0]))
        b.waits("V", [d1])
        d2 = b.inc("V", nc.vector.tensor_copy(dst_sl, sb["kint"][:, :]))
        b.waits("V", [d2])
        d3 = b.inc("V", nc.vector.tensor_sub(sb["tabscr"][:, W0], sb["tabscr"][:, W0],
                                             dst_sl))
        d4 = ts_op(sb["tabscr"][:, SC2], sb["tabscr"][:, W0], 0.5, None, ALU.is_ge,
                   deps=[d3])
        b.waits("V", [d4])
        d5 = b.inc("V", nc.vector.tensor_add(dst_sl, dst_sl, sb["tabscr"][:, SC2]))
        d6 = ts_op(sb["tabscr"][:, SC2], sb["tabscr"][:, W0], -0.5, None, ALU.is_lt,
                   deps=[d5])
        b.waits("V", [d6])
        d7 = b.inc("V", nc.vector.tensor_sub(dst_sl, dst_sl, sb["tabscr"][:, SC2]))
        d8 = ts_op(sb["tabscr"][:, W0], sb["tabscr"][:, srcs], 0.5, None, ALU.mult,
                   deps=[d7])
        b.waits("V", [d8])
        d9 = b.inc("V", nc.vector.scalar_tensor_tensor(
            sb["tabscr"][:, W0], dst_sl, -PI1, sb["tabscr"][:, W0], ALU.mult, ALU.add))
        b.waits("V", [d9])
        da = b.inc("V", nc.vector.scalar_tensor_tensor(
            sb["tabscr"][:, W0], dst_sl, -HPIC, sb["tabscr"][:, W0], ALU.mult, ALU.add))
        b.waits("A", [da])
        db = b.inc("A", nc.scalar.activation(sb["tabscr"][:, SC2], sb["tabscr"][:, W0],
                                             AF.Abs))
        dc = b.inc("A", nc.scalar.activation(dst_sl, sb["tabscr"][:, W0], AF.Sin))
        dd2 = ts_op(sb["tabscr"][:, SC2], sb["tabscr"][:, SC2], HPI2, None,
                    ALU.subtract, deps=[db, dc], eng="V")
        b.waits("A", [dd2])
        de = b.inc("A", nc.scalar.activation(sb["tabscr"][:, SC2], sb["tabscr"][:, SC2],
                                             AF.Sin, scale=-1.0, bias=sb["cc"][:, 1:2]))
        b.waits("V", [dc, de])
        df = b.inc("V", nc.vector.scalar_tensor_tensor(
            dst_sl, dst_sl, 2.0, sb["tabscr"][:, SC2], ALU.mult, ALU.mult))
        return df

    t_sn = build_sintab(A_ANG, sb["tabs"][:, T:2 * T], False, t_ang)
    t_sn = ts_op(sb["tabs"][:, T:2 * T], sb["tabs"][:, T:2 * T], sb["cc"][:, 6:7], None,
                 ALU.mult, deps=[t_sn])
    t_cs = build_sintab(A_ANG, sb["tabs"][:, 0:T], True, t_sn)
    b.drain_dma()
    b.barrier()
    es_tab.close()
    if upto == "p1b_t":
        b.drain_dma()
        b.barrier()
        es_tabs.close()
        es_qkv.close()
        es_raw.close()
        es_qm.close()
        es.close()
        return nc

    es2 = ExitStack()
    sb["c1q"] = es2.enter_context(nc.sbuf_tensor("c1q", [128, 8 * PT], BF16))
    sb["c1k"] = es2.enter_context(nc.sbuf_tensor("c1k", [128, 2 * PT], BF16))
    es_c1 = ExitStack()
    sb["wc1q"] = es_c1.enter_context(nc.sbuf_tensor("wc1q", [128, 24 * 128], BF16, side="right"))
    sb["wc1k"] = es_c1.enter_context(nc.sbuf_tensor("wc1k", [128, 6 * 128], BF16, side="right"))

    tok_wc1 = b.dma("G", [
        (sb["wc1q"][:, :].rearrange("p (t c b) -> p t c b", t=3, c=8),
         dr["qc1bd"].ap().transpose([2, 0, 1, 3])),
        (sb["wc1k"][:, :].rearrange("p (t c b) -> p t c b", t=3, c=2),
         dr["kc1bd"].ap().transpose([2, 0, 1, 3]))])

    tzp = b.inc("V", nc.vector.memset(sb["c1q"].ap().rearrange("p (c t) -> p c t", c=8)[:, :, 0:2], 0.0))
    tzp = b.inc("V", nc.vector.memset(sb["c1k"].ap().rearrange("p (c t) -> p c t", c=2)[:, :, 0:2], 0.0))
    tvo = b.inc("V", nc.vector.memset(
        sb["vtm"].ap().rearrange("p (g j w) -> p g j w", g=4, j=NJ)[:, :, :, 64:65], 1.0))

    sb["c1wq"] = es_c1.enter_context(nc.sbuf_tensor("c1wq", [128, 2 * 8 * (TC1 + 2)], BF16, side="right"))
    sb["c1wk"] = es_c1.enter_context(nc.sbuf_tensor("c1wk", [128, 2 * 2 * (TC1 + 2)], BF16, side="right"))
    c1q_c = sb["c1q"].ap().rearrange("p (c t) -> p c t", c=8)
    c1k_c = sb["c1k"].ap().rearrange("p (c t) -> p c t", c=2)
    pv_guard = {}
    WQ = 8 * (TC1 + 2)
    WK = 2 * (TC1 + 2)
    for ti in range(T // TC1):
        t0 = ti * TC1
        wsl = ti % 2
        prs = [(sb["c1wq"][:, wsl * WQ + c * (TC1 + 2): wsl * WQ + (c + 1) * (TC1 + 2)],
                dr["qrawD"][:, c * PT + t0: c * PT + t0 + TC1 + 2]) for c in range(8)]
        prs += [(sb["c1wk"][:, wsl * WK + c * (TC1 + 2): wsl * WK + (c + 1) * (TC1 + 2)],
                 dr["krawD"][:, c * PT + t0: c * PT + t0 + TC1 + 2]) for c in range(2)]
        win_tok = b.dma("S", prs, deps=[tok_zp, pv_guard.get(("win", wsl))])
        for side, nch_s, wsb, wbuf, woff, dst_c, bias_off in (
                ("q", 8, "wc1q", "c1wq", wsl * WQ, c1q_c, 0),
                ("k", 2, "wc1k", "c1wk", wsl * WK, c1k_c, 16)):
            for c in range(nch_s):
                bank = ps[c % 2]
                if ("c1", c % 2) in pv_guard:
                    b.wait("T", pv_guard[("c1", c % 2)])
                b.wait("T", tok_wc1)
                b.wait("T", win_tok)
                for tap in range(3):
                    ins = nc.tensor.matmul(
                        bank[:, 0:TC1],
                        sb[wsb][:, (tap * nch_s + c) * 128:(tap * nch_s + c + 1) * 128],
                        sb[wbuf][:, woff + c * (TC1 + 2) + tap: woff + c * (TC1 + 2) + tap + TC1],
                        start=(tap == 0), stop=(tap == 2))
                tmm = b.inc("T", ins)
                b.waits("V", [tmm, tzp])
                ta = b.inc("V", nc.vector.tensor_scalar(
                    dst_c[:, c, 2 + t0: 2 + t0 + TC1], bank[:, 0:TC1],
                    sb["biascols"][:, bias_off + c: bias_off + c + 1], None, ALU.add))
                pv_guard[("c1", c % 2)] = ta
        pv_guard[("win", wsl)] = tmm
    for g in range(4):
        bpv = 64 * (g % 2)
        for j in range(NJ):
            if ("vt",) in pv_guard:
                b.wait("T", pv_guard[("vt",)])
            tt = b.inc("T", nc.tensor.transpose(
                psb[:, 0:64],
                sb["vfm"][bpv:bpv + 64, (g // 2) * T + 128 * j: (g // 2) * T + 128 * (j + 1)],
                sb["eyeb"][bpv:bpv + 64, bpv:bpv + 64]))
            b.wait("V", tt)
            tv = b.inc("V", nc.vector.tensor_copy(
                sb["vtm"][:, g * NJ * VW + j * VW: g * NJ * VW + j * VW + 64],
                psb[:, 0:64]))
            pv_guard[("vt",)] = tv
    b.drain_dma()
    b.barrier()
    es_c1.close()
    es_raw.close()
    if upto == "p1b_c1":
        b.barrier()
        es2.close()
        es_tabs.close()
        es_qkv.close()
        es_qm.close()
        es.close()
        return nc

    es3 = ExitStack()
    for nm, shp, dt_ in (("wc2q", [128, 24 * 1024], BF16), ("wc2k", [128, 6 * 256], BF16),
                         ("qpost", [128, 8 * TN2], BF16), ("kpost", [128, 2 * TN2], BF16),
                         ("sqb", [128, 2 * TN2], BF16), ("mm2b", [128, 2 * TN2], BF16),
                         ("rqf", [128, TN2], F32), ("rqb", [128, TN2], BF16),
                         ("rsw", [128, TN2], BF16), ("rt1", [128, TN2], F32),
                         ("rt2", [128, TN2], F32)):
        sb[nm] = es3.enter_context(nc.sbuf_tensor(nm, shp, dt_, side="right"))
    wq_toks2 = []
    for tap in range(3):
        prs = [(sb["wc2q"][:, (tap * 8 + kc) * 1024:(tap * 8 + kc + 1) * 1024],
                dr["qc2T"][tap, 128 * kc:128 * kc + 128, :]) for kc in range(8)]
        wq_toks2.append(b.dma("G", prs))
    tok_wc2k = b.dma("G", [(sb["wc2k"][:, (tap * 2 + kc) * 256:(tap * 2 + kc + 1) * 256],
                            dr["kc2T"][tap, 128 * kc:128 * kc + 128, :])
                           for tap in range(3) for kc in range(2)])

    PAIRSWAP = []
    for i in range(16):
        PAIRSWAP += [2 * i + 1, 2 * i]

    guard = {}
    for ti in range(nt2):
        t0 = ti * TN2
        mdeps = [guard.get("mm2b")]
        b.waits("V", [d for d in mdeps if d])
        tv = b.inc("V", nc.vector.tensor_copy(sb["mm2b"][0:64, 0:TN2], sb["qmkm"][64:128, t0:t0 + TN2]))
        tv = b.inc("V", nc.vector.tensor_copy(sb["mm2b"][64:128, 0:TN2], sb["qmkm"][64:128, t0:t0 + TN2]))
        tv = b.inc("V", nc.vector.tensor_copy(sb["mm2b"][0:64, TN2:2 * TN2], sb["qmkm"][0:64, t0:t0 + TN2]))
        t_mm2b = b.inc("V", nc.vector.tensor_copy(sb["mm2b"][64:128, TN2:2 * TN2], sb["qmkm"][0:64, t0:t0 + TN2]))

        for side, nch_s, wsb, src_c, nkc, bias_off, post_sl, dst_post, ps_ssq, selb, selw in (
                ("q", 8, "wc2q", c1q_c, 8, 8, slice(0, TN2), "qpost", ps[3], "sel16b", 16),
                ("k", 2, "wc2k", c1k_c, 2, 18, slice(TN2, 2 * TN2), "kpost", ps[4], "sel4b", 4)):
            for oc in range(nch_s):
                bank = ps[oc % 2]
                if ("c2", oc % 2) in guard:
                    b.wait("T", guard[("c2", oc % 2)])
                for tok in (wq_toks2 if side == "q" else [tok_wc2k]):
                    b.wait("T", tok)
                first = True
                for tap in range(3):
                    for kc in range(nkc):
                        if side == "q":
                            wsl = sb[wsb][:, (tap * 8 + kc) * 1024 + oc * 128:
                                          (tap * 8 + kc) * 1024 + (oc + 1) * 128]
                        else:
                            wsl = sb[wsb][:, (tap * 2 + kc) * 256 + oc * 128:
                                          (tap * 2 + kc) * 256 + (oc + 1) * 128]
                        ins = nc.tensor.matmul(
                            bank[:, 0:TN2], wsl, src_c[:, kc, t0 + tap: t0 + tap + TN2],
                            start=first, stop=(tap == 2 and kc == nkc - 1))
                        first = False
                tmm = b.inc("T", ins)
                b.waits("V", [tmm, t_mm2b, guard.get(("rope", side, oc))])
                tpost = b.inc("V", nc.vector.scalar_tensor_tensor(
                    sb[dst_post][:, oc * TN2:(oc + 1) * TN2], bank[:, 0:TN2],
                    sb["biascols"][:, bias_off + oc: bias_off + oc + 1],
                    sb["mm2b"][:, post_sl], ALU.add, ALU.add))
                guard[("c2", oc % 2)] = tpost
                guard[("post", side, oc)] = tpost
                b.waits("A", [tpost, guard.get(("sq", oc % 2))])
                tsq = b.inc("A", nc.scalar.activation(
                    sb["sqb"][:, (oc % 2) * TN2:(oc % 2 + 1) * TN2],
                    sb[dst_post][:, oc * TN2:(oc + 1) * TN2], AF.Square))
                b.waits("T", [tsq, guard.get("ssq_read")])
                ins = nc.tensor.matmul(
                    ps_ssq[0:selw, 0:TN2], sb[selb][:, oc * selw:(oc + 1) * selw],
                    sb["sqb"][:, (oc % 2) * TN2:(oc % 2 + 1) * TN2],
                    start=(oc == 0), stop=(oc == nch_s - 1))
                tssq = b.inc("T", ins)
                guard[("sq", oc % 2)] = tssq
            guard[("ssq_done", side)] = tssq

        b.waits("A", [guard[("ssq_done", "q")], guard[("ssq_done", "k")], guard.get("rq_used")])
        ta = b.inc("A", nc.scalar.activation(sb["rqf"][0:16, :], ps[3][0:16, 0:TN2], AF.Sqrt))
        ta2 = b.inc("A", nc.scalar.activation(sb["rqf"][32:36, :], ps[4][0:4, 0:TN2], AF.Sqrt))
        b.waits("V", [ta, ta2])
        tvq = b.inc("V", nc.vector.reciprocal(sb["rqf"][0:16, :], sb["rqf"][0:16, :]))
        tvk = b.inc("V", nc.vector.reciprocal(sb["rqf"][32:36, :], sb["rqf"][32:36, :]))
        b.waits("V", [tvq, tvk, tok_r2])
        tvk = b.inc("V", nc.vector.tensor_scalar(sb["rqf"][32:36, :], sb["rqf"][32:36, :],
                                                 sb["ktcol"][32:36, 0:1], None, ALU.mult))
        b.waits("V", [tvk, guard.get("rqb_used")])
        tvq = b.inc("V", nc.vector.tensor_copy(sb["rqb"][0:16, :], sb["rqf"][0:16, :]))
        tvk = b.inc("V", nc.vector.tensor_copy(sb["rqb"][32:36, :], sb["rqf"][32:36, :]))
        guard["ssq_read"] = tvk
        guard["rqb_ready"] = tvk

        for side, nch_s, dst_post in (("q", 8, "qpost"), ("k", 2, "kpost")):
            for oc in range(nch_s):
                post_ap = sb[dst_post][:, oc * TN2:(oc + 1) * TN2]
                bankR = ps[5 + oc % 2]
                if ("bR", oc % 2) in guard:
                    b.wait("T", guard[("bR", oc % 2)])
                b.wait("T", guard["rqb_ready"])
                if side == "q":
                    tbc = b.inc("T", nc.tensor.matmul(
                        bankR[:, 0:TN2], sb["bc16b"][:, oc * 128:(oc + 1) * 128],
                        sb["rqb"][0:16, :], start=True, stop=True))
                else:
                    tbc = b.inc("T", nc.tensor.matmul(
                        bankR[:, 0:TN2], sb["bc4b"][32:36, oc * 128:(oc + 1) * 128],
                        sb["rqb"][32:36, :], start=True, stop=True))
                deps = [guard.get(("rope", side, oc)), guard.get("rope_bufs")]
                b.waits("V", [d for d in deps if d] + [guard[("post", side, oc)]])
                s1 = b.inc("V", nc.vector.stream_shuffle(sb["rsw"][:, :], post_ap, PAIRSWAP))
                s2 = b.inc("V", nc.vector.tensor_mul(sb["rt1"][:, :], post_ap,
                                                     sb["tabs"][:, t0:t0 + TN2]))
                b.waits("V", [s1])
                s3 = b.inc("V", nc.vector.tensor_mul(sb["rt2"][:, :], sb["rsw"][:, :],
                                                     sb["tabs"][:, T + t0:T + t0 + TN2]))
                b.waits("V", [s2, s3])
                s4 = b.inc("V", nc.vector.tensor_add(sb["rt1"][:, :], sb["rt1"][:, :],
                                                     sb["rt2"][:, :]))
                b.waits("V", [s4, tbc])
                if side == "q":
                    for half in range(2):
                        h = 2 * oc + half
                        g, l = h // 4, h % 4
                        q4t = sb["q4a"] if g < 2 else sb["q4b"]
                        bpq = 64 * (g % 2)
                        for j in range(TN2 // 128):
                            qt = t0 // 128 + j
                            sv = b.inc("V", nc.vector.tensor_mul(
                                q4t[bpq:bpq + 64, 512 * qt + 128 * l: 512 * qt + 128 * (l + 1)],
                                sb["rt1"][64 * half:64 * half + 64, 128 * j:128 * (j + 1)],
                                bankR[64 * half:64 * half + 64, 128 * j:128 * (j + 1)]))
                else:
                    sv = b.inc("V", nc.vector.tensor_mul(
                        sb["kfin"][:, oc * T + t0: oc * T + t0 + TN2],
                        sb["rt1"][:, :], bankR[:, 0:TN2]))
                guard[("rope", side, oc)] = sv
                guard[("bR", oc % 2)] = sv
                guard["rope_bufs"] = sv
                guard["rqb_used"] = tbc
        guard["mm2b"] = tpost
    b.drain_dma()
    b.barrier()
    es3.close()
    es_qm.close()
    es2.close()
    es_tabs.close()
    if upto == "p1b":
        b.drain_dma()
        b.barrier()
        es_qkv.close()
        es.close()
        return nc

    es_oan = ExitStack()
    sb["oan"] = es_oan.enter_context(nc.sbuf_tensor("oan", [128, 8 * T], BF16, side="right"))
    es_p2a = ExitStack()
    sb["pt"] = es_p2a.enter_context(nc.sbuf_tensor("pt", [128, 3 * 512], BF16, side="right"))
    sb["o4"] = es_p2a.enter_context(nc.sbuf_tensor("o4", [128, 2 * 512], F32, side="right"))
    sb["lrb"] = es_p2a.enter_context(nc.sbuf_tensor("lrb", [128, 2 * 512], BF16, side="right"))
    sb["maskTs"] = es_p2a.enter_context(nc.sbuf_tensor("maskTs", [128, 512], F32, side="right"))
    tok_mT = b.dma("S", [(sb["maskTs"][:, :], dr["maskT"][:, :])])
    b.wait("V", tok_mT)
    NQT = T // 128
    ag = {}
    b.waits("A", [tok_r2])

    def p2a_scores(k, i, g):
        gp = 64 * (g % 2)
        q4t = sb["q4a"] if g < 2 else sb["q4b"]
        psO = ps[3 + k % 2]
        b.wait("T", ag.get(("psO", k % 2)))
        texp = {}
        tpv = None

        def emit_s(j):
            psS = ps[j % 3]
            b.wait("T", ag.get(("psS", j % 3)))
            tmm = b.inc("T", nc.tensor.matmul(
                psS[:, 0:512],
                sb["kfin"][gp:gp + 64, (g // 2) * T + 128 * j: (g // 2) * T + 128 * (j + 1)],
                q4t[gp:gp + 64, 512 * i: 512 * (i + 1)],
                start=True, stop=True))
            expdep = tmm
            if j == i:
                b.waits("V", [tmm])
                expdep = b.inc("V", nc.vector.tensor_add(psS[:, 0:512], psS[:, 0:512],
                                                         sb["maskTs"][:, :]))
            b.waits("A", [expdep, ag.get(("pt", j % 3))])
            texp[j] = b.inc("A", nc.scalar.activation(
                sb["pt"][:, (j % 3) * 512:(j % 3 + 1) * 512], psS[:, 0:512],
                AF.Exp, scale=0.125, bias=sb["negm"][:, 0:1]))
            ag[("psS", j % 3)] = texp[j]

        def emit_pv(j):
            nonlocal tpv
            b.waits("T", [texp[j]])
            tpv = b.inc("T", nc.tensor.matmul(
                psO[0:65, 0:512],
                sb["vtm"][:, g * NJ * VW + j * VW: g * NJ * VW + (j + 1) * VW],
                sb["pt"][:, (j % 3) * 512:(j % 3 + 1) * 512],
                start=(j == 0), stop=(j == i)))
            ag[("pt", j % 3)] = tpv

        for j in range(i + 1):
            emit_s(j)
            if j >= 2:
                emit_pv(j - 2)
        for j in range(max(0, i - 1), i + 1):
            emit_pv(j)
        return tpv

    def p2a_norm(k, i, g, tpv):
        psO = ps[3 + k % 2]
        osl = (k % 2) * 512
        b.waits("V", [tpv, ag.get(("o4", k % 2))])
        tev = b.inc("V", nc.vector.tensor_copy(sb["o4"][0:65, osl:osl + 512],
                                               psO[0:65, 0:512]))
        ag[("psO", k % 2)] = tev
        b.waits("V", [tev])
        trc = b.inc("V", nc.vector.reciprocal(sb["o4"][64:65, osl:osl + 512],
                                              sb["o4"][64:65, osl:osl + 512]))
        b.waits("V", [trc, ag.get(("lrb", k % 2))])
        tcb = b.inc("V", nc.vector.tensor_copy(sb["lrb"][64:65, osl:osl + 512],
                                               sb["o4"][64:65, osl:osl + 512]))
        b.waits("T", [tcb, ag.get(("psR", k % 2))])
        tbr = b.inc("T", nc.tensor.matmul(ps[5 + k % 2][0:64, 0:512],
                                          sb["onesrow"][64:65, 0:64],
                                          sb["lrb"][64:65, osl:osl + 512],
                                          start=True, stop=True))
        b.waits("V", [tbr])
        for l in range(4):
            tm = b.inc("V", nc.vector.tensor_mul(
                sb["oan"][64 * (l % 2):64 * (l % 2) + 64,
                          (2 * g + l // 2) * T + 128 * i: (2 * g + l // 2) * T + 128 * (i + 1)],
                sb["o4"][0:64, osl + 128 * l: osl + 128 * (l + 1)],
                ps[5 + k % 2][0:64, 128 * l: 128 * (l + 1)]))
        ag[("psR", k % 2)] = tm
        ag[("o4", k % 2)] = tm
        ag[("lrb", k % 2)] = tbr

    pending = None
    k = 0
    for i in range(NQT):
        for g in range(4):
            tpv = p2a_scores(k, i, g)
            if pending is not None:
                p2a_norm(*pending)
            pending = (k, i, g, tpv)
            k += 1
    p2a_norm(*pending)
    b.drain_dma()
    b.barrier()
    es_p2a.close()
    es_qkv.close()
    if upto == "p2a":
        b.drain_dma()
        b.barrier()
        es_oan.close()
        es.close()
        return nc

    TNB = min(512, T)
    ntb = T // TNB
    es_p2bl = ExitStack()
    sb["woTb"] = es_p2bl.enter_context(nc.sbuf_tensor("woTb", [128, 8 * D], BF16))
    sb["ztile"] = es_p2bl.enter_context(nc.sbuf_tensor("ztile", [128, NCH * TNB], BF16))
    es_p2bt = ExitStack()
    for nm, shp, dt_ in (("xc", [128, 3 * TNB], F32), ("x1b", [128, 2 * TNB], F32),
                         ("zsq", [128, TNB], BF16), ("x1q", [128, TNB], BF16),
                         ("za", [128, TNB], F32), ("wocsb", [128, 8], BF16),
                         ("wocsf", [128, 8], F32)):
        sb[nm] = es_p2bt.enter_context(nc.sbuf_tensor(nm, shp, dt_, side="right"))
    wo_toks = []
    for g4 in range(2):
        prs = [(sb["woTb"][:, kc * D:(kc + 1) * D], dr["woT"][128 * kc:128 * kc + 128, :])
               for kc in range(4 * g4, 4 * g4 + 4)]
        wo_toks.append(b.dma("G", prs))
    for kc in range(8):
        b.wait("V", wo_toks[kc // 4])
        tw = b.inc("V", nc.vector.reduce_sum(sb["wocsf"][:, kc:kc + 1],
                                             sb["woTb"][:, kc * D:(kc + 1) * D],
                                             mybir.AxisListType.X))
    b.wait("V", tw)
    twb = b.inc("V", nc.vector.tensor_copy(sb["wocsb"][:, :], sb["wocsf"][:, :]))

    bg = {}
    for ti in range(ntb):
        t0 = ti * TNB
        b.waits("T", [twb, bg.get("sz_read")])
        for kc in range(8):
            ins = nc.tensor.matmul(ps[5][0:1, 0:TNB], sb["wocsb"][:, kc:kc + 1],
                                   sb["oan"][:, kc * T + t0: kc * T + t0 + TNB],
                                   start=(kc == 0), stop=(kc == 7))
        tsz = b.inc("T", ins)
        for oc in range(NCH):
            bank = ps[oc % 2]
            b.wait("T", bg.get(("pz", oc % 2)))
            for kc in range(8):
                ins = nc.tensor.matmul(
                    bank[:, 0:TNB],
                    sb["woTb"][:, kc * D + oc * 128: kc * D + (oc + 1) * 128],
                    sb["oan"][:, kc * T + t0: kc * T + t0 + TNB],
                    start=(kc == 0), stop=(kc == 7))
            tmm = b.inc("T", ins)
            b.waits("A", [tmm, bg.get(("zt", oc))])
            tze = b.inc("A", nc.scalar.activation(sb["ztile"][:, oc * TNB:(oc + 1) * TNB],
                                                  bank[:, 0:TNB], AF.Copy))
            b.waits("A", [bg.get("zsq_read")])
            tzq = b.inc("A", nc.scalar.activation(sb["zsq"][:, :], bank[:, 0:TNB], AF.Square))
            bg[("pz", oc % 2)] = tzq
            b.waits("T", [tzq])
            ins = nc.tensor.matmul(ps[4][0:1, 0:TNB], sb["onescol"][:, :], sb["zsq"][:, :],
                                   start=(oc == 0), stop=(oc == NCH - 1))
            tq2 = b.inc("T", ins)
            bg["zsq_read"] = tq2
            bg[("ze", oc)] = tze
        b.waits("A", [tsz, tq2])
        b.waits("V", [tsz, tq2])
        rt2b = ln_rows(TNB, ps[5][0:1, 0:TNB], ps[4][0:1, 0:TNB],
                       sfs(6, TNB), rlbs(0, TNB), rlbs(1, TNB),
                       dep=bg.get("rows_dep"), rbdep=bg.get("rlb_used"))
        bg["rows_dep"] = rt2b
        bg["sz_read"] = rt2b
        b.waits("T", [rt2b, bg.get("bc2_read")])
        tb1 = b.inc("T", nc.tensor.matmul(ps[2][:, 0:TNB], sb["onesrow"][0:1, :],
                                          rlbs(0, TNB), start=True, stop=True))
        tb2 = b.inc("T", nc.tensor.matmul(ps[3][:, 0:TNB], sb["onesrow"][0:1, :],
                                          rlbs(1, TNB), start=True, stop=True))
        bg["rlb_used"] = tb2
        xtoks = {}
        for oc in range(3):
            xtoks[oc] = b.dma("S", [(sb["xc"][:, (oc % 3) * TNB:(oc % 3 + 1) * TNB],
                                     dr["x_in"][128 * oc:128 * (oc + 1), t0:t0 + TNB])],
                              deps=[bg.get(("xc", oc % 3))])
        for oc in range(NCH):
            b.waits("V", [tb1, tb2, bg.get(("ze", oc)), bg.get("za_free")])
            tu = b.inc("V", nc.vector.tensor_mul(sb["za"][:, :],
                                                 sb["ztile"][:, oc * TNB:(oc + 1) * TNB],
                                                 ps[2][:, 0:TNB]))
            b.waits("V", [tu])
            tu = b.inc("V", nc.vector.tensor_sub(sb["za"][:, :], sb["za"][:, :], ps[3][:, 0:TNB]))
            x1sl = oc % 2
            b.waits("V", [tu, xtoks[oc], bg.get(("x1b", x1sl)), bg.get(("x1ba", x1sl))])
            tx1 = b.inc("V", nc.vector.tensor_add(sb["x1b"][:, x1sl * TNB:(x1sl + 1) * TNB],
                                                  sb["za"][:, :],
                                                  sb["xc"][:, (oc % 3) * TNB:(oc % 3 + 1) * TNB]))
            bg["za_free"] = tx1
            bg[("xc", oc % 3)] = tx1
            if oc + 3 < NCH:
                xtoks[oc + 3] = b.dma("S", [(sb["xc"][:, ((oc + 3) % 3) * TNB:((oc + 3) % 3 + 1) * TNB],
                                             dr["x_in"][128 * (oc + 3):128 * (oc + 4), t0:t0 + TNB])],
                                      deps=[bg.get(("xc", (oc + 3) % 3))])
            tws = b.dma("S", [(dr["x1s"][128 * oc:128 * (oc + 1), t0:t0 + TNB],
                               sb["x1b"][:, x1sl * TNB:(x1sl + 1) * TNB])], deps=[tx1])
            bg[("x1b", x1sl)] = tws
            b.waits("A", [tx1, bg.get("x1q_read")])
            txq = b.inc("A", nc.scalar.activation(sb["x1q"][:, :],
                                                  sb["x1b"][:, x1sl * TNB:(x1sl + 1) * TNB],
                                                  AF.Square))
            bg[("x1ba", x1sl)] = txq
            b.waits("T", [txq])
            ins = nc.tensor.matmul(ps[6][0:1, 0:TNB], sb["onescol"][:, :], sb["x1q"][:, :],
                                   start=(oc == 0), stop=(oc == NCH - 1))
            tx2 = b.inc("T", ins)
            bg["x1q_read"] = tx2
        bg["bc2_read"] = tx1
        b.waits("A", [tx2])
        b.waits("V", [tx2])
        rt3 = ln_rows(TNB, None, ps[6][0:1, 0:TNB],
                      m1row(t0, TNB), r3row(t0, TNB), c3row(t0, TNB),
                      dep=bg.get("rows_dep"))
        bg["rows_dep"] = rt3
    b.drain_dma()
    b.barrier()
    es_p2bt.close()
    es_oan.close()
    es_p2bl.close()
    if upto == "p2b":
        b.drain_dma()
        b.barrier()
        es.close()
        return nc

    TN3 = min(256, T)
    nt3 = T // TN3
    es_p3l = ExitStack()
    sb["upb"] = es_p3l.enter_context(nc.sbuf_tensor("upb", [128, 32 * 1024], BF16))
    sb["downb"] = es_p3l.enter_context(nc.sbuf_tensor("downb", [128, 128 * 256], BF16))
    es_p3r = ExitStack()
    for nm, shp, dt_ in (("x1r", [128, 6 * TN3], F32), ("ftile", [128, NCH * TN3], BF16),
                         ("hidb", [128, 2 * 8 * TN3], BF16), ("yh", [128, 2 * 2 * TN3], BF16),
                         ("ysc", [128, TN3], F32), ("fsq", [128, 2 * TN3], BF16),
                         ("outb", [128, 2 * TN3], F32)):
        sb[nm] = es_p3r.enter_context(nc.sbuf_tensor(nm, shp, dt_, side="right"))
    up_toks = []
    for h4 in range(4):
        prs = []
        for h in range(4 * h4, 4 * h4 + 4):
            for kc in range(2):
                prs.append((sb["upb"][:, (2 * h + kc) * 1024:(2 * h + kc + 1) * 1024],
                            dr["up_w"][h, 128 * kc:128 * (kc + 1), :]))
        up_toks.append(b.dma("G", prs))
    dn_toks = []
    for h4 in range(4):
        prs = []
        for h in range(4 * h4, 4 * h4 + 4):
            for mc in range(8):
                prs.append((sb["downb"][:, (8 * h + mc) * 256:(8 * h + mc + 1) * 256],
                            dr["down_w"][h, 128 * mc:128 * (mc + 1), :]))
        dn_toks.append(b.dma("G", prs))

    fg = {}
    out_toks = []
    for ti in range(nt3):
        t0 = ti * TN3
        xr = {"tok": {}, "cons": fg.get("xr_cons", {})}
        def x1_load(li):
            sl = li % 6
            c = li % 32
            xr["tok"][li] = b.dma("S", [(sb["x1r"][:, sl * TN3:(sl + 1) * TN3],
                                         dr["x1s"][128 * c:128 * (c + 1), t0:t0 + TN3])],
                                  deps=[d for d in (xr["cons"].get(sl),) if d])
        def x1_slot(li):
            sl = li % 6
            return sb["x1r"][:, sl * TN3:(sl + 1) * TN3]
        def x1_done(li, tok):
            xr["cons"][li % 6] = tok
            if li + 6 < 64:
                x1_load(li + 6)
        for li in range(6):
            x1_load(li)
        b.waits("T", [fg.get("bc3_read")])
        tb3 = b.inc("T", nc.tensor.matmul(ps[4][:, 0:TN3], sb["onesrow"][0:1, :],
                                          r3row(t0, TN3), start=True, stop=False))
        tb3 = b.inc("T", nc.tensor.matmul(ps[4][:, TN3:2 * TN3], sb["onesrow"][0:1, :],
                                          c3row(t0, TN3), start=False, stop=True))
        for h in range(NH):
            ysl = h % 2
            for ci, c in enumerate((2 * h, 2 * h + 1)):
                b.waits("V", [tb3, xr["tok"][c], fg.get("ysc_free"), fg.get(("yh", ysl))])
                tyu = b.inc("V", nc.vector.tensor_mul(sb["ysc"][:, :],
                                                      x1_slot(c), ps[4][:, 0:TN3]))
                b.waits("V", [tyu])
                ty = b.inc("V", nc.vector.tensor_sub(
                    sb["yh"][:, (2 * ysl + ci) * TN3:(2 * ysl + ci + 1) * TN3],
                    sb["ysc"][:, :], ps[4][:, TN3:2 * TN3]))
                fg["ysc_free"] = ty
                x1_done(c, tyu)
            hsl = h % 2
            b.wait("T", up_toks[h // 4])
            b.wait("T", ty)
            for mc in range(8):
                bank = ps[mc // 2 % 2]
                half = mc % 2
                if half == 0:
                    b.wait("T", fg.get(("hid", mc // 2 % 2)))
                for kc in range(2):
                    ins = nc.tensor.matmul(
                        bank[:, half * TN3:(half + 1) * TN3],
                        sb["upb"][:, (2 * h + kc) * 1024 + mc * 128:(2 * h + kc) * 1024 + (mc + 1) * 128],
                        sb["yh"][:, (2 * ysl + kc) * TN3:(2 * ysl + kc + 1) * TN3],
                        start=(half == 0 and kc == 0), stop=(half == 1 and kc == 1))
                if half == 1:
                    tup = b.inc("T", ins)
                    b.waits("A", [tup, fg.get(("hidb", hsl, mc // 2))])
                    tg_ = b.inc("A", nc.scalar.activation(
                        sb["hidb"][:, hsl * 8 * TN3 + (mc - 1) * TN3: hsl * 8 * TN3 + (mc + 1) * TN3],
                        bank[:, 0:2 * TN3], AF.Gelu))
                    fg[("hid", mc // 2 % 2)] = tg_
                    fg[("gel", mc // 2)] = tg_
            fg[("yh", ysl)] = tup
            b.wait("T", dn_toks[h // 4])
            b.wait("T", fg.get("psF_read"))
            first = True
            for dc in range(2):
                for mc in range(8):
                    b.wait("T", fg[("gel", mc // 2)])
                    ins = nc.tensor.matmul(
                        ps[2][:, dc * TN3:(dc + 1) * TN3],
                        sb["downb"][:, (8 * h + mc) * 256 + dc * 128:(8 * h + mc) * 256 + (dc + 1) * 128],
                        sb["hidb"][:, hsl * 8 * TN3 + mc * TN3: hsl * 8 * TN3 + (mc + 1) * TN3],
                        start=first, stop=(dc == 1 and mc == 7))
                    first = False
            tdn = b.inc("T", ins)
            fg[("hidb", hsl, 0)] = tdn
            fg[("hidb", hsl, 1)] = tdn
            fg[("hidb", hsl, 2)] = tdn
            fg[("hidb", hsl, 3)] = tdn
            b.waits("A", [tdn, fg.get(("ft", h))])
            tfe = b.inc("A", nc.scalar.activation(sb["ftile"][:, 2 * h * TN3:(2 * h + 2) * TN3],
                                                  ps[2][:, 0:2 * TN3], AF.Copy))
            b.waits("A", [fg.get("fsq_read")])
            tfq = b.inc("A", nc.scalar.activation(sb["fsq"][:, :], ps[2][:, 0:2 * TN3], AF.Square))
            fg["psF_read"] = tfq
            fg[("fe", h)] = tfe
            b.waits("T", [tfe, tfq])
            ins = nc.tensor.matmul(ps[5][0:1, 0:2 * TN3], sb["onescol"][:, :],
                                   sb["ftile"][:, 2 * h * TN3:(2 * h + 2) * TN3],
                                   start=(h == 0), stop=(h == NH - 1))
            tS1 = b.inc("T", ins)
            ins = nc.tensor.matmul(ps[6][0:1, 0:2 * TN3], sb["onescol"][:, :], sb["fsq"][:, :],
                                   start=(h == 0), stop=(h == NH - 1))
            tS2 = b.inc("T", ins)
            fg["fsq_read"] = tS2
        fg["x1t_read"] = tyu
        b.waits("A", [tS1, tS2, fg.get("rows_dep4")])
        SF4, SF5 = 4 * 512, 5 * 512
        th1 = b.inc("A", nc.scalar.activation(sb["smallf"][0:1, SF4:SF4 + 2 * TN3],
                                              ps[5][0:1, 0:2 * TN3], AF.Copy))
        th2 = b.inc("A", nc.scalar.activation(sb["smallf"][0:1, SF5:SF5 + 2 * TN3],
                                              ps[6][0:1, 0:2 * TN3], AF.Copy))
        b.waits("V", [th1, th2, fg.get("rows_dep4")])
        th3 = b.inc("V", nc.vector.tensor_add(sb["smallf"][0:1, SF4:SF4 + TN3],
                                              sb["smallf"][0:1, SF4:SF4 + TN3],
                                              sb["smallf"][0:1, SF4 + TN3:SF4 + 2 * TN3]))
        th4 = b.inc("V", nc.vector.tensor_add(sb["smallf"][0:1, SF5:SF5 + TN3],
                                              sb["smallf"][0:1, SF5:SF5 + TN3],
                                              sb["smallf"][0:1, SF5 + TN3:SF5 + 2 * TN3]))
        b.waits("A", [th3, th4])
        b.waits("V", [th3, th4])
        rt4 = ln_rows(TN3, sb["smallf"][0:1, SF4:SF4 + TN3], sb["smallf"][0:1, SF5:SF5 + TN3],
                      sfs(6, TN3), rlbs(0, TN3), rlbs(1, TN3),
                      dep=fg.get("rows_dep4"), rbdep=fg.get("rlb_used"))
        fg["rows_dep4"] = rt4
        fg["sums_read"] = rt4
        b.waits("T", [rt4, fg.get("bc4_read")])
        tb4 = b.inc("T", nc.tensor.matmul(ps[3][:, 0:TN3], sb["onesrow"][0:1, :],
                                          rlbs(0, TN3), start=True, stop=False))
        tb4 = b.inc("T", nc.tensor.matmul(ps[3][:, TN3:2 * TN3], sb["onesrow"][0:1, :],
                                          rlbs(1, TN3), start=False, stop=True))
        fg["rlb_used"] = tb4
        for oc in range(NCH):
            osl2 = oc % 2
            b.waits("V", [tb4, fg[("fe", oc // 2)], fg.get("ysc_free"), fg.get(("outb", osl2))])
            tu = b.inc("V", nc.vector.tensor_mul(sb["ysc"][:, :],
                                                 sb["ftile"][:, oc * TN3:(oc + 1) * TN3],
                                                 ps[3][:, 0:TN3]))
            b.waits("V", [tu])
            tu = b.inc("V", nc.vector.tensor_sub(sb["ysc"][:, :], sb["ysc"][:, :],
                                                 ps[3][:, TN3:2 * TN3]))
            b.waits("V", [tu, xr["tok"][32 + oc]])
            to = b.inc("V", nc.vector.tensor_add(sb["outb"][:, osl2 * TN3:(osl2 + 1) * TN3],
                                                 sb["ysc"][:, :], x1_slot(32 + oc)))
            fg["ysc_free"] = to
            x1_done(32 + oc, to)
            tso = b.dma("S", [(dr["y_out"][128 * oc:128 * (oc + 1), t0:t0 + TN3],
                               sb["outb"][:, osl2 * TN3:(osl2 + 1) * TN3])], deps=[to])
            fg[("outb", osl2)] = tso
            out_toks.append(tso)
        fg["xr_cons"] = xr["cons"]
        fg["bc4_read"] = tu
        fg["bc3_read"] = ty
    b.drain_dma()
    b.barrier()
    es_p3r.close()
    es_p3l.close()
    es.close()
    return nc


def kernel(**inputs):
    in_maps, perm, iperm, B, T = host_stage(inputs)
    nc = build_program(T)
    res = run_bass_kernel_spmd(nc, in_maps, core_ids=list(range(B)))
    return assemble_output(res.results, perm, B, T)


# version 08
# speedup vs baseline: 1.2743x; 1.1351x over previous
import math
from contextlib import ExitStack

import numpy as np
import concourse.bass as bass
import concourse.mybir as mybir
from concourse.bass_utils import run_bass_kernel_spmd

F32 = mybir.dt.float32
BF16 = mybir.dt.bfloat16
I32 = mybir.dt.int32
AF = mybir.ActivationFunctionType
ALU = mybir.AluOpType

D = 4096
NH = 16
NKV = 4
HD = 64
LAT = NH * HD
KVD = NKV * HD
QKV = LAT + 2 * KVD
NCH = D // 128
EPS = 1e-5
NEG_BIG = -3.0e38

def _split(x):
    a = float(np.float32(x))
    b = float(np.float32(x - a))
    return a, b

LN1E4_S1, LN1E4_S2 = _split(-math.log(10000.0) / 32.0)
PI1, _PI2 = _split(math.pi)
TWOPI1, TWOPI2 = _split(2.0 * math.pi)
INV2PI1 = float(np.float32(1.0 / TWOPI1))
HPI1, HPI2 = _split(math.pi / 2.0)


class Bld:

    ENG = ("T", "V", "A", "S", "G")

    def __init__(self, nc, es):
        self.nc = nc
        self.es = es
        self.eng = {"T": nc.tensor, "V": nc.vector, "A": nc.scalar,
                    "S": nc.sync, "G": nc.gpsimd}
        self.sems = {}
        self.counts = {}
        self.wmark = {e: {} for e in self.ENG}
        for e in self.ENG:
            self._mksem("p" + e)
        self.rings = {}
        for q, nslots in (("S", 8), ("G", 8)):
            slots = []
            for i in range(nslots):
                nm = f"dq{q}{i}"
                self._mksem(nm)
                slots.append(nm)
            self.rings[q] = {"slots": slots, "next": 0, "intok": {}}

    def _mksem(self, name):
        self.sems[name] = self.es.enter_context(self.nc.semaphore(name))
        self.counts[name] = 0

    def wait(self, e, tok):
        if tok is None:
            return
        sem, val = tok
        if self.wmark[e].get(sem, 0) >= val:
            return
        self.eng[e].wait_ge(self.sems[sem], val)
        self.wmark[e][sem] = val

    def waits(self, e, toks):
        for t in toks:
            self.wait(e, t)

    def tok(self, e):
        return ("p" + e, self.counts["p" + e])

    def inc(self, e, instr):
        sem = "p" + e
        self.counts[sem] += 1
        instr.then_inc(self.sems[sem], 1)
        return (sem, self.counts[sem])

    def dma(self, q, pairs, deps=()):
        ring = self.rings[q]
        slot = ring["slots"][ring["next"]]
        ring["next"] = (ring["next"] + 1) % len(ring["slots"])
        if self.counts[slot]:
            self.wait(q, (slot, self.counts[slot]))
        self.waits(q, deps)
        for p in pairs:
            kw = p[2] if len(p) > 2 else {}
            self.eng[q].dma_start(p[0], p[1], **kw).then_inc(self.sems[slot], 16)
            self.counts[slot] += 16
        return (slot, self.counts[slot])

    def barrier(self, engines="TVASG"):
        for e in engines:
            self.wait(e, ("p" + e, self.counts["p" + e]))
        toks = [self.inc(e, self.eng[e].nop()) for e in engines]
        for e in engines:
            for t in toks:
                self.wait(e, t)

    def drain_dma(self):
        for q, ring in self.rings.items():
            for slot in ring["slots"]:
                if self.counts[slot]:
                    self.wait(q, (slot, self.counts[slot]))


def host_stage(inp):
    perm = np.asarray(inp["L0_ffn_perm"])
    iperm = np.argsort(perm)
    x = np.asarray(inp["L0_attn_x"])
    B, T, _ = x.shape

    shared = {}
    shared["wqkvT"] = np.ascontiguousarray(np.asarray(inp["L0_attn_w_qkv"]).T[perm])
    shared["woT"] = np.ascontiguousarray(np.asarray(inp["L0_attn_w_o"])[perm].T)

    qc1 = np.asarray(inp["L0_attn_qc1_w"])
    qc1bd = np.zeros((3, 8, 128, 128), dtype=np.float32)
    for tap in range(3):
        for c in range(8):
            for gg in range(2):
                qc1bd[tap, c, 64 * gg:64 * gg + 64, 64 * gg:64 * gg + 64] = \
                    qc1[128 * c + 64 * gg:128 * c + 64 * gg + 64, :, tap].T
    shared["qc1bd"] = qc1bd
    shared["qc2T"] = np.ascontiguousarray(np.asarray(inp["L0_attn_qc2_w"]).transpose(2, 1, 0))
    kc1 = np.asarray(inp["L0_attn_kc1_w"])
    kc1bd = np.zeros((3, 2, 128, 128), dtype=np.float32)
    for tap in range(3):
        for c in range(2):
            for gg in range(2):
                kc1bd[tap, c, 64 * gg:64 * gg + 64, 64 * gg:64 * gg + 64] = \
                    kc1[128 * c + 64 * gg:128 * c + 64 * gg + 64, :, tap].T
    shared["kc1bd"] = kc1bd
    shared["kc2T"] = np.ascontiguousarray(np.asarray(inp["L0_attn_kc2_w"]).transpose(2, 1, 0))

    shared["qc1b"] = np.ascontiguousarray(np.asarray(inp["L0_attn_qc1_b"]).reshape(8, 128).T)
    shared["qc2b"] = np.ascontiguousarray(np.asarray(inp["L0_attn_qc2_b"]).reshape(8, 128).T)
    shared["kc1b"] = np.ascontiguousarray(np.asarray(inp["L0_attn_kc1_b"]).reshape(2, 128).T)
    shared["kc2b"] = np.ascontiguousarray(np.asarray(inp["L0_attn_kc2_b"]).reshape(2, 128).T)
    shared["ktemp"] = np.asarray(inp["L0_attn_key_temp"]).reshape(1, 1).astype(np.float32)
    shared["up_w"] = np.ascontiguousarray(np.asarray(inp["L0_ffn_up"]))
    shared["down_w"] = np.ascontiguousarray(np.asarray(inp["L0_ffn_down"]))

    shared["eye128"] = np.eye(128, dtype=np.float32)
    kk = np.arange(128)[:, None]
    qq = np.tile(np.arange(128)[None, :], (1, 4)).reshape(1, 512)
    shared["maskT"] = np.where(kk > qq, np.float32(NEG_BIG), np.float32(0.0)).astype(np.float32)
    dl = np.arange(128) % 64
    shared["msk16"] = (dl[:, None] == np.arange(64)[None, :]).astype(np.float32) / 32.0
    shared["msk4"] = (dl[:, None] == np.arange(64)[None, :]).astype(np.float32) / 8.0
    sel16 = np.zeros((8, 128, 16), dtype=np.float32)
    bc16 = np.zeros((8, 16, 128), dtype=np.float32)
    for c in range(8):
        for p in range(128):
            h = 2 * c + p // 64
            sel16[c, p, h] = 1.0
            bc16[c, h, p] = 1.0
    pvec = np.arange(128)
    ivec = np.zeros((128, 2), dtype=np.float32)
    ivec[:, 0] = (pvec % 64) // 2
    ivec[:, 1] = np.where(pvec % 2 == 0, -1.0, 1.0)
    shared["ivec"] = ivec
    shared["sel16"] = np.ascontiguousarray(sel16.transpose(1, 0, 2)).reshape(128, 8 * 16)
    shared["bc16"] = np.ascontiguousarray(bc16.transpose(1, 0, 2)).reshape(16, 8 * 128)
    sel4 = np.zeros((2, 128, 4), dtype=np.float32)
    bc4 = np.zeros((2, 4, 128), dtype=np.float32)
    for c in range(2):
        for p in range(128):
            h = 2 * c + p // 64
            sel4[c, p, h] = 1.0
            bc4[c, h, p] = 1.0
    shared["sel4"] = np.ascontiguousarray(sel4.transpose(1, 0, 2)).reshape(128, 2 * 4)
    bc4p = np.zeros((36, 2 * 128), dtype=np.float32)
    bc4p[32:36] = np.ascontiguousarray(bc4.transpose(1, 0, 2)).reshape(4, 2 * 128)
    shared["bc4"] = bc4p

    in_maps = []
    for b in range(B):
        m = dict(shared)
        m["x_in"] = np.ascontiguousarray(x[b].T[perm])
        in_maps.append(m)
    return in_maps, perm, iperm, B, T


def assemble_output(results, perm, B, T):
    out = np.empty((B, T, D), dtype=np.float32)
    for b in range(B):
        y = np.asarray(results[b]["y_out"])
        out[b][:, perm] = y.T
    return out


def build_program(T, upto="p3"):
    nc = bass.Bass("TRN2", target_bir_lowering=False)
    es = ExitStack()

    dr = {}
    dr["x_in"] = nc.dram_tensor("x_in", [D, T], F32, kind="ExternalInput")
    dr["wqkvT"] = nc.dram_tensor("wqkvT", [D, QKV], F32, kind="ExternalInput")
    dr["woT"] = nc.dram_tensor("woT", [LAT, D], F32, kind="ExternalInput")
    dr["qc1bd"] = nc.dram_tensor("qc1bd", [3, 8, 128, 128], F32, kind="ExternalInput")
    dr["qc2T"] = nc.dram_tensor("qc2T", [3, 1024, 1024], F32, kind="ExternalInput")
    dr["kc1bd"] = nc.dram_tensor("kc1bd", [3, 2, 128, 128], F32, kind="ExternalInput")
    dr["kc2T"] = nc.dram_tensor("kc2T", [3, 256, 256], F32, kind="ExternalInput")
    dr["qc1b"] = nc.dram_tensor("qc1b", [128, 8], F32, kind="ExternalInput")
    dr["qc2b"] = nc.dram_tensor("qc2b", [128, 8], F32, kind="ExternalInput")
    dr["kc1b"] = nc.dram_tensor("kc1b", [128, 2], F32, kind="ExternalInput")
    dr["kc2b"] = nc.dram_tensor("kc2b", [128, 2], F32, kind="ExternalInput")
    dr["ktemp"] = nc.dram_tensor("ktemp", [1, 1], F32, kind="ExternalInput")
    dr["up_w"] = nc.dram_tensor("up_w", [16, 256, 1024], F32, kind="ExternalInput")
    dr["down_w"] = nc.dram_tensor("down_w", [16, 1024, 256], F32, kind="ExternalInput")
    dr["eye128"] = nc.dram_tensor("eye128", [128, 128], F32, kind="ExternalInput")
    dr["maskT"] = nc.dram_tensor("maskT", [128, 512], F32, kind="ExternalInput")
    dr["msk16"] = nc.dram_tensor("msk16", [128, 64], F32, kind="ExternalInput")
    dr["msk4"] = nc.dram_tensor("msk4", [128, 64], F32, kind="ExternalInput")
    dr["ivec"] = nc.dram_tensor("ivec", [128, 2], F32, kind="ExternalInput")
    dr["sel16"] = nc.dram_tensor("sel16", [128, 8 * 16], F32, kind="ExternalInput")
    dr["bc16"] = nc.dram_tensor("bc16", [16, 8 * 128], F32, kind="ExternalInput")
    dr["sel4"] = nc.dram_tensor("sel4", [128, 2 * 4], F32, kind="ExternalInput")
    dr["bc4"] = nc.dram_tensor("bc4", [36, 2 * 128], F32, kind="ExternalInput")
    dr["y_out"] = nc.dram_tensor("y_out", [D, T], F32, kind="ExternalOutput")
    dr["x1s"] = nc.dram_tensor("x1s", [D, T], F32)
    dr["qrawD"] = nc.dram_tensor("qrawD", [128, 8 * (T + 2)], BF16)
    dr["krawD"] = nc.dram_tensor("krawD", [128, 2 * (T + 2)], BF16)
    dr["rowscr"] = nc.dram_tensor("rowscr", [1, max(T, 2048)], F32)

    b = Bld(nc, es)
    blk = es.enter_context(nc.Block())

    PT = T + 2
    sb = {}
    def alloc(name, shape, dtype):
        sb[name] = es.enter_context(nc.sbuf_tensor(name, shape, dtype))
        return sb[name]

    es_qm = ExitStack()
    sb["qmkm"] = es_qm.enter_context(nc.sbuf_tensor("qmkm", [128, T], F32, side="right"))
    es_raw = ExitStack()
    sb["vfm"] = es_raw.enter_context(nc.sbuf_tensor("vfm", [128, 2 * T], BF16, side="right"))
    sb["zrow"] = es_raw.enter_context(nc.sbuf_tensor("zrow", [128, 16], BF16, side="right"))
    alloc("rows", [128, T], BF16)
    def m1row(t0, TN):
        return sb["rows"][0:1, t0:t0 + TN]
    alloc("rlb", [128, 2 * 512], BF16)
    def rlbs(i, TN):
        return sb["rlb"][0:1, i * 512:i * 512 + TN]
    alloc("r3c3b", [128, 2 * T], BF16)
    def r3row(t0, TN):
        return sb["r3c3b"][0:1, t0:t0 + TN]
    def c3row(t0, TN):
        return sb["r3c3b"][0:1, T + t0:T + t0 + TN]
    alloc("eyeb", [128, 128], BF16)
    alloc("msk16b", [128, 64], BF16)
    alloc("msk4b", [128, 64], BF16)
    alloc("sel16b", [128, 8 * 16], BF16)
    alloc("bc16b", [16, 8 * 128], BF16)
    alloc("sel4b", [128, 2 * 4], BF16)
    alloc("bc4b", [36, 2 * 128], BF16)
    alloc("biascols", [128, 20], F32)
    alloc("onescol", [128, 1], BF16)
    alloc("onesrow", [128, 128], BF16)
    alloc("negm", [128, 1], F32)
    alloc("cc", [128, 8], F32)
    alloc("ktcol", [128, 1], F32)
    alloc("smallf", [128, 7 * 512 + 64], F32)
    def sfs(i, TN):
        return sb["smallf"][0:1, i * 512:i * 512 + TN]

    ps = []
    for i in range(7):
        ps.append(es.enter_context(nc.psum_tensor(f"pb{i}", [128, 512], F32)))
    psb = es.enter_context(nc.psum_tensor("pbT", [128, 1024], BF16))

    tok_c = b.dma("S", [
        (sb["biascols"][:, 0:8], dr["qc1b"][:, :]),
        (sb["biascols"][:, 8:16], dr["qc2b"][:, :]),
        (sb["biascols"][:, 16:18], dr["kc1b"][:, :]),
        (sb["biascols"][:, 18:20], dr["kc2b"][:, :]),
        (sb["smallf"][0:1, 7 * 512:7 * 512 + 1], dr["ktemp"][:, :]),
        (sb["cc"][:, 5:6], dr["ivec"][:, 0:1], dict(allow_slow_non_contiguous=True)),
        (sb["cc"][:, 6:7], dr["ivec"][:, 1:2], dict(allow_slow_non_contiguous=True)),
    ])
    tok_cb = b.dma("G", [
        (sb["eyeb"][:, :], dr["eye128"][:, :]),
        (sb["msk16b"][:, :], dr["msk16"][:, :]),
        (sb["msk4b"][:, :], dr["msk4"][:, :]),
        (sb["sel16b"][:, :], dr["sel16"][:, :]),
        (sb["bc16b"][:, :], dr["bc16"][:, :]),
        (sb["sel4b"][:, :], dr["sel4"][:, :]),
        (sb["bc4b"][:, :], dr["bc4"][:, :]),
    ])
    b.wait("V", tok_c)
    tv = b.inc("V", nc.vector.memset(sb["onescol"][:, :], 1.0))
    tv = b.inc("V", nc.vector.memset(sb["onesrow"][:, :], 1.0))
    tcc = b.inc("V", nc.vector.memset(sb["cc"][:, 0:1], EPS))
    tcc = b.inc("V", nc.vector.memset(sb["cc"][:, 1:2], HPI1))
    b.waits("A", [tcc])
    ta = b.inc("A", nc.scalar.activation(sb["smallf"][0:1, 7 * 512 + 1:7 * 512 + 2],
                                         sb["smallf"][0:1, 7 * 512:7 * 512 + 1],
                                         AF.Abs, scale=0.125))
    b.wait("A", ta)
    ta = b.inc("A", nc.scalar.activation(sb["smallf"][0:1, 7 * 512 + 2:7 * 512 + 3],
                                         sb["smallf"][0:1, 7 * 512 + 1:7 * 512 + 2],
                                         AF.Copy, scale=-1.0))
    tok_r = b.dma("S", [(dr["rowscr"][0:1, 0:2],
                         sb["smallf"][0:1, 7 * 512:7 * 512 + 3:2],
                         dict(allow_slow_non_contiguous=True))], deps=[ta])
    tok_r2 = b.dma("S", [
        (sb["negm"][:, 0:1], bass.AP(dr["rowscr"], 1, [[0, 128], [1, 1]])),
        (sb["ktcol"][32:36, 0:1], bass.AP(dr["rowscr"], 0, [[0, 4], [1, 1]])),
    ], deps=[tok_r])

    b.wait("V", tok_cb)
    tv0 = b.inc("V", nc.vector.memset(sb["zrow"][:, :], 0.0))
    tvv = b.inc("V", nc.vector.memset(sb["vfm"][:, T:T + 1], 0.0))
    tok_zp = b.dma("S", [
        (dr["qrawD"].ap().rearrange("p (c t) -> p c t", c=8)[:, :, 0:2],
         sb["zrow"][:, 0:16].rearrange("p (c t) -> p c t", c=8)),
        (dr["krawD"].ap().rearrange("p (c t) -> p c t", c=2)[:, :, 0:2],
         sb["zrow"][:, 0:4].rearrange("p (c t) -> p c t", c=2)),
    ], deps=[tv0])

    if upto == "p0":
        b.drain_dma()
        b.barrier()
        es_raw.close()
        es_qm.close()
        es.close()
        return nc
    TN1 = min(256, T)
    nt1 = T // TN1
    es1 = ExitStack()
    def alloc1(name, shape, dtype):
        sb[name] = es1.enter_context(nc.sbuf_tensor(name, shape, dtype, side="right"))
        return sb[name]
    alloc1("wqkvb", [128, NCH * QKV], BF16)
    alloc1("x32", [128, (NCH // 4) * TN1], F32)
    alloc1("xb", [128, 2 * NCH * TN1], BF16)
    alloc1("xsq", [128, 4 * TN1], BF16)
    alloc1("r1s", [128, 2 * TN1], F32)
    alloc1("uevac", [128, TN1], F32)
    alloc1("stg", [128, 3 * TN1], BF16)
    alloc1("wsumb", [1, QKV], BF16)

    wq_toks = []
    for g in range(4):
        prs = []
        for kc in range(8 * g, 8 * g + 8):
            prs.append((sb["wqkvb"][:, kc * QKV:(kc + 1) * QKV],
                        dr["wqkvT"][128 * kc:128 * kc + 128, :]))
        wq_toks.append(b.dma("G", prs))

    b.wait("T", tv)
    for seg in range(3):
        for kc in range(NCH):
            b.wait("T", wq_toks[kc // 8])
            ins = nc.tensor.matmul(ps[6][0:1, 0:512], sb["onescol"][:, :],
                                   sb["wqkvb"][:, kc * QKV + seg * 512: kc * QKV + (seg + 1) * 512],
                                   start=(kc == 0), stop=(kc == NCH - 1))
        tt = b.inc("T", ins)
        b.wait("V", tt)
        tv = b.inc("V", nc.vector.tensor_copy(sb["wsumb"][0:1, seg * 512:(seg + 1) * 512],
                                              ps[6][0:1, 0:512]))
        b.wait("T", tv)

    if upto == "p1a_w":
        b.drain_dma()
        b.barrier()
        es1.close()
        es_raw.close()
        es_qm.close()
        es.close()
        return nc

    def ln_rows(TN, ps_sx, ps_sx2, m_ap, rb_r_ap, rb_c_ap, dep=None, rbdep=None):
        b.wait("A", dep)
        if ps_sx is not None:
            ta = b.inc("A", nc.scalar.activation(m_ap, ps_sx, AF.Copy, scale=1.0 / D))
            b.wait("A", ta)
        ta = b.inc("A", nc.scalar.activation(sfs(0, TN), ps_sx2, AF.Copy, scale=1.0 / D))
        ta2 = b.inc("A", nc.scalar.activation(sfs(1, TN), m_ap, AF.Square))
        b.waits("V", [ta, ta2, dep])
        tv = b.inc("V", nc.vector.tensor_sub(sfs(0, TN), sfs(0, TN), sfs(1, TN)))
        b.wait("A", tv)
        ta = b.inc("A", nc.scalar.activation(sfs(2, TN), sfs(0, TN), AF.Sqrt,
                                             bias=sb["cc"][0:1, 0:1]))
        b.wait("V", ta)
        tv = b.inc("V", nc.vector.reciprocal(sfs(2, TN), sfs(2, TN)))
        b.wait("V", tv)
        tv = b.inc("V", nc.vector.tensor_mul(sfs(3, TN), m_ap, sfs(2, TN)))
        b.waits("V", [tv, rbdep])
        tv = b.inc("V", nc.vector.tensor_copy(rb_r_ap, sfs(2, TN)))
        tv = b.inc("V", nc.vector.tensor_copy(rb_c_ap, sfs(3, TN)))
        return tv

    last_evac = {}

    for t in wq_toks:
        b.wait("T", t)

    HN = NCH // 4
    mm_tok = None
    xb_used = [None, None]
    cast_tok = [None, None, None, None]

    def load_cast_tile(ti):
        t0 = ti * TN1
        par = ti % 2
        for hf in range(4):
            prs = []
            for kc in range(HN * hf, HN * (hf + 1)):
                prs.append((sb["x32"][:, (kc - HN * hf) * TN1:(kc - HN * hf + 1) * TN1],
                            dr["x_in"][128 * kc:128 * kc + 128, t0:t0 + TN1]))
            x_tok = b.dma("S", prs, deps=[d for d in (cast_tok[(hf + 3) % 4],) if d])
            b.waits("V", [x_tok, xb_used[par]])
            cast_tok[hf] = b.inc("V", nc.vector.tensor_copy(
                sb["xb"][:, (par * NCH + hf * HN) * TN1:(par * NCH + (hf + 1) * HN) * TN1],
                sb["x32"][:, :]))
        return cast_tok[3]

    tile_cast = {0: load_cast_tile(0)}
    for ti in range(nt1):
        t0 = ti * TN1
        par = ti % 2
        xo = par * NCH * TN1
        if ti + 1 < nt1:
            tile_cast[ti + 1] = load_cast_tile(ti + 1)
        cast_now = tile_cast[ti]

        b.waits("A", [cast_now])
        if ti > 0:
            b.wait("T", last_evac["rows_done"])
        tsq = None
        for gq in range(8):
            if tsq is not None:
                b.wait("A", tsq)
            ta = b.inc("A", nc.scalar.activation(
                sb["xsq"][:, :], sb["xb"][:, xo + gq * 4 * TN1: xo + (gq + 1) * 4 * TN1],
                AF.Square))
            b.wait("T", ta)
            b.waits("T", [cast_now])
            for c4 in range(4):
                kc = gq * 4 + c4
                ins = nc.tensor.matmul(ps[5][0:1, 0:TN1], sb["onescol"][:, :],
                                       sb["xb"][:, xo + kc * TN1: xo + (kc + 1) * TN1],
                                       start=(kc == 0), stop=False)
                ins = nc.tensor.matmul(ps[5][0:1, TN1:2 * TN1], sb["onescol"][:, :],
                                       sb["xsq"][:, c4 * TN1:(c4 + 1) * TN1],
                                       start=False, stop=(kc == NCH - 1))
            tsq = b.inc("T", ins)
        stats_tok = tsq
        if upto == "p1a_s":
            break
        b.wait("A", stats_tok)
        b.wait("V", stats_tok)
        rows_tok = ln_rows(TN1, ps[5][0:1, 0:TN1], ps[5][0:1, TN1:2 * TN1],
                           m1row(t0, TN1), rlbs(0, TN1), rlbs(1, TN1),
                           dep=last_evac.get("rows_done"),
                           rbdep=last_evac.get("rlb_used"))
        last_evac["rows_done"] = rows_tok

        if upto == "p1a_r":
            break
        b.wait("T", rows_tok)
        tt = b.inc("T", nc.tensor.matmul(ps[4][:, 0:TN1], sb["onesrow"][0:1, :],
                                         rlbs(0, TN1), start=True, stop=True))
        b.wait("V", tt)
        if last_evac.get("r1s_free") is not None:
            b.wait("V", last_evac["r1s_free"])
        par = ti % 2
        r1s_tok = b.inc("V", nc.vector.tensor_copy(sb["r1s"][:, par * TN1:(par + 1) * TN1],
                                                   ps[4][:, 0:TN1]))
        b.wait("T", r1s_tok)

        if upto == "p1a_b":
            break
        for oc in range(12):
            pq = ps[oc % 2]
            pc = ps[2 + oc % 2]
            if ("pq", oc % 2) in last_evac:
                b.wait("T", last_evac[("pq", oc % 2)])
            for kc in range(NCH):
                ins = nc.tensor.matmul(pq[:, 0:TN1],
                                       sb["wqkvb"][:, kc * QKV + oc * 128: kc * QKV + (oc + 1) * 128],
                                       sb["xb"][:, xo + kc * TN1: xo + (kc + 1) * TN1],
                                       start=(kc == 0), stop=(kc == NCH - 1))
            mm_tok = b.inc("T", ins)
            xb_used[par] = mm_tok
            if ("pc", oc % 2) in last_evac:
                b.wait("T", last_evac[("pc", oc % 2)])
            tt2 = b.inc("T", nc.tensor.matmul(pc[:, 0:TN1],
                                              sb["wsumb"][0:1, oc * 128:(oc + 1) * 128],
                                              rlbs(1, TN1),
                                              start=True, stop=True))
            last_evac["rlb_used"] = tt2
            b.waits("V", [mm_tok, tt2, r1s_tok])
            if ("ue",) in last_evac:
                b.wait("V", last_evac[("ue",)])
            tv1 = b.inc("V", nc.vector.tensor_mul(
                sb["uevac"][:, 0:TN1],
                sb["r1s"][:, par * TN1:(par + 1) * TN1], pq[:, 0:TN1]))
            last_evac[("pq", oc % 2)] = tv1
            b.wait("V", tv1)
            if oc < 10:
                ss = oc % 3
                b.waits("V", [last_evac.get(("stgd", ss)), last_evac.get(("stgm", ss))])
                dst = sb["stg"][:, ss * TN1:(ss + 1) * TN1]
            elif oc == 10:
                dst = sb["vfm"][:, t0:t0 + TN1]
            else:
                hi = min(t0 + 1 + TN1, T)
                dst = sb["vfm"][:, T + t0 + 1: T + hi]
            if oc == 11:
                w = hi - (t0 + 1)
                tv2 = b.inc("V", nc.vector.tensor_sub(
                    dst, sb["uevac"][:, 0:w], pc[:, 0:w]))
            else:
                tv2 = b.inc("V", nc.vector.tensor_sub(
                    dst, sb["uevac"][:, 0:TN1], pc[:, 0:TN1]))
            last_evac[("pc", oc % 2)] = tv2
            last_evac[("ue",)] = tv2
            if oc < 10:
                if oc < 8:
                    ddst = dr["qrawD"][:, oc * PT + 2 + t0: oc * PT + 2 + t0 + TN1]
                else:
                    ddst = dr["krawD"][:, (oc - 8) * PT + 2 + t0: (oc - 8) * PT + 2 + t0 + TN1]
                last_evac[("stgd", ss)] = b.dma("S", [(ddst, dst)], deps=[tv2])
                b.wait("T", tv2)
                if oc < 8:
                    tmn = b.inc("T", nc.tensor.matmul(ps[4][0:64, 0:TN1], sb["msk16b"][:, :],
                                                      dst, start=(oc == 0), stop=(oc == 7)))
                else:
                    tmn = b.inc("T", nc.tensor.matmul(ps[6][0:64, 0:TN1], sb["msk4b"][:, :],
                                                      dst, start=(oc == 8), stop=(oc == 9)))
                last_evac[("stgm", ss)] = tmn
                if oc == 7:
                    tqm = tmn
                if oc == 9:
                    tkm = tmn
        if upto == "p1a_q":
            break
        b.waits("V", [tqm, tkm])
        tv = b.inc("V", nc.vector.tensor_copy(sb["qmkm"][0:64, t0:t0 + TN1], ps[4][0:64, 0:TN1]))
        b.wait("V", tv)
        tv = b.inc("V", nc.vector.tensor_copy(sb["qmkm"][64:128, t0:t0 + TN1], ps[6][0:64, 0:TN1]))
        b.wait("T", tv)
        last_evac["r1s_free"] = last_evac[("pq", 1)]

    b.drain_dma()
    b.barrier()
    es1.close()
    if upto.startswith("p1a"):
        b.drain_dma()
        b.barrier()
        es.close()
        return nc

    TC1 = min(512, T)
    TN2 = min(512, T)
    nt2 = T // TN2
    NJ = T // 128
    VW = 65
    es_qkv = ExitStack()
    sb["vtm"] = es_qkv.enter_context(nc.sbuf_tensor("vtm", [128, 4 * (NJ * VW)], BF16))
    sb["q4a"] = es_qkv.enter_context(nc.sbuf_tensor("q4a", [128, 4 * T], BF16))
    sb["q4b"] = es_qkv.enter_context(nc.sbuf_tensor("q4b", [128, 4 * T], BF16))
    sb["kfin"] = es_qkv.enter_context(nc.sbuf_tensor("kfin", [128, 2 * T], BF16))
    es_tabs = ExitStack()
    sb["tabs"] = es_tabs.enter_context(nc.sbuf_tensor("tabs", [128, 2 * T], BF16))
    es_tab = ExitStack()
    sb["tabscr"] = es_tab.enter_context(nc.sbuf_tensor("tabscr", [128, 4 * T], F32, side="right"))
    sb["kint"] = es_tab.enter_context(nc.sbuf_tensor("kint", [128, T], I32, side="right"))

    def ts_op(out, in0, s1, s2, op0, op1=ALU.bypass, deps=(), eng="V"):
        b.waits(eng, deps)
        return b.inc(eng, nc.vector.tensor_scalar(out, in0, s1, s2, op0, op1))

    A_TB, A_ANG, A_S = (slice(0, T), slice(T, 2 * T), slice(2 * T, 3 * T))
    SC2 = slice(3 * T, 4 * T)
    tg = b.inc("G", nc.gpsimd.iota(sb["tabscr"][0:1, 0:T], [[1, T]],
                                   channel_multiplier=0,
                                   allow_small_or_imprecise_dtypes=True))
    tok_tr = b.dma("S", [(dr["rowscr"][0:1, 0:T], sb["tabscr"][0:1, 0:T])], deps=[tg, tok_r2])
    tok_tb = b.dma("S", [(sb["tabscr"][:, A_TB], bass.AP(dr["rowscr"], 0, [[0, 128], [1, T]]))],
                   deps=[tok_tr])
    b.waits("A", [tok_c])
    t_e1 = b.inc("A", nc.scalar.activation(sb["cc"][:, 7:8], sb["cc"][:, 5:6], AF.Exp, scale=LN1E4_S1))
    t_e2 = b.inc("A", nc.scalar.activation(sb["cc"][:, 4:5], sb["cc"][:, 5:6], AF.Exp, scale=LN1E4_S2))
    b.waits("V", [t_e1, t_e2])
    t_inv = b.inc("V", nc.vector.tensor_mul(sb["cc"][:, 4:5], sb["cc"][:, 7:8], sb["cc"][:, 4:5]))
    b.wait("V", tok_tb)
    t_ang = ts_op(sb["tabscr"][:, A_ANG], sb["tabscr"][:, A_TB], sb["cc"][:, 4:5], None,
                  ALU.mult, deps=[t_inv])

    HPIC = float(np.float32(math.pi - PI1))
    def build_sintab(src_sl, dst_sl, shift, dep):
        W0, SC = A_TB, A_S
        dd = dep
        if shift:
            dd = ts_op(sb["tabscr"][:, SC], sb["tabscr"][:, src_sl], HPI1, HPI2,
                       ALU.add, ALU.add, deps=[dd])
            srcs = SC
        else:
            srcs = src_sl
        d0 = ts_op(sb["tabscr"][:, W0], sb["tabscr"][:, srcs], INV2PI1, None,
                   ALU.mult, deps=[dd])
        b.waits("V", [d0])
        d1 = b.inc("V", nc.vector.tensor_copy(sb["kint"][:, :], sb["tabscr"][:, W0]))
        b.waits("V", [d1])
        d2 = b.inc("V", nc.vector.tensor_copy(dst_sl, sb["kint"][:, :]))
        b.waits("V", [d2])
        d3 = b.inc("V", nc.vector.tensor_sub(sb["tabscr"][:, W0], sb["tabscr"][:, W0],
                                             dst_sl))
        d4 = ts_op(sb["tabscr"][:, SC2], sb["tabscr"][:, W0], 0.5, None, ALU.is_ge,
                   deps=[d3])
        b.waits("V", [d4])
        d5 = b.inc("V", nc.vector.tensor_add(dst_sl, dst_sl, sb["tabscr"][:, SC2]))
        d6 = ts_op(sb["tabscr"][:, SC2], sb["tabscr"][:, W0], -0.5, None, ALU.is_lt,
                   deps=[d5])
        b.waits("V", [d6])
        d7 = b.inc("V", nc.vector.tensor_sub(dst_sl, dst_sl, sb["tabscr"][:, SC2]))
        d8 = ts_op(sb["tabscr"][:, W0], sb["tabscr"][:, srcs], 0.5, None, ALU.mult,
                   deps=[d7])
        b.waits("V", [d8])
        d9 = b.inc("V", nc.vector.scalar_tensor_tensor(
            sb["tabscr"][:, W0], dst_sl, -PI1, sb["tabscr"][:, W0], ALU.mult, ALU.add))
        b.waits("V", [d9])
        da = b.inc("V", nc.vector.scalar_tensor_tensor(
            sb["tabscr"][:, W0], dst_sl, -HPIC, sb["tabscr"][:, W0], ALU.mult, ALU.add))
        b.waits("A", [da])
        db = b.inc("A", nc.scalar.activation(sb["tabscr"][:, SC2], sb["tabscr"][:, W0],
                                             AF.Abs))
        dc = b.inc("A", nc.scalar.activation(dst_sl, sb["tabscr"][:, W0], AF.Sin))
        dd2 = ts_op(sb["tabscr"][:, SC2], sb["tabscr"][:, SC2], HPI2, None,
                    ALU.subtract, deps=[db, dc], eng="V")
        b.waits("A", [dd2])
        de = b.inc("A", nc.scalar.activation(sb["tabscr"][:, SC2], sb["tabscr"][:, SC2],
                                             AF.Sin, scale=-1.0, bias=sb["cc"][:, 1:2]))
        b.waits("V", [dc, de])
        df = b.inc("V", nc.vector.scalar_tensor_tensor(
            dst_sl, dst_sl, 2.0, sb["tabscr"][:, SC2], ALU.mult, ALU.mult))
        return df

    t_sn = build_sintab(A_ANG, sb["tabs"][:, T:2 * T], False, t_ang)
    t_sn = ts_op(sb["tabs"][:, T:2 * T], sb["tabs"][:, T:2 * T], sb["cc"][:, 6:7], None,
                 ALU.mult, deps=[t_sn])
    t_cs = build_sintab(A_ANG, sb["tabs"][:, 0:T], True, t_sn)
    b.drain_dma()
    b.barrier()
    es_tab.close()
    if upto == "p1b_t":
        b.drain_dma()
        b.barrier()
        es_tabs.close()
        es_qkv.close()
        es_raw.close()
        es_qm.close()
        es.close()
        return nc

    es2 = ExitStack()
    sb["c1q"] = es2.enter_context(nc.sbuf_tensor("c1q", [128, 8 * PT], BF16))
    sb["c1k"] = es2.enter_context(nc.sbuf_tensor("c1k", [128, 2 * PT], BF16))
    es_c1 = ExitStack()
    sb["wc1q"] = es_c1.enter_context(nc.sbuf_tensor("wc1q", [128, 24 * 128], BF16, side="right"))
    sb["wc1k"] = es_c1.enter_context(nc.sbuf_tensor("wc1k", [128, 6 * 128], BF16, side="right"))

    tok_wc1 = b.dma("G", [
        (sb["wc1q"][:, :].rearrange("p (t c b) -> p t c b", t=3, c=8),
         dr["qc1bd"].ap().transpose([2, 0, 1, 3])),
        (sb["wc1k"][:, :].rearrange("p (t c b) -> p t c b", t=3, c=2),
         dr["kc1bd"].ap().transpose([2, 0, 1, 3]))])

    tzp = b.inc("V", nc.vector.memset(sb["c1q"].ap().rearrange("p (c t) -> p c t", c=8)[:, :, 0:2], 0.0))
    tzp = b.inc("V", nc.vector.memset(sb["c1k"].ap().rearrange("p (c t) -> p c t", c=2)[:, :, 0:2], 0.0))
    tvo = b.inc("V", nc.vector.memset(
        sb["vtm"].ap().rearrange("p (g j w) -> p g j w", g=4, j=NJ)[:, :, :, 64:65], 1.0))

    sb["c1wq"] = es_c1.enter_context(nc.sbuf_tensor("c1wq", [128, 2 * 8 * (TC1 + 2)], BF16, side="right"))
    sb["c1wk"] = es_c1.enter_context(nc.sbuf_tensor("c1wk", [128, 2 * 2 * (TC1 + 2)], BF16, side="right"))
    c1q_c = sb["c1q"].ap().rearrange("p (c t) -> p c t", c=8)
    c1k_c = sb["c1k"].ap().rearrange("p (c t) -> p c t", c=2)
    pv_guard = {}
    WQ = 8 * (TC1 + 2)
    WK = 2 * (TC1 + 2)
    for ti in range(T // TC1):
        t0 = ti * TC1
        wsl = ti % 2
        prs = [(sb["c1wq"][:, wsl * WQ + c * (TC1 + 2): wsl * WQ + (c + 1) * (TC1 + 2)],
                dr["qrawD"][:, c * PT + t0: c * PT + t0 + TC1 + 2]) for c in range(8)]
        prs += [(sb["c1wk"][:, wsl * WK + c * (TC1 + 2): wsl * WK + (c + 1) * (TC1 + 2)],
                 dr["krawD"][:, c * PT + t0: c * PT + t0 + TC1 + 2]) for c in range(2)]
        win_tok = b.dma("S", prs, deps=[tok_zp, pv_guard.get(("win", wsl))])
        for side, nch_s, wsb, wbuf, woff, dst_c, bias_off in (
                ("q", 8, "wc1q", "c1wq", wsl * WQ, c1q_c, 0),
                ("k", 2, "wc1k", "c1wk", wsl * WK, c1k_c, 16)):
            for c in range(nch_s):
                bank = ps[c % 2]
                if ("c1", c % 2) in pv_guard:
                    b.wait("T", pv_guard[("c1", c % 2)])
                b.wait("T", tok_wc1)
                b.wait("T", win_tok)
                for tap in range(3):
                    ins = nc.tensor.matmul(
                        bank[:, 0:TC1],
                        sb[wsb][:, (tap * nch_s + c) * 128:(tap * nch_s + c + 1) * 128],
                        sb[wbuf][:, woff + c * (TC1 + 2) + tap: woff + c * (TC1 + 2) + tap + TC1],
                        start=(tap == 0), stop=(tap == 2))
                tmm = b.inc("T", ins)
                b.waits("V", [tmm, tzp])
                ta = b.inc("V", nc.vector.tensor_scalar(
                    dst_c[:, c, 2 + t0: 2 + t0 + TC1], bank[:, 0:TC1],
                    sb["biascols"][:, bias_off + c: bias_off + c + 1], None, ALU.add))
                pv_guard[("c1", c % 2)] = ta
        pv_guard[("win", wsl)] = tmm
    for g in range(4):
        bpv = 64 * (g % 2)
        for j in range(NJ):
            if ("vt",) in pv_guard:
                b.wait("T", pv_guard[("vt",)])
            tt = b.inc("T", nc.tensor.transpose(
                psb[:, 0:64],
                sb["vfm"][bpv:bpv + 64, (g // 2) * T + 128 * j: (g // 2) * T + 128 * (j + 1)],
                sb["eyeb"][bpv:bpv + 64, bpv:bpv + 64]))
            b.wait("V", tt)
            tv = b.inc("V", nc.vector.tensor_copy(
                sb["vtm"][:, g * NJ * VW + j * VW: g * NJ * VW + j * VW + 64],
                psb[:, 0:64]))
            pv_guard[("vt",)] = tv
    b.drain_dma()
    b.barrier()
    es_c1.close()
    es_raw.close()
    if upto == "p1b_c1":
        b.barrier()
        es2.close()
        es_tabs.close()
        es_qkv.close()
        es_qm.close()
        es.close()
        return nc

    es3 = ExitStack()
    for nm, shp, dt_ in (("wc2q", [128, 24 * 1024], BF16), ("wc2k", [128, 6 * 256], BF16),
                         ("qpost", [128, 8 * TN2], BF16), ("kpost", [128, 2 * TN2], BF16),
                         ("sqb", [128, 2 * TN2], BF16), ("mm2b", [128, 2 * TN2], BF16),
                         ("rqf", [128, TN2], F32), ("rqb", [128, TN2], BF16),
                         ("rsw", [128, TN2], BF16), ("rt1", [128, TN2], BF16),
                         ("rt2", [128, TN2], BF16)):
        sb[nm] = es3.enter_context(nc.sbuf_tensor(nm, shp, dt_, side="right"))
    wq_toks2 = []
    for tap in range(3):
        prs = [(sb["wc2q"][:, (tap * 8 + kc) * 1024:(tap * 8 + kc + 1) * 1024],
                dr["qc2T"][tap, 128 * kc:128 * kc + 128, :]) for kc in range(8)]
        wq_toks2.append(b.dma("G", prs))
    tok_wc2k = b.dma("G", [(sb["wc2k"][:, (tap * 2 + kc) * 256:(tap * 2 + kc + 1) * 256],
                            dr["kc2T"][tap, 128 * kc:128 * kc + 128, :])
                           for tap in range(3) for kc in range(2)])

    PAIRSWAP = []
    for i in range(16):
        PAIRSWAP += [2 * i + 1, 2 * i]

    guard = {}
    SIDES = [("q", 8, "wc2q", c1q_c, 8, 8, slice(0, TN2), "qpost", ps[3], "sel16b", 16),
             ("k", 2, "wc2k", c1k_c, 2, 18, slice(TN2, 2 * TN2), "kpost", ps[4], "sel4b", 4)]
    for ti in range(nt2):
        t0 = ti * TN2
        mdeps = [guard.get("mm2b")]
        b.waits("V", [d for d in mdeps if d])
        tv = b.inc("V", nc.vector.tensor_copy(sb["mm2b"][0:64, 0:TN2], sb["qmkm"][64:128, t0:t0 + TN2]))
        tv = b.inc("V", nc.vector.tensor_copy(sb["mm2b"][64:128, 0:TN2], sb["qmkm"][64:128, t0:t0 + TN2]))
        tv = b.inc("V", nc.vector.tensor_copy(sb["mm2b"][0:64, TN2:2 * TN2], sb["qmkm"][0:64, t0:t0 + TN2]))
        t_mm2b = b.inc("V", nc.vector.tensor_copy(sb["mm2b"][64:128, TN2:2 * TN2], sb["qmkm"][0:64, t0:t0 + TN2]))

        pending_ssq = None

        def emit_ssq(p):
            side_i, oc, tsq = p
            _, nch_s, _, _, _, _, _, _, ps_ssq, selb, selw = SIDES[side_i]
            b.waits("T", [tsq, guard.get("ssq_read")])
            ins = nc.tensor.matmul(
                ps_ssq[0:selw, 0:TN2], sb[selb][:, oc * selw:(oc + 1) * selw],
                sb["sqb"][:, ((side_i * 8 + oc) % 2) * TN2:((side_i * 8 + oc) % 2 + 1) * TN2],
                start=(oc == 0), stop=(oc == nch_s - 1))
            return b.inc("T", ins)

        for side_i, (side, nch_s, wsb, src_c, nkc, bias_off, post_sl, dst_post,
                     ps_ssq, selb, selw) in enumerate(SIDES):
            for oc in range(nch_s):
                gi = side_i * 8 + oc
                bank = ps[gi % 3]
                if ("c2", gi % 3) in guard:
                    b.wait("T", guard[("c2", gi % 3)])
                for tok in (wq_toks2 if side == "q" else [tok_wc2k]):
                    b.wait("T", tok)
                first = True
                for tap in range(3):
                    for kc in range(nkc):
                        if side == "q":
                            wsl = sb[wsb][:, (tap * 8 + kc) * 1024 + oc * 128:
                                          (tap * 8 + kc) * 1024 + (oc + 1) * 128]
                        else:
                            wsl = sb[wsb][:, (tap * 2 + kc) * 256 + oc * 128:
                                          (tap * 2 + kc) * 256 + (oc + 1) * 128]
                        ins = nc.tensor.matmul(
                            bank[:, 0:TN2], wsl, src_c[:, kc, t0 + tap: t0 + tap + TN2],
                            start=first, stop=(tap == 2 and kc == nkc - 1))
                        first = False
                tmm = b.inc("T", ins)
                if pending_ssq is not None:
                    tprev = emit_ssq(pending_ssq)
                    guard[("sq", (pending_ssq[0] * 8 + pending_ssq[1]) % 2)] = tprev
                b.waits("V", [tmm, t_mm2b, guard.get(("rope", side, oc))])
                tpost = b.inc("V", nc.vector.scalar_tensor_tensor(
                    sb[dst_post][:, oc * TN2:(oc + 1) * TN2], bank[:, 0:TN2],
                    sb["biascols"][:, bias_off + oc: bias_off + oc + 1],
                    sb["mm2b"][:, post_sl], ALU.add, ALU.add))
                guard[("c2", gi % 3)] = tpost
                guard[("post", side, oc)] = tpost
                b.waits("A", [tpost, guard.get(("sq", gi % 2))])
                tsq = b.inc("A", nc.scalar.activation(
                    sb["sqb"][:, (gi % 2) * TN2:(gi % 2 + 1) * TN2],
                    sb[dst_post][:, oc * TN2:(oc + 1) * TN2], AF.Square))
                pending_ssq = (side_i, oc, tsq)
        tssq_last = emit_ssq(pending_ssq)
        guard[("ssq_done", "k")] = tssq_last
        pending_ssq = None

        b.waits("A", [tssq_last, guard.get("rq_used")])
        ta = b.inc("A", nc.scalar.activation(sb["rqf"][0:16, :], ps[3][0:16, 0:TN2], AF.Sqrt))
        ta2 = b.inc("A", nc.scalar.activation(sb["rqf"][32:36, :], ps[4][0:4, 0:TN2], AF.Sqrt))
        b.waits("V", [ta, ta2])
        tvq = b.inc("V", nc.vector.reciprocal(sb["rqf"][0:16, :], sb["rqf"][0:16, :]))
        tvk = b.inc("V", nc.vector.reciprocal(sb["rqf"][32:36, :], sb["rqf"][32:36, :]))
        b.waits("V", [tvq, tvk, tok_r2])
        tvk = b.inc("V", nc.vector.tensor_scalar(sb["rqf"][32:36, :], sb["rqf"][32:36, :],
                                                 sb["ktcol"][32:36, 0:1], None, ALU.mult))
        b.waits("V", [tvk, guard.get("rqb_used")])
        tvq = b.inc("V", nc.vector.tensor_copy(sb["rqb"][0:16, :], sb["rqf"][0:16, :]))
        tvk = b.inc("V", nc.vector.tensor_copy(sb["rqb"][32:36, :], sb["rqf"][32:36, :]))
        guard["ssq_read"] = tvk
        guard["rqb_ready"] = tvk

        for side, nch_s, dst_post in (("q", 8, "qpost"), ("k", 2, "kpost")):
            for oc in range(nch_s):
                post_ap = sb[dst_post][:, oc * TN2:(oc + 1) * TN2]
                bankR = ps[5 + oc % 2]
                if ("bR", oc % 2) in guard:
                    b.wait("T", guard[("bR", oc % 2)])
                b.wait("T", guard["rqb_ready"])
                if side == "q":
                    tbc = b.inc("T", nc.tensor.matmul(
                        bankR[:, 0:TN2], sb["bc16b"][:, oc * 128:(oc + 1) * 128],
                        sb["rqb"][0:16, :], start=True, stop=True))
                else:
                    tbc = b.inc("T", nc.tensor.matmul(
                        bankR[:, 0:TN2], sb["bc4b"][32:36, oc * 128:(oc + 1) * 128],
                        sb["rqb"][32:36, :], start=True, stop=True))
                deps = [guard.get(("rope", side, oc)), guard.get("rope_bufs")]
                b.waits("V", [d for d in deps if d] + [guard[("post", side, oc)]])
                s1 = b.inc("V", nc.vector.stream_shuffle(sb["rsw"][:, :], post_ap, PAIRSWAP))
                s2 = b.inc("V", nc.vector.tensor_mul(sb["rt1"][:, :], post_ap,
                                                     sb["tabs"][:, t0:t0 + TN2]))
                b.waits("V", [s1])
                s3 = b.inc("V", nc.vector.tensor_mul(sb["rt2"][:, :], sb["rsw"][:, :],
                                                     sb["tabs"][:, T + t0:T + t0 + TN2]))
                b.waits("V", [s2, s3])
                s4 = b.inc("V", nc.vector.tensor_add(sb["rt1"][:, :], sb["rt1"][:, :],
                                                     sb["rt2"][:, :]))
                b.waits("V", [s4, tbc])
                if side == "q":
                    nqt2 = TN2 // 128
                    for half in range(2):
                        h = 2 * oc + half
                        g, l = h // 4, h % 4
                        q4t = sb["q4a"] if g < 2 else sb["q4b"]
                        bpq = 64 * (g % 2)
                        qt0 = t0 // 128
                        dst = q4t.ap()[bpq:bpq + 64].rearrange(
                            "p (qt l c) -> p qt l c", l=4, c=128)[:, qt0:qt0 + nqt2, l, :]
                        sv = b.inc("V", nc.vector.tensor_mul(
                            dst,
                            sb["rt1"][64 * half:64 * half + 64, :].rearrange(
                                "p (qt c) -> p qt c", c=128),
                            bankR[64 * half:64 * half + 64, 0:TN2].rearrange(
                                "p (qt c) -> p qt c", c=128)))
                else:
                    sv = b.inc("V", nc.vector.tensor_mul(
                        sb["kfin"][:, oc * T + t0: oc * T + t0 + TN2],
                        sb["rt1"][:, :], bankR[:, 0:TN2]))
                guard[("rope", side, oc)] = sv
                guard[("bR", oc % 2)] = sv
                guard["rope_bufs"] = sv
                guard["rqb_used"] = tbc
        guard["mm2b"] = tpost
    b.drain_dma()
    b.barrier()
    es3.close()
    es_qm.close()
    es2.close()
    es_tabs.close()
    if upto == "p1b":
        b.drain_dma()
        b.barrier()
        es_qkv.close()
        es.close()
        return nc

    es_oan = ExitStack()
    sb["oan"] = es_oan.enter_context(nc.sbuf_tensor("oan", [128, 8 * T], BF16, side="right"))
    es_p2a = ExitStack()
    sb["pt"] = es_p2a.enter_context(nc.sbuf_tensor("pt", [128, 3 * 512], BF16, side="right"))
    sb["o4"] = es_p2a.enter_context(nc.sbuf_tensor("o4", [128, 2 * 512], F32, side="right"))
    sb["lrb"] = es_p2a.enter_context(nc.sbuf_tensor("lrb", [128, 2 * 512], BF16, side="right"))
    sb["maskTs"] = es_p2a.enter_context(nc.sbuf_tensor("maskTs", [128, 512], F32, side="right"))
    tok_mT = b.dma("S", [(sb["maskTs"][:, :], dr["maskT"][:, :])])
    b.wait("V", tok_mT)
    NQT = T // 128
    ag = {}
    b.waits("A", [tok_r2])

    def p2a_scores(k, i, g):
        gp = 64 * (g % 2)
        q4t = sb["q4a"] if g < 2 else sb["q4b"]
        psO = ps[3 + k % 2]
        b.wait("T", ag.get(("psO", k % 2)))
        texp = {}
        tpv = None

        def emit_s(j):
            psS = ps[j % 3]
            b.wait("T", ag.get(("psS", j % 3)))
            tmm = b.inc("T", nc.tensor.matmul(
                psS[:, 0:512],
                sb["kfin"][gp:gp + 64, (g // 2) * T + 128 * j: (g // 2) * T + 128 * (j + 1)],
                q4t[gp:gp + 64, 512 * i: 512 * (i + 1)],
                start=True, stop=True))
            expdep = tmm
            if j == i:
                b.waits("V", [tmm])
                expdep = b.inc("V", nc.vector.tensor_add(psS[:, 0:512], psS[:, 0:512],
                                                         sb["maskTs"][:, :]))
            b.waits("A", [expdep, ag.get(("pt", j % 3))])
            texp[j] = b.inc("A", nc.scalar.activation(
                sb["pt"][:, (j % 3) * 512:(j % 3 + 1) * 512], psS[:, 0:512],
                AF.Exp, scale=0.125, bias=sb["negm"][:, 0:1]))
            ag[("psS", j % 3)] = texp[j]

        def emit_pv(j):
            nonlocal tpv
            b.waits("T", [texp[j]])
            tpv = b.inc("T", nc.tensor.matmul(
                psO[0:65, 0:512],
                sb["vtm"][:, g * NJ * VW + j * VW: g * NJ * VW + (j + 1) * VW],
                sb["pt"][:, (j % 3) * 512:(j % 3 + 1) * 512],
                start=(j == 0), stop=(j == i)))
            ag[("pt", j % 3)] = tpv

        for j in range(i + 1):
            emit_s(j)
            if j >= 2:
                emit_pv(j - 2)
        for j in range(max(0, i - 1), i + 1):
            emit_pv(j)
        return tpv

    def p2a_norm(k, i, g, tpv):
        psO = ps[3 + k % 2]
        osl = (k % 2) * 512
        b.waits("V", [tpv, ag.get(("o4", k % 2))])
        tev = b.inc("V", nc.vector.tensor_copy(sb["o4"][0:65, osl:osl + 512],
                                               psO[0:65, 0:512]))
        ag[("psO", k % 2)] = tev
        b.waits("V", [tev])
        trc = b.inc("V", nc.vector.reciprocal(sb["o4"][64:65, osl:osl + 512],
                                              sb["o4"][64:65, osl:osl + 512]))
        b.waits("V", [trc, ag.get(("lrb", k % 2))])
        tcb = b.inc("V", nc.vector.tensor_copy(sb["lrb"][64:65, osl:osl + 512],
                                               sb["o4"][64:65, osl:osl + 512]))
        b.waits("T", [tcb, ag.get(("psR", k % 2))])
        tbr = b.inc("T", nc.tensor.matmul(ps[5 + k % 2][0:64, 0:512],
                                          sb["onesrow"][64:65, 0:64],
                                          sb["lrb"][64:65, osl:osl + 512],
                                          start=True, stop=True))
        b.waits("V", [tbr])
        for l in range(4):
            tm = b.inc("V", nc.vector.tensor_mul(
                sb["oan"][64 * (l % 2):64 * (l % 2) + 64,
                          (2 * g + l // 2) * T + 128 * i: (2 * g + l // 2) * T + 128 * (i + 1)],
                sb["o4"][0:64, osl + 128 * l: osl + 128 * (l + 1)],
                ps[5 + k % 2][0:64, 128 * l: 128 * (l + 1)]))
        ag[("psR", k % 2)] = tm
        ag[("o4", k % 2)] = tm
        ag[("lrb", k % 2)] = tbr

    pending = None
    k = 0
    for i in range(NQT):
        for g in range(4):
            tpv = p2a_scores(k, i, g)
            if pending is not None:
                p2a_norm(*pending)
            pending = (k, i, g, tpv)
            k += 1
    p2a_norm(*pending)
    b.drain_dma()
    b.barrier()
    es_p2a.close()
    es_qkv.close()
    if upto == "p2a":
        b.drain_dma()
        b.barrier()
        es_oan.close()
        es.close()
        return nc

    TNB = min(512, T)
    ntb = T // TNB
    es_p2bl = ExitStack()
    sb["woTb"] = es_p2bl.enter_context(nc.sbuf_tensor("woTb", [128, 8 * D], BF16))
    sb["ztile"] = es_p2bl.enter_context(nc.sbuf_tensor("ztile", [128, NCH * TNB], BF16))
    es_p2bt = ExitStack()
    for nm, shp, dt_ in (("xc", [128, 3 * TNB], F32), ("x1b", [128, 2 * TNB], F32),
                         ("zsq", [128, 2 * TNB], BF16), ("x1q", [128, 2 * TNB], BF16),
                         ("za", [128, TNB], F32), ("wocsb", [128, 8], BF16),
                         ("wocsf", [128, 8], F32)):
        sb[nm] = es_p2bt.enter_context(nc.sbuf_tensor(nm, shp, dt_, side="right"))
    wo_toks = []
    for g4 in range(2):
        prs = [(sb["woTb"][:, kc * D:(kc + 1) * D], dr["woT"][128 * kc:128 * kc + 128, :])
               for kc in range(4 * g4, 4 * g4 + 4)]
        wo_toks.append(b.dma("G", prs))
    for kc in range(8):
        b.wait("V", wo_toks[kc // 4])
        tw = b.inc("V", nc.vector.reduce_sum(sb["wocsf"][:, kc:kc + 1],
                                             sb["woTb"][:, kc * D:(kc + 1) * D],
                                             mybir.AxisListType.X))
    b.wait("V", tw)
    twb = b.inc("V", nc.vector.tensor_copy(sb["wocsb"][:, :], sb["wocsf"][:, :]))

    bg = {}
    for ti in range(ntb):
        t0 = ti * TNB
        b.waits("T", [twb, bg.get("sz_read")])
        for kc in range(8):
            ins = nc.tensor.matmul(ps[5][0:1, 0:TNB], sb["wocsb"][:, kc:kc + 1],
                                   sb["oan"][:, kc * T + t0: kc * T + t0 + TNB],
                                   start=(kc == 0), stop=(kc == 7))
        tsz = b.inc("T", ins)
        pending_q2 = None

        def emit_q2(p):
            oc, tzq = p
            b.waits("T", [tzq])
            ins = nc.tensor.matmul(ps[4][0:1, 0:TNB], sb["onescol"][:, :],
                                   sb["zsq"][:, (oc % 2) * TNB:(oc % 2 + 1) * TNB],
                                   start=(oc == 0), stop=(oc == NCH - 1))
            return b.inc("T", ins)

        for oc in range(NCH):
            bank = ps[oc % 3]
            b.wait("T", bg.get(("pz", oc % 3)))
            for kc in range(8):
                ins = nc.tensor.matmul(
                    bank[:, 0:TNB],
                    sb["woTb"][:, kc * D + oc * 128: kc * D + (oc + 1) * 128],
                    sb["oan"][:, kc * T + t0: kc * T + t0 + TNB],
                    start=(kc == 0), stop=(kc == 7))
            tmm = b.inc("T", ins)
            if pending_q2 is not None:
                bg[("zsq_read", pending_q2[0] % 2)] = emit_q2(pending_q2)
            b.waits("A", [tmm, bg.get(("zt", oc))])
            tze = b.inc("A", nc.scalar.activation(sb["ztile"][:, oc * TNB:(oc + 1) * TNB],
                                                  bank[:, 0:TNB], AF.Copy))
            b.waits("A", [bg.get(("zsq_read", oc % 2))])
            tzq = b.inc("A", nc.scalar.activation(sb["zsq"][:, (oc % 2) * TNB:(oc % 2 + 1) * TNB],
                                                  bank[:, 0:TNB], AF.Square))
            bg[("pz", oc % 3)] = tzq
            pending_q2 = (oc, tzq)
            bg[("ze", oc)] = tze
        tq2 = emit_q2(pending_q2)
        bg["zsq_read"] = tq2
        b.waits("A", [tsz, tq2])
        b.waits("V", [tsz, tq2])
        rt2b = ln_rows(TNB, ps[5][0:1, 0:TNB], ps[4][0:1, 0:TNB],
                       sfs(6, TNB), rlbs(0, TNB), rlbs(1, TNB),
                       dep=bg.get("rows_dep"), rbdep=bg.get("rlb_used"))
        bg["rows_dep"] = rt2b
        bg["sz_read"] = rt2b
        b.waits("T", [rt2b, bg.get("bc2_read")])
        tb1 = b.inc("T", nc.tensor.matmul(ps[3][:, 0:TNB], sb["onesrow"][0:1, :],
                                          rlbs(0, TNB), start=True, stop=True))
        tb2 = b.inc("T", nc.tensor.matmul(ps[4][:, 0:TNB], sb["onesrow"][0:1, :],
                                          rlbs(1, TNB), start=True, stop=True))
        bg["rlb_used"] = tb2
        xtoks = {}
        for oc in range(3):
            xtoks[oc] = b.dma("S", [(sb["xc"][:, (oc % 3) * TNB:(oc % 3 + 1) * TNB],
                                     dr["x_in"][128 * oc:128 * (oc + 1), t0:t0 + TNB])],
                              deps=[bg.get(("xc", oc % 3))])
        pending_x2 = None

        def emit_x2(p):
            oc, txq = p
            b.waits("T", [txq])
            ins = nc.tensor.matmul(ps[6][0:1, 0:TNB], sb["onescol"][:, :],
                                   sb["x1q"][:, (oc % 2) * TNB:(oc % 2 + 1) * TNB],
                                   start=(oc == 0), stop=(oc == NCH - 1))
            return b.inc("T", ins)

        for oc in range(NCH):
            b.waits("V", [tb1, tb2, bg.get(("ze", oc)), bg.get("za_free")])
            tu = b.inc("V", nc.vector.tensor_mul(sb["za"][:, :],
                                                 sb["ztile"][:, oc * TNB:(oc + 1) * TNB],
                                                 ps[3][:, 0:TNB]))
            b.waits("V", [tu])
            tu = b.inc("V", nc.vector.tensor_sub(sb["za"][:, :], sb["za"][:, :], ps[4][:, 0:TNB]))
            x1sl = oc % 2
            b.waits("V", [tu, xtoks[oc], bg.get(("x1b", x1sl)), bg.get(("x1ba", x1sl))])
            tx1 = b.inc("V", nc.vector.tensor_add(sb["x1b"][:, x1sl * TNB:(x1sl + 1) * TNB],
                                                  sb["za"][:, :],
                                                  sb["xc"][:, (oc % 3) * TNB:(oc % 3 + 1) * TNB]))
            bg["za_free"] = tx1
            bg[("xc", oc % 3)] = tx1
            if oc + 3 < NCH:
                xtoks[oc + 3] = b.dma("S", [(sb["xc"][:, ((oc + 3) % 3) * TNB:((oc + 3) % 3 + 1) * TNB],
                                             dr["x_in"][128 * (oc + 3):128 * (oc + 4), t0:t0 + TNB])],
                                      deps=[bg.get(("xc", (oc + 3) % 3))])
            tws = b.dma("S", [(dr["x1s"][128 * oc:128 * (oc + 1), t0:t0 + TNB],
                               sb["x1b"][:, x1sl * TNB:(x1sl + 1) * TNB])], deps=[tx1])
            bg[("x1b", x1sl)] = tws
            b.waits("A", [tx1, bg.get(("x1q_read", oc % 2))])
            txq = b.inc("A", nc.scalar.activation(sb["x1q"][:, (oc % 2) * TNB:(oc % 2 + 1) * TNB],
                                                  sb["x1b"][:, x1sl * TNB:(x1sl + 1) * TNB],
                                                  AF.Square))
            bg[("x1ba", x1sl)] = txq
            if pending_x2 is not None:
                bg[("x1q_read", pending_x2[0] % 2)] = emit_x2(pending_x2)
            pending_x2 = (oc, txq)
        tx2 = emit_x2(pending_x2)
        bg[("x1q_read", pending_x2[0] % 2)] = tx2
        bg["bc2_read"] = tx1
        b.waits("A", [tx2])
        b.waits("V", [tx2])
        rt3 = ln_rows(TNB, None, ps[6][0:1, 0:TNB],
                      m1row(t0, TNB), r3row(t0, TNB), c3row(t0, TNB),
                      dep=bg.get("rows_dep"))
        bg["rows_dep"] = rt3
    b.drain_dma()
    b.barrier()
    es_p2bt.close()
    es_oan.close()
    es_p2bl.close()
    if upto == "p2b":
        b.drain_dma()
        b.barrier()
        es.close()
        return nc

    TN3 = min(256, T)
    nt3 = T // TN3
    es_p3l = ExitStack()
    sb["upb"] = es_p3l.enter_context(nc.sbuf_tensor("upb", [128, 32 * 1024], BF16))
    sb["downb"] = es_p3l.enter_context(nc.sbuf_tensor("downb", [128, 128 * 256], BF16))
    es_p3r = ExitStack()
    for nm, shp, dt_ in (("x1r", [128, 6 * TN3], F32), ("ftile", [128, NCH * TN3], BF16),
                         ("hidb", [128, 2 * 8 * TN3], BF16), ("yh", [128, 2 * 2 * TN3], BF16),
                         ("ysc", [128, TN3], F32), ("fsq", [128, 4 * TN3], BF16),
                         ("outb", [128, 2 * TN3], F32)):
        sb[nm] = es_p3r.enter_context(nc.sbuf_tensor(nm, shp, dt_, side="right"))
    up_toks = []
    for h4 in range(4):
        prs = []
        for h in range(4 * h4, 4 * h4 + 4):
            for kc in range(2):
                prs.append((sb["upb"][:, (2 * h + kc) * 1024:(2 * h + kc + 1) * 1024],
                            dr["up_w"][h, 128 * kc:128 * (kc + 1), :]))
        up_toks.append(b.dma("G", prs))
    dn_toks = []
    for h4 in range(4):
        prs = []
        for h in range(4 * h4, 4 * h4 + 4):
            for mc in range(8):
                prs.append((sb["downb"][:, (8 * h + mc) * 256:(8 * h + mc + 1) * 256],
                            dr["down_w"][h, 128 * mc:128 * (mc + 1), :]))
        dn_toks.append(b.dma("G", prs))

    fg = {}
    out_toks = []
    for ti in range(nt3):
        t0 = ti * TN3
        xr = {"tok": {}, "cons": fg.get("xr_cons", {})}
        def x1_load(li):
            sl = li % 6
            c = li % 32
            xr["tok"][li] = b.dma("S", [(sb["x1r"][:, sl * TN3:(sl + 1) * TN3],
                                         dr["x1s"][128 * c:128 * (c + 1), t0:t0 + TN3])],
                                  deps=[d for d in (xr["cons"].get(sl),) if d])
        def x1_slot(li):
            sl = li % 6
            return sb["x1r"][:, sl * TN3:(sl + 1) * TN3]
        def x1_done(li, tok):
            xr["cons"][li % 6] = tok
            if li + 6 < 64:
                x1_load(li + 6)
        for li in range(6):
            x1_load(li)
        b.waits("T", [fg.get("bc3_read")])
        tb3 = b.inc("T", nc.tensor.matmul(ps[4][:, 0:TN3], sb["onesrow"][0:1, :],
                                          r3row(t0, TN3), start=True, stop=False))
        tb3 = b.inc("T", nc.tensor.matmul(ps[4][:, TN3:2 * TN3], sb["onesrow"][0:1, :],
                                          c3row(t0, TN3), start=False, stop=True))
        pending_sum = None

        def emit_fsums(p):
            h, tfe, tfq = p
            b.waits("T", [tfe, tfq])
            nc.tensor.matmul(ps[5][0:1, 0:2 * TN3], sb["onescol"][:, :],
                             sb["ftile"][:, 2 * h * TN3:(2 * h + 2) * TN3],
                             start=(h == 0), stop=(h == NH - 1))
            ins = nc.tensor.matmul(ps[6][0:1, 0:2 * TN3], sb["onescol"][:, :],
                                   sb["fsq"][:, (h % 2) * 2 * TN3:(h % 2 + 1) * 2 * TN3],
                                   start=(h == 0), stop=(h == NH - 1))
            return b.inc("T", ins)

        for h in range(NH):
            ysl = h % 2
            for ci, c in enumerate((2 * h, 2 * h + 1)):
                b.waits("V", [tb3, xr["tok"][c], fg.get("ysc_free"), fg.get(("yh", ysl))])
                tyu = b.inc("V", nc.vector.tensor_mul(sb["ysc"][:, :],
                                                      x1_slot(c), ps[4][:, 0:TN3]))
                b.waits("V", [tyu])
                ty = b.inc("V", nc.vector.tensor_sub(
                    sb["yh"][:, (2 * ysl + ci) * TN3:(2 * ysl + ci + 1) * TN3],
                    sb["ysc"][:, :], ps[4][:, TN3:2 * TN3]))
                fg["ysc_free"] = ty
                x1_done(c, tyu)
            hsl = h % 2
            b.wait("T", up_toks[h // 4])
            b.wait("T", ty)
            for mc in range(8):
                bank = ps[mc // 2 % 2]
                half = mc % 2
                if half == 0:
                    b.wait("T", fg.get(("hid", mc // 2 % 2)))
                for kc in range(2):
                    ins = nc.tensor.matmul(
                        bank[:, half * TN3:(half + 1) * TN3],
                        sb["upb"][:, (2 * h + kc) * 1024 + mc * 128:(2 * h + kc) * 1024 + (mc + 1) * 128],
                        sb["yh"][:, (2 * ysl + kc) * TN3:(2 * ysl + kc + 1) * TN3],
                        start=(half == 0 and kc == 0), stop=(half == 1 and kc == 1))
                if half == 1:
                    tup = b.inc("T", ins)
                    b.waits("A", [tup, fg.get(("hidb", hsl, mc // 2))])
                    tg_ = b.inc("A", nc.scalar.activation(
                        sb["hidb"][:, hsl * 8 * TN3 + (mc - 1) * TN3: hsl * 8 * TN3 + (mc + 1) * TN3],
                        bank[:, 0:2 * TN3], AF.Gelu))
                    fg[("hid", mc // 2 % 2)] = tg_
                    fg[("gel", mc // 2)] = tg_
            fg[("yh", ysl)] = tup
            b.wait("T", dn_toks[h // 4])
            b.wait("T", fg.get("psF_read"))
            first = True
            for dc in range(2):
                for mc in range(8):
                    b.wait("T", fg[("gel", mc // 2)])
                    ins = nc.tensor.matmul(
                        ps[2][:, dc * TN3:(dc + 1) * TN3],
                        sb["downb"][:, (8 * h + mc) * 256 + dc * 128:(8 * h + mc) * 256 + (dc + 1) * 128],
                        sb["hidb"][:, hsl * 8 * TN3 + mc * TN3: hsl * 8 * TN3 + (mc + 1) * TN3],
                        start=first, stop=(dc == 1 and mc == 7))
                    first = False
            tdn = b.inc("T", ins)
            fg[("hidb", hsl, 0)] = tdn
            fg[("hidb", hsl, 1)] = tdn
            fg[("hidb", hsl, 2)] = tdn
            fg[("hidb", hsl, 3)] = tdn
            b.waits("A", [tdn, fg.get(("ft", h))])
            tfe = b.inc("A", nc.scalar.activation(sb["ftile"][:, 2 * h * TN3:(2 * h + 2) * TN3],
                                                  ps[2][:, 0:2 * TN3], AF.Copy))
            b.waits("A", [fg.get(("fsq_read", h % 2))])
            tfq = b.inc("A", nc.scalar.activation(sb["fsq"][:, (h % 2) * 2 * TN3:(h % 2 + 1) * 2 * TN3],
                                                  ps[2][:, 0:2 * TN3], AF.Square))
            fg["psF_read"] = tfq
            fg[("fe", h)] = tfe
            if pending_sum is not None:
                fg[("fsq_read", pending_sum[0] % 2)] = emit_fsums(pending_sum)
            pending_sum = (h, tfe, tfq)
        tS2 = emit_fsums(pending_sum)
        fg[("fsq_read", pending_sum[0] % 2)] = tS2
        pending_sum = None
        tS1 = tS2
        fg["x1t_read"] = tyu
        b.waits("A", [tS1, tS2, fg.get("rows_dep4")])
        SF4, SF5 = 4 * 512, 5 * 512
        th1 = b.inc("A", nc.scalar.activation(sb["smallf"][0:1, SF4:SF4 + 2 * TN3],
                                              ps[5][0:1, 0:2 * TN3], AF.Copy))
        th2 = b.inc("A", nc.scalar.activation(sb["smallf"][0:1, SF5:SF5 + 2 * TN3],
                                              ps[6][0:1, 0:2 * TN3], AF.Copy))
        b.waits("V", [th1, th2, fg.get("rows_dep4")])
        th3 = b.inc("V", nc.vector.tensor_add(sb["smallf"][0:1, SF4:SF4 + TN3],
                                              sb["smallf"][0:1, SF4:SF4 + TN3],
                                              sb["smallf"][0:1, SF4 + TN3:SF4 + 2 * TN3]))
        th4 = b.inc("V", nc.vector.tensor_add(sb["smallf"][0:1, SF5:SF5 + TN3],
                                              sb["smallf"][0:1, SF5:SF5 + TN3],
                                              sb["smallf"][0:1, SF5 + TN3:SF5 + 2 * TN3]))
        b.waits("A", [th3, th4])
        b.waits("V", [th3, th4])
        rt4 = ln_rows(TN3, sb["smallf"][0:1, SF4:SF4 + TN3], sb["smallf"][0:1, SF5:SF5 + TN3],
                      sfs(6, TN3), rlbs(0, TN3), rlbs(1, TN3),
                      dep=fg.get("rows_dep4"), rbdep=fg.get("rlb_used"))
        fg["rows_dep4"] = rt4
        fg["sums_read"] = rt4
        b.waits("T", [rt4, fg.get("bc4_read")])
        tb4 = b.inc("T", nc.tensor.matmul(ps[3][:, 0:TN3], sb["onesrow"][0:1, :],
                                          rlbs(0, TN3), start=True, stop=False))
        tb4 = b.inc("T", nc.tensor.matmul(ps[3][:, TN3:2 * TN3], sb["onesrow"][0:1, :],
                                          rlbs(1, TN3), start=False, stop=True))
        fg["rlb_used"] = tb4
        for oc in range(NCH):
            osl2 = oc % 2
            b.waits("V", [tb4, fg[("fe", oc // 2)], fg.get("ysc_free"), fg.get(("outb", osl2))])
            tu = b.inc("V", nc.vector.tensor_mul(sb["ysc"][:, :],
                                                 sb["ftile"][:, oc * TN3:(oc + 1) * TN3],
                                                 ps[3][:, 0:TN3]))
            b.waits("V", [tu])
            tu = b.inc("V", nc.vector.tensor_sub(sb["ysc"][:, :], sb["ysc"][:, :],
                                                 ps[3][:, TN3:2 * TN3]))
            b.waits("V", [tu, xr["tok"][32 + oc]])
            to = b.inc("V", nc.vector.tensor_add(sb["outb"][:, osl2 * TN3:(osl2 + 1) * TN3],
                                                 sb["ysc"][:, :], x1_slot(32 + oc)))
            fg["ysc_free"] = to
            x1_done(32 + oc, to)
            tso = b.dma("S", [(dr["y_out"][128 * oc:128 * (oc + 1), t0:t0 + TN3],
                               sb["outb"][:, osl2 * TN3:(osl2 + 1) * TN3])], deps=[to])
            fg[("outb", osl2)] = tso
            out_toks.append(tso)
        fg["xr_cons"] = xr["cons"]
        fg["bc4_read"] = tu
        fg["bc3_read"] = ty
    b.drain_dma()
    b.barrier()
    es_p3r.close()
    es_p3l.close()
    es.close()
    return nc


def kernel(**inputs):
    in_maps, perm, iperm, B, T = host_stage(inputs)
    nc = build_program(T)
    res = run_bass_kernel_spmd(nc, in_maps, core_ids=list(range(B)))
    return assemble_output(res.results, perm, B, T)


# version 10
# speedup vs baseline: 1.4662x; 1.1506x over previous
import math
from contextlib import ExitStack

import numpy as np
import concourse.bass as bass
import concourse.mybir as mybir
from concourse.bass_utils import run_bass_kernel_spmd

F32 = mybir.dt.float32
BF16 = mybir.dt.bfloat16
I32 = mybir.dt.int32
AF = mybir.ActivationFunctionType
ALU = mybir.AluOpType

D = 4096
NH = 16
NKV = 4
HD = 64
LAT = NH * HD
KVD = NKV * HD
QKV = LAT + 2 * KVD
NCH = D // 128
EPS = 1e-5
NEG_BIG = -3.0e38

def _split(x):
    a = float(np.float32(x))
    b = float(np.float32(x - a))
    return a, b

LN1E4_S1, LN1E4_S2 = _split(-math.log(10000.0) / 32.0)
PI1, _PI2 = _split(math.pi)
TWOPI1, TWOPI2 = _split(2.0 * math.pi)
INV2PI1 = float(np.float32(1.0 / TWOPI1))
HPI1, HPI2 = _split(math.pi / 2.0)


class Bld:

    ENG = ("T", "V", "A", "S", "G")

    def __init__(self, nc, es):
        self.nc = nc
        self.es = es
        self.eng = {"T": nc.tensor, "V": nc.vector, "A": nc.scalar,
                    "S": nc.sync, "G": nc.gpsimd}
        self.sems = {}
        self.counts = {}
        self.wmark = {e: {} for e in self.ENG}
        for e in self.ENG:
            self._mksem("p" + e)
        self.rings = {}
        for q, nslots in (("S", 8), ("G", 8)):
            slots = []
            for i in range(nslots):
                nm = f"dq{q}{i}"
                self._mksem(nm)
                slots.append(nm)
            self.rings[q] = {"slots": slots, "next": 0, "intok": {}}

    def _mksem(self, name):
        self.sems[name] = self.es.enter_context(self.nc.semaphore(name))
        self.counts[name] = 0

    def wait(self, e, tok):
        if tok is None:
            return
        sem, val = tok
        if self.wmark[e].get(sem, 0) >= val:
            return
        self.eng[e].wait_ge(self.sems[sem], val)
        self.wmark[e][sem] = val

    def waits(self, e, toks):
        for t in toks:
            self.wait(e, t)

    def tok(self, e):
        return ("p" + e, self.counts["p" + e])

    def inc(self, e, instr):
        sem = "p" + e
        self.counts[sem] += 1
        instr.then_inc(self.sems[sem], 1)
        return (sem, self.counts[sem])

    def dma(self, q, pairs, deps=()):
        ring = self.rings[q]
        slot = ring["slots"][ring["next"]]
        ring["next"] = (ring["next"] + 1) % len(ring["slots"])
        if self.counts[slot]:
            self.wait(q, (slot, self.counts[slot]))
        self.waits(q, deps)
        for p in pairs:
            kw = p[2] if len(p) > 2 else {}
            self.eng[q].dma_start(p[0], p[1], **kw).then_inc(self.sems[slot], 16)
            self.counts[slot] += 16
        return (slot, self.counts[slot])

    def barrier(self, engines="TVASG"):
        for e in engines:
            self.wait(e, ("p" + e, self.counts["p" + e]))
        toks = [self.inc(e, self.eng[e].nop()) for e in engines]
        for e in engines:
            for t in toks:
                self.wait(e, t)

    def drain_dma(self):
        for q, ring in self.rings.items():
            for slot in ring["slots"]:
                if self.counts[slot]:
                    self.wait(q, (slot, self.counts[slot]))


def host_stage(inp):
    perm = np.asarray(inp["L0_ffn_perm"])
    iperm = np.argsort(perm)
    x = np.asarray(inp["L0_attn_x"])
    B, T, _ = x.shape

    shared = {}
    shared["wqkvT"] = np.ascontiguousarray(np.asarray(inp["L0_attn_w_qkv"]).T[perm])
    shared["woT"] = np.ascontiguousarray(np.asarray(inp["L0_attn_w_o"])[perm].T)

    qc1 = np.asarray(inp["L0_attn_qc1_w"])
    qc1bd = np.zeros((3, 8, 128, 128), dtype=np.float32)
    for tap in range(3):
        for c in range(8):
            for gg in range(2):
                qc1bd[tap, c, 64 * gg:64 * gg + 64, 64 * gg:64 * gg + 64] = \
                    qc1[128 * c + 64 * gg:128 * c + 64 * gg + 64, :, tap].T
    shared["qc1bd"] = qc1bd
    shared["qc2T"] = np.ascontiguousarray(np.asarray(inp["L0_attn_qc2_w"]).transpose(2, 1, 0))
    kc1 = np.asarray(inp["L0_attn_kc1_w"])
    kc1bd = np.zeros((3, 2, 128, 128), dtype=np.float32)
    for tap in range(3):
        for c in range(2):
            for gg in range(2):
                kc1bd[tap, c, 64 * gg:64 * gg + 64, 64 * gg:64 * gg + 64] = \
                    kc1[128 * c + 64 * gg:128 * c + 64 * gg + 64, :, tap].T
    shared["kc1bd"] = kc1bd
    shared["kc2T"] = np.ascontiguousarray(np.asarray(inp["L0_attn_kc2_w"]).transpose(2, 1, 0))

    shared["qc1b"] = np.ascontiguousarray(np.asarray(inp["L0_attn_qc1_b"]).reshape(8, 128).T)
    shared["qc2b"] = np.ascontiguousarray(np.asarray(inp["L0_attn_qc2_b"]).reshape(8, 128).T)
    shared["kc1b"] = np.ascontiguousarray(np.asarray(inp["L0_attn_kc1_b"]).reshape(2, 128).T)
    shared["kc2b"] = np.ascontiguousarray(np.asarray(inp["L0_attn_kc2_b"]).reshape(2, 128).T)
    shared["ktemp"] = np.asarray(inp["L0_attn_key_temp"]).reshape(1, 1).astype(np.float32)
    shared["up_w"] = np.ascontiguousarray(np.asarray(inp["L0_ffn_up"]))
    shared["down_w"] = np.ascontiguousarray(np.asarray(inp["L0_ffn_down"]))

    shared["eye128"] = np.eye(128, dtype=np.float32)
    kk = np.arange(128)[:, None]
    qq = np.tile(np.arange(128)[None, :], (1, 4)).reshape(1, 512)
    shared["maskT"] = np.where(kk > qq, np.float32(NEG_BIG), np.float32(0.0)).astype(np.float32)
    dl = np.arange(128) % 64
    shared["msk16"] = (dl[:, None] == np.arange(64)[None, :]).astype(np.float32) / 32.0
    shared["msk4"] = (dl[:, None] == np.arange(64)[None, :]).astype(np.float32) / 8.0
    sel16 = np.zeros((8, 128, 16), dtype=np.float32)
    bc16 = np.zeros((8, 16, 128), dtype=np.float32)
    for c in range(8):
        for p in range(128):
            h = 2 * c + p // 64
            sel16[c, p, h] = 1.0
            bc16[c, h, p] = 1.0
    pvec = np.arange(128)
    ivec = np.zeros((128, 2), dtype=np.float32)
    ivec[:, 0] = (pvec % 64) // 2
    ivec[:, 1] = np.where(pvec % 2 == 0, -1.0, 1.0)
    shared["ivec"] = ivec
    shared["sel16"] = np.ascontiguousarray(sel16.transpose(1, 0, 2)).reshape(128, 8 * 16)
    shared["bc16"] = np.ascontiguousarray(bc16.transpose(1, 0, 2)).reshape(16, 8 * 128)
    sel4 = np.zeros((2, 128, 4), dtype=np.float32)
    bc4 = np.zeros((2, 4, 128), dtype=np.float32)
    for c in range(2):
        for p in range(128):
            h = 2 * c + p // 64
            sel4[c, p, h] = 1.0
            bc4[c, h, p] = 1.0
    shared["sel4"] = np.ascontiguousarray(sel4.transpose(1, 0, 2)).reshape(128, 2 * 4)
    bc4p = np.zeros((36, 2 * 128), dtype=np.float32)
    bc4p[32:36] = np.ascontiguousarray(bc4.transpose(1, 0, 2)).reshape(4, 2 * 128)
    shared["bc4"] = bc4p

    in_maps = []
    for b in range(B):
        m = dict(shared)
        m["x_in"] = np.ascontiguousarray(x[b].T[perm])
        in_maps.append(m)
    return in_maps, perm, iperm, B, T


def assemble_output(results, perm, B, T):
    out = np.empty((B, T, D), dtype=np.float32)
    for b in range(B):
        y = np.asarray(results[b]["y_out"])
        out[b][:, perm] = y.T
    return out


def build_program(T, upto="p3"):
    nc = bass.Bass("TRN2", target_bir_lowering=False)
    es = ExitStack()

    dr = {}
    dr["x_in"] = nc.dram_tensor("x_in", [D, T], F32, kind="ExternalInput")
    dr["wqkvT"] = nc.dram_tensor("wqkvT", [D, QKV], F32, kind="ExternalInput")
    dr["woT"] = nc.dram_tensor("woT", [LAT, D], F32, kind="ExternalInput")
    dr["qc1bd"] = nc.dram_tensor("qc1bd", [3, 8, 128, 128], F32, kind="ExternalInput")
    dr["qc2T"] = nc.dram_tensor("qc2T", [3, 1024, 1024], F32, kind="ExternalInput")
    dr["kc1bd"] = nc.dram_tensor("kc1bd", [3, 2, 128, 128], F32, kind="ExternalInput")
    dr["kc2T"] = nc.dram_tensor("kc2T", [3, 256, 256], F32, kind="ExternalInput")
    dr["qc1b"] = nc.dram_tensor("qc1b", [128, 8], F32, kind="ExternalInput")
    dr["qc2b"] = nc.dram_tensor("qc2b", [128, 8], F32, kind="ExternalInput")
    dr["kc1b"] = nc.dram_tensor("kc1b", [128, 2], F32, kind="ExternalInput")
    dr["kc2b"] = nc.dram_tensor("kc2b", [128, 2], F32, kind="ExternalInput")
    dr["ktemp"] = nc.dram_tensor("ktemp", [1, 1], F32, kind="ExternalInput")
    dr["up_w"] = nc.dram_tensor("up_w", [16, 256, 1024], F32, kind="ExternalInput")
    dr["down_w"] = nc.dram_tensor("down_w", [16, 1024, 256], F32, kind="ExternalInput")
    dr["eye128"] = nc.dram_tensor("eye128", [128, 128], F32, kind="ExternalInput")
    dr["maskT"] = nc.dram_tensor("maskT", [128, 512], F32, kind="ExternalInput")
    dr["msk16"] = nc.dram_tensor("msk16", [128, 64], F32, kind="ExternalInput")
    dr["msk4"] = nc.dram_tensor("msk4", [128, 64], F32, kind="ExternalInput")
    dr["ivec"] = nc.dram_tensor("ivec", [128, 2], F32, kind="ExternalInput")
    dr["sel16"] = nc.dram_tensor("sel16", [128, 8 * 16], F32, kind="ExternalInput")
    dr["bc16"] = nc.dram_tensor("bc16", [16, 8 * 128], F32, kind="ExternalInput")
    dr["sel4"] = nc.dram_tensor("sel4", [128, 2 * 4], F32, kind="ExternalInput")
    dr["bc4"] = nc.dram_tensor("bc4", [36, 2 * 128], F32, kind="ExternalInput")
    dr["y_out"] = nc.dram_tensor("y_out", [D, T], F32, kind="ExternalOutput")
    dr["x1s"] = nc.dram_tensor("x1s", [D, T], F32)
    dr["qrawD"] = nc.dram_tensor("qrawD", [128, 8 * (T + 2)], BF16)
    dr["krawD"] = nc.dram_tensor("krawD", [128, 2 * (T + 2)], BF16)
    dr["rowscr"] = nc.dram_tensor("rowscr", [1, max(T, 2048)], F32)

    b = Bld(nc, es)
    blk = es.enter_context(nc.Block())

    PT = T + 2
    sb = {}
    def alloc(name, shape, dtype):
        sb[name] = es.enter_context(nc.sbuf_tensor(name, shape, dtype))
        return sb[name]

    es_qm = ExitStack()
    sb["qmkm"] = es_qm.enter_context(nc.sbuf_tensor("qmkm", [128, T], F32, side="right"))
    es_raw = ExitStack()
    sb["vfm"] = es_raw.enter_context(nc.sbuf_tensor("vfm", [128, 2 * T], BF16, side="right"))
    sb["zrow"] = es_raw.enter_context(nc.sbuf_tensor("zrow", [128, 16], BF16, side="right"))
    alloc("rows", [128, T], BF16)
    def m1row(t0, TN):
        return sb["rows"][0:1, t0:t0 + TN]
    alloc("rlb", [128, 2 * 512], BF16)
    def rlbs(i, TN):
        return sb["rlb"][0:1, i * 512:i * 512 + TN]
    alloc("r3c3b", [128, 2 * T], BF16)
    def r3row(t0, TN):
        return sb["r3c3b"][0:1, t0:t0 + TN]
    def c3row(t0, TN):
        return sb["r3c3b"][0:1, T + t0:T + t0 + TN]
    alloc("eyeb", [128, 128], BF16)
    alloc("msk16b", [128, 64], BF16)
    alloc("msk4b", [128, 64], BF16)
    alloc("sel16b", [128, 8 * 16], BF16)
    alloc("bc16b", [16, 8 * 128], BF16)
    alloc("sel4b", [128, 2 * 4], BF16)
    alloc("bc4b", [36, 2 * 128], BF16)
    alloc("biascols", [128, 20], F32)
    alloc("onescol", [128, 1], BF16)
    alloc("onesrow", [128, 128], BF16)
    alloc("negm", [128, 1], F32)
    alloc("cc", [128, 8], F32)
    alloc("ktcol", [128, 1], F32)
    alloc("smallf", [128, 7 * 512 + 64], F32)
    def sfs(i, TN):
        return sb["smallf"][0:1, i * 512:i * 512 + TN]

    ps = []
    for i in range(7):
        ps.append(es.enter_context(nc.psum_tensor(f"pb{i}", [128, 512], F32)))
    es_psb = ExitStack()
    psb = es_psb.enter_context(nc.psum_tensor("pbT", [128, 1024], BF16))

    tok_c = b.dma("S", [
        (sb["biascols"][:, 0:8], dr["qc1b"][:, :]),
        (sb["biascols"][:, 8:16], dr["qc2b"][:, :]),
        (sb["biascols"][:, 16:18], dr["kc1b"][:, :]),
        (sb["biascols"][:, 18:20], dr["kc2b"][:, :]),
        (sb["smallf"][0:1, 7 * 512:7 * 512 + 1], dr["ktemp"][:, :]),
        (sb["cc"][:, 5:6], dr["ivec"][:, 0:1], dict(allow_slow_non_contiguous=True)),
        (sb["cc"][:, 6:7], dr["ivec"][:, 1:2], dict(allow_slow_non_contiguous=True)),
    ])
    tok_cb = b.dma("G", [
        (sb["eyeb"][:, :], dr["eye128"][:, :]),
        (sb["msk16b"][:, :], dr["msk16"][:, :]),
        (sb["msk4b"][:, :], dr["msk4"][:, :]),
        (sb["sel16b"][:, :], dr["sel16"][:, :]),
        (sb["bc16b"][:, :], dr["bc16"][:, :]),
        (sb["sel4b"][:, :], dr["sel4"][:, :]),
        (sb["bc4b"][:, :], dr["bc4"][:, :]),
    ])
    b.wait("V", tok_c)
    tv = b.inc("V", nc.vector.memset(sb["onescol"][:, :], 1.0))
    tv = b.inc("V", nc.vector.memset(sb["onesrow"][:, :], 1.0))
    tcc = b.inc("V", nc.vector.memset(sb["cc"][:, 0:1], EPS))
    tcc = b.inc("V", nc.vector.memset(sb["cc"][:, 1:2], HPI1))
    b.waits("A", [tcc])
    ta = b.inc("A", nc.scalar.activation(sb["smallf"][0:1, 7 * 512 + 1:7 * 512 + 2],
                                         sb["smallf"][0:1, 7 * 512:7 * 512 + 1],
                                         AF.Abs, scale=0.125))
    b.wait("A", ta)
    ta = b.inc("A", nc.scalar.activation(sb["smallf"][0:1, 7 * 512 + 2:7 * 512 + 3],
                                         sb["smallf"][0:1, 7 * 512 + 1:7 * 512 + 2],
                                         AF.Copy, scale=-1.0))
    tok_r = b.dma("S", [(dr["rowscr"][0:1, 0:2],
                         sb["smallf"][0:1, 7 * 512:7 * 512 + 3:2],
                         dict(allow_slow_non_contiguous=True))], deps=[ta])
    tok_r2 = b.dma("S", [
        (sb["negm"][:, 0:1], bass.AP(dr["rowscr"], 1, [[0, 128], [1, 1]])),
        (sb["ktcol"][32:36, 0:1], bass.AP(dr["rowscr"], 0, [[0, 4], [1, 1]])),
    ], deps=[tok_r])

    b.wait("V", tok_cb)
    tv0 = b.inc("V", nc.vector.memset(sb["zrow"][:, :], 0.0))
    tvv = b.inc("V", nc.vector.memset(sb["vfm"][:, T:T + 1], 0.0))
    tok_zp = b.dma("S", [
        (dr["qrawD"].ap().rearrange("p (c t) -> p c t", c=8)[:, :, 0:2],
         sb["zrow"][:, 0:16].rearrange("p (c t) -> p c t", c=8)),
        (dr["krawD"].ap().rearrange("p (c t) -> p c t", c=2)[:, :, 0:2],
         sb["zrow"][:, 0:4].rearrange("p (c t) -> p c t", c=2)),
    ], deps=[tv0])

    if upto == "p0":
        b.drain_dma()
        b.barrier()
        es_raw.close()
        es_qm.close()
        es_psb.close()
        es.close()
        return nc
    TN1 = min(256, T)
    nt1 = T // TN1
    es1 = ExitStack()
    def alloc1(name, shape, dtype):
        sb[name] = es1.enter_context(nc.sbuf_tensor(name, shape, dtype, side="right"))
        return sb[name]
    alloc1("wqkvb", [128, NCH * QKV], BF16)
    alloc1("x32", [128, (NCH // 4) * TN1], F32)
    alloc1("xb", [128, 2 * NCH * TN1], BF16)
    alloc1("xsq", [128, 4 * TN1], BF16)
    alloc1("r1s", [128, 2 * TN1], F32)
    alloc1("uevac", [128, TN1], F32)
    alloc1("stg", [128, 3 * TN1], BF16)
    alloc1("wsumb", [1, QKV], BF16)

    wq_toks = []
    for g in range(4):
        prs = []
        for kc in range(8 * g, 8 * g + 8):
            prs.append((sb["wqkvb"][:, kc * QKV:(kc + 1) * QKV],
                        dr["wqkvT"][128 * kc:128 * kc + 128, :]))
        wq_toks.append(b.dma("G", prs))

    b.wait("T", tv)
    for seg in range(3):
        for kc in range(NCH):
            b.wait("T", wq_toks[kc // 8])
            ins = nc.tensor.matmul(ps[6][0:1, 0:512], sb["onescol"][:, :],
                                   sb["wqkvb"][:, kc * QKV + seg * 512: kc * QKV + (seg + 1) * 512],
                                   start=(kc == 0), stop=(kc == NCH - 1))
        tt = b.inc("T", ins)
        b.wait("V", tt)
        tv = b.inc("V", nc.vector.tensor_copy(sb["wsumb"][0:1, seg * 512:(seg + 1) * 512],
                                              ps[6][0:1, 0:512]))
        b.wait("T", tv)

    if upto == "p1a_w":
        b.drain_dma()
        b.barrier()
        es1.close()
        es_raw.close()
        es_qm.close()
        es.close()
        return nc

    def ln_rows(TN, ps_sx, ps_sx2, m_ap, rb_r_ap, rb_c_ap, dep=None, rbdep=None):
        b.wait("A", dep)
        if ps_sx is not None:
            ta = b.inc("A", nc.scalar.activation(m_ap, ps_sx, AF.Copy, scale=1.0 / D))
            b.wait("A", ta)
        ta = b.inc("A", nc.scalar.activation(sfs(0, TN), ps_sx2, AF.Copy, scale=1.0 / D))
        ta2 = b.inc("A", nc.scalar.activation(sfs(1, TN), m_ap, AF.Square))
        b.waits("V", [ta, ta2, dep])
        tv = b.inc("V", nc.vector.tensor_sub(sfs(0, TN), sfs(0, TN), sfs(1, TN)))
        b.wait("A", tv)
        ta = b.inc("A", nc.scalar.activation(sfs(2, TN), sfs(0, TN), AF.Sqrt,
                                             bias=sb["cc"][0:1, 0:1]))
        b.wait("V", ta)
        tv = b.inc("V", nc.vector.reciprocal(sfs(2, TN), sfs(2, TN)))
        b.wait("V", tv)
        tv = b.inc("V", nc.vector.tensor_mul(sfs(3, TN), m_ap, sfs(2, TN)))
        b.waits("V", [tv, rbdep])
        tv = b.inc("V", nc.vector.tensor_copy(rb_r_ap, sfs(2, TN)))
        tv = b.inc("V", nc.vector.tensor_copy(rb_c_ap, sfs(3, TN)))
        return tv

    last_evac = {}

    for t in wq_toks:
        b.wait("T", t)

    HN = NCH // 4
    mm_tok = None
    xb_used = [None, None]
    cast_tok = [None, None, None, None]

    def load_cast_tile(ti):
        t0 = ti * TN1
        par = ti % 2
        for hf in range(4):
            prs = []
            for kc in range(HN * hf, HN * (hf + 1)):
                prs.append((sb["x32"][:, (kc - HN * hf) * TN1:(kc - HN * hf + 1) * TN1],
                            dr["x_in"][128 * kc:128 * kc + 128, t0:t0 + TN1]))
            x_tok = b.dma("S", prs, deps=[d for d in (cast_tok[(hf + 3) % 4],) if d])
            b.waits("V", [x_tok, xb_used[par]])
            cast_tok[hf] = b.inc("V", nc.vector.tensor_copy(
                sb["xb"][:, (par * NCH + hf * HN) * TN1:(par * NCH + (hf + 1) * HN) * TN1],
                sb["x32"][:, :]))
        return cast_tok[3]

    tile_cast = {0: load_cast_tile(0)}
    for ti in range(nt1):
        t0 = ti * TN1
        par = ti % 2
        xo = par * NCH * TN1
        if ti + 1 < nt1:
            tile_cast[ti + 1] = load_cast_tile(ti + 1)
        cast_now = tile_cast[ti]

        b.waits("A", [cast_now])
        if ti > 0:
            b.wait("T", last_evac["rows_done"])
        tsq = None
        for gq in range(8):
            if tsq is not None:
                b.wait("A", tsq)
            ta = b.inc("A", nc.scalar.activation(
                sb["xsq"][:, :], sb["xb"][:, xo + gq * 4 * TN1: xo + (gq + 1) * 4 * TN1],
                AF.Square))
            b.wait("T", ta)
            b.waits("T", [cast_now])
            for c4 in range(4):
                kc = gq * 4 + c4
                ins = nc.tensor.matmul(ps[5][0:1, 0:TN1], sb["onescol"][:, :],
                                       sb["xb"][:, xo + kc * TN1: xo + (kc + 1) * TN1],
                                       start=(kc == 0), stop=False)
                ins = nc.tensor.matmul(ps[5][0:1, TN1:2 * TN1], sb["onescol"][:, :],
                                       sb["xsq"][:, c4 * TN1:(c4 + 1) * TN1],
                                       start=False, stop=(kc == NCH - 1))
            tsq = b.inc("T", ins)
        stats_tok = tsq
        if upto == "p1a_s":
            break
        b.wait("A", stats_tok)
        b.wait("V", stats_tok)
        rows_tok = ln_rows(TN1, ps[5][0:1, 0:TN1], ps[5][0:1, TN1:2 * TN1],
                           m1row(t0, TN1), rlbs(0, TN1), rlbs(1, TN1),
                           dep=last_evac.get("rows_done"),
                           rbdep=last_evac.get("rlb_used"))
        last_evac["rows_done"] = rows_tok

        if upto == "p1a_r":
            break
        b.wait("T", rows_tok)
        tt = b.inc("T", nc.tensor.matmul(ps[4][:, 0:TN1], sb["onesrow"][0:1, :],
                                         rlbs(0, TN1), start=True, stop=True))
        b.wait("V", tt)
        if last_evac.get("r1s_free") is not None:
            b.wait("V", last_evac["r1s_free"])
        par = ti % 2
        r1s_tok = b.inc("V", nc.vector.tensor_copy(sb["r1s"][:, par * TN1:(par + 1) * TN1],
                                                   ps[4][:, 0:TN1]))
        b.wait("T", r1s_tok)

        if upto == "p1a_b":
            break
        for oc in range(12):
            pq = ps[oc % 2]
            pc = ps[2 + oc % 2]
            if ("pq", oc % 2) in last_evac:
                b.wait("T", last_evac[("pq", oc % 2)])
            for kc in range(NCH):
                ins = nc.tensor.matmul(pq[:, 0:TN1],
                                       sb["wqkvb"][:, kc * QKV + oc * 128: kc * QKV + (oc + 1) * 128],
                                       sb["xb"][:, xo + kc * TN1: xo + (kc + 1) * TN1],
                                       start=(kc == 0), stop=(kc == NCH - 1))
            mm_tok = b.inc("T", ins)
            xb_used[par] = mm_tok
            if ("pc", oc % 2) in last_evac:
                b.wait("T", last_evac[("pc", oc % 2)])
            tt2 = b.inc("T", nc.tensor.matmul(pc[:, 0:TN1],
                                              sb["wsumb"][0:1, oc * 128:(oc + 1) * 128],
                                              rlbs(1, TN1),
                                              start=True, stop=True))
            last_evac["rlb_used"] = tt2
            b.waits("V", [mm_tok, tt2, r1s_tok])
            if ("ue",) in last_evac:
                b.wait("V", last_evac[("ue",)])
            tv1 = b.inc("V", nc.vector.tensor_mul(
                sb["uevac"][:, 0:TN1],
                sb["r1s"][:, par * TN1:(par + 1) * TN1], pq[:, 0:TN1]))
            last_evac[("pq", oc % 2)] = tv1
            b.wait("V", tv1)
            if oc < 10:
                ss = oc % 3
                b.waits("V", [last_evac.get(("stgd", ss)), last_evac.get(("stgm", ss))])
                dst = sb["stg"][:, ss * TN1:(ss + 1) * TN1]
            elif oc == 10:
                dst = sb["vfm"][:, t0:t0 + TN1]
            else:
                hi = min(t0 + 1 + TN1, T)
                dst = sb["vfm"][:, T + t0 + 1: T + hi]
            if oc == 11:
                w = hi - (t0 + 1)
                tv2 = b.inc("V", nc.vector.tensor_sub(
                    dst, sb["uevac"][:, 0:w], pc[:, 0:w]))
            else:
                tv2 = b.inc("V", nc.vector.tensor_sub(
                    dst, sb["uevac"][:, 0:TN1], pc[:, 0:TN1]))
            last_evac[("pc", oc % 2)] = tv2
            last_evac[("ue",)] = tv2
            if oc < 10:
                if oc < 8:
                    ddst = dr["qrawD"][:, oc * PT + 2 + t0: oc * PT + 2 + t0 + TN1]
                else:
                    ddst = dr["krawD"][:, (oc - 8) * PT + 2 + t0: (oc - 8) * PT + 2 + t0 + TN1]
                last_evac[("stgd", ss)] = b.dma("S", [(ddst, dst)], deps=[tv2])
                b.wait("T", tv2)
                if oc < 8:
                    tmn = b.inc("T", nc.tensor.matmul(ps[4][0:64, 0:TN1], sb["msk16b"][:, :],
                                                      dst, start=(oc == 0), stop=(oc == 7)))
                else:
                    tmn = b.inc("T", nc.tensor.matmul(ps[6][0:64, 0:TN1], sb["msk4b"][:, :],
                                                      dst, start=(oc == 8), stop=(oc == 9)))
                last_evac[("stgm", ss)] = tmn
                if oc == 7:
                    tqm = tmn
                if oc == 9:
                    tkm = tmn
        if upto == "p1a_q":
            break
        b.waits("V", [tqm, tkm])
        tv = b.inc("V", nc.vector.tensor_copy(sb["qmkm"][0:64, t0:t0 + TN1], ps[4][0:64, 0:TN1]))
        b.wait("V", tv)
        tv = b.inc("V", nc.vector.tensor_copy(sb["qmkm"][64:128, t0:t0 + TN1], ps[6][0:64, 0:TN1]))
        b.wait("T", tv)
        last_evac["r1s_free"] = last_evac[("pq", 1)]

    b.drain_dma()
    b.barrier()
    es1.close()
    if upto.startswith("p1a"):
        b.drain_dma()
        b.barrier()
        es_psb.close()
        es.close()
        return nc

    TC1 = min(512, T)
    TN2 = min(512, T)
    nt2 = T // TN2
    NJ = T // 128
    VW = 65
    es_qkv = ExitStack()
    sb["vtm"] = es_qkv.enter_context(nc.sbuf_tensor("vtm", [128, 4 * (NJ * VW)], BF16))
    sb["q4a"] = es_qkv.enter_context(nc.sbuf_tensor("q4a", [128, 4 * T], BF16))
    sb["q4b"] = es_qkv.enter_context(nc.sbuf_tensor("q4b", [128, 4 * T], BF16))
    sb["kfin"] = es_qkv.enter_context(nc.sbuf_tensor("kfin", [128, 2 * T], BF16))
    es_tabs = ExitStack()
    sb["tabs"] = es_tabs.enter_context(nc.sbuf_tensor("tabs", [128, 2 * T], BF16))
    es_tab = ExitStack()
    sb["tabscr"] = es_tab.enter_context(nc.sbuf_tensor("tabscr", [128, 4 * T], F32, side="right"))
    sb["kint"] = es_tab.enter_context(nc.sbuf_tensor("kint", [128, T], I32, side="right"))

    def ts_op(out, in0, s1, s2, op0, op1=ALU.bypass, deps=(), eng="V"):
        b.waits(eng, deps)
        return b.inc(eng, nc.vector.tensor_scalar(out, in0, s1, s2, op0, op1))

    A_TB, A_ANG, A_S = (slice(0, T), slice(T, 2 * T), slice(2 * T, 3 * T))
    SC2 = slice(3 * T, 4 * T)
    tg = b.inc("G", nc.gpsimd.iota(sb["tabscr"][0:1, 0:T], [[1, T]],
                                   channel_multiplier=0,
                                   allow_small_or_imprecise_dtypes=True))
    tok_tr = b.dma("S", [(dr["rowscr"][0:1, 0:T], sb["tabscr"][0:1, 0:T])], deps=[tg, tok_r2])
    tok_tb = b.dma("S", [(sb["tabscr"][:, A_TB], bass.AP(dr["rowscr"], 0, [[0, 128], [1, T]]))],
                   deps=[tok_tr])
    b.waits("A", [tok_c])
    t_e1 = b.inc("A", nc.scalar.activation(sb["cc"][:, 7:8], sb["cc"][:, 5:6], AF.Exp, scale=LN1E4_S1))
    t_e2 = b.inc("A", nc.scalar.activation(sb["cc"][:, 4:5], sb["cc"][:, 5:6], AF.Exp, scale=LN1E4_S2))
    b.waits("V", [t_e1, t_e2])
    t_inv = b.inc("V", nc.vector.tensor_mul(sb["cc"][:, 4:5], sb["cc"][:, 7:8], sb["cc"][:, 4:5]))
    b.wait("V", tok_tb)
    t_ang = ts_op(sb["tabscr"][:, A_ANG], sb["tabscr"][:, A_TB], sb["cc"][:, 4:5], None,
                  ALU.mult, deps=[t_inv])

    HPIC = float(np.float32(math.pi - PI1))
    def build_sintab(src_sl, dst_sl, shift, dep):
        W0, SC = A_TB, A_S
        dd = dep
        if shift:
            dd = ts_op(sb["tabscr"][:, SC], sb["tabscr"][:, src_sl], HPI1, HPI2,
                       ALU.add, ALU.add, deps=[dd])
            srcs = SC
        else:
            srcs = src_sl
        d0 = ts_op(sb["tabscr"][:, W0], sb["tabscr"][:, srcs], INV2PI1, None,
                   ALU.mult, deps=[dd])
        b.waits("V", [d0])
        d1 = b.inc("V", nc.vector.tensor_copy(sb["kint"][:, :], sb["tabscr"][:, W0]))
        b.waits("V", [d1])
        d2 = b.inc("V", nc.vector.tensor_copy(dst_sl, sb["kint"][:, :]))
        b.waits("V", [d2])
        d3 = b.inc("V", nc.vector.tensor_sub(sb["tabscr"][:, W0], sb["tabscr"][:, W0],
                                             dst_sl))
        d4 = ts_op(sb["tabscr"][:, SC2], sb["tabscr"][:, W0], 0.5, None, ALU.is_ge,
                   deps=[d3])
        b.waits("V", [d4])
        d5 = b.inc("V", nc.vector.tensor_add(dst_sl, dst_sl, sb["tabscr"][:, SC2]))
        d6 = ts_op(sb["tabscr"][:, SC2], sb["tabscr"][:, W0], -0.5, None, ALU.is_lt,
                   deps=[d5])
        b.waits("V", [d6])
        d7 = b.inc("V", nc.vector.tensor_sub(dst_sl, dst_sl, sb["tabscr"][:, SC2]))
        d8 = ts_op(sb["tabscr"][:, W0], sb["tabscr"][:, srcs], 0.5, None, ALU.mult,
                   deps=[d7])
        b.waits("V", [d8])
        d9 = b.inc("V", nc.vector.scalar_tensor_tensor(
            sb["tabscr"][:, W0], dst_sl, -PI1, sb["tabscr"][:, W0], ALU.mult, ALU.add))
        b.waits("V", [d9])
        da = b.inc("V", nc.vector.scalar_tensor_tensor(
            sb["tabscr"][:, W0], dst_sl, -HPIC, sb["tabscr"][:, W0], ALU.mult, ALU.add))
        b.waits("A", [da])
        db = b.inc("A", nc.scalar.activation(sb["tabscr"][:, SC2], sb["tabscr"][:, W0],
                                             AF.Abs))
        dc = b.inc("A", nc.scalar.activation(dst_sl, sb["tabscr"][:, W0], AF.Sin))
        dd2 = ts_op(sb["tabscr"][:, SC2], sb["tabscr"][:, SC2], HPI2, None,
                    ALU.subtract, deps=[db, dc], eng="V")
        b.waits("A", [dd2])
        de = b.inc("A", nc.scalar.activation(sb["tabscr"][:, SC2], sb["tabscr"][:, SC2],
                                             AF.Sin, scale=-1.0, bias=sb["cc"][:, 1:2]))
        b.waits("V", [dc, de])
        df = b.inc("V", nc.vector.scalar_tensor_tensor(
            dst_sl, dst_sl, 2.0, sb["tabscr"][:, SC2], ALU.mult, ALU.mult))
        return df

    t_sn = build_sintab(A_ANG, sb["tabs"][:, T:2 * T], False, t_ang)
    t_sn = ts_op(sb["tabs"][:, T:2 * T], sb["tabs"][:, T:2 * T], sb["cc"][:, 6:7], None,
                 ALU.mult, deps=[t_sn])
    t_cs = build_sintab(A_ANG, sb["tabs"][:, 0:T], True, t_sn)
    b.drain_dma()
    b.barrier()
    es_tab.close()
    if upto == "p1b_t":
        b.drain_dma()
        b.barrier()
        es_tabs.close()
        es_qkv.close()
        es_raw.close()
        es_qm.close()
        es_psb.close()
        es.close()
        return nc

    es2 = ExitStack()
    sb["c1q"] = es2.enter_context(nc.sbuf_tensor("c1q", [128, 8 * PT], BF16))
    sb["c1k"] = es2.enter_context(nc.sbuf_tensor("c1k", [128, 2 * PT], BF16))
    es_c1 = ExitStack()
    sb["wc1q"] = es_c1.enter_context(nc.sbuf_tensor("wc1q", [128, 24 * 128], BF16, side="right"))
    sb["wc1k"] = es_c1.enter_context(nc.sbuf_tensor("wc1k", [128, 6 * 128], BF16, side="right"))

    tok_wc1 = b.dma("G", [
        (sb["wc1q"][:, :].rearrange("p (t c b) -> p t c b", t=3, c=8),
         dr["qc1bd"].ap().transpose([2, 0, 1, 3])),
        (sb["wc1k"][:, :].rearrange("p (t c b) -> p t c b", t=3, c=2),
         dr["kc1bd"].ap().transpose([2, 0, 1, 3]))])

    tzp = b.inc("V", nc.vector.memset(sb["c1q"].ap().rearrange("p (c t) -> p c t", c=8)[:, :, 0:2], 0.0))
    tzp = b.inc("V", nc.vector.memset(sb["c1k"].ap().rearrange("p (c t) -> p c t", c=2)[:, :, 0:2], 0.0))
    tvo = b.inc("V", nc.vector.memset(
        sb["vtm"].ap().rearrange("p (g j w) -> p g j w", g=4, j=NJ)[:, :, :, 64:65], 1.0))

    sb["c1wq"] = es_c1.enter_context(nc.sbuf_tensor("c1wq", [128, 2 * 8 * (TC1 + 2)], BF16, side="right"))
    sb["c1wk"] = es_c1.enter_context(nc.sbuf_tensor("c1wk", [128, 2 * 2 * (TC1 + 2)], BF16, side="right"))
    c1q_c = sb["c1q"].ap().rearrange("p (c t) -> p c t", c=8)
    c1k_c = sb["c1k"].ap().rearrange("p (c t) -> p c t", c=2)
    pv_guard = {}
    WQ = 8 * (TC1 + 2)
    WK = 2 * (TC1 + 2)
    for ti in range(T // TC1):
        t0 = ti * TC1
        wsl = ti % 2
        prs = [(sb["c1wq"][:, wsl * WQ + c * (TC1 + 2): wsl * WQ + (c + 1) * (TC1 + 2)],
                dr["qrawD"][:, c * PT + t0: c * PT + t0 + TC1 + 2]) for c in range(8)]
        prs += [(sb["c1wk"][:, wsl * WK + c * (TC1 + 2): wsl * WK + (c + 1) * (TC1 + 2)],
                 dr["krawD"][:, c * PT + t0: c * PT + t0 + TC1 + 2]) for c in range(2)]
        win_tok = b.dma("S", prs, deps=[tok_zp, pv_guard.get(("win", wsl))])
        for side, nch_s, wsb, wbuf, woff, dst_c, bias_off in (
                ("q", 8, "wc1q", "c1wq", wsl * WQ, c1q_c, 0),
                ("k", 2, "wc1k", "c1wk", wsl * WK, c1k_c, 16)):
            for c in range(nch_s):
                bank = ps[c % 2]
                if ("c1", c % 2) in pv_guard:
                    b.wait("T", pv_guard[("c1", c % 2)])
                b.wait("T", tok_wc1)
                b.wait("T", win_tok)
                for tap in range(3):
                    ins = nc.tensor.matmul(
                        bank[:, 0:TC1],
                        sb[wsb][:, (tap * nch_s + c) * 128:(tap * nch_s + c + 1) * 128],
                        sb[wbuf][:, woff + c * (TC1 + 2) + tap: woff + c * (TC1 + 2) + tap + TC1],
                        start=(tap == 0), stop=(tap == 2))
                tmm = b.inc("T", ins)
                b.waits("V", [tmm, tzp])
                ta = b.inc("V", nc.vector.tensor_scalar(
                    dst_c[:, c, 2 + t0: 2 + t0 + TC1], bank[:, 0:TC1],
                    sb["biascols"][:, bias_off + c: bias_off + c + 1], None, ALU.add))
                pv_guard[("c1", c % 2)] = ta
        pv_guard[("win", wsl)] = tmm
    for g in range(4):
        bpv = 64 * (g % 2)
        for j in range(NJ):
            if ("vt",) in pv_guard:
                b.wait("T", pv_guard[("vt",)])
            tt = b.inc("T", nc.tensor.transpose(
                psb[:, 0:64],
                sb["vfm"][bpv:bpv + 64, (g // 2) * T + 128 * j: (g // 2) * T + 128 * (j + 1)],
                sb["eyeb"][bpv:bpv + 64, bpv:bpv + 64]))
            b.wait("V", tt)
            tv = b.inc("V", nc.vector.tensor_copy(
                sb["vtm"][:, g * NJ * VW + j * VW: g * NJ * VW + j * VW + 64],
                psb[:, 0:64]))
            pv_guard[("vt",)] = tv
    b.drain_dma()
    b.barrier()
    es_c1.close()
    es_raw.close()
    if upto == "p1b_c1":
        b.barrier()
        es2.close()
        es_tabs.close()
        es_qkv.close()
        es_qm.close()
        es_psb.close()
        es.close()
        return nc

    es3 = ExitStack()
    for nm, shp, dt_ in (("wc2q", [128, 24 * 1024], BF16), ("wc2k", [128, 6 * 256], BF16),
                         ("qpost", [128, 8 * TN2], BF16), ("kpost", [128, 2 * TN2], BF16),
                         ("sqb", [128, 2 * TN2], BF16), ("mm2b", [128, 2 * TN2], BF16),
                         ("rqf", [128, TN2], F32), ("rqb", [128, TN2], BF16),
                         ("rsw", [128, TN2], BF16), ("rt1", [128, TN2], BF16),
                         ("rt2", [128, TN2], BF16)):
        sb[nm] = es3.enter_context(nc.sbuf_tensor(nm, shp, dt_, side="right"))
    wq_toks2 = []
    for tap in range(3):
        prs = [(sb["wc2q"][:, (tap * 8 + kc) * 1024:(tap * 8 + kc + 1) * 1024],
                dr["qc2T"][tap, 128 * kc:128 * kc + 128, :]) for kc in range(8)]
        wq_toks2.append(b.dma("G", prs))
    tok_wc2k = b.dma("G", [(sb["wc2k"][:, (tap * 2 + kc) * 256:(tap * 2 + kc + 1) * 256],
                            dr["kc2T"][tap, 128 * kc:128 * kc + 128, :])
                           for tap in range(3) for kc in range(2)])

    PAIRSWAP = []
    for i in range(16):
        PAIRSWAP += [2 * i + 1, 2 * i]

    guard = {}
    SIDES = [("q", 8, "wc2q", c1q_c, 8, 8, slice(0, TN2), "qpost", ps[3], "sel16b", 16),
             ("k", 2, "wc2k", c1k_c, 2, 18, slice(TN2, 2 * TN2), "kpost", ps[4], "sel4b", 4)]
    for ti in range(nt2):
        t0 = ti * TN2
        mdeps = [guard.get("mm2b")]
        b.waits("V", [d for d in mdeps if d])
        tv = b.inc("V", nc.vector.tensor_copy(sb["mm2b"][0:64, 0:TN2], sb["qmkm"][64:128, t0:t0 + TN2]))
        tv = b.inc("V", nc.vector.tensor_copy(sb["mm2b"][64:128, 0:TN2], sb["qmkm"][64:128, t0:t0 + TN2]))
        tv = b.inc("V", nc.vector.tensor_copy(sb["mm2b"][0:64, TN2:2 * TN2], sb["qmkm"][0:64, t0:t0 + TN2]))
        t_mm2b = b.inc("V", nc.vector.tensor_copy(sb["mm2b"][64:128, TN2:2 * TN2], sb["qmkm"][0:64, t0:t0 + TN2]))

        pending_ssq = None

        def emit_ssq(p):
            side_i, oc, tsq = p
            _, nch_s, _, _, _, _, _, _, ps_ssq, selb, selw = SIDES[side_i]
            b.waits("T", [tsq, guard.get("ssq_read")])
            ins = nc.tensor.matmul(
                ps_ssq[0:selw, 0:TN2], sb[selb][:, oc * selw:(oc + 1) * selw],
                sb["sqb"][:, ((side_i * 8 + oc) % 2) * TN2:((side_i * 8 + oc) % 2 + 1) * TN2],
                start=(oc == 0), stop=(oc == nch_s - 1))
            return b.inc("T", ins)

        for side_i, (side, nch_s, wsb, src_c, nkc, bias_off, post_sl, dst_post,
                     ps_ssq, selb, selw) in enumerate(SIDES):
            for oc in range(nch_s):
                gi = side_i * 8 + oc
                bank = ps[gi % 3]
                if ("c2", gi % 3) in guard:
                    b.wait("T", guard[("c2", gi % 3)])
                for tok in (wq_toks2 if side == "q" else [tok_wc2k]):
                    b.wait("T", tok)
                first = True
                for tap in range(3):
                    for kc in range(nkc):
                        if side == "q":
                            wsl = sb[wsb][:, (tap * 8 + kc) * 1024 + oc * 128:
                                          (tap * 8 + kc) * 1024 + (oc + 1) * 128]
                        else:
                            wsl = sb[wsb][:, (tap * 2 + kc) * 256 + oc * 128:
                                          (tap * 2 + kc) * 256 + (oc + 1) * 128]
                        ins = nc.tensor.matmul(
                            bank[:, 0:TN2], wsl, src_c[:, kc, t0 + tap: t0 + tap + TN2],
                            start=first, stop=(tap == 2 and kc == nkc - 1))
                        first = False
                tmm = b.inc("T", ins)
                if pending_ssq is not None:
                    tprev = emit_ssq(pending_ssq)
                    guard[("sq", (pending_ssq[0] * 8 + pending_ssq[1]) % 2)] = tprev
                b.waits("V", [tmm, t_mm2b, guard.get(("rope", side, oc))])
                tpost = b.inc("V", nc.vector.scalar_tensor_tensor(
                    sb[dst_post][:, oc * TN2:(oc + 1) * TN2], bank[:, 0:TN2],
                    sb["biascols"][:, bias_off + oc: bias_off + oc + 1],
                    sb["mm2b"][:, post_sl], ALU.add, ALU.add))
                guard[("c2", gi % 3)] = tpost
                guard[("post", side, oc)] = tpost
                b.waits("A", [tpost, guard.get(("sq", gi % 2))])
                tsq = b.inc("A", nc.scalar.activation(
                    sb["sqb"][:, (gi % 2) * TN2:(gi % 2 + 1) * TN2],
                    sb[dst_post][:, oc * TN2:(oc + 1) * TN2], AF.Square))
                pending_ssq = (side_i, oc, tsq)
        tssq_last = emit_ssq(pending_ssq)
        guard[("ssq_done", "k")] = tssq_last
        pending_ssq = None

        b.waits("A", [tssq_last, guard.get("rq_used")])
        ta = b.inc("A", nc.scalar.activation(sb["rqf"][0:16, :], ps[3][0:16, 0:TN2], AF.Sqrt))
        ta2 = b.inc("A", nc.scalar.activation(sb["rqf"][32:36, :], ps[4][0:4, 0:TN2], AF.Sqrt))
        b.waits("V", [ta, ta2])
        tvq = b.inc("V", nc.vector.reciprocal(sb["rqf"][0:16, :], sb["rqf"][0:16, :]))
        tvk = b.inc("V", nc.vector.reciprocal(sb["rqf"][32:36, :], sb["rqf"][32:36, :]))
        b.waits("V", [tvq, tvk, tok_r2])
        tvk = b.inc("V", nc.vector.tensor_scalar(sb["rqf"][32:36, :], sb["rqf"][32:36, :],
                                                 sb["ktcol"][32:36, 0:1], None, ALU.mult))
        b.waits("V", [tvk, guard.get("rqb_used")])
        tvq = b.inc("V", nc.vector.tensor_copy(sb["rqb"][0:16, :], sb["rqf"][0:16, :]))
        tvk = b.inc("V", nc.vector.tensor_copy(sb["rqb"][32:36, :], sb["rqf"][32:36, :]))
        guard["ssq_read"] = tvk
        guard["rqb_ready"] = tvk

        for side, nch_s, dst_post in (("q", 8, "qpost"), ("k", 2, "kpost")):
            for oc in range(nch_s):
                post_ap = sb[dst_post][:, oc * TN2:(oc + 1) * TN2]
                bankR = ps[5 + oc % 2]
                if ("bR", oc % 2) in guard:
                    b.wait("T", guard[("bR", oc % 2)])
                b.wait("T", guard["rqb_ready"])
                if side == "q":
                    tbc = b.inc("T", nc.tensor.matmul(
                        bankR[:, 0:TN2], sb["bc16b"][:, oc * 128:(oc + 1) * 128],
                        sb["rqb"][0:16, :], start=True, stop=True))
                else:
                    tbc = b.inc("T", nc.tensor.matmul(
                        bankR[:, 0:TN2], sb["bc4b"][32:36, oc * 128:(oc + 1) * 128],
                        sb["rqb"][32:36, :], start=True, stop=True))
                deps = [guard.get(("rope", side, oc)), guard.get("rope_bufs")]
                b.waits("V", [d for d in deps if d] + [guard[("post", side, oc)]])
                s1 = b.inc("V", nc.vector.stream_shuffle(sb["rsw"][:, :], post_ap, PAIRSWAP))
                s2 = b.inc("V", nc.vector.tensor_mul(sb["rt1"][:, :], post_ap,
                                                     sb["tabs"][:, t0:t0 + TN2]))
                b.waits("V", [s1])
                s3 = b.inc("V", nc.vector.tensor_mul(sb["rt2"][:, :], sb["rsw"][:, :],
                                                     sb["tabs"][:, T + t0:T + t0 + TN2]))
                b.waits("V", [s2, s3])
                s4 = b.inc("V", nc.vector.tensor_add(sb["rt1"][:, :], sb["rt1"][:, :],
                                                     sb["rt2"][:, :]))
                b.waits("V", [s4, tbc])
                if side == "q":
                    nqt2 = TN2 // 128
                    for half in range(2):
                        h = 2 * oc + half
                        g, l = h // 4, h % 4
                        q4t = sb["q4a"] if g < 2 else sb["q4b"]
                        bpq = 64 * (g % 2)
                        qt0 = t0 // 128
                        dst = q4t.ap()[bpq:bpq + 64].rearrange(
                            "p (qt l c) -> p qt l c", l=4, c=128)[:, qt0:qt0 + nqt2, l, :]
                        sv = b.inc("V", nc.vector.tensor_mul(
                            dst,
                            sb["rt1"][64 * half:64 * half + 64, :].rearrange(
                                "p (qt c) -> p qt c", c=128),
                            bankR[64 * half:64 * half + 64, 0:TN2].rearrange(
                                "p (qt c) -> p qt c", c=128)))
                else:
                    sv = b.inc("V", nc.vector.tensor_mul(
                        sb["kfin"][:, oc * T + t0: oc * T + t0 + TN2],
                        sb["rt1"][:, :], bankR[:, 0:TN2]))
                guard[("rope", side, oc)] = sv
                guard[("bR", oc % 2)] = sv
                guard["rope_bufs"] = sv
                guard["rqb_used"] = tbc
        guard["mm2b"] = tpost
    b.drain_dma()
    b.barrier()
    es3.close()
    es_qm.close()
    es2.close()
    es_tabs.close()
    if upto == "p1b":
        b.drain_dma()
        b.barrier()
        es_psb.close()
        es_qkv.close()
        es.close()
        return nc
    es_psb.close()

    es_oan = ExitStack()
    sb["oan"] = es_oan.enter_context(nc.sbuf_tensor("oan", [128, 8 * T], BF16, side="right"))
    es_p2a = ExitStack()
    sb["pt"] = es_p2a.enter_context(nc.sbuf_tensor("pt", [128, 3 * 512], BF16, side="right"))
    sb["o4"] = es_p2a.enter_context(nc.sbuf_tensor("o4", [128, 2 * 512], F32, side="right"))
    sb["lrb"] = es_p2a.enter_context(nc.sbuf_tensor("lrb", [128, 2 * 512], BF16, side="right"))
    sb["maskTs"] = es_p2a.enter_context(nc.sbuf_tensor("maskTs", [128, 512], F32, side="right"))
    tok_mT = b.dma("S", [(sb["maskTs"][:, :], dr["maskT"][:, :])])
    b.wait("V", tok_mT)
    NQT = T // 128
    ag = {}
    b.waits("A", [tok_r2])

    def p2a_scores(k, i, g):
        gp = 64 * (g % 2)
        q4t = sb["q4a"] if g < 2 else sb["q4b"]
        psO = ps[3 + k % 2]
        b.wait("T", ag.get(("psO", k % 2)))
        texp = {}
        tpv = None

        def emit_s(j):
            psS = ps[j % 3]
            b.wait("T", ag.get(("psS", j % 3)))
            tmm = b.inc("T", nc.tensor.matmul(
                psS[:, 0:512],
                sb["kfin"][gp:gp + 64, (g // 2) * T + 128 * j: (g // 2) * T + 128 * (j + 1)],
                q4t[gp:gp + 64, 512 * i: 512 * (i + 1)],
                start=True, stop=True))
            expdep = tmm
            if j == i:
                b.waits("V", [tmm])
                expdep = b.inc("V", nc.vector.tensor_add(psS[:, 0:512], psS[:, 0:512],
                                                         sb["maskTs"][:, :]))
            b.waits("A", [expdep, ag.get(("pt", j % 3))])
            texp[j] = b.inc("A", nc.scalar.activation(
                sb["pt"][:, (j % 3) * 512:(j % 3 + 1) * 512], psS[:, 0:512],
                AF.Exp, scale=0.125, bias=sb["negm"][:, 0:1]))
            ag[("psS", j % 3)] = texp[j]

        def emit_pv(j):
            nonlocal tpv
            b.waits("T", [texp[j]])
            tpv = b.inc("T", nc.tensor.matmul(
                psO[0:65, 0:512],
                sb["vtm"][:, g * NJ * VW + j * VW: g * NJ * VW + (j + 1) * VW],
                sb["pt"][:, (j % 3) * 512:(j % 3 + 1) * 512],
                start=(j == 0), stop=(j == i)))
            ag[("pt", j % 3)] = tpv

        for j in range(i + 1):
            emit_s(j)
            if j >= 2:
                emit_pv(j - 2)
        for j in range(max(0, i - 1), i + 1):
            emit_pv(j)
        return tpv

    def p2a_norm(k, i, g, tpv):
        psO = ps[3 + k % 2]
        osl = (k % 2) * 512
        b.waits("V", [tpv, ag.get(("o4", k % 2))])
        tev = b.inc("V", nc.vector.tensor_copy(sb["o4"][0:65, osl:osl + 512],
                                               psO[0:65, 0:512]))
        ag[("psO", k % 2)] = tev
        b.waits("V", [tev])
        trc = b.inc("V", nc.vector.reciprocal(sb["o4"][64:65, osl:osl + 512],
                                              sb["o4"][64:65, osl:osl + 512]))
        b.waits("V", [trc, ag.get(("lrb", k % 2))])
        tcb = b.inc("V", nc.vector.tensor_copy(sb["lrb"][64:65, osl:osl + 512],
                                               sb["o4"][64:65, osl:osl + 512]))
        b.waits("T", [tcb, ag.get(("psR", k % 2))])
        tbr = b.inc("T", nc.tensor.matmul(ps[5 + k % 2][0:64, 0:512],
                                          sb["onesrow"][64:65, 0:64],
                                          sb["lrb"][64:65, osl:osl + 512],
                                          start=True, stop=True))
        b.waits("V", [tbr])
        for l in range(4):
            tm = b.inc("V", nc.vector.tensor_mul(
                sb["oan"][64 * (l % 2):64 * (l % 2) + 64,
                          (2 * g + l // 2) * T + 128 * i: (2 * g + l // 2) * T + 128 * (i + 1)],
                sb["o4"][0:64, osl + 128 * l: osl + 128 * (l + 1)],
                ps[5 + k % 2][0:64, 128 * l: 128 * (l + 1)]))
        ag[("psR", k % 2)] = tm
        ag[("o4", k % 2)] = tm
        ag[("lrb", k % 2)] = tbr

    pending = None
    k = 0
    for i in range(NQT):
        for g in range(4):
            tpv = p2a_scores(k, i, g)
            if pending is not None:
                p2a_norm(*pending)
            pending = (k, i, g, tpv)
            k += 1
    p2a_norm(*pending)
    b.drain_dma()
    b.barrier()
    es_p2a.close()
    es_qkv.close()
    if upto == "p2a":
        b.drain_dma()
        b.barrier()
        es_oan.close()
        es.close()
        return nc

    TNB = min(512, T)
    ntb = T // TNB
    es_p2bl = ExitStack()
    sb["woTb"] = es_p2bl.enter_context(nc.sbuf_tensor("woTb", [128, 8 * D], BF16))
    sb["ztile"] = es_p2bl.enter_context(nc.sbuf_tensor("ztile", [128, NCH * TNB], BF16))
    es_p2bt = ExitStack()
    for nm, shp, dt_ in (("xc", [128, 3 * TNB], F32), ("x1b", [128, 2 * TNB], F32),
                         ("zsq", [128, 2 * TNB], BF16), ("x1q", [128, 2 * TNB], BF16),
                         ("za", [128, TNB], F32), ("wocsb", [128, 8], BF16),
                         ("wocsf", [128, 8], F32)):
        sb[nm] = es_p2bt.enter_context(nc.sbuf_tensor(nm, shp, dt_, side="right"))
    wo_toks = []
    for g4 in range(2):
        prs = [(sb["woTb"][:, kc * D:(kc + 1) * D], dr["woT"][128 * kc:128 * kc + 128, :])
               for kc in range(4 * g4, 4 * g4 + 4)]
        wo_toks.append(b.dma("G", prs))
    for kc in range(8):
        b.wait("V", wo_toks[kc // 4])
        tw = b.inc("V", nc.vector.reduce_sum(sb["wocsf"][:, kc:kc + 1],
                                             sb["woTb"][:, kc * D:(kc + 1) * D],
                                             mybir.AxisListType.X))
    b.wait("V", tw)
    twb = b.inc("V", nc.vector.tensor_copy(sb["wocsb"][:, :], sb["wocsf"][:, :]))

    bg = {}
    for ti in range(ntb):
        t0 = ti * TNB
        b.waits("T", [twb, bg.get("sz_read")])
        for kc in range(8):
            ins = nc.tensor.matmul(ps[5][0:1, 0:TNB], sb["wocsb"][:, kc:kc + 1],
                                   sb["oan"][:, kc * T + t0: kc * T + t0 + TNB],
                                   start=(kc == 0), stop=(kc == 7))
        tsz = b.inc("T", ins)
        pending_q2 = None

        def emit_q2(p):
            oc, tzq = p
            b.waits("T", [tzq])
            ins = nc.tensor.matmul(ps[4][0:1, 0:TNB], sb["onescol"][:, :],
                                   sb["zsq"][:, (oc % 2) * TNB:(oc % 2 + 1) * TNB],
                                   start=(oc == 0), stop=(oc == NCH - 1))
            return b.inc("T", ins)

        for oc in range(NCH):
            bank = ps[oc % 3]
            b.wait("T", bg.get(("pz", oc % 3)))
            for kc in range(8):
                ins = nc.tensor.matmul(
                    bank[:, 0:TNB],
                    sb["woTb"][:, kc * D + oc * 128: kc * D + (oc + 1) * 128],
                    sb["oan"][:, kc * T + t0: kc * T + t0 + TNB],
                    start=(kc == 0), stop=(kc == 7))
            tmm = b.inc("T", ins)
            if pending_q2 is not None:
                bg[("zsq_read", pending_q2[0] % 2)] = emit_q2(pending_q2)
            b.waits("A", [tmm, bg.get(("zt", oc))])
            tze = b.inc("A", nc.scalar.activation(sb["ztile"][:, oc * TNB:(oc + 1) * TNB],
                                                  bank[:, 0:TNB], AF.Copy))
            b.waits("A", [bg.get(("zsq_read", oc % 2))])
            tzq = b.inc("A", nc.scalar.activation(sb["zsq"][:, (oc % 2) * TNB:(oc % 2 + 1) * TNB],
                                                  bank[:, 0:TNB], AF.Square))
            bg[("pz", oc % 3)] = tzq
            pending_q2 = (oc, tzq)
            bg[("ze", oc)] = tze
        tq2 = emit_q2(pending_q2)
        bg["zsq_read"] = tq2
        b.waits("A", [tsz, tq2])
        b.waits("V", [tsz, tq2])
        rt2b = ln_rows(TNB, ps[5][0:1, 0:TNB], ps[4][0:1, 0:TNB],
                       sfs(6, TNB), rlbs(0, TNB), rlbs(1, TNB),
                       dep=bg.get("rows_dep"), rbdep=bg.get("rlb_used"))
        bg["rows_dep"] = rt2b
        bg["sz_read"] = rt2b
        b.waits("T", [rt2b, bg.get("bc2_read")])
        tb1 = b.inc("T", nc.tensor.matmul(ps[3][:, 0:TNB], sb["onesrow"][0:1, :],
                                          rlbs(0, TNB), start=True, stop=True))
        tb2 = b.inc("T", nc.tensor.matmul(ps[4][:, 0:TNB], sb["onesrow"][0:1, :],
                                          rlbs(1, TNB), start=True, stop=True))
        bg["rlb_used"] = tb2
        xtoks = {}
        for oc in range(3):
            xtoks[oc] = b.dma("S", [(sb["xc"][:, (oc % 3) * TNB:(oc % 3 + 1) * TNB],
                                     dr["x_in"][128 * oc:128 * (oc + 1), t0:t0 + TNB])],
                              deps=[bg.get(("xc", oc % 3))])
        pending_x2 = None

        def emit_x2(p):
            oc, txq = p
            b.waits("T", [txq])
            ins = nc.tensor.matmul(ps[6][0:1, 0:TNB], sb["onescol"][:, :],
                                   sb["x1q"][:, (oc % 2) * TNB:(oc % 2 + 1) * TNB],
                                   start=(oc == 0), stop=(oc == NCH - 1))
            return b.inc("T", ins)

        for oc in range(NCH):
            b.waits("V", [tb1, tb2, bg.get(("ze", oc)), bg.get("za_free")])
            tu = b.inc("V", nc.vector.tensor_mul(sb["za"][:, :],
                                                 sb["ztile"][:, oc * TNB:(oc + 1) * TNB],
                                                 ps[3][:, 0:TNB]))
            b.waits("V", [tu])
            tu = b.inc("V", nc.vector.tensor_sub(sb["za"][:, :], sb["za"][:, :], ps[4][:, 0:TNB]))
            x1sl = oc % 2
            b.waits("V", [tu, xtoks[oc], bg.get(("x1b", x1sl)), bg.get(("x1ba", x1sl))])
            tx1 = b.inc("V", nc.vector.tensor_add(sb["x1b"][:, x1sl * TNB:(x1sl + 1) * TNB],
                                                  sb["za"][:, :],
                                                  sb["xc"][:, (oc % 3) * TNB:(oc % 3 + 1) * TNB]))
            bg["za_free"] = tx1
            bg[("xc", oc % 3)] = tx1
            if oc + 3 < NCH:
                xtoks[oc + 3] = b.dma("S", [(sb["xc"][:, ((oc + 3) % 3) * TNB:((oc + 3) % 3 + 1) * TNB],
                                             dr["x_in"][128 * (oc + 3):128 * (oc + 4), t0:t0 + TNB])],
                                      deps=[bg.get(("xc", (oc + 3) % 3))])
            tws = b.dma("S", [(dr["x1s"][128 * oc:128 * (oc + 1), t0:t0 + TNB],
                               sb["x1b"][:, x1sl * TNB:(x1sl + 1) * TNB])], deps=[tx1])
            bg[("x1b", x1sl)] = tws
            b.waits("A", [tx1, bg.get(("x1q_read", oc % 2))])
            txq = b.inc("A", nc.scalar.activation(sb["x1q"][:, (oc % 2) * TNB:(oc % 2 + 1) * TNB],
                                                  sb["x1b"][:, x1sl * TNB:(x1sl + 1) * TNB],
                                                  AF.Square))
            bg[("x1ba", x1sl)] = txq
            if pending_x2 is not None:
                bg[("x1q_read", pending_x2[0] % 2)] = emit_x2(pending_x2)
            pending_x2 = (oc, txq)
        tx2 = emit_x2(pending_x2)
        bg[("x1q_read", pending_x2[0] % 2)] = tx2
        bg["bc2_read"] = tx1
        b.waits("A", [tx2])
        b.waits("V", [tx2])
        rt3 = ln_rows(TNB, None, ps[6][0:1, 0:TNB],
                      m1row(t0, TNB), r3row(t0, TNB), c3row(t0, TNB),
                      dep=bg.get("rows_dep"))
        bg["rows_dep"] = rt3
    b.drain_dma()
    b.barrier()
    es_p2bt.close()
    es_oan.close()
    es_p2bl.close()
    if upto == "p2b":
        b.drain_dma()
        b.barrier()
        es.close()
        return nc

    TN3 = min(256, T)
    nt3 = T // TN3
    es_p3ps = ExitStack()
    ps7 = es_p3ps.enter_context(nc.psum_tensor("pb7", [128, 512], F32))
    es_p3l = ExitStack()
    sb["upb"] = es_p3l.enter_context(nc.sbuf_tensor("upb", [128, 32 * 1024], BF16))
    sb["downb"] = es_p3l.enter_context(nc.sbuf_tensor("downb", [128, 128 * 256], BF16))
    es_p3r = ExitStack()
    for nm, shp, dt_ in (("x1r", [128, 6 * TN3], F32), ("resr", [128, 4 * TN3], F32),
                         ("ftile", [128, NCH * TN3], BF16),
                         ("hidb", [128, 2 * 8 * TN3], BF16), ("yh", [128, 2 * 2 * TN3], BF16),
                         ("ysc", [128, TN3], F32), ("rsc", [128, TN3], F32),
                         ("fsq", [128, 4 * TN3], BF16),
                         ("bc3t", [128, 2 * TN3], F32), ("bc4t", [128, 2 * TN3], F32),
                         ("outb", [128, 2 * TN3], F32)):
        sb[nm] = es_p3r.enter_context(nc.sbuf_tensor(nm, shp, dt_, side="right"))
    up_toks = []
    for h4 in range(4):
        prs = []
        for h in range(4 * h4, 4 * h4 + 4):
            for kc in range(2):
                prs.append((sb["upb"][:, (2 * h + kc) * 1024:(2 * h + kc + 1) * 1024],
                            dr["up_w"][h, 128 * kc:128 * (kc + 1), :]))
        up_toks.append(b.dma("G", prs))
    dn_toks = []
    for h4 in range(4):
        prs = []
        for h in range(4 * h4, 4 * h4 + 4):
            for mc in range(8):
                prs.append((sb["downb"][:, (8 * h + mc) * 256:(8 * h + mc) * 256 + 256],
                            dr["down_w"][h, 128 * mc:128 * (mc + 1), :]))
        dn_toks.append(b.dma("G", prs))

    fg = {}
    out_toks = []
    res_pend = [None]

    def make_residual(ti_, tevac4_):
        t0r = ti_ * TN3
        rr = {"tok": {}, "cons": fg.get("rr_cons", {})}
        def r_load(li):
            sl = li % 4
            rr["tok"][li] = b.dma("S", [(sb["resr"][:, sl * TN3:(sl + 1) * TN3],
                                         dr["x1s"][128 * li:128 * (li + 1), t0r:t0r + TN3])],
                                  deps=[d for d in (rr["cons"].get(sl),) if d])
        for li in range(4):
            r_load(li)
        fg["rr_cons"] = rr["cons"]
        def step(hs):
            tu2 = None
            for oc in (2 * hs, 2 * hs + 1):
                osl2 = oc % 2
                b.waits("V", [tevac4_, fg[("fe", oc // 2)], fg.get("rscf"),
                              fg.get(("outb", osl2))])
                tu = b.inc("V", nc.vector.tensor_mul(sb["rsc"][:, :],
                                                     sb["ftile"][:, oc * TN3:(oc + 1) * TN3],
                                                     sb["bc4t"][:, 0:TN3]))
                b.waits("V", [tu])
                tu2 = b.inc("V", nc.vector.tensor_sub(sb["rsc"][:, :], sb["rsc"][:, :],
                                                      sb["bc4t"][:, TN3:2 * TN3]))
                b.waits("V", [tu2, rr["tok"][oc]])
                to = b.inc("V", nc.vector.tensor_add(
                    sb["outb"][:, osl2 * TN3:(osl2 + 1) * TN3], sb["rsc"][:, :],
                    sb["resr"][:, (oc % 4) * TN3:(oc % 4 + 1) * TN3]))
                fg[("ftg", oc // 2)] = tu
                fg["rscf"] = to
                rr["cons"][oc % 4] = to
                if oc + 4 < NCH:
                    r_load(oc + 4)
                tso = b.dma("S", [(dr["y_out"][128 * oc:128 * (oc + 1), t0r:t0r + TN3],
                                   sb["outb"][:, osl2 * TN3:(osl2 + 1) * TN3])], deps=[to])
                fg[("outb", osl2)] = tso
                out_toks.append(tso)
            if hs == NH - 1:
                fg["bc4_read"] = tu2
        return step

    for ti in range(nt3):
        t0 = ti * TN3
        xr = {"tok": {}, "cons": fg.get("xr_cons", {})}
        def x1_load(li):
            sl = li % 6
            xr["tok"][li] = b.dma("S", [(sb["x1r"][:, sl * TN3:(sl + 1) * TN3],
                                         dr["x1s"][128 * li:128 * (li + 1), t0:t0 + TN3])],
                                  deps=[d for d in (xr["cons"].get(sl),) if d])
        def x1_slot(li):
            return sb["x1r"][:, (li % 6) * TN3:(li % 6 + 1) * TN3]
        def x1_done(li, tok):
            xr["cons"][li % 6] = tok
            if li + 6 < NCH:
                x1_load(li + 6)
        for li in range(6):
            x1_load(li)
        fg["xr_cons"] = xr["cons"]
        b.waits("T", [fg.get(("hid", 1, 0)), fg.get("bc3_read")])
        tb3 = b.inc("T", nc.tensor.matmul(ps[2][:, 0:TN3], sb["onesrow"][0:1, :],
                                          r3row(t0, TN3), start=True, stop=False))
        tb3 = b.inc("T", nc.tensor.matmul(ps[2][:, TN3:2 * TN3], sb["onesrow"][0:1, :],
                                          c3row(t0, TN3), start=False, stop=True))
        b.waits("V", [tb3, fg.get("bc3_read")])
        tevac3 = b.inc("V", nc.vector.tensor_copy(sb["bc3t"][:, :], ps[2][:, 0:2 * TN3]))

        pending_sum = [None]

        def emit_fsums(p):
            hs, tfe_, tfq_ = p
            extra = [fg.get("ps6_tok"), fg.get("ps7_tok")] if hs == 0 else []
            b.waits("T", [tfe_, tfq_] + extra)
            nc.tensor.matmul(ps[6][0:1, 0:2 * TN3], sb["onescol"][:, :],
                             sb["ftile"][:, 2 * hs * TN3:(2 * hs + 2) * TN3],
                             start=(hs == 0), stop=(hs == NH - 1))
            ins = nc.tensor.matmul(ps7[0:1, 0:2 * TN3], sb["onescol"][:, :],
                                   sb["fsq"][:, (hs % 2) * 2 * TN3:(hs % 2 + 1) * 2 * TN3],
                                   start=(hs == 0), stop=(hs == NH - 1))
            t = b.inc("T", ins)
            fg[("fsq_read", hs % 2)] = t
            fg[("fsum", hs)] = t
            return t

        def emit_down(hd):
            pf = ps[4 + hd % 2]
            b.waits("T", [dn_toks[hd // 4]] + (fg.get(("psFr", hd % 2)) or []))
            first = True
            ins = None
            for dc in range(2):
                for mc in range(8):
                    b.wait("T", fg[("gel", hd % 2, mc // 2)])
                    ins = nc.tensor.matmul(
                        pf[:, dc * TN3:(dc + 1) * TN3],
                        sb["downb"][:, (8 * hd + mc) * 256 + dc * 128:(8 * hd + mc) * 256 + (dc + 1) * 128],
                        sb["hidb"][:, (hd % 2) * 8 * TN3 + mc * TN3:(hd % 2) * 8 * TN3 + (mc + 1) * TN3],
                        start=first, stop=(dc == 1 and mc == 7))
                    first = False
            tdn = b.inc("T", ins)
            fg[("hidbG", hd % 2)] = tdn
            b.waits("A", [tdn, fg.get(("ftg", hd)), fg.get(("fsum", hd))])
            tfe = b.inc("A", nc.scalar.activation(sb["ftile"][:, 2 * hd * TN3:(2 * hd + 2) * TN3],
                                                  pf[:, 0:2 * TN3], AF.Copy))
            b.waits("A", [fg.get(("fsq_read", hd % 2))])
            tfq = b.inc("A", nc.scalar.activation(
                sb["fsq"][:, (hd % 2) * 2 * TN3:(hd % 2 + 1) * 2 * TN3],
                pf[:, 0:2 * TN3], AF.Square))
            fg[("psFr", hd % 2)] = [tfe, tfq]
            fg[("fe", hd)] = tfe
            return (hd, tfe, tfq)

        for h in range(NH):
            ysl = h % 2
            ty = None
            for ci, c in enumerate((2 * h, 2 * h + 1)):
                b.waits("V", [tevac3, xr["tok"][c], fg.get("yscf"), fg.get(("yh", ysl))])
                tyu = b.inc("V", nc.vector.tensor_mul(sb["ysc"][:, :], x1_slot(c),
                                                      sb["bc3t"][:, 0:TN3]))
                b.waits("V", [tyu])
                ty = b.inc("V", nc.vector.tensor_sub(
                    sb["yh"][:, (2 * ysl + ci) * TN3:(2 * ysl + ci + 1) * TN3],
                    sb["ysc"][:, :], sb["bc3t"][:, TN3:2 * TN3]))
                fg["yscf"] = ty
                x1_done(c, tyu)
            hb = (ps[2 * (h % 2)], ps[2 * (h % 2) + 1])
            b.wait("T", up_toks[h // 4])
            b.wait("T", ty)
            if h == 1:
                b.wait("T", tevac3)
            tup = None
            for mc in range(8):
                bank = hb[mc // 2 % 2]
                half = mc % 2
                if half == 0:
                    b.wait("T", fg.get(("hid", h % 2, mc // 2 % 2)))
                for kc in range(2):
                    ins = nc.tensor.matmul(
                        bank[:, half * TN3:(half + 1) * TN3],
                        sb["upb"][:, (2 * h + kc) * 1024 + mc * 128:(2 * h + kc) * 1024 + (mc + 1) * 128],
                        sb["yh"][:, (2 * ysl + kc) * TN3:(2 * ysl + kc + 1) * TN3],
                        start=(half == 0 and kc == 0), stop=(half == 1 and kc == 1))
                if half == 1:
                    tup = b.inc("T", ins)
                    b.waits("A", [tup, fg.get(("hidbG", h % 2))])
                    tg_ = b.inc("A", nc.scalar.activation(
                        sb["hidb"][:, (h % 2) * 8 * TN3 + (mc - 1) * TN3:(h % 2) * 8 * TN3 + (mc + 1) * TN3],
                        bank[:, 0:2 * TN3], AF.Gelu))
                    fg[("hid", h % 2, mc // 2 % 2)] = tg_
                    fg[("gel", h % 2, mc // 2)] = tg_
            fg[("yh", ysl)] = tup
            if h > 0:
                pdn = emit_down(h - 1)
                if pending_sum[0] is not None:
                    emit_fsums(pending_sum[0])
                pending_sum[0] = pdn
            if res_pend[0] is not None:
                res_pend[0](h)
        pdn = emit_down(NH - 1)
        if pending_sum[0] is not None:
            emit_fsums(pending_sum[0])
        tS2 = emit_fsums(pdn)
        b.waits("A", [tS2, fg.get("rows_dep4")])
        SF4, SF5 = 4 * 512, 5 * 512
        th1 = b.inc("A", nc.scalar.activation(sb["smallf"][0:1, SF4:SF4 + 2 * TN3],
                                              ps[6][0:1, 0:2 * TN3], AF.Copy))
        th2 = b.inc("A", nc.scalar.activation(sb["smallf"][0:1, SF5:SF5 + 2 * TN3],
                                              ps7[0:1, 0:2 * TN3], AF.Copy))
        fg["ps7_tok"] = th2
        b.waits("V", [th1, th2, fg.get("rows_dep4")])
        th3 = b.inc("V", nc.vector.tensor_add(sb["smallf"][0:1, SF4:SF4 + TN3],
                                              sb["smallf"][0:1, SF4:SF4 + TN3],
                                              sb["smallf"][0:1, SF4 + TN3:SF4 + 2 * TN3]))
        th4 = b.inc("V", nc.vector.tensor_add(sb["smallf"][0:1, SF5:SF5 + TN3],
                                              sb["smallf"][0:1, SF5:SF5 + TN3],
                                              sb["smallf"][0:1, SF5 + TN3:SF5 + 2 * TN3]))
        b.waits("A", [th3, th4])
        b.waits("V", [th3, th4])
        rt4 = ln_rows(TN3, sb["smallf"][0:1, SF4:SF4 + TN3], sb["smallf"][0:1, SF5:SF5 + TN3],
                      sfs(6, TN3), rlbs(0, TN3), rlbs(1, TN3),
                      dep=fg.get("rows_dep4"), rbdep=fg.get("rlb_used"))
        fg["rows_dep4"] = rt4
        b.waits("T", [rt4, th1])
        tb4 = b.inc("T", nc.tensor.matmul(ps[6][:, 0:TN3], sb["onesrow"][0:1, :],
                                          rlbs(0, TN3), start=True, stop=False))
        tb4 = b.inc("T", nc.tensor.matmul(ps[6][:, TN3:2 * TN3], sb["onesrow"][0:1, :],
                                          rlbs(1, TN3), start=False, stop=True))
        fg["rlb_used"] = tb4
        b.waits("V", [tb4, fg.get("bc4_read")])
        tevac4 = b.inc("V", nc.vector.tensor_copy(sb["bc4t"][:, :], ps[6][:, 0:2 * TN3]))
        fg["ps6_tok"] = tevac4
        fg["bc3_read"] = ty
        res_pend[0] = make_residual(ti, tevac4)
    for hs in range(NH):
        res_pend[0](hs)
    b.drain_dma()
    b.barrier()
    es_p3r.close()
    es_p3l.close()
    es_p3ps.close()
    es.close()
    return nc


def kernel(**inputs):
    in_maps, perm, iperm, B, T = host_stage(inputs)
    nc = build_program(T)
    res = run_bass_kernel_spmd(nc, in_maps, core_ids=list(range(B)))
    return assemble_output(res.results, perm, B, T)


# version 14
# speedup vs baseline: 1.4734x; 1.0049x over previous
import math
from contextlib import ExitStack

import numpy as np
import concourse.bass as bass
import concourse.mybir as mybir
from concourse.bass_utils import run_bass_kernel_spmd

F32 = mybir.dt.float32
BF16 = mybir.dt.bfloat16
I32 = mybir.dt.int32
AF = mybir.ActivationFunctionType
ALU = mybir.AluOpType

D = 4096
NH = 16
NKV = 4
HD = 64
LAT = NH * HD
KVD = NKV * HD
QKV = LAT + 2 * KVD
NCH = D // 128
EPS = 1e-5
NEG_BIG = -3.0e38

def _split(x):
    a = float(np.float32(x))
    b = float(np.float32(x - a))
    return a, b

LN1E4_S1, LN1E4_S2 = _split(-math.log(10000.0) / 32.0)
PI1, _PI2 = _split(math.pi)
TWOPI1, TWOPI2 = _split(2.0 * math.pi)
INV2PI1 = float(np.float32(1.0 / TWOPI1))
HPI1, HPI2 = _split(math.pi / 2.0)


class Bld:

    ENG = ("T", "V", "A", "S", "G")

    def __init__(self, nc, es):
        self.nc = nc
        self.es = es
        self.eng = {"T": nc.tensor, "V": nc.vector, "A": nc.scalar,
                    "S": nc.sync, "G": nc.gpsimd}
        self.sems = {}
        self.counts = {}
        self.wmark = {e: {} for e in self.ENG}
        for e in self.ENG:
            self._mksem("p" + e)
        self.rings = {}
        for q, nslots in (("S", 8), ("G", 8)):
            slots = []
            for i in range(nslots):
                nm = f"dq{q}{i}"
                self._mksem(nm)
                slots.append(nm)
            self.rings[q] = {"slots": slots, "next": 0, "intok": {}}

    def _mksem(self, name):
        self.sems[name] = self.es.enter_context(self.nc.semaphore(name))
        self.counts[name] = 0

    def wait(self, e, tok):
        if tok is None:
            return
        sem, val = tok
        if self.wmark[e].get(sem, 0) >= val:
            return
        self.eng[e].wait_ge(self.sems[sem], val)
        self.wmark[e][sem] = val

    def waits(self, e, toks):
        for t in toks:
            self.wait(e, t)

    def tok(self, e):
        return ("p" + e, self.counts["p" + e])

    def inc(self, e, instr):
        sem = "p" + e
        self.counts[sem] += 1
        instr.then_inc(self.sems[sem], 1)
        return (sem, self.counts[sem])

    def dma(self, q, pairs, deps=()):
        ring = self.rings[q]
        slot = ring["slots"][ring["next"]]
        ring["next"] = (ring["next"] + 1) % len(ring["slots"])
        if self.counts[slot]:
            self.wait(q, (slot, self.counts[slot]))
        self.waits(q, deps)
        for p in pairs:
            kw = p[2] if len(p) > 2 else {}
            self.eng[q].dma_start(p[0], p[1], **kw).then_inc(self.sems[slot], 16)
            self.counts[slot] += 16
        return (slot, self.counts[slot])

    def barrier(self, engines="TVASG"):
        for e in engines:
            self.wait(e, ("p" + e, self.counts["p" + e]))
        toks = [self.inc(e, self.eng[e].nop()) for e in engines]
        for e in engines:
            for t in toks:
                self.wait(e, t)

    def drain_dma(self):
        for q, ring in self.rings.items():
            for slot in ring["slots"]:
                if self.counts[slot]:
                    self.wait(q, (slot, self.counts[slot]))


def host_stage(inp):
    perm = np.asarray(inp["L0_ffn_perm"])
    iperm = np.argsort(perm)
    x = np.asarray(inp["L0_attn_x"])
    B, T, _ = x.shape

    shared = {}
    shared["wqkvT"] = np.ascontiguousarray(np.asarray(inp["L0_attn_w_qkv"]).T[perm])
    shared["woT"] = np.ascontiguousarray(np.asarray(inp["L0_attn_w_o"])[perm].T)

    qc1 = np.asarray(inp["L0_attn_qc1_w"])
    qc1bd = np.zeros((3, 8, 128, 128), dtype=np.float32)
    for tap in range(3):
        for c in range(8):
            for gg in range(2):
                qc1bd[tap, c, 64 * gg:64 * gg + 64, 64 * gg:64 * gg + 64] = \
                    qc1[128 * c + 64 * gg:128 * c + 64 * gg + 64, :, tap].T
    shared["qc1bd"] = qc1bd
    shared["qc2T"] = np.ascontiguousarray(np.asarray(inp["L0_attn_qc2_w"]).transpose(2, 1, 0))
    kc1 = np.asarray(inp["L0_attn_kc1_w"])
    kc1bd = np.zeros((3, 2, 128, 128), dtype=np.float32)
    for tap in range(3):
        for c in range(2):
            for gg in range(2):
                kc1bd[tap, c, 64 * gg:64 * gg + 64, 64 * gg:64 * gg + 64] = \
                    kc1[128 * c + 64 * gg:128 * c + 64 * gg + 64, :, tap].T
    shared["kc1bd"] = kc1bd
    shared["kc2T"] = np.ascontiguousarray(np.asarray(inp["L0_attn_kc2_w"]).transpose(2, 1, 0))

    shared["qc1b"] = np.ascontiguousarray(np.asarray(inp["L0_attn_qc1_b"]).reshape(8, 128).T)
    shared["qc2b"] = np.ascontiguousarray(np.asarray(inp["L0_attn_qc2_b"]).reshape(8, 128).T)
    shared["kc1b"] = np.ascontiguousarray(np.asarray(inp["L0_attn_kc1_b"]).reshape(2, 128).T)
    shared["kc2b"] = np.ascontiguousarray(np.asarray(inp["L0_attn_kc2_b"]).reshape(2, 128).T)
    shared["ktemp"] = np.asarray(inp["L0_attn_key_temp"]).reshape(1, 1).astype(np.float32)
    shared["up_w"] = np.ascontiguousarray(np.asarray(inp["L0_ffn_up"]))
    shared["down_w"] = np.ascontiguousarray(np.asarray(inp["L0_ffn_down"]))

    shared["eye128"] = np.eye(128, dtype=np.float32)
    kk = np.arange(128)[:, None]
    qq = np.tile(np.arange(128)[None, :], (1, 4)).reshape(1, 512)
    shared["maskT"] = np.where(kk > qq, np.float32(NEG_BIG), np.float32(0.0)).astype(np.float32)
    dl = np.arange(128) % 64
    shared["msk16"] = (dl[:, None] == np.arange(64)[None, :]).astype(np.float32) / 32.0
    shared["msk4"] = (dl[:, None] == np.arange(64)[None, :]).astype(np.float32) / 8.0
    sel16 = np.zeros((8, 128, 16), dtype=np.float32)
    bc16 = np.zeros((8, 16, 128), dtype=np.float32)
    for c in range(8):
        for p in range(128):
            h = 2 * c + p // 64
            sel16[c, p, h] = 1.0
            bc16[c, h, p] = 1.0
    pvec = np.arange(128)
    ivec = np.zeros((128, 2), dtype=np.float32)
    ivec[:, 0] = (pvec % 64) // 2
    ivec[:, 1] = np.where(pvec % 2 == 0, -1.0, 1.0)
    shared["ivec"] = ivec
    shared["sel16"] = np.ascontiguousarray(sel16.transpose(1, 0, 2)).reshape(128, 8 * 16)
    shared["bc16"] = np.ascontiguousarray(bc16.transpose(1, 0, 2)).reshape(16, 8 * 128)
    sel4 = np.zeros((2, 128, 4), dtype=np.float32)
    bc4 = np.zeros((2, 4, 128), dtype=np.float32)
    for c in range(2):
        for p in range(128):
            h = 2 * c + p // 64
            sel4[c, p, h] = 1.0
            bc4[c, h, p] = 1.0
    shared["sel4"] = np.ascontiguousarray(sel4.transpose(1, 0, 2)).reshape(128, 2 * 4)
    bc4p = np.zeros((36, 2 * 128), dtype=np.float32)
    bc4p[32:36] = np.ascontiguousarray(bc4.transpose(1, 0, 2)).reshape(4, 2 * 128)
    shared["bc4"] = bc4p

    in_maps = []
    for b in range(B):
        m = dict(shared)
        m["x_in"] = np.ascontiguousarray(x[b].T[perm])
        in_maps.append(m)
    return in_maps, perm, iperm, B, T


def assemble_output(results, perm, B, T):
    out = np.empty((B, T, D), dtype=np.float32)
    for b in range(B):
        y = np.asarray(results[b]["y_out"])
        out[b][:, perm] = y.T
    return out


def build_program(T, upto="p3"):
    nc = bass.Bass("TRN2", target_bir_lowering=False)
    es = ExitStack()

    dr = {}
    dr["x_in"] = nc.dram_tensor("x_in", [D, T], F32, kind="ExternalInput")
    dr["wqkvT"] = nc.dram_tensor("wqkvT", [D, QKV], F32, kind="ExternalInput")
    dr["woT"] = nc.dram_tensor("woT", [LAT, D], F32, kind="ExternalInput")
    dr["qc1bd"] = nc.dram_tensor("qc1bd", [3, 8, 128, 128], F32, kind="ExternalInput")
    dr["qc2T"] = nc.dram_tensor("qc2T", [3, 1024, 1024], F32, kind="ExternalInput")
    dr["kc1bd"] = nc.dram_tensor("kc1bd", [3, 2, 128, 128], F32, kind="ExternalInput")
    dr["kc2T"] = nc.dram_tensor("kc2T", [3, 256, 256], F32, kind="ExternalInput")
    dr["qc1b"] = nc.dram_tensor("qc1b", [128, 8], F32, kind="ExternalInput")
    dr["qc2b"] = nc.dram_tensor("qc2b", [128, 8], F32, kind="ExternalInput")
    dr["kc1b"] = nc.dram_tensor("kc1b", [128, 2], F32, kind="ExternalInput")
    dr["kc2b"] = nc.dram_tensor("kc2b", [128, 2], F32, kind="ExternalInput")
    dr["ktemp"] = nc.dram_tensor("ktemp", [1, 1], F32, kind="ExternalInput")
    dr["up_w"] = nc.dram_tensor("up_w", [16, 256, 1024], F32, kind="ExternalInput")
    dr["down_w"] = nc.dram_tensor("down_w", [16, 1024, 256], F32, kind="ExternalInput")
    dr["eye128"] = nc.dram_tensor("eye128", [128, 128], F32, kind="ExternalInput")
    dr["maskT"] = nc.dram_tensor("maskT", [128, 512], F32, kind="ExternalInput")
    dr["msk16"] = nc.dram_tensor("msk16", [128, 64], F32, kind="ExternalInput")
    dr["msk4"] = nc.dram_tensor("msk4", [128, 64], F32, kind="ExternalInput")
    dr["ivec"] = nc.dram_tensor("ivec", [128, 2], F32, kind="ExternalInput")
    dr["sel16"] = nc.dram_tensor("sel16", [128, 8 * 16], F32, kind="ExternalInput")
    dr["bc16"] = nc.dram_tensor("bc16", [16, 8 * 128], F32, kind="ExternalInput")
    dr["sel4"] = nc.dram_tensor("sel4", [128, 2 * 4], F32, kind="ExternalInput")
    dr["bc4"] = nc.dram_tensor("bc4", [36, 2 * 128], F32, kind="ExternalInput")
    dr["y_out"] = nc.dram_tensor("y_out", [D, T], F32, kind="ExternalOutput")
    dr["x1s"] = nc.dram_tensor("x1s", [D, T], F32)
    dr["qrawD"] = nc.dram_tensor("qrawD", [128, 8 * (T + 2)], BF16)
    dr["krawD"] = nc.dram_tensor("krawD", [128, 2 * (T + 2)], BF16)
    dr["rowscr"] = nc.dram_tensor("rowscr", [1, max(T, 2048)], F32)

    b = Bld(nc, es)
    blk = es.enter_context(nc.Block())

    PT = T + 2
    sb = {}
    def alloc(name, shape, dtype):
        sb[name] = es.enter_context(nc.sbuf_tensor(name, shape, dtype))
        return sb[name]

    es_qm = ExitStack()
    sb["qmkm"] = es_qm.enter_context(nc.sbuf_tensor("qmkm", [128, T], F32, side="right"))
    es_raw = ExitStack()
    sb["vfm"] = es_raw.enter_context(nc.sbuf_tensor("vfm", [128, 2 * T], BF16, side="right"))
    sb["zrow"] = es_raw.enter_context(nc.sbuf_tensor("zrow", [128, 16], BF16, side="right"))
    alloc("rows", [128, T], BF16)
    def m1row(t0, TN):
        return sb["rows"][0:1, t0:t0 + TN]
    alloc("rlb", [128, 2 * 512], BF16)
    def rlbs(i, TN):
        return sb["rlb"][0:1, i * 512:i * 512 + TN]
    alloc("r3c3b", [128, 2 * T], BF16)
    def r3row(t0, TN):
        return sb["r3c3b"][0:1, t0:t0 + TN]
    def c3row(t0, TN):
        return sb["r3c3b"][0:1, T + t0:T + t0 + TN]
    alloc("onescol", [128, 1], BF16)
    alloc("onesrow", [128, 128], BF16)
    alloc("cc", [128, 8], F32)
    alloc("ktcol", [128, 1], F32)
    alloc("smallf", [128, 7 * 512 + 8], F32)
    def sfs(i, TN):
        return sb["smallf"][0:1, i * 512:i * 512 + TN]

    es_cst = ExitStack()
    for _n, _s, _d in (("eyeb", [128, 128], BF16), ("msk16b", [128, 64], BF16),
                       ("msk4b", [128, 64], BF16), ("sel16b", [128, 8 * 16], BF16),
                       ("bc16b", [16, 8 * 128], BF16), ("sel4b", [128, 2 * 4], BF16),
                       ("bc4b", [36, 2 * 128], BF16), ("biascols", [128, 20], F32),
                       ("negm", [128, 1], F32)):
        sb[_n] = es_cst.enter_context(nc.sbuf_tensor(_n, _s, _d))

    ps = []
    for i in range(7):
        ps.append(es.enter_context(nc.psum_tensor(f"pb{i}", [128, 512], F32)))
    es_psb = ExitStack()
    psb = es_psb.enter_context(nc.psum_tensor("pbT", [128, 1024], BF16))

    tok_c = b.dma("S", [
        (sb["biascols"][:, 0:8], dr["qc1b"][:, :]),
        (sb["biascols"][:, 8:16], dr["qc2b"][:, :]),
        (sb["biascols"][:, 16:18], dr["kc1b"][:, :]),
        (sb["biascols"][:, 18:20], dr["kc2b"][:, :]),
        (sb["smallf"][0:1, 7 * 512:7 * 512 + 1], dr["ktemp"][:, :]),
        (sb["cc"][:, 5:6], dr["ivec"][:, 0:1], dict(allow_slow_non_contiguous=True)),
        (sb["cc"][:, 6:7], dr["ivec"][:, 1:2], dict(allow_slow_non_contiguous=True)),
    ])
    tok_cb = b.dma("G", [
        (sb["eyeb"][:, :], dr["eye128"][:, :]),
        (sb["msk16b"][:, :], dr["msk16"][:, :]),
        (sb["msk4b"][:, :], dr["msk4"][:, :]),
        (sb["sel16b"][:, :], dr["sel16"][:, :]),
        (sb["bc16b"][:, :], dr["bc16"][:, :]),
        (sb["sel4b"][:, :], dr["sel4"][:, :]),
        (sb["bc4b"][:, :], dr["bc4"][:, :]),
    ])
    b.wait("V", tok_c)
    tv = b.inc("V", nc.vector.memset(sb["onescol"][:, :], 1.0))
    tv = b.inc("V", nc.vector.memset(sb["onesrow"][:, :], 1.0))
    tcc = b.inc("V", nc.vector.memset(sb["cc"][:, 0:1], EPS))
    tcc = b.inc("V", nc.vector.memset(sb["cc"][:, 1:2], HPI1))
    b.waits("A", [tcc])
    ta = b.inc("A", nc.scalar.activation(sb["smallf"][0:1, 7 * 512 + 1:7 * 512 + 2],
                                         sb["smallf"][0:1, 7 * 512:7 * 512 + 1],
                                         AF.Abs, scale=0.125))
    b.wait("A", ta)
    ta = b.inc("A", nc.scalar.activation(sb["smallf"][0:1, 7 * 512 + 2:7 * 512 + 3],
                                         sb["smallf"][0:1, 7 * 512 + 1:7 * 512 + 2],
                                         AF.Copy, scale=-1.0))
    tok_r = b.dma("S", [(dr["rowscr"][0:1, 0:2],
                         sb["smallf"][0:1, 7 * 512:7 * 512 + 3:2],
                         dict(allow_slow_non_contiguous=True))], deps=[ta])
    tok_r2 = b.dma("S", [
        (sb["negm"][:, 0:1], bass.AP(dr["rowscr"], 1, [[0, 128], [1, 1]])),
        (sb["ktcol"][32:36, 0:1], bass.AP(dr["rowscr"], 0, [[0, 4], [1, 1]])),
    ], deps=[tok_r])

    b.wait("V", tok_cb)
    tv0 = b.inc("V", nc.vector.memset(sb["zrow"][:, :], 0.0))
    tvv = b.inc("V", nc.vector.memset(sb["vfm"][:, T:T + 1], 0.0))
    tok_zp = b.dma("S", [
        (dr["qrawD"].ap().rearrange("p (c t) -> p c t", c=8)[:, :, 0:2],
         sb["zrow"][:, 0:16].rearrange("p (c t) -> p c t", c=8)),
        (dr["krawD"].ap().rearrange("p (c t) -> p c t", c=2)[:, :, 0:2],
         sb["zrow"][:, 0:4].rearrange("p (c t) -> p c t", c=2)),
    ], deps=[tv0])

    if upto == "p0":
        b.drain_dma()
        b.barrier()
        es_raw.close()
        es_qm.close()
        es_psb.close()
        es_cst.close()
        es.close()
        return nc
    TN1 = min(256, T)
    nt1 = T // TN1
    es1 = ExitStack()
    def alloc1(name, shape, dtype):
        sb[name] = es1.enter_context(nc.sbuf_tensor(name, shape, dtype, side="right"))
        return sb[name]
    alloc1("wqkvb", [128, NCH * QKV], BF16)
    alloc1("x32", [128, (NCH // 4) * TN1], F32)
    alloc1("xb", [128, 2 * NCH * TN1], BF16)
    alloc1("xsq", [128, 4 * TN1], BF16)
    alloc1("r1s", [128, 2 * TN1], F32)
    alloc1("uevac", [128, TN1], F32)
    alloc1("stg", [128, 3 * TN1], BF16)
    alloc1("wsumb", [1, QKV], BF16)

    wq_toks = []
    for g in range(4):
        prs = []
        for kc in range(8 * g, 8 * g + 8):
            prs.append((sb["wqkvb"][:, kc * QKV:(kc + 1) * QKV],
                        dr["wqkvT"][128 * kc:128 * kc + 128, :]))
        wq_toks.append(b.dma("G", prs))

    b.wait("T", tv)
    for seg in range(3):
        for kc in range(NCH):
            b.wait("T", wq_toks[kc // 8])
            ins = nc.tensor.matmul(ps[6][0:1, 0:512], sb["onescol"][:, :],
                                   sb["wqkvb"][:, kc * QKV + seg * 512: kc * QKV + (seg + 1) * 512],
                                   start=(kc == 0), stop=(kc == NCH - 1))
        tt = b.inc("T", ins)
        b.wait("V", tt)
        tv = b.inc("V", nc.vector.tensor_copy(sb["wsumb"][0:1, seg * 512:(seg + 1) * 512],
                                              ps[6][0:1, 0:512]))
        b.wait("T", tv)

    if upto == "p1a_w":
        b.drain_dma()
        b.barrier()
        es1.close()
        es_raw.close()
        es_qm.close()
        es_cst.close()
        es.close()
        return nc

    def ln_rows(TN, ps_sx, ps_sx2, m_ap, rb_r_ap, rb_c_ap, dep=None, rbdep=None):
        b.wait("A", dep)
        if ps_sx is not None:
            ta = b.inc("A", nc.scalar.activation(m_ap, ps_sx, AF.Copy, scale=1.0 / D))
            b.wait("A", ta)
        ta = b.inc("A", nc.scalar.activation(sfs(0, TN), ps_sx2, AF.Copy, scale=1.0 / D))
        ta2 = b.inc("A", nc.scalar.activation(sfs(1, TN), m_ap, AF.Square))
        b.waits("V", [ta, ta2, dep])
        tv = b.inc("V", nc.vector.tensor_sub(sfs(0, TN), sfs(0, TN), sfs(1, TN)))
        b.wait("A", tv)
        ta = b.inc("A", nc.scalar.activation(sfs(2, TN), sfs(0, TN), AF.Sqrt,
                                             bias=sb["cc"][0:1, 0:1]))
        b.wait("V", ta)
        tv = b.inc("V", nc.vector.reciprocal(sfs(2, TN), sfs(2, TN)))
        b.wait("V", tv)
        tv = b.inc("V", nc.vector.tensor_mul(sfs(3, TN), m_ap, sfs(2, TN)))
        b.waits("V", [tv, rbdep])
        tv = b.inc("V", nc.vector.tensor_copy(rb_r_ap, sfs(2, TN)))
        tv = b.inc("V", nc.vector.tensor_copy(rb_c_ap, sfs(3, TN)))
        return tv

    last_evac = {}

    for t in wq_toks:
        b.wait("T", t)

    HN = NCH // 4
    mm_tok = None
    xb_used = [None, None]
    cast_tok = [None, None, None, None]

    def load_cast_tile(ti):
        t0 = ti * TN1
        par = ti % 2
        for hf in range(4):
            prs = []
            for kc in range(HN * hf, HN * (hf + 1)):
                prs.append((sb["x32"][:, (kc - HN * hf) * TN1:(kc - HN * hf + 1) * TN1],
                            dr["x_in"][128 * kc:128 * kc + 128, t0:t0 + TN1]))
            x_tok = b.dma("S", prs, deps=[d for d in (cast_tok[(hf + 3) % 4],) if d])
            b.waits("V", [x_tok, xb_used[par]])
            cast_tok[hf] = b.inc("V", nc.vector.tensor_copy(
                sb["xb"][:, (par * NCH + hf * HN) * TN1:(par * NCH + (hf + 1) * HN) * TN1],
                sb["x32"][:, :]))
        return cast_tok[3]

    tile_cast = {0: load_cast_tile(0)}
    for ti in range(nt1):
        t0 = ti * TN1
        par = ti % 2
        xo = par * NCH * TN1
        if ti + 1 < nt1:
            tile_cast[ti + 1] = load_cast_tile(ti + 1)
        cast_now = tile_cast[ti]

        b.waits("A", [cast_now])
        if ti > 0:
            b.wait("T", last_evac["rows_done"])
        tsq = None
        for gq in range(8):
            if tsq is not None:
                b.wait("A", tsq)
            ta = b.inc("A", nc.scalar.activation(
                sb["xsq"][:, :], sb["xb"][:, xo + gq * 4 * TN1: xo + (gq + 1) * 4 * TN1],
                AF.Square))
            b.wait("T", ta)
            b.waits("T", [cast_now])
            for c4 in range(4):
                kc = gq * 4 + c4
                ins = nc.tensor.matmul(ps[5][0:1, 0:TN1], sb["onescol"][:, :],
                                       sb["xb"][:, xo + kc * TN1: xo + (kc + 1) * TN1],
                                       start=(kc == 0), stop=False)
                ins = nc.tensor.matmul(ps[5][0:1, TN1:2 * TN1], sb["onescol"][:, :],
                                       sb["xsq"][:, c4 * TN1:(c4 + 1) * TN1],
                                       start=False, stop=(kc == NCH - 1))
            tsq = b.inc("T", ins)
        stats_tok = tsq
        if upto == "p1a_s":
            break
        b.wait("A", stats_tok)
        b.wait("V", stats_tok)
        rows_tok = ln_rows(TN1, ps[5][0:1, 0:TN1], ps[5][0:1, TN1:2 * TN1],
                           m1row(t0, TN1), rlbs(0, TN1), rlbs(1, TN1),
                           dep=last_evac.get("rows_done"),
                           rbdep=last_evac.get("rlb_used"))
        last_evac["rows_done"] = rows_tok

        if upto == "p1a_r":
            break
        b.wait("T", rows_tok)
        tt = b.inc("T", nc.tensor.matmul(ps[4][:, 0:TN1], sb["onesrow"][0:1, :],
                                         rlbs(0, TN1), start=True, stop=True))
        b.wait("V", tt)
        if last_evac.get("r1s_free") is not None:
            b.wait("V", last_evac["r1s_free"])
        par = ti % 2
        r1s_tok = b.inc("V", nc.vector.tensor_copy(sb["r1s"][:, par * TN1:(par + 1) * TN1],
                                                   ps[4][:, 0:TN1]))
        b.wait("T", r1s_tok)

        if upto == "p1a_b":
            break
        for oc in range(12):
            pq = ps[oc % 2]
            pc = ps[2 + oc % 2]
            if ("pq", oc % 2) in last_evac:
                b.wait("T", last_evac[("pq", oc % 2)])
            for kc in range(NCH):
                ins = nc.tensor.matmul(pq[:, 0:TN1],
                                       sb["wqkvb"][:, kc * QKV + oc * 128: kc * QKV + (oc + 1) * 128],
                                       sb["xb"][:, xo + kc * TN1: xo + (kc + 1) * TN1],
                                       start=(kc == 0), stop=(kc == NCH - 1))
            mm_tok = b.inc("T", ins)
            xb_used[par] = mm_tok
            if ("pc", oc % 2) in last_evac:
                b.wait("T", last_evac[("pc", oc % 2)])
            tt2 = b.inc("T", nc.tensor.matmul(pc[:, 0:TN1],
                                              sb["wsumb"][0:1, oc * 128:(oc + 1) * 128],
                                              rlbs(1, TN1),
                                              start=True, stop=True))
            last_evac["rlb_used"] = tt2
            b.waits("V", [mm_tok, tt2, r1s_tok])
            if ("ue",) in last_evac:
                b.wait("V", last_evac[("ue",)])
            tv1 = b.inc("V", nc.vector.tensor_mul(
                sb["uevac"][:, 0:TN1],
                sb["r1s"][:, par * TN1:(par + 1) * TN1], pq[:, 0:TN1]))
            last_evac[("pq", oc % 2)] = tv1
            b.wait("V", tv1)
            if oc < 10:
                ss = oc % 3
                b.waits("V", [last_evac.get(("stgd", ss)), last_evac.get(("stgm", ss))])
                dst = sb["stg"][:, ss * TN1:(ss + 1) * TN1]
            elif oc == 10:
                dst = sb["vfm"][:, t0:t0 + TN1]
            else:
                hi = min(t0 + 1 + TN1, T)
                dst = sb["vfm"][:, T + t0 + 1: T + hi]
            if oc == 11:
                w = hi - (t0 + 1)
                tv2 = b.inc("V", nc.vector.tensor_sub(
                    dst, sb["uevac"][:, 0:w], pc[:, 0:w]))
            else:
                tv2 = b.inc("V", nc.vector.tensor_sub(
                    dst, sb["uevac"][:, 0:TN1], pc[:, 0:TN1]))
            last_evac[("pc", oc % 2)] = tv2
            last_evac[("ue",)] = tv2
            if oc < 10:
                if oc < 8:
                    ddst = dr["qrawD"][:, oc * PT + 2 + t0: oc * PT + 2 + t0 + TN1]
                else:
                    ddst = dr["krawD"][:, (oc - 8) * PT + 2 + t0: (oc - 8) * PT + 2 + t0 + TN1]
                last_evac[("stgd", ss)] = b.dma("S", [(ddst, dst)], deps=[tv2])
                b.wait("T", tv2)
                if oc < 8:
                    tmn = b.inc("T", nc.tensor.matmul(ps[4][0:64, 0:TN1], sb["msk16b"][:, :],
                                                      dst, start=(oc == 0), stop=(oc == 7)))
                else:
                    tmn = b.inc("T", nc.tensor.matmul(ps[6][0:64, 0:TN1], sb["msk4b"][:, :],
                                                      dst, start=(oc == 8), stop=(oc == 9)))
                last_evac[("stgm", ss)] = tmn
                if oc == 7:
                    tqm = tmn
                if oc == 9:
                    tkm = tmn
        if upto == "p1a_q":
            break
        b.waits("V", [tqm, tkm])
        tv = b.inc("V", nc.vector.tensor_copy(sb["qmkm"][0:64, t0:t0 + TN1], ps[4][0:64, 0:TN1]))
        b.wait("V", tv)
        tv = b.inc("V", nc.vector.tensor_copy(sb["qmkm"][64:128, t0:t0 + TN1], ps[6][0:64, 0:TN1]))
        b.wait("T", tv)
        last_evac["r1s_free"] = last_evac[("pq", 1)]

    b.drain_dma()
    b.barrier()
    es1.close()
    if upto.startswith("p1a"):
        b.drain_dma()
        b.barrier()
        es_psb.close()
        es_cst.close()
        es.close()
        return nc

    TC1 = min(512, T)
    TN2 = min(512, T)
    nt2 = T // TN2
    NJ = T // 128
    VW = 65
    es_qkv = ExitStack()
    sb["vtm"] = es_qkv.enter_context(nc.sbuf_tensor("vtm", [128, 4 * (NJ * VW)], BF16))
    sb["q4a"] = es_qkv.enter_context(nc.sbuf_tensor("q4a", [128, 4 * T], BF16))
    sb["q4b"] = es_qkv.enter_context(nc.sbuf_tensor("q4b", [128, 4 * T], BF16))
    sb["kfin"] = es_qkv.enter_context(nc.sbuf_tensor("kfin", [128, 2 * T], BF16))
    es_tabs = ExitStack()
    sb["tabs"] = es_tabs.enter_context(nc.sbuf_tensor("tabs", [128, 2 * T], BF16))
    es_tab = ExitStack()
    sb["tabscr"] = es_tab.enter_context(nc.sbuf_tensor("tabscr", [128, 4 * T], F32, side="right"))
    sb["kint"] = es_tab.enter_context(nc.sbuf_tensor("kint", [128, T], I32, side="right"))

    def ts_op(out, in0, s1, s2, op0, op1=ALU.bypass, deps=(), eng="V"):
        b.waits(eng, deps)
        return b.inc(eng, nc.vector.tensor_scalar(out, in0, s1, s2, op0, op1))

    A_TB, A_ANG, A_S = (slice(0, T), slice(T, 2 * T), slice(2 * T, 3 * T))
    SC2 = slice(3 * T, 4 * T)
    tg = b.inc("G", nc.gpsimd.iota(sb["tabscr"][0:1, 0:T], [[1, T]],
                                   channel_multiplier=0,
                                   allow_small_or_imprecise_dtypes=True))
    tok_tr = b.dma("S", [(dr["rowscr"][0:1, 0:T], sb["tabscr"][0:1, 0:T])], deps=[tg, tok_r2])
    tok_tb = b.dma("S", [(sb["tabscr"][:, A_TB], bass.AP(dr["rowscr"], 0, [[0, 128], [1, T]]))],
                   deps=[tok_tr])
    b.waits("A", [tok_c])
    t_e1 = b.inc("A", nc.scalar.activation(sb["cc"][:, 7:8], sb["cc"][:, 5:6], AF.Exp, scale=LN1E4_S1))
    t_e2 = b.inc("A", nc.scalar.activation(sb["cc"][:, 4:5], sb["cc"][:, 5:6], AF.Exp, scale=LN1E4_S2))
    b.waits("V", [t_e1, t_e2])
    t_inv = b.inc("V", nc.vector.tensor_mul(sb["cc"][:, 4:5], sb["cc"][:, 7:8], sb["cc"][:, 4:5]))
    b.wait("V", tok_tb)
    t_ang = ts_op(sb["tabscr"][:, A_ANG], sb["tabscr"][:, A_TB], sb["cc"][:, 4:5], None,
                  ALU.mult, deps=[t_inv])

    HPIC = float(np.float32(math.pi - PI1))
    def build_sintab(src_sl, dst_sl, shift, dep):
        W0, SC = A_TB, A_S
        dd = dep
        if shift:
            dd = ts_op(sb["tabscr"][:, SC], sb["tabscr"][:, src_sl], HPI1, HPI2,
                       ALU.add, ALU.add, deps=[dd])
            srcs = SC
        else:
            srcs = src_sl
        d0 = ts_op(sb["tabscr"][:, W0], sb["tabscr"][:, srcs], INV2PI1, None,
                   ALU.mult, deps=[dd])
        b.waits("V", [d0])
        d1 = b.inc("V", nc.vector.tensor_copy(sb["kint"][:, :], sb["tabscr"][:, W0]))
        b.waits("V", [d1])
        d2 = b.inc("V", nc.vector.tensor_copy(dst_sl, sb["kint"][:, :]))
        b.waits("V", [d2])
        d3 = b.inc("V", nc.vector.tensor_sub(sb["tabscr"][:, W0], sb["tabscr"][:, W0],
                                             dst_sl))
        d4 = ts_op(sb["tabscr"][:, SC2], sb["tabscr"][:, W0], 0.5, None, ALU.is_ge,
                   deps=[d3])
        b.waits("V", [d4])
        d5 = b.inc("V", nc.vector.tensor_add(dst_sl, dst_sl, sb["tabscr"][:, SC2]))
        d6 = ts_op(sb["tabscr"][:, SC2], sb["tabscr"][:, W0], -0.5, None, ALU.is_lt,
                   deps=[d5])
        b.waits("V", [d6])
        d7 = b.inc("V", nc.vector.tensor_sub(dst_sl, dst_sl, sb["tabscr"][:, SC2]))
        d8 = ts_op(sb["tabscr"][:, W0], sb["tabscr"][:, srcs], 0.5, None, ALU.mult,
                   deps=[d7])
        b.waits("V", [d8])
        d9 = b.inc("V", nc.vector.scalar_tensor_tensor(
            sb["tabscr"][:, W0], dst_sl, -PI1, sb["tabscr"][:, W0], ALU.mult, ALU.add))
        b.waits("V", [d9])
        da = b.inc("V", nc.vector.scalar_tensor_tensor(
            sb["tabscr"][:, W0], dst_sl, -HPIC, sb["tabscr"][:, W0], ALU.mult, ALU.add))
        b.waits("A", [da])
        db = b.inc("A", nc.scalar.activation(sb["tabscr"][:, SC2], sb["tabscr"][:, W0],
                                             AF.Abs))
        dc = b.inc("A", nc.scalar.activation(dst_sl, sb["tabscr"][:, W0], AF.Sin))
        dd2 = ts_op(sb["tabscr"][:, SC2], sb["tabscr"][:, SC2], HPI2, None,
                    ALU.subtract, deps=[db, dc], eng="V")
        b.waits("A", [dd2])
        de = b.inc("A", nc.scalar.activation(sb["tabscr"][:, SC2], sb["tabscr"][:, SC2],
                                             AF.Sin, scale=-1.0, bias=sb["cc"][:, 1:2]))
        b.waits("V", [dc, de])
        df = b.inc("V", nc.vector.scalar_tensor_tensor(
            dst_sl, dst_sl, 2.0, sb["tabscr"][:, SC2], ALU.mult, ALU.mult))
        return df

    t_sn = build_sintab(A_ANG, sb["tabs"][:, T:2 * T], False, t_ang)
    t_sn = ts_op(sb["tabs"][:, T:2 * T], sb["tabs"][:, T:2 * T], sb["cc"][:, 6:7], None,
                 ALU.mult, deps=[t_sn])
    t_cs = build_sintab(A_ANG, sb["tabs"][:, 0:T], True, t_sn)
    b.drain_dma()
    b.barrier()
    es_tab.close()
    if upto == "p1b_t":
        b.drain_dma()
        b.barrier()
        es_tabs.close()
        es_qkv.close()
        es_raw.close()
        es_qm.close()
        es_psb.close()
        es_cst.close()
        es.close()
        return nc

    es2 = ExitStack()
    sb["c1q"] = es2.enter_context(nc.sbuf_tensor("c1q", [128, 8 * PT], BF16))
    sb["c1k"] = es2.enter_context(nc.sbuf_tensor("c1k", [128, 2 * PT], BF16))
    es_c1 = ExitStack()
    sb["wc1q"] = es_c1.enter_context(nc.sbuf_tensor("wc1q", [128, 24 * 128], BF16, side="right"))
    sb["wc1k"] = es_c1.enter_context(nc.sbuf_tensor("wc1k", [128, 6 * 128], BF16, side="right"))

    tok_wc1 = b.dma("G", [
        (sb["wc1q"][:, :].rearrange("p (t c b) -> p t c b", t=3, c=8),
         dr["qc1bd"].ap().transpose([2, 0, 1, 3])),
        (sb["wc1k"][:, :].rearrange("p (t c b) -> p t c b", t=3, c=2),
         dr["kc1bd"].ap().transpose([2, 0, 1, 3]))])

    tzp = b.inc("V", nc.vector.memset(sb["c1q"].ap().rearrange("p (c t) -> p c t", c=8)[:, :, 0:2], 0.0))
    tzp = b.inc("V", nc.vector.memset(sb["c1k"].ap().rearrange("p (c t) -> p c t", c=2)[:, :, 0:2], 0.0))
    tvo = b.inc("V", nc.vector.memset(
        sb["vtm"].ap().rearrange("p (g j w) -> p g j w", g=4, j=NJ)[:, :, :, 64:65], 1.0))

    sb["c1wq"] = es_c1.enter_context(nc.sbuf_tensor("c1wq", [128, 2 * 8 * (TC1 + 2)], BF16, side="right"))
    sb["c1wk"] = es_c1.enter_context(nc.sbuf_tensor("c1wk", [128, 2 * 2 * (TC1 + 2)], BF16, side="right"))
    c1q_c = sb["c1q"].ap().rearrange("p (c t) -> p c t", c=8)
    c1k_c = sb["c1k"].ap().rearrange("p (c t) -> p c t", c=2)
    pv_guard = {}
    WQ = 8 * (TC1 + 2)
    WK = 2 * (TC1 + 2)
    for ti in range(T // TC1):
        t0 = ti * TC1
        wsl = ti % 2
        prs = [(sb["c1wq"][:, wsl * WQ + c * (TC1 + 2): wsl * WQ + (c + 1) * (TC1 + 2)],
                dr["qrawD"][:, c * PT + t0: c * PT + t0 + TC1 + 2]) for c in range(8)]
        prs += [(sb["c1wk"][:, wsl * WK + c * (TC1 + 2): wsl * WK + (c + 1) * (TC1 + 2)],
                 dr["krawD"][:, c * PT + t0: c * PT + t0 + TC1 + 2]) for c in range(2)]
        win_tok = b.dma("S", prs, deps=[tok_zp, pv_guard.get(("win", wsl))])
        for side, nch_s, wsb, wbuf, woff, dst_c, bias_off in (
                ("q", 8, "wc1q", "c1wq", wsl * WQ, c1q_c, 0),
                ("k", 2, "wc1k", "c1wk", wsl * WK, c1k_c, 16)):
            for c in range(nch_s):
                bank = ps[c % 2]
                if ("c1", c % 2) in pv_guard:
                    b.wait("T", pv_guard[("c1", c % 2)])
                b.wait("T", tok_wc1)
                b.wait("T", win_tok)
                for tap in range(3):
                    ins = nc.tensor.matmul(
                        bank[:, 0:TC1],
                        sb[wsb][:, (tap * nch_s + c) * 128:(tap * nch_s + c + 1) * 128],
                        sb[wbuf][:, woff + c * (TC1 + 2) + tap: woff + c * (TC1 + 2) + tap + TC1],
                        start=(tap == 0), stop=(tap == 2))
                tmm = b.inc("T", ins)
                b.waits("V", [tmm, tzp])
                ta = b.inc("V", nc.vector.tensor_scalar(
                    dst_c[:, c, 2 + t0: 2 + t0 + TC1], bank[:, 0:TC1],
                    sb["biascols"][:, bias_off + c: bias_off + c + 1], None, ALU.add))
                pv_guard[("c1", c % 2)] = ta
        pv_guard[("win", wsl)] = tmm
    for g in range(4):
        bpv = 64 * (g % 2)
        for j in range(NJ):
            if ("vt",) in pv_guard:
                b.wait("T", pv_guard[("vt",)])
            tt = b.inc("T", nc.tensor.transpose(
                psb[:, 0:64],
                sb["vfm"][bpv:bpv + 64, (g // 2) * T + 128 * j: (g // 2) * T + 128 * (j + 1)],
                sb["eyeb"][bpv:bpv + 64, bpv:bpv + 64]))
            b.wait("V", tt)
            tv = b.inc("V", nc.vector.tensor_copy(
                sb["vtm"][:, g * NJ * VW + j * VW: g * NJ * VW + j * VW + 64],
                psb[:, 0:64]))
            pv_guard[("vt",)] = tv
    b.drain_dma()
    b.barrier()
    es_c1.close()
    es_raw.close()
    if upto == "p1b_c1":
        b.barrier()
        es2.close()
        es_tabs.close()
        es_qkv.close()
        es_qm.close()
        es_psb.close()
        es_cst.close()
        es.close()
        return nc

    es3 = ExitStack()
    for nm, shp, dt_ in (("wc2q", [128, 24 * 1024], BF16), ("wc2k", [128, 6 * 256], BF16),
                         ("qpost", [128, 8 * TN2], BF16), ("kpost", [128, 2 * TN2], BF16),
                         ("sqb", [128, 2 * TN2], BF16), ("mm2b", [128, 2 * TN2], BF16),
                         ("rqf", [128, TN2], F32), ("rqb", [128, TN2], BF16),
                         ("rsw", [128, TN2], BF16), ("rt1", [128, TN2], BF16),
                         ("rt2", [128, TN2], BF16)):
        sb[nm] = es3.enter_context(nc.sbuf_tensor(nm, shp, dt_, side="right"))
    wq_toks2 = []
    for tap in range(3):
        prs = [(sb["wc2q"][:, (tap * 8 + kc) * 1024:(tap * 8 + kc + 1) * 1024],
                dr["qc2T"][tap, 128 * kc:128 * kc + 128, :]) for kc in range(8)]
        wq_toks2.append(b.dma("G", prs))
    tok_wc2k = b.dma("G", [(sb["wc2k"][:, (tap * 2 + kc) * 256:(tap * 2 + kc + 1) * 256],
                            dr["kc2T"][tap, 128 * kc:128 * kc + 128, :])
                           for tap in range(3) for kc in range(2)])

    PAIRSWAP = []
    for i in range(16):
        PAIRSWAP += [2 * i + 1, 2 * i]

    guard = {}
    SIDES = [("q", 8, "wc2q", c1q_c, 8, 8, slice(0, TN2), "qpost", ps[3], "sel16b", 16),
             ("k", 2, "wc2k", c1k_c, 2, 18, slice(TN2, 2 * TN2), "kpost", ps[4], "sel4b", 4)]
    for ti in range(nt2):
        t0 = ti * TN2
        mdeps = [guard.get("mm2b")]
        b.waits("V", [d for d in mdeps if d])
        tv = b.inc("V", nc.vector.tensor_copy(sb["mm2b"][0:64, 0:TN2], sb["qmkm"][64:128, t0:t0 + TN2]))
        tv = b.inc("V", nc.vector.tensor_copy(sb["mm2b"][64:128, 0:TN2], sb["qmkm"][64:128, t0:t0 + TN2]))
        tv = b.inc("V", nc.vector.tensor_copy(sb["mm2b"][0:64, TN2:2 * TN2], sb["qmkm"][0:64, t0:t0 + TN2]))
        t_mm2b = b.inc("V", nc.vector.tensor_copy(sb["mm2b"][64:128, TN2:2 * TN2], sb["qmkm"][0:64, t0:t0 + TN2]))

        pending_ssq = None

        def emit_ssq(p):
            side_i, oc, tsq = p
            _, nch_s, _, _, _, _, _, _, ps_ssq, selb, selw = SIDES[side_i]
            b.waits("T", [tsq, guard.get("ssq_read")])
            ins = nc.tensor.matmul(
                ps_ssq[0:selw, 0:TN2], sb[selb][:, oc * selw:(oc + 1) * selw],
                sb["sqb"][:, ((side_i * 8 + oc) % 2) * TN2:((side_i * 8 + oc) % 2 + 1) * TN2],
                start=(oc == 0), stop=(oc == nch_s - 1))
            return b.inc("T", ins)

        for side_i, (side, nch_s, wsb, src_c, nkc, bias_off, post_sl, dst_post,
                     ps_ssq, selb, selw) in enumerate(SIDES):
            for oc in range(nch_s):
                gi = side_i * 8 + oc
                bank = ps[gi % 3]
                if ("c2", gi % 3) in guard:
                    b.wait("T", guard[("c2", gi % 3)])
                for tok in (wq_toks2 if side == "q" else [tok_wc2k]):
                    b.wait("T", tok)
                first = True
                for tap in range(3):
                    for kc in range(nkc):
                        if side == "q":
                            wsl = sb[wsb][:, (tap * 8 + kc) * 1024 + oc * 128:
                                          (tap * 8 + kc) * 1024 + (oc + 1) * 128]
                        else:
                            wsl = sb[wsb][:, (tap * 2 + kc) * 256 + oc * 128:
                                          (tap * 2 + kc) * 256 + (oc + 1) * 128]
                        ins = nc.tensor.matmul(
                            bank[:, 0:TN2], wsl, src_c[:, kc, t0 + tap: t0 + tap + TN2],
                            start=first, stop=(tap == 2 and kc == nkc - 1))
                        first = False
                tmm = b.inc("T", ins)
                if pending_ssq is not None:
                    tprev = emit_ssq(pending_ssq)
                    guard[("sq", (pending_ssq[0] * 8 + pending_ssq[1]) % 2)] = tprev
                b.waits("V", [tmm, t_mm2b, guard.get(("rope", side, oc))])
                tpost = b.inc("V", nc.vector.scalar_tensor_tensor(
                    sb[dst_post][:, oc * TN2:(oc + 1) * TN2], bank[:, 0:TN2],
                    sb["biascols"][:, bias_off + oc: bias_off + oc + 1],
                    sb["mm2b"][:, post_sl], ALU.add, ALU.add))
                guard[("c2", gi % 3)] = tpost
                guard[("post", side, oc)] = tpost
                b.waits("A", [tpost, guard.get(("sq", gi % 2))])
                tsq = b.inc("A", nc.scalar.activation(
                    sb["sqb"][:, (gi % 2) * TN2:(gi % 2 + 1) * TN2],
                    sb[dst_post][:, oc * TN2:(oc + 1) * TN2], AF.Square))
                pending_ssq = (side_i, oc, tsq)
        tssq_last = emit_ssq(pending_ssq)
        guard[("ssq_done", "k")] = tssq_last
        pending_ssq = None

        b.waits("A", [tssq_last, guard.get("rq_used")])
        ta = b.inc("A", nc.scalar.activation(sb["rqf"][0:16, :], ps[3][0:16, 0:TN2], AF.Sqrt))
        ta2 = b.inc("A", nc.scalar.activation(sb["rqf"][32:36, :], ps[4][0:4, 0:TN2], AF.Sqrt))
        b.waits("V", [ta, ta2])
        tvq = b.inc("V", nc.vector.reciprocal(sb["rqf"][0:16, :], sb["rqf"][0:16, :]))
        tvk = b.inc("V", nc.vector.reciprocal(sb["rqf"][32:36, :], sb["rqf"][32:36, :]))
        b.waits("V", [tvq, tvk, tok_r2])
        tvk = b.inc("V", nc.vector.tensor_scalar(sb["rqf"][32:36, :], sb["rqf"][32:36, :],
                                                 sb["ktcol"][32:36, 0:1], None, ALU.mult))
        b.waits("V", [tvk, guard.get("rqb_used")])
        tvq = b.inc("V", nc.vector.tensor_copy(sb["rqb"][0:16, :], sb["rqf"][0:16, :]))
        tvk = b.inc("V", nc.vector.tensor_copy(sb["rqb"][32:36, :], sb["rqf"][32:36, :]))
        guard["ssq_read"] = tvk
        guard["rqb_ready"] = tvk

        for side, nch_s, dst_post in (("q", 8, "qpost"), ("k", 2, "kpost")):
            for oc in range(nch_s):
                post_ap = sb[dst_post][:, oc * TN2:(oc + 1) * TN2]
                bankR = ps[5 + oc % 2]
                if ("bR", oc % 2) in guard:
                    b.wait("T", guard[("bR", oc % 2)])
                b.wait("T", guard["rqb_ready"])
                if side == "q":
                    tbc = b.inc("T", nc.tensor.matmul(
                        bankR[:, 0:TN2], sb["bc16b"][:, oc * 128:(oc + 1) * 128],
                        sb["rqb"][0:16, :], start=True, stop=True))
                else:
                    tbc = b.inc("T", nc.tensor.matmul(
                        bankR[:, 0:TN2], sb["bc4b"][32:36, oc * 128:(oc + 1) * 128],
                        sb["rqb"][32:36, :], start=True, stop=True))
                deps = [guard.get(("rope", side, oc)), guard.get("rope_bufs")]
                b.waits("V", [d for d in deps if d] + [guard[("post", side, oc)]])
                s1 = b.inc("V", nc.vector.stream_shuffle(sb["rsw"][:, :], post_ap, PAIRSWAP))
                s2 = b.inc("V", nc.vector.tensor_mul(sb["rt1"][:, :], post_ap,
                                                     sb["tabs"][:, t0:t0 + TN2]))
                b.waits("V", [s1])
                s3 = b.inc("V", nc.vector.tensor_mul(sb["rt2"][:, :], sb["rsw"][:, :],
                                                     sb["tabs"][:, T + t0:T + t0 + TN2]))
                b.waits("V", [s2, s3])
                s4 = b.inc("V", nc.vector.tensor_add(sb["rt1"][:, :], sb["rt1"][:, :],
                                                     sb["rt2"][:, :]))
                b.waits("V", [s4, tbc])
                if side == "q":
                    nqt2 = TN2 // 128
                    for half in range(2):
                        h = 2 * oc + half
                        g, l = h // 4, h % 4
                        q4t = sb["q4a"] if g < 2 else sb["q4b"]
                        bpq = 64 * (g % 2)
                        qt0 = t0 // 128
                        dst = q4t.ap()[bpq:bpq + 64].rearrange(
                            "p (qt l c) -> p qt l c", l=4, c=128)[:, qt0:qt0 + nqt2, l, :]
                        sv = b.inc("V", nc.vector.tensor_mul(
                            dst,
                            sb["rt1"][64 * half:64 * half + 64, :].rearrange(
                                "p (qt c) -> p qt c", c=128),
                            bankR[64 * half:64 * half + 64, 0:TN2].rearrange(
                                "p (qt c) -> p qt c", c=128)))
                else:
                    sv = b.inc("V", nc.vector.tensor_mul(
                        sb["kfin"][:, oc * T + t0: oc * T + t0 + TN2],
                        sb["rt1"][:, :], bankR[:, 0:TN2]))
                guard[("rope", side, oc)] = sv
                guard[("bR", oc % 2)] = sv
                guard["rope_bufs"] = sv
                guard["rqb_used"] = tbc
        guard["mm2b"] = tpost
    b.drain_dma()
    b.barrier()
    es3.close()
    es_qm.close()
    es2.close()
    es_tabs.close()
    if upto == "p1b":
        b.drain_dma()
        b.barrier()
        es_psb.close()
        es_qkv.close()
        es_cst.close()
        es.close()
        return nc
    es_psb.close()

    es_oan = ExitStack()
    sb["oan"] = es_oan.enter_context(nc.sbuf_tensor("oan", [128, 8 * T], BF16, side="right"))
    es_p2a = ExitStack()
    sb["pt"] = es_p2a.enter_context(nc.sbuf_tensor("pt", [128, 3 * 512], BF16, side="right"))
    sb["o4"] = es_p2a.enter_context(nc.sbuf_tensor("o4", [128, 2 * 512], F32, side="right"))
    sb["lrb"] = es_p2a.enter_context(nc.sbuf_tensor("lrb", [128, 2 * 512], BF16, side="right"))
    sb["maskTs"] = es_p2a.enter_context(nc.sbuf_tensor("maskTs", [128, 512], F32, side="right"))
    tok_mT = b.dma("S", [(sb["maskTs"][:, :], dr["maskT"][:, :])])
    b.wait("V", tok_mT)
    NQT = T // 128
    ag = {}
    b.waits("A", [tok_r2])

    def p2a_scores(k, i, g):
        gp = 64 * (g % 2)
        q4t = sb["q4a"] if g < 2 else sb["q4b"]
        psO = ps[3 + k % 2]
        b.wait("T", ag.get(("psO", k % 2)))
        texp = {}
        tpv = None

        def emit_s(j):
            psS = ps[j % 3]
            b.wait("T", ag.get(("psS", j % 3)))
            tmm = b.inc("T", nc.tensor.matmul(
                psS[:, 0:512],
                sb["kfin"][gp:gp + 64, (g // 2) * T + 128 * j: (g // 2) * T + 128 * (j + 1)],
                q4t[gp:gp + 64, 512 * i: 512 * (i + 1)],
                start=True, stop=True))
            expdep = tmm
            if j == i:
                b.waits("V", [tmm])
                expdep = b.inc("V", nc.vector.tensor_add(psS[:, 0:512], psS[:, 0:512],
                                                         sb["maskTs"][:, :]))
            b.waits("A", [expdep, ag.get(("pt", j % 3))])
            texp[j] = b.inc("A", nc.scalar.activation(
                sb["pt"][:, (j % 3) * 512:(j % 3 + 1) * 512], psS[:, 0:512],
                AF.Exp, scale=0.125, bias=sb["negm"][:, 0:1]))
            ag[("psS", j % 3)] = texp[j]

        def emit_pv(j):
            nonlocal tpv
            b.waits("T", [texp[j]])
            tpv = b.inc("T", nc.tensor.matmul(
                psO[0:65, 0:512],
                sb["vtm"][:, g * NJ * VW + j * VW: g * NJ * VW + (j + 1) * VW],
                sb["pt"][:, (j % 3) * 512:(j % 3 + 1) * 512],
                start=(j == 0), stop=(j == i)))
            ag[("pt", j % 3)] = tpv

        for j in range(i + 1):
            emit_s(j)
            if j >= 2:
                emit_pv(j - 2)
        for j in range(max(0, i - 1), i + 1):
            emit_pv(j)
        return tpv

    def p2a_norm(k, i, g, tpv):
        psO = ps[3 + k % 2]
        osl = (k % 2) * 512
        b.waits("V", [tpv, ag.get(("o4", k % 2))])
        tev = b.inc("V", nc.vector.tensor_copy(sb["o4"][0:65, osl:osl + 512],
                                               psO[0:65, 0:512]))
        ag[("psO", k % 2)] = tev
        b.waits("V", [tev])
        trc = b.inc("V", nc.vector.reciprocal(sb["o4"][64:65, osl:osl + 512],
                                              sb["o4"][64:65, osl:osl + 512]))
        b.waits("V", [trc, ag.get(("lrb", k % 2))])
        tcb = b.inc("V", nc.vector.tensor_copy(sb["lrb"][64:65, osl:osl + 512],
                                               sb["o4"][64:65, osl:osl + 512]))
        b.waits("T", [tcb, ag.get(("psR", k % 2))])
        tbr = b.inc("T", nc.tensor.matmul(ps[5 + k % 2][0:64, 0:512],
                                          sb["onesrow"][64:65, 0:64],
                                          sb["lrb"][64:65, osl:osl + 512],
                                          start=True, stop=True))
        b.waits("V", [tbr])
        for l in range(4):
            tm = b.inc("V", nc.vector.tensor_mul(
                sb["oan"][64 * (l % 2):64 * (l % 2) + 64,
                          (2 * g + l // 2) * T + 128 * i: (2 * g + l // 2) * T + 128 * (i + 1)],
                sb["o4"][0:64, osl + 128 * l: osl + 128 * (l + 1)],
                ps[5 + k % 2][0:64, 128 * l: 128 * (l + 1)]))
        ag[("psR", k % 2)] = tm
        ag[("o4", k % 2)] = tm
        ag[("lrb", k % 2)] = tbr

    pending = None
    k = 0
    for i in range(NQT):
        for g in range(4):
            tpv = p2a_scores(k, i, g)
            if pending is not None:
                p2a_norm(*pending)
            pending = (k, i, g, tpv)
            k += 1
    p2a_norm(*pending)
    b.drain_dma()
    b.barrier()
    es_p2a.close()
    es_qkv.close()
    es_cst.close()
    if upto == "p2a":
        b.drain_dma()
        b.barrier()
        es_oan.close()
        es.close()
        return nc

    TNB = min(512, T)
    ntb = T // TNB
    es_p2bl = ExitStack()
    sb["woTb"] = es_p2bl.enter_context(nc.sbuf_tensor("woTb", [128, 8 * D], BF16))
    sb["ztile"] = es_p2bl.enter_context(nc.sbuf_tensor("ztile", [128, 2 * NCH * TNB], BF16))
    es_p2bt = ExitStack()
    for nm, shp, dt_ in (("xc", [128, 3 * TNB], F32), ("x1b", [128, 2 * TNB], F32),
                         ("zsq", [128, 2 * TNB], BF16), ("x1q", [128, 2 * TNB], BF16),
                         ("za", [128, TNB], F32), ("bc2t", [128, 2 * TNB], BF16),
                         ("wocsb", [128, 8], BF16), ("wocsf", [128, 8], F32)):
        sb[nm] = es_p2bt.enter_context(nc.sbuf_tensor(nm, shp, dt_, side="right"))
    wo_toks = []
    for g4 in range(2):
        prs = [(sb["woTb"][:, kc * D:(kc + 1) * D], dr["woT"][128 * kc:128 * kc + 128, :])
               for kc in range(4 * g4, 4 * g4 + 4)]
        wo_toks.append(b.dma("G", prs))
    for kc in range(8):
        b.wait("V", wo_toks[kc // 4])
        tw = b.inc("V", nc.vector.reduce_sum(sb["wocsf"][:, kc:kc + 1],
                                             sb["woTb"][:, kc * D:(kc + 1) * D],
                                             mybir.AxisListType.X))
    b.wait("V", tw)
    twb = b.inc("V", nc.vector.tensor_copy(sb["wocsb"][:, :], sb["wocsf"][:, :]))

    bg = {}

    def emit_bmm2(k):
        b.waits("T", [bg[("rt2b", k)], bg.get("ps3_tok")])
        tb1 = b.inc("T", nc.tensor.matmul(ps[3][:, 0:TNB], sb["onesrow"][0:1, :],
                                          rlbs(0, TNB), start=True, stop=True))
        b.waits("A", [tb1, bg.get("bc2_read")])
        e1 = b.inc("A", nc.scalar.activation(sb["bc2t"][:, 0:TNB], ps[3][:, 0:TNB], AF.Copy))
        b.waits("T", [e1])
        tb2 = b.inc("T", nc.tensor.matmul(ps[3][:, 0:TNB], sb["onesrow"][0:1, :],
                                          rlbs(1, TNB), start=True, stop=True))
        b.waits("A", [tb2, bg.get("bc2_read")])
        e2 = b.inc("A", nc.scalar.activation(sb["bc2t"][:, TNB:2 * TNB], ps[3][:, 0:TNB], AF.Copy))
        bg["ps3_tok"] = e2
        bg["rlb_used"] = tb2
        bg["ebc2"] = e2

    def emit_zsweep(k):
        t0 = k * TNB
        zof = (k % 2) * NCH * TNB
        b.waits("T", [twb, bg.get("sz_read")])
        for kc in range(8):
            ins = nc.tensor.matmul(ps[5][0:1, 0:TNB], sb["wocsb"][:, kc:kc + 1],
                                   sb["oan"][:, kc * T + t0: kc * T + t0 + TNB],
                                   start=(kc == 0), stop=(kc == 7))
        bg["tsz"] = b.inc("T", ins)
        pending_q2 = [None]

        def emit_q2(p):
            oc, tzq = p
            b.waits("T", [tzq] + ([bg.get("sz_read")] if oc == 0 else []))
            ins = nc.tensor.matmul(ps[4][0:1, 0:TNB], sb["onescol"][:, :],
                                   sb["zsq"][:, (oc % 2) * TNB:(oc % 2 + 1) * TNB],
                                   start=(oc == 0), stop=(oc == NCH - 1))
            return b.inc("T", ins)

        for oc in range(NCH):
            bank = ps[oc % 3]
            b.wait("T", bg.get(("pz", oc % 3)))
            for kc in range(8):
                ins = nc.tensor.matmul(
                    bank[:, 0:TNB],
                    sb["woTb"][:, kc * D + oc * 128: kc * D + (oc + 1) * 128],
                    sb["oan"][:, kc * T + t0: kc * T + t0 + TNB],
                    start=(kc == 0), stop=(kc == 7))
            tmm = b.inc("T", ins)
            if pending_q2[0] is not None:
                bg[("zsq_read", pending_q2[0][0] % 2)] = emit_q2(pending_q2[0])
            b.waits("A", [tmm, bg.get(("zt", k % 2, oc))])
            tze = b.inc("A", nc.scalar.activation(sb["ztile"][:, zof + oc * TNB:zof + (oc + 1) * TNB],
                                                  bank[:, 0:TNB], AF.Copy))
            b.waits("A", [bg.get(("zsq_read", oc % 2))])
            tzq = b.inc("A", nc.scalar.activation(sb["zsq"][:, (oc % 2) * TNB:(oc % 2 + 1) * TNB],
                                                  bank[:, 0:TNB], AF.Square))
            bg[("pz", oc % 3)] = tzq
            pending_q2[0] = (oc, tzq)
            bg[("ze", k % 2, oc)] = tze
        bg["tq2"] = emit_q2(pending_q2[0])
        bg[("zsq_read", pending_q2[0][0] % 2)] = bg["tq2"]

    def emit_apply(k):
        t0 = k * TNB
        zof = (k % 2) * NCH * TNB
        ebc2 = bg["ebc2"]
        xtoks = {}
        def x_load(oc):
            xtoks[oc] = b.dma("S", [(sb["xc"][:, (oc % 3) * TNB:(oc % 3 + 1) * TNB],
                                     dr["x_in"][128 * oc:128 * (oc + 1), t0:t0 + TNB])],
                              deps=[bg.get(("xc", oc % 3))])
        for oc in range(3):
            x_load(oc)
        pending_x2 = [None]

        def emit_x2(p):
            oc, txq = p
            b.waits("T", [txq])
            ins = nc.tensor.matmul(ps[6][0:1, 0:TNB], sb["onescol"][:, :],
                                   sb["x1q"][:, (oc % 2) * TNB:(oc % 2 + 1) * TNB],
                                   start=(oc == 0), stop=(oc == NCH - 1))
            return b.inc("T", ins)

        for oc in range(NCH):
            b.waits("V", [ebc2, bg.get(("ze", k % 2, oc)), bg.get("za_free")])
            tu = b.inc("V", nc.vector.tensor_mul(sb["za"][:, :],
                                                 sb["ztile"][:, zof + oc * TNB:zof + (oc + 1) * TNB],
                                                 sb["bc2t"][:, 0:TNB]))
            bg[("zt", k % 2, oc)] = tu
            b.waits("V", [tu])
            tu2 = b.inc("V", nc.vector.tensor_sub(sb["za"][:, :], sb["za"][:, :],
                                                  sb["bc2t"][:, TNB:2 * TNB]))
            x1sl = oc % 2
            b.waits("V", [tu2, xtoks[oc], bg.get(("x1b", x1sl)), bg.get(("x1ba", x1sl))])
            tx1 = b.inc("V", nc.vector.tensor_add(sb["x1b"][:, x1sl * TNB:(x1sl + 1) * TNB],
                                                  sb["za"][:, :],
                                                  sb["xc"][:, (oc % 3) * TNB:(oc % 3 + 1) * TNB]))
            bg["za_free"] = tx1
            bg[("xc", oc % 3)] = tx1
            if oc + 3 < NCH:
                x_load(oc + 3)
            tws = b.dma("S", [(dr["x1s"][128 * oc:128 * (oc + 1), t0:t0 + TNB],
                               sb["x1b"][:, x1sl * TNB:(x1sl + 1) * TNB])], deps=[tx1])
            bg[("x1b", x1sl)] = tws
            b.waits("A", [tx1, bg.get(("x1q_read", oc % 2))])
            txq = b.inc("A", nc.scalar.activation(sb["x1q"][:, (oc % 2) * TNB:(oc % 2 + 1) * TNB],
                                                  sb["x1b"][:, x1sl * TNB:(x1sl + 1) * TNB],
                                                  AF.Square))
            bg[("x1ba", x1sl)] = txq
            if pending_x2[0] is not None:
                bg[("x1q_read", pending_x2[0][0] % 2)] = emit_x2(pending_x2[0])
            pending_x2[0] = (oc, txq)
        tx2 = emit_x2(pending_x2[0])
        bg[("x1q_read", pending_x2[0][0] % 2)] = tx2
        bg[("tx2", k)] = tx2
        bg["bc2_read"] = tx1

    def emit_ln2(k):
        b.waits("A", [bg["tsz"], bg["tq2"]])
        b.waits("V", [bg["tsz"], bg["tq2"]])
        rt2b = ln_rows(TNB, ps[5][0:1, 0:TNB], ps[4][0:1, 0:TNB],
                       sfs(6, TNB), rlbs(0, TNB), rlbs(1, TNB),
                       dep=bg.get("rows_dep"), rbdep=bg.get("rlb_used"))
        bg["rows_dep"] = rt2b
        bg[("rt2b", k)] = rt2b
        bg["sz_read"] = rt2b

    def emit_ln3(k):
        t0 = k * TNB
        tx2 = bg[("tx2", k)]
        b.waits("A", [tx2])
        b.waits("V", [tx2])
        rt3 = ln_rows(TNB, None, ps[6][0:1, 0:TNB],
                      m1row(t0, TNB), r3row(t0, TNB), c3row(t0, TNB),
                      dep=bg.get("rows_dep"))
        bg["rows_dep"] = rt3

    for k in range(ntb):
        if k >= 1:
            emit_bmm2(k - 1)
        emit_zsweep(k)
        if k >= 1:
            emit_apply(k - 1)
        emit_ln2(k)
        if k >= 1:
            emit_ln3(k - 1)
    emit_bmm2(ntb - 1)
    emit_apply(ntb - 1)
    emit_ln3(ntb - 1)
    b.drain_dma()
    b.barrier()
    es_p2bt.close()
    es_oan.close()
    es_p2bl.close()
    if upto == "p2b":
        b.drain_dma()
        b.barrier()
        es.close()
        return nc

    TN3 = min(256, T)
    nt3 = T // TN3
    es_p3ps = ExitStack()
    ps7 = es_p3ps.enter_context(nc.psum_tensor("pb7", [128, 512], F32))
    es_p3l = ExitStack()
    sb["upb"] = es_p3l.enter_context(nc.sbuf_tensor("upb", [128, 32 * 1024], BF16))
    sb["downb"] = es_p3l.enter_context(nc.sbuf_tensor("downb", [128, 128 * 256], BF16))
    es_p3r = ExitStack()
    for nm, shp, dt_ in (("x1r", [128, 6 * TN3], F32), ("resr", [128, 4 * TN3], F32),
                         ("ftile", [128, NCH * TN3], BF16),
                         ("hidb", [128, 2 * 8 * TN3], BF16), ("yh", [128, 2 * 2 * TN3], BF16),
                         ("ysc", [128, TN3], F32), ("rsc", [128, TN3], F32),
                         ("fsq", [128, 4 * TN3], BF16),
                         ("bc3t", [128, 2 * TN3], F32), ("bc4t", [128, 2 * TN3], F32),
                         ("outb", [128, 2 * TN3], F32)):
        sb[nm] = es_p3r.enter_context(nc.sbuf_tensor(nm, shp, dt_, side="right"))
    up_toks = []
    for h4 in range(4):
        prs = []
        for h in range(4 * h4, 4 * h4 + 4):
            for kc in range(2):
                prs.append((sb["upb"][:, (2 * h + kc) * 1024:(2 * h + kc + 1) * 1024],
                            dr["up_w"][h, 128 * kc:128 * (kc + 1), :]))
        up_toks.append(b.dma("G", prs))
    dn_toks = []
    for h4 in range(4):
        prs = []
        for h in range(4 * h4, 4 * h4 + 4):
            for mc in range(8):
                prs.append((sb["downb"][:, (8 * h + mc) * 256:(8 * h + mc) * 256 + 256],
                            dr["down_w"][h, 128 * mc:128 * (mc + 1), :]))
        dn_toks.append(b.dma("G", prs))

    fg = {}
    out_toks = []
    res_pend = [None]

    def make_residual(ti_, tevac4_):
        t0r = ti_ * TN3
        rr = {"tok": {}, "cons": fg.get("rr_cons", {})}
        def r_load(li):
            sl = li % 4
            rr["tok"][li] = b.dma("S", [(sb["resr"][:, sl * TN3:(sl + 1) * TN3],
                                         dr["x1s"][128 * li:128 * (li + 1), t0r:t0r + TN3])],
                                  deps=[d for d in (rr["cons"].get(sl),) if d])
        for li in range(4):
            r_load(li)
        fg["rr_cons"] = rr["cons"]
        def step(hs):
            tu2 = None
            for oc in (2 * hs, 2 * hs + 1):
                osl2 = oc % 2
                b.waits("V", [tevac4_, fg[("fe", oc // 2)], fg.get("rscf"),
                              fg.get(("outb", osl2))])
                tu = b.inc("V", nc.vector.tensor_mul(sb["rsc"][:, :],
                                                     sb["ftile"][:, oc * TN3:(oc + 1) * TN3],
                                                     sb["bc4t"][:, 0:TN3]))
                b.waits("V", [tu])
                tu2 = b.inc("V", nc.vector.tensor_sub(sb["rsc"][:, :], sb["rsc"][:, :],
                                                      sb["bc4t"][:, TN3:2 * TN3]))
                b.waits("V", [tu2, rr["tok"][oc]])
                to = b.inc("V", nc.vector.tensor_add(
                    sb["outb"][:, osl2 * TN3:(osl2 + 1) * TN3], sb["rsc"][:, :],
                    sb["resr"][:, (oc % 4) * TN3:(oc % 4 + 1) * TN3]))
                fg[("ftg", oc // 2)] = tu
                fg["rscf"] = to
                rr["cons"][oc % 4] = to
                if oc + 4 < NCH:
                    r_load(oc + 4)
                tso = b.dma("S", [(dr["y_out"][128 * oc:128 * (oc + 1), t0r:t0r + TN3],
                                   sb["outb"][:, osl2 * TN3:(osl2 + 1) * TN3])], deps=[to])
                fg[("outb", osl2)] = tso
                out_toks.append(tso)
            if hs == NH - 1:
                fg["bc4_read"] = tu2
        return step

    for ti in range(nt3):
        t0 = ti * TN3
        xr = {"tok": {}, "cons": fg.get("xr_cons", {})}
        def x1_load(li):
            sl = li % 6
            xr["tok"][li] = b.dma("S", [(sb["x1r"][:, sl * TN3:(sl + 1) * TN3],
                                         dr["x1s"][128 * li:128 * (li + 1), t0:t0 + TN3])],
                                  deps=[d for d in (xr["cons"].get(sl),) if d])
        def x1_slot(li):
            return sb["x1r"][:, (li % 6) * TN3:(li % 6 + 1) * TN3]
        def x1_done(li, tok):
            xr["cons"][li % 6] = tok
            if li + 6 < NCH:
                x1_load(li + 6)
        for li in range(6):
            x1_load(li)
        fg["xr_cons"] = xr["cons"]
        b.waits("T", [fg.get(("hid", 1, 0)), fg.get("bc3_read")])
        tb3 = b.inc("T", nc.tensor.matmul(ps[2][:, 0:TN3], sb["onesrow"][0:1, :],
                                          r3row(t0, TN3), start=True, stop=False))
        tb3 = b.inc("T", nc.tensor.matmul(ps[2][:, TN3:2 * TN3], sb["onesrow"][0:1, :],
                                          c3row(t0, TN3), start=False, stop=True))
        b.waits("V", [tb3, fg.get("bc3_read")])
        tevac3 = b.inc("V", nc.vector.tensor_copy(sb["bc3t"][:, :], ps[2][:, 0:2 * TN3]))

        pending_sum = [None]

        def emit_fsums(p):
            hs, tfe_, tfq_ = p
            extra = [fg.get("ps6_tok"), fg.get("ps7_tok")] if hs == 0 else []
            b.waits("T", [tfe_, tfq_] + extra)
            nc.tensor.matmul(ps[6][0:1, 0:2 * TN3], sb["onescol"][:, :],
                             sb["ftile"][:, 2 * hs * TN3:(2 * hs + 2) * TN3],
                             start=(hs == 0), stop=(hs == NH - 1))
            ins = nc.tensor.matmul(ps7[0:1, 0:2 * TN3], sb["onescol"][:, :],
                                   sb["fsq"][:, (hs % 2) * 2 * TN3:(hs % 2 + 1) * 2 * TN3],
                                   start=(hs == 0), stop=(hs == NH - 1))
            t = b.inc("T", ins)
            fg[("fsq_read", hs % 2)] = t
            fg[("fsum", hs)] = t
            return t

        def emit_down(hd):
            pf = ps[4 + hd % 2]
            b.waits("T", [dn_toks[hd // 4]] + (fg.get(("psFr", hd % 2)) or []))
            first = True
            ins = None
            for dc in range(2):
                for mc in range(8):
                    b.wait("T", fg[("gel", hd % 2, mc // 2)])
                    ins = nc.tensor.matmul(
                        pf[:, dc * TN3:(dc + 1) * TN3],
                        sb["downb"][:, (8 * hd + mc) * 256 + dc * 128:(8 * hd + mc) * 256 + (dc + 1) * 128],
                        sb["hidb"][:, (hd % 2) * 8 * TN3 + mc * TN3:(hd % 2) * 8 * TN3 + (mc + 1) * TN3],
                        start=first, stop=(dc == 1 and mc == 7))
                    first = False
            tdn = b.inc("T", ins)
            fg[("hidbG", hd % 2)] = tdn
            b.waits("A", [tdn, fg.get(("ftg", hd)), fg.get(("fsum", hd))])
            tfe = b.inc("A", nc.scalar.activation(sb["ftile"][:, 2 * hd * TN3:(2 * hd + 2) * TN3],
                                                  pf[:, 0:2 * TN3], AF.Copy))
            b.waits("A", [fg.get(("fsq_read", hd % 2))])
            tfq = b.inc("A", nc.scalar.activation(
                sb["fsq"][:, (hd % 2) * 2 * TN3:(hd % 2 + 1) * 2 * TN3],
                pf[:, 0:2 * TN3], AF.Square))
            fg[("psFr", hd % 2)] = [tfe, tfq]
            fg[("fe", hd)] = tfe
            return (hd, tfe, tfq)

        for h in range(NH):
            ysl = h % 2
            ty = None
            for ci, c in enumerate((2 * h, 2 * h + 1)):
                b.waits("V", [tevac3, xr["tok"][c], fg.get("yscf"), fg.get(("yh", ysl))])
                tyu = b.inc("V", nc.vector.tensor_mul(sb["ysc"][:, :], x1_slot(c),
                                                      sb["bc3t"][:, 0:TN3]))
                b.waits("V", [tyu])
                ty = b.inc("V", nc.vector.tensor_sub(
                    sb["yh"][:, (2 * ysl + ci) * TN3:(2 * ysl + ci + 1) * TN3],
                    sb["ysc"][:, :], sb["bc3t"][:, TN3:2 * TN3]))
                fg["yscf"] = ty
                x1_done(c, tyu)
            hb = (ps[2 * (h % 2)], ps[2 * (h % 2) + 1])
            b.wait("T", up_toks[h // 4])
            b.wait("T", ty)
            if h == 1:
                b.wait("T", tevac3)
            tup = None
            for mc in range(8):
                bank = hb[mc // 2 % 2]
                half = mc % 2
                if half == 0:
                    b.wait("T", fg.get(("hid", h % 2, mc // 2 % 2)))
                for kc in range(2):
                    ins = nc.tensor.matmul(
                        bank[:, half * TN3:(half + 1) * TN3],
                        sb["upb"][:, (2 * h + kc) * 1024 + mc * 128:(2 * h + kc) * 1024 + (mc + 1) * 128],
                        sb["yh"][:, (2 * ysl + kc) * TN3:(2 * ysl + kc + 1) * TN3],
                        start=(half == 0 and kc == 0), stop=(half == 1 and kc == 1))
                if half == 1:
                    tup = b.inc("T", ins)
                    b.waits("A", [tup, fg.get(("hidbG", h % 2))])
                    tg_ = b.inc("A", nc.scalar.activation(
                        sb["hidb"][:, (h % 2) * 8 * TN3 + (mc - 1) * TN3:(h % 2) * 8 * TN3 + (mc + 1) * TN3],
                        bank[:, 0:2 * TN3], AF.Gelu))
                    fg[("hid", h % 2, mc // 2 % 2)] = tg_
                    fg[("gel", h % 2, mc // 2)] = tg_
            fg[("yh", ysl)] = tup
            if h > 0:
                pdn = emit_down(h - 1)
                if pending_sum[0] is not None:
                    emit_fsums(pending_sum[0])
                pending_sum[0] = pdn
            if res_pend[0] is not None:
                res_pend[0](h)
        pdn = emit_down(NH - 1)
        if pending_sum[0] is not None:
            emit_fsums(pending_sum[0])
        tS2 = emit_fsums(pdn)
        b.waits("A", [tS2, fg.get("rows_dep4")])
        SF4, SF5 = 4 * 512, 5 * 512
        th1 = b.inc("A", nc.scalar.activation(sb["smallf"][0:1, SF4:SF4 + 2 * TN3],
                                              ps[6][0:1, 0:2 * TN3], AF.Copy))
        th2 = b.inc("A", nc.scalar.activation(sb["smallf"][0:1, SF5:SF5 + 2 * TN3],
                                              ps7[0:1, 0:2 * TN3], AF.Copy))
        fg["ps7_tok"] = th2
        b.waits("V", [th1, th2, fg.get("rows_dep4")])
        th3 = b.inc("V", nc.vector.tensor_add(sb["smallf"][0:1, SF4:SF4 + TN3],
                                              sb["smallf"][0:1, SF4:SF4 + TN3],
                                              sb["smallf"][0:1, SF4 + TN3:SF4 + 2 * TN3]))
        th4 = b.inc("V", nc.vector.tensor_add(sb["smallf"][0:1, SF5:SF5 + TN3],
                                              sb["smallf"][0:1, SF5:SF5 + TN3],
                                              sb["smallf"][0:1, SF5 + TN3:SF5 + 2 * TN3]))
        b.waits("A", [th3, th4])
        b.waits("V", [th3, th4])
        rt4 = ln_rows(TN3, sb["smallf"][0:1, SF4:SF4 + TN3], sb["smallf"][0:1, SF5:SF5 + TN3],
                      sfs(6, TN3), rlbs(0, TN3), rlbs(1, TN3),
                      dep=fg.get("rows_dep4"), rbdep=fg.get("rlb_used"))
        fg["rows_dep4"] = rt4
        b.waits("T", [rt4, th1])
        tb4 = b.inc("T", nc.tensor.matmul(ps[6][:, 0:TN3], sb["onesrow"][0:1, :],
                                          rlbs(0, TN3), start=True, stop=False))
        tb4 = b.inc("T", nc.tensor.matmul(ps[6][:, TN3:2 * TN3], sb["onesrow"][0:1, :],
                                          rlbs(1, TN3), start=False, stop=True))
        fg["rlb_used"] = tb4
        b.waits("V", [tb4, fg.get("bc4_read")])
        tevac4 = b.inc("V", nc.vector.tensor_copy(sb["bc4t"][:, :], ps[6][:, 0:2 * TN3]))
        fg["ps6_tok"] = tevac4
        fg["bc3_read"] = ty
        res_pend[0] = make_residual(ti, tevac4)
    for hs in range(NH):
        res_pend[0](hs)
    b.drain_dma()
    b.barrier()
    es_p3r.close()
    es_p3l.close()
    es_p3ps.close()
    es.close()
    return nc


def kernel(**inputs):
    in_maps, perm, iperm, B, T = host_stage(inputs)
    nc = build_program(T)
    res = run_bass_kernel_spmd(nc, in_maps, core_ids=list(range(B)))
    return assemble_output(res.results, perm, B, T)


# version 15
# speedup vs baseline: 1.4981x; 1.0168x over previous
import math
from contextlib import ExitStack

import numpy as np
import concourse.bass as bass
import concourse.mybir as mybir
from concourse.bass_utils import run_bass_kernel_spmd

F32 = mybir.dt.float32
BF16 = mybir.dt.bfloat16
I32 = mybir.dt.int32
AF = mybir.ActivationFunctionType
ALU = mybir.AluOpType

D = 4096
NH = 16
NKV = 4
HD = 64
LAT = NH * HD
KVD = NKV * HD
QKV = LAT + 2 * KVD
NCH = D // 128
EPS = 1e-5
NEG_BIG = -3.0e38

def _split(x):
    a = float(np.float32(x))
    b = float(np.float32(x - a))
    return a, b

LN1E4_S1, LN1E4_S2 = _split(-math.log(10000.0) / 32.0)
PI1, _PI2 = _split(math.pi)
TWOPI1, TWOPI2 = _split(2.0 * math.pi)
INV2PI1 = float(np.float32(1.0 / TWOPI1))
HPI1, HPI2 = _split(math.pi / 2.0)


class Bld:

    ENG = ("T", "V", "A", "S", "G")

    def __init__(self, nc, es):
        self.nc = nc
        self.es = es
        self.eng = {"T": nc.tensor, "V": nc.vector, "A": nc.scalar,
                    "S": nc.sync, "G": nc.gpsimd}
        self.sems = {}
        self.counts = {}
        self.wmark = {e: {} for e in self.ENG}
        for e in self.ENG:
            self._mksem("p" + e)
        self.rings = {}
        for q, nslots in (("S", 8), ("G", 8)):
            slots = []
            for i in range(nslots):
                nm = f"dq{q}{i}"
                self._mksem(nm)
                slots.append(nm)
            self.rings[q] = {"slots": slots, "next": 0, "intok": {}}

    def _mksem(self, name):
        self.sems[name] = self.es.enter_context(self.nc.semaphore(name))
        self.counts[name] = 0

    def wait(self, e, tok):
        if tok is None:
            return
        sem, val = tok
        if self.wmark[e].get(sem, 0) >= val:
            return
        self.eng[e].wait_ge(self.sems[sem], val)
        self.wmark[e][sem] = val

    def waits(self, e, toks):
        for t in toks:
            self.wait(e, t)

    def tok(self, e):
        return ("p" + e, self.counts["p" + e])

    def inc(self, e, instr):
        sem = "p" + e
        self.counts[sem] += 1
        instr.then_inc(self.sems[sem], 1)
        return (sem, self.counts[sem])

    def dma(self, q, pairs, deps=()):
        ring = self.rings[q]
        slot = ring["slots"][ring["next"]]
        ring["next"] = (ring["next"] + 1) % len(ring["slots"])
        if self.counts[slot]:
            self.wait(q, (slot, self.counts[slot]))
        self.waits(q, deps)
        for p in pairs:
            kw = p[2] if len(p) > 2 else {}
            self.eng[q].dma_start(p[0], p[1], **kw).then_inc(self.sems[slot], 16)
            self.counts[slot] += 16
        return (slot, self.counts[slot])

    def barrier(self, engines="TVASG"):
        for e in engines:
            self.wait(e, ("p" + e, self.counts["p" + e]))
        toks = [self.inc(e, self.eng[e].nop()) for e in engines]
        for e in engines:
            for t in toks:
                self.wait(e, t)

    def drain_dma(self):
        for q, ring in self.rings.items():
            for slot in ring["slots"]:
                if self.counts[slot]:
                    self.wait(q, (slot, self.counts[slot]))


def host_stage(inp):
    perm = np.asarray(inp["L0_ffn_perm"])
    iperm = np.argsort(perm)
    x = np.asarray(inp["L0_attn_x"])
    B, T, _ = x.shape

    shared = {}
    shared["wqkvT"] = np.ascontiguousarray(np.asarray(inp["L0_attn_w_qkv"]).T[perm])
    shared["woT"] = np.ascontiguousarray(np.asarray(inp["L0_attn_w_o"])[perm].T)

    qc1 = np.asarray(inp["L0_attn_qc1_w"])
    qc1bd = np.zeros((3, 8, 128, 128), dtype=np.float32)
    for tap in range(3):
        for c in range(8):
            for gg in range(2):
                qc1bd[tap, c, 64 * gg:64 * gg + 64, 64 * gg:64 * gg + 64] = \
                    qc1[128 * c + 64 * gg:128 * c + 64 * gg + 64, :, tap].T
    shared["qc1bd"] = qc1bd
    shared["qc2T"] = np.ascontiguousarray(np.asarray(inp["L0_attn_qc2_w"]).transpose(2, 1, 0))
    kc1 = np.asarray(inp["L0_attn_kc1_w"])
    kc1bd = np.zeros((3, 2, 128, 128), dtype=np.float32)
    for tap in range(3):
        for c in range(2):
            for gg in range(2):
                kc1bd[tap, c, 64 * gg:64 * gg + 64, 64 * gg:64 * gg + 64] = \
                    kc1[128 * c + 64 * gg:128 * c + 64 * gg + 64, :, tap].T
    shared["kc1bd"] = kc1bd
    shared["kc2T"] = np.ascontiguousarray(np.asarray(inp["L0_attn_kc2_w"]).transpose(2, 1, 0))

    shared["qc1b"] = np.ascontiguousarray(np.asarray(inp["L0_attn_qc1_b"]).reshape(8, 128).T)
    shared["qc2b"] = np.ascontiguousarray(np.asarray(inp["L0_attn_qc2_b"]).reshape(8, 128).T)
    shared["kc1b"] = np.ascontiguousarray(np.asarray(inp["L0_attn_kc1_b"]).reshape(2, 128).T)
    shared["kc2b"] = np.ascontiguousarray(np.asarray(inp["L0_attn_kc2_b"]).reshape(2, 128).T)
    shared["ktemp"] = np.asarray(inp["L0_attn_key_temp"]).reshape(1, 1).astype(np.float32)
    shared["up_w"] = np.ascontiguousarray(np.asarray(inp["L0_ffn_up"]))
    shared["down_w"] = np.ascontiguousarray(np.asarray(inp["L0_ffn_down"]))

    shared["eye128"] = np.eye(128, dtype=np.float32)
    kk = np.arange(128)[:, None]
    qq = np.tile(np.arange(128)[None, :], (1, 4)).reshape(1, 512)
    shared["maskT"] = np.where(kk > qq, np.float32(NEG_BIG), np.float32(0.0)).astype(np.float32)
    dl = np.arange(128) % 64
    shared["msk16"] = (dl[:, None] == np.arange(64)[None, :]).astype(np.float32) / 32.0
    shared["msk4"] = (dl[:, None] == np.arange(64)[None, :]).astype(np.float32) / 8.0
    sel16 = np.zeros((8, 128, 16), dtype=np.float32)
    bc16 = np.zeros((8, 16, 128), dtype=np.float32)
    for c in range(8):
        for p in range(128):
            h = 2 * c + p // 64
            sel16[c, p, h] = 1.0
            bc16[c, h, p] = 1.0
    pvec = np.arange(128)
    ivec = np.zeros((128, 2), dtype=np.float32)
    ivec[:, 0] = (pvec % 64) // 2
    ivec[:, 1] = np.where(pvec % 2 == 0, -1.0, 1.0)
    shared["ivec"] = ivec
    shared["sel16"] = np.ascontiguousarray(sel16.transpose(1, 0, 2)).reshape(128, 8 * 16)
    shared["bc16"] = np.ascontiguousarray(bc16.transpose(1, 0, 2)).reshape(16, 8 * 128)
    sel4 = np.zeros((2, 128, 4), dtype=np.float32)
    bc4 = np.zeros((2, 4, 128), dtype=np.float32)
    for c in range(2):
        for p in range(128):
            h = 2 * c + p // 64
            sel4[c, p, h] = 1.0
            bc4[c, h, p] = 1.0
    shared["sel4"] = np.ascontiguousarray(sel4.transpose(1, 0, 2)).reshape(128, 2 * 4)
    bc4p = np.zeros((36, 2 * 128), dtype=np.float32)
    bc4p[32:36] = np.ascontiguousarray(bc4.transpose(1, 0, 2)).reshape(4, 2 * 128)
    shared["bc4"] = bc4p

    in_maps = []
    for b in range(B):
        m = dict(shared)
        m["x_in"] = np.ascontiguousarray(x[b].T[perm])
        in_maps.append(m)
    return in_maps, perm, iperm, B, T


def assemble_output(results, perm, B, T):
    out = np.empty((B, T, D), dtype=np.float32)
    for b in range(B):
        y = np.asarray(results[b]["y_out"])
        out[b][:, perm] = y.T
    return out


def build_program(T, upto="p3"):
    nc = bass.Bass("TRN2", target_bir_lowering=False)
    es = ExitStack()

    dr = {}
    dr["x_in"] = nc.dram_tensor("x_in", [D, T], F32, kind="ExternalInput")
    dr["wqkvT"] = nc.dram_tensor("wqkvT", [D, QKV], F32, kind="ExternalInput")
    dr["woT"] = nc.dram_tensor("woT", [LAT, D], F32, kind="ExternalInput")
    dr["qc1bd"] = nc.dram_tensor("qc1bd", [3, 8, 128, 128], F32, kind="ExternalInput")
    dr["qc2T"] = nc.dram_tensor("qc2T", [3, 1024, 1024], F32, kind="ExternalInput")
    dr["kc1bd"] = nc.dram_tensor("kc1bd", [3, 2, 128, 128], F32, kind="ExternalInput")
    dr["kc2T"] = nc.dram_tensor("kc2T", [3, 256, 256], F32, kind="ExternalInput")
    dr["qc1b"] = nc.dram_tensor("qc1b", [128, 8], F32, kind="ExternalInput")
    dr["qc2b"] = nc.dram_tensor("qc2b", [128, 8], F32, kind="ExternalInput")
    dr["kc1b"] = nc.dram_tensor("kc1b", [128, 2], F32, kind="ExternalInput")
    dr["kc2b"] = nc.dram_tensor("kc2b", [128, 2], F32, kind="ExternalInput")
    dr["ktemp"] = nc.dram_tensor("ktemp", [1, 1], F32, kind="ExternalInput")
    dr["up_w"] = nc.dram_tensor("up_w", [16, 256, 1024], F32, kind="ExternalInput")
    dr["down_w"] = nc.dram_tensor("down_w", [16, 1024, 256], F32, kind="ExternalInput")
    dr["eye128"] = nc.dram_tensor("eye128", [128, 128], F32, kind="ExternalInput")
    dr["maskT"] = nc.dram_tensor("maskT", [128, 512], F32, kind="ExternalInput")
    dr["msk16"] = nc.dram_tensor("msk16", [128, 64], F32, kind="ExternalInput")
    dr["msk4"] = nc.dram_tensor("msk4", [128, 64], F32, kind="ExternalInput")
    dr["ivec"] = nc.dram_tensor("ivec", [128, 2], F32, kind="ExternalInput")
    dr["sel16"] = nc.dram_tensor("sel16", [128, 8 * 16], F32, kind="ExternalInput")
    dr["bc16"] = nc.dram_tensor("bc16", [16, 8 * 128], F32, kind="ExternalInput")
    dr["sel4"] = nc.dram_tensor("sel4", [128, 2 * 4], F32, kind="ExternalInput")
    dr["bc4"] = nc.dram_tensor("bc4", [36, 2 * 128], F32, kind="ExternalInput")
    dr["y_out"] = nc.dram_tensor("y_out", [D, T], F32, kind="ExternalOutput")
    dr["x1s"] = nc.dram_tensor("x1s", [D, T], F32)
    dr["qrawD"] = nc.dram_tensor("qrawD", [128, 8 * (T + 2)], BF16)
    dr["krawD"] = nc.dram_tensor("krawD", [128, 2 * (T + 2)], BF16)
    dr["rowscr"] = nc.dram_tensor("rowscr", [1, max(T, 2048)], F32)

    b = Bld(nc, es)
    blk = es.enter_context(nc.Block())

    PT = T + 2
    sb = {}
    def alloc(name, shape, dtype):
        sb[name] = es.enter_context(nc.sbuf_tensor(name, shape, dtype))
        return sb[name]

    es_qm = ExitStack()
    sb["qmkm"] = es_qm.enter_context(nc.sbuf_tensor("qmkm", [128, T], F32, side="right"))
    es_raw = ExitStack()
    sb["vfm"] = es_raw.enter_context(nc.sbuf_tensor("vfm", [128, 2 * T], BF16, side="right"))
    sb["zrow"] = es_raw.enter_context(nc.sbuf_tensor("zrow", [128, 16], BF16, side="right"))
    alloc("rows", [128, T], BF16)
    def m1row(t0, TN):
        return sb["rows"][0:1, t0:t0 + TN]
    alloc("rlb", [128, 2 * 512], BF16)
    def rlbs(i, TN):
        return sb["rlb"][0:1, i * 512:i * 512 + TN]
    alloc("r3c3b", [128, 2 * T], BF16)
    def r3row(t0, TN):
        return sb["r3c3b"][0:1, t0:t0 + TN]
    def c3row(t0, TN):
        return sb["r3c3b"][0:1, T + t0:T + t0 + TN]
    alloc("onescol", [128, 1], BF16)
    alloc("onesrow", [128, 128], BF16)
    alloc("cc", [128, 8], F32)
    alloc("ktcol", [128, 1], F32)
    alloc("smallf", [128, 7 * 512 + 8], F32)
    def sfs(i, TN):
        return sb["smallf"][0:1, i * 512:i * 512 + TN]

    es_cst = ExitStack()
    for _n, _s, _d in (("eyeb", [128, 128], BF16), ("msk16b", [128, 64], BF16),
                       ("msk4b", [128, 64], BF16), ("sel16b", [128, 8 * 16], BF16),
                       ("bc16b", [16, 8 * 128], BF16), ("sel4b", [128, 2 * 4], BF16),
                       ("bc4b", [36, 2 * 128], BF16), ("biascols", [128, 20], F32),
                       ("negm", [128, 1], F32)):
        sb[_n] = es_cst.enter_context(nc.sbuf_tensor(_n, _s, _d))

    ps = []
    for i in range(7):
        ps.append(es.enter_context(nc.psum_tensor(f"pb{i}", [128, 512], F32)))
    es_psb = ExitStack()
    psb = es_psb.enter_context(nc.psum_tensor("pbT", [128, 1024], BF16))

    tok_c = b.dma("S", [
        (sb["biascols"][:, 0:8], dr["qc1b"][:, :]),
        (sb["biascols"][:, 8:16], dr["qc2b"][:, :]),
        (sb["biascols"][:, 16:18], dr["kc1b"][:, :]),
        (sb["biascols"][:, 18:20], dr["kc2b"][:, :]),
        (sb["smallf"][0:1, 7 * 512:7 * 512 + 1], dr["ktemp"][:, :]),
        (sb["cc"][:, 5:6], dr["ivec"][:, 0:1], dict(allow_slow_non_contiguous=True)),
        (sb["cc"][:, 6:7], dr["ivec"][:, 1:2], dict(allow_slow_non_contiguous=True)),
    ])
    tok_cb = b.dma("G", [
        (sb["eyeb"][:, :], dr["eye128"][:, :]),
        (sb["msk16b"][:, :], dr["msk16"][:, :]),
        (sb["msk4b"][:, :], dr["msk4"][:, :]),
        (sb["sel16b"][:, :], dr["sel16"][:, :]),
        (sb["bc16b"][:, :], dr["bc16"][:, :]),
        (sb["sel4b"][:, :], dr["sel4"][:, :]),
        (sb["bc4b"][:, :], dr["bc4"][:, :]),
    ])
    b.wait("V", tok_c)
    tv = b.inc("V", nc.vector.memset(sb["onescol"][:, :], 1.0))
    tv = b.inc("V", nc.vector.memset(sb["onesrow"][:, :], 1.0))
    tcc = b.inc("V", nc.vector.memset(sb["cc"][:, 0:1], EPS))
    tcc = b.inc("V", nc.vector.memset(sb["cc"][:, 1:2], HPI1))
    b.waits("A", [tcc])
    ta = b.inc("A", nc.scalar.activation(sb["smallf"][0:1, 7 * 512 + 1:7 * 512 + 2],
                                         sb["smallf"][0:1, 7 * 512:7 * 512 + 1],
                                         AF.Abs, scale=0.125))
    b.wait("A", ta)
    ta = b.inc("A", nc.scalar.activation(sb["smallf"][0:1, 7 * 512 + 2:7 * 512 + 3],
                                         sb["smallf"][0:1, 7 * 512 + 1:7 * 512 + 2],
                                         AF.Copy, scale=-1.0))
    tok_r = b.dma("S", [(dr["rowscr"][0:1, 0:2],
                         sb["smallf"][0:1, 7 * 512:7 * 512 + 3:2],
                         dict(allow_slow_non_contiguous=True))], deps=[ta])
    tok_r2 = b.dma("S", [
        (sb["negm"][:, 0:1], bass.AP(dr["rowscr"], 1, [[0, 128], [1, 1]])),
        (sb["ktcol"][32:36, 0:1], bass.AP(dr["rowscr"], 0, [[0, 4], [1, 1]])),
    ], deps=[tok_r])

    b.wait("V", tok_cb)
    tv0 = b.inc("V", nc.vector.memset(sb["zrow"][:, :], 0.0))
    tvv = b.inc("V", nc.vector.memset(sb["vfm"][:, T:T + 1], 0.0))
    tok_zp = b.dma("S", [
        (dr["qrawD"].ap().rearrange("p (c t) -> p c t", c=8)[:, :, 0:2],
         sb["zrow"][:, 0:16].rearrange("p (c t) -> p c t", c=8)),
        (dr["krawD"].ap().rearrange("p (c t) -> p c t", c=2)[:, :, 0:2],
         sb["zrow"][:, 0:4].rearrange("p (c t) -> p c t", c=2)),
    ], deps=[tv0])

    if upto == "p0":
        b.drain_dma()
        b.barrier()
        es_raw.close()
        es_qm.close()
        es_psb.close()
        es_cst.close()
        es.close()
        return nc
    TN1 = min(256, T)
    nt1 = T // TN1
    es1 = ExitStack()
    def alloc1(name, shape, dtype):
        sb[name] = es1.enter_context(nc.sbuf_tensor(name, shape, dtype, side="right"))
        return sb[name]
    alloc1("wqkvb", [128, NCH * QKV], BF16)
    alloc1("x32", [128, (NCH // 4) * TN1], F32)
    alloc1("xb", [128, 2 * NCH * TN1], BF16)
    alloc1("xsq", [128, 4 * TN1], BF16)
    alloc1("r1s", [128, 2 * TN1], F32)
    alloc1("uevac", [128, TN1], F32)
    alloc1("stg", [128, 3 * TN1], BF16)
    alloc1("wsumb", [1, QKV], BF16)

    wq_toks = []
    for g in range(4):
        prs = []
        for kc in range(8 * g, 8 * g + 8):
            prs.append((sb["wqkvb"][:, kc * QKV:(kc + 1) * QKV],
                        dr["wqkvT"][128 * kc:128 * kc + 128, :]))
        wq_toks.append(b.dma("G", prs))

    b.wait("T", tv)
    for seg in range(3):
        for kc in range(NCH):
            b.wait("T", wq_toks[kc // 8])
            ins = nc.tensor.matmul(ps[6][0:1, 0:512], sb["onescol"][:, :],
                                   sb["wqkvb"][:, kc * QKV + seg * 512: kc * QKV + (seg + 1) * 512],
                                   start=(kc == 0), stop=(kc == NCH - 1))
        tt = b.inc("T", ins)
        b.wait("V", tt)
        tv = b.inc("V", nc.vector.tensor_copy(sb["wsumb"][0:1, seg * 512:(seg + 1) * 512],
                                              ps[6][0:1, 0:512]))
        b.wait("T", tv)

    if upto == "p1a_w":
        b.drain_dma()
        b.barrier()
        es1.close()
        es_raw.close()
        es_qm.close()
        es_cst.close()
        es.close()
        return nc

    def ln_rows(TN, ps_sx, ps_sx2, m_ap, rb_r_ap, rb_c_ap, dep=None, rbdep=None):
        b.wait("A", dep)
        if ps_sx is not None:
            ta = b.inc("A", nc.scalar.activation(m_ap, ps_sx, AF.Copy, scale=1.0 / D))
            b.wait("A", ta)
        ta = b.inc("A", nc.scalar.activation(sfs(0, TN), ps_sx2, AF.Copy, scale=1.0 / D))
        ta2 = b.inc("A", nc.scalar.activation(sfs(1, TN), m_ap, AF.Square))
        b.waits("V", [ta, ta2, dep])
        tv = b.inc("V", nc.vector.tensor_sub(sfs(0, TN), sfs(0, TN), sfs(1, TN)))
        b.wait("A", tv)
        ta = b.inc("A", nc.scalar.activation(sfs(2, TN), sfs(0, TN), AF.Sqrt,
                                             bias=sb["cc"][0:1, 0:1]))
        b.wait("V", ta)
        tv = b.inc("V", nc.vector.reciprocal(sfs(2, TN), sfs(2, TN)))
        b.wait("V", tv)
        tv = b.inc("V", nc.vector.tensor_mul(sfs(3, TN), m_ap, sfs(2, TN)))
        b.waits("V", [tv, rbdep])
        tv = b.inc("V", nc.vector.tensor_copy(rb_r_ap, sfs(2, TN)))
        tv = b.inc("V", nc.vector.tensor_copy(rb_c_ap, sfs(3, TN)))
        return tv

    last_evac = {}

    for t in wq_toks:
        b.wait("T", t)

    HN = NCH // 4
    mm_tok = None
    xb_used = [None, None]
    cast_tok = [None, None, None, None]

    def load_cast_tile(ti):
        t0 = ti * TN1
        par = ti % 2
        for hf in range(4):
            prs = []
            for kc in range(HN * hf, HN * (hf + 1)):
                prs.append((sb["x32"][:, (kc - HN * hf) * TN1:(kc - HN * hf + 1) * TN1],
                            dr["x_in"][128 * kc:128 * kc + 128, t0:t0 + TN1]))
            x_tok = b.dma("S", prs, deps=[d for d in (cast_tok[(hf + 3) % 4],) if d])
            b.waits("V", [x_tok, xb_used[par]])
            cast_tok[hf] = b.inc("V", nc.vector.tensor_copy(
                sb["xb"][:, (par * NCH + hf * HN) * TN1:(par * NCH + (hf + 1) * HN) * TN1],
                sb["x32"][:, :]))
        return cast_tok[3]

    tile_cast = {0: load_cast_tile(0)}
    for ti in range(nt1):
        t0 = ti * TN1
        par = ti % 2
        xo = par * NCH * TN1
        if ti + 1 < nt1:
            tile_cast[ti + 1] = load_cast_tile(ti + 1)
        cast_now = tile_cast[ti]

        b.waits("A", [cast_now])
        if ti > 0:
            b.wait("T", last_evac["rows_done"])
        tsq = None
        for gq in range(8):
            if tsq is not None:
                b.wait("A", tsq)
            ta = b.inc("A", nc.scalar.activation(
                sb["xsq"][:, :], sb["xb"][:, xo + gq * 4 * TN1: xo + (gq + 1) * 4 * TN1],
                AF.Square))
            b.wait("T", ta)
            b.waits("T", [cast_now])
            for c4 in range(4):
                kc = gq * 4 + c4
                ins = nc.tensor.matmul(ps[5][0:1, 0:TN1], sb["onescol"][:, :],
                                       sb["xb"][:, xo + kc * TN1: xo + (kc + 1) * TN1],
                                       start=(kc == 0), stop=False)
                ins = nc.tensor.matmul(ps[5][0:1, TN1:2 * TN1], sb["onescol"][:, :],
                                       sb["xsq"][:, c4 * TN1:(c4 + 1) * TN1],
                                       start=False, stop=(kc == NCH - 1))
            tsq = b.inc("T", ins)
        stats_tok = tsq
        if upto == "p1a_s":
            break
        b.wait("A", stats_tok)
        b.wait("V", stats_tok)
        rows_tok = ln_rows(TN1, ps[5][0:1, 0:TN1], ps[5][0:1, TN1:2 * TN1],
                           m1row(t0, TN1), rlbs(0, TN1), rlbs(1, TN1),
                           dep=last_evac.get("rows_done"),
                           rbdep=last_evac.get("rlb_used"))
        last_evac["rows_done"] = rows_tok

        if upto == "p1a_r":
            break
        b.wait("T", rows_tok)
        tt = b.inc("T", nc.tensor.matmul(ps[4][:, 0:TN1], sb["onesrow"][0:1, :],
                                         rlbs(0, TN1), start=True, stop=True))
        b.wait("V", tt)
        if last_evac.get("r1s_free") is not None:
            b.wait("V", last_evac["r1s_free"])
        par = ti % 2
        r1s_tok = b.inc("V", nc.vector.tensor_copy(sb["r1s"][:, par * TN1:(par + 1) * TN1],
                                                   ps[4][:, 0:TN1]))
        b.wait("T", r1s_tok)

        if upto == "p1a_b":
            break
        for oc in range(12):
            pq = ps[oc % 2]
            pc = ps[2 + oc % 2]
            if ("pq", oc % 2) in last_evac:
                b.wait("T", last_evac[("pq", oc % 2)])
            for kc in range(NCH):
                ins = nc.tensor.matmul(pq[:, 0:TN1],
                                       sb["wqkvb"][:, kc * QKV + oc * 128: kc * QKV + (oc + 1) * 128],
                                       sb["xb"][:, xo + kc * TN1: xo + (kc + 1) * TN1],
                                       start=(kc == 0), stop=(kc == NCH - 1))
            mm_tok = b.inc("T", ins)
            xb_used[par] = mm_tok
            if ("pc", oc % 2) in last_evac:
                b.wait("T", last_evac[("pc", oc % 2)])
            tt2 = b.inc("T", nc.tensor.matmul(pc[:, 0:TN1],
                                              sb["wsumb"][0:1, oc * 128:(oc + 1) * 128],
                                              rlbs(1, TN1),
                                              start=True, stop=True))
            last_evac["rlb_used"] = tt2
            b.waits("V", [mm_tok, tt2, r1s_tok])
            if ("ue",) in last_evac:
                b.wait("V", last_evac[("ue",)])
            tv1 = b.inc("V", nc.vector.tensor_mul(
                sb["uevac"][:, 0:TN1],
                sb["r1s"][:, par * TN1:(par + 1) * TN1], pq[:, 0:TN1]))
            last_evac[("pq", oc % 2)] = tv1
            b.wait("V", tv1)
            if oc < 10:
                ss = oc % 3
                b.waits("V", [last_evac.get(("stgd", ss)), last_evac.get(("stgm", ss))])
                dst = sb["stg"][:, ss * TN1:(ss + 1) * TN1]
            elif oc == 10:
                dst = sb["vfm"][:, t0:t0 + TN1]
            else:
                hi = min(t0 + 1 + TN1, T)
                dst = sb["vfm"][:, T + t0 + 1: T + hi]
            if oc == 11:
                w = hi - (t0 + 1)
                tv2 = b.inc("V", nc.vector.tensor_sub(
                    dst, sb["uevac"][:, 0:w], pc[:, 0:w]))
            else:
                tv2 = b.inc("V", nc.vector.tensor_sub(
                    dst, sb["uevac"][:, 0:TN1], pc[:, 0:TN1]))
            last_evac[("pc", oc % 2)] = tv2
            last_evac[("ue",)] = tv2
            if oc < 10:
                if oc < 8:
                    ddst = dr["qrawD"][:, oc * PT + 2 + t0: oc * PT + 2 + t0 + TN1]
                else:
                    ddst = dr["krawD"][:, (oc - 8) * PT + 2 + t0: (oc - 8) * PT + 2 + t0 + TN1]
                last_evac[("stgd", ss)] = b.dma("S", [(ddst, dst)], deps=[tv2])
                b.wait("T", tv2)
                if oc < 8:
                    tmn = b.inc("T", nc.tensor.matmul(ps[4][0:64, 0:TN1], sb["msk16b"][:, :],
                                                      dst, start=(oc == 0), stop=(oc == 7)))
                else:
                    tmn = b.inc("T", nc.tensor.matmul(ps[6][0:64, 0:TN1], sb["msk4b"][:, :],
                                                      dst, start=(oc == 8), stop=(oc == 9)))
                last_evac[("stgm", ss)] = tmn
                if oc == 7:
                    tqm = tmn
                if oc == 9:
                    tkm = tmn
        if upto == "p1a_q":
            break
        b.waits("V", [tqm, tkm])
        tv = b.inc("V", nc.vector.tensor_copy(sb["qmkm"][0:64, t0:t0 + TN1], ps[4][0:64, 0:TN1]))
        b.wait("V", tv)
        tv = b.inc("V", nc.vector.tensor_copy(sb["qmkm"][64:128, t0:t0 + TN1], ps[6][0:64, 0:TN1]))
        b.wait("T", tv)
        last_evac["r1s_free"] = last_evac[("pq", 1)]

    b.drain_dma()
    b.barrier()
    es1.close()
    if upto.startswith("p1a"):
        b.drain_dma()
        b.barrier()
        es_psb.close()
        es_cst.close()
        es.close()
        return nc

    TC1 = min(512, T)
    TN2 = min(512, T)
    nt2 = T // TN2
    NJ = T // 128
    VW = 65
    es_qkv = ExitStack()
    sb["vtm"] = es_qkv.enter_context(nc.sbuf_tensor("vtm", [128, 4 * (NJ * VW)], BF16))
    sb["q4a"] = es_qkv.enter_context(nc.sbuf_tensor("q4a", [128, 4 * T], BF16))
    sb["q4b"] = es_qkv.enter_context(nc.sbuf_tensor("q4b", [128, 4 * T], BF16))
    sb["kfin"] = es_qkv.enter_context(nc.sbuf_tensor("kfin", [128, 2 * T], BF16))
    es_tabs = ExitStack()
    sb["tabs"] = es_tabs.enter_context(nc.sbuf_tensor("tabs", [128, 2 * T], BF16))
    es_tab = ExitStack()
    sb["tabscr"] = es_tab.enter_context(nc.sbuf_tensor("tabscr", [128, 4 * T], F32, side="right"))
    sb["kint"] = es_tab.enter_context(nc.sbuf_tensor("kint", [128, T], I32, side="right"))

    def ts_op(out, in0, s1, s2, op0, op1=ALU.bypass, deps=(), eng="V"):
        b.waits(eng, deps)
        return b.inc(eng, nc.vector.tensor_scalar(out, in0, s1, s2, op0, op1))

    A_TB, A_ANG, A_S = (slice(0, T), slice(T, 2 * T), slice(2 * T, 3 * T))
    SC2 = slice(3 * T, 4 * T)
    tg = b.inc("G", nc.gpsimd.iota(sb["tabscr"][0:1, 0:T], [[1, T]],
                                   channel_multiplier=0,
                                   allow_small_or_imprecise_dtypes=True))
    tok_tr = b.dma("S", [(dr["rowscr"][0:1, 0:T], sb["tabscr"][0:1, 0:T])], deps=[tg, tok_r2])
    tok_tb = b.dma("S", [(sb["tabscr"][:, A_TB], bass.AP(dr["rowscr"], 0, [[0, 128], [1, T]]))],
                   deps=[tok_tr])
    b.waits("A", [tok_c])
    t_e1 = b.inc("A", nc.scalar.activation(sb["cc"][:, 7:8], sb["cc"][:, 5:6], AF.Exp, scale=LN1E4_S1))
    t_e2 = b.inc("A", nc.scalar.activation(sb["cc"][:, 4:5], sb["cc"][:, 5:6], AF.Exp, scale=LN1E4_S2))
    b.waits("V", [t_e1, t_e2])
    t_inv = b.inc("V", nc.vector.tensor_mul(sb["cc"][:, 4:5], sb["cc"][:, 7:8], sb["cc"][:, 4:5]))
    b.wait("V", tok_tb)
    t_ang = ts_op(sb["tabscr"][:, A_ANG], sb["tabscr"][:, A_TB], sb["cc"][:, 4:5], None,
                  ALU.mult, deps=[t_inv])

    HPIC = float(np.float32(math.pi - PI1))
    def build_sintab(src_sl, dst_sl, shift, dep):
        W0, SC = A_TB, A_S
        dd = dep
        if shift:
            dd = ts_op(sb["tabscr"][:, SC], sb["tabscr"][:, src_sl], HPI1, HPI2,
                       ALU.add, ALU.add, deps=[dd])
            srcs = SC
        else:
            srcs = src_sl
        d0 = ts_op(sb["tabscr"][:, W0], sb["tabscr"][:, srcs], INV2PI1, None,
                   ALU.mult, deps=[dd])
        b.waits("V", [d0])
        d1 = b.inc("V", nc.vector.tensor_copy(sb["kint"][:, :], sb["tabscr"][:, W0]))
        b.waits("V", [d1])
        d2 = b.inc("V", nc.vector.tensor_copy(dst_sl, sb["kint"][:, :]))
        b.waits("V", [d2])
        d3 = b.inc("V", nc.vector.tensor_sub(sb["tabscr"][:, W0], sb["tabscr"][:, W0],
                                             dst_sl))
        d4 = ts_op(sb["tabscr"][:, SC2], sb["tabscr"][:, W0], 0.5, None, ALU.is_ge,
                   deps=[d3])
        b.waits("V", [d4])
        d5 = b.inc("V", nc.vector.tensor_add(dst_sl, dst_sl, sb["tabscr"][:, SC2]))
        d6 = ts_op(sb["tabscr"][:, SC2], sb["tabscr"][:, W0], -0.5, None, ALU.is_lt,
                   deps=[d5])
        b.waits("V", [d6])
        d7 = b.inc("V", nc.vector.tensor_sub(dst_sl, dst_sl, sb["tabscr"][:, SC2]))
        d8 = ts_op(sb["tabscr"][:, W0], sb["tabscr"][:, srcs], 0.5, None, ALU.mult,
                   deps=[d7])
        b.waits("V", [d8])
        d9 = b.inc("V", nc.vector.scalar_tensor_tensor(
            sb["tabscr"][:, W0], dst_sl, -PI1, sb["tabscr"][:, W0], ALU.mult, ALU.add))
        b.waits("V", [d9])
        da = b.inc("V", nc.vector.scalar_tensor_tensor(
            sb["tabscr"][:, W0], dst_sl, -HPIC, sb["tabscr"][:, W0], ALU.mult, ALU.add))
        b.waits("A", [da])
        db = b.inc("A", nc.scalar.activation(sb["tabscr"][:, SC2], sb["tabscr"][:, W0],
                                             AF.Abs))
        dc = b.inc("A", nc.scalar.activation(dst_sl, sb["tabscr"][:, W0], AF.Sin))
        dd2 = ts_op(sb["tabscr"][:, SC2], sb["tabscr"][:, SC2], HPI2, None,
                    ALU.subtract, deps=[db, dc], eng="V")
        b.waits("A", [dd2])
        de = b.inc("A", nc.scalar.activation(sb["tabscr"][:, SC2], sb["tabscr"][:, SC2],
                                             AF.Sin, scale=-1.0, bias=sb["cc"][:, 1:2]))
        b.waits("V", [dc, de])
        df = b.inc("V", nc.vector.scalar_tensor_tensor(
            dst_sl, dst_sl, 2.0, sb["tabscr"][:, SC2], ALU.mult, ALU.mult))
        return df

    t_sn = build_sintab(A_ANG, sb["tabs"][:, T:2 * T], False, t_ang)
    t_sn = ts_op(sb["tabs"][:, T:2 * T], sb["tabs"][:, T:2 * T], sb["cc"][:, 6:7], None,
                 ALU.mult, deps=[t_sn])
    t_cs = build_sintab(A_ANG, sb["tabs"][:, 0:T], True, t_sn)
    b.drain_dma()
    b.barrier()
    es_tab.close()
    if upto == "p1b_t":
        b.drain_dma()
        b.barrier()
        es_tabs.close()
        es_qkv.close()
        es_raw.close()
        es_qm.close()
        es_psb.close()
        es_cst.close()
        es.close()
        return nc

    es2 = ExitStack()
    sb["c1q"] = es2.enter_context(nc.sbuf_tensor("c1q", [128, 8 * PT], BF16))
    sb["c1k"] = es2.enter_context(nc.sbuf_tensor("c1k", [128, 2 * PT], BF16))
    es_c1 = ExitStack()
    sb["wc1q"] = es_c1.enter_context(nc.sbuf_tensor("wc1q", [128, 24 * 128], BF16, side="right"))
    sb["wc1k"] = es_c1.enter_context(nc.sbuf_tensor("wc1k", [128, 6 * 128], BF16, side="right"))

    tok_wc1 = b.dma("G", [
        (sb["wc1q"][:, :].rearrange("p (t c b) -> p t c b", t=3, c=8),
         dr["qc1bd"].ap().transpose([2, 0, 1, 3])),
        (sb["wc1k"][:, :].rearrange("p (t c b) -> p t c b", t=3, c=2),
         dr["kc1bd"].ap().transpose([2, 0, 1, 3]))])

    tzp = b.inc("V", nc.vector.memset(sb["c1q"].ap().rearrange("p (c t) -> p c t", c=8)[:, :, 0:2], 0.0))
    tzp = b.inc("V", nc.vector.memset(sb["c1k"].ap().rearrange("p (c t) -> p c t", c=2)[:, :, 0:2], 0.0))
    tvo = b.inc("V", nc.vector.memset(
        sb["vtm"].ap().rearrange("p (g j w) -> p g j w", g=4, j=NJ)[:, :, :, 64:65], 1.0))

    sb["c1wq"] = es_c1.enter_context(nc.sbuf_tensor("c1wq", [128, 2 * 8 * (TC1 + 2)], BF16, side="right"))
    sb["c1wk"] = es_c1.enter_context(nc.sbuf_tensor("c1wk", [128, 2 * 2 * (TC1 + 2)], BF16, side="right"))
    c1q_c = sb["c1q"].ap().rearrange("p (c t) -> p c t", c=8)
    c1k_c = sb["c1k"].ap().rearrange("p (c t) -> p c t", c=2)
    pv_guard = {}
    WQ = 8 * (TC1 + 2)
    WK = 2 * (TC1 + 2)
    for ti in range(T // TC1):
        t0 = ti * TC1
        wsl = ti % 2
        prs = [(sb["c1wq"][:, wsl * WQ + c * (TC1 + 2): wsl * WQ + (c + 1) * (TC1 + 2)],
                dr["qrawD"][:, c * PT + t0: c * PT + t0 + TC1 + 2]) for c in range(8)]
        prs += [(sb["c1wk"][:, wsl * WK + c * (TC1 + 2): wsl * WK + (c + 1) * (TC1 + 2)],
                 dr["krawD"][:, c * PT + t0: c * PT + t0 + TC1 + 2]) for c in range(2)]
        win_tok = b.dma("S", prs, deps=[tok_zp, pv_guard.get(("win", wsl))])
        for side, nch_s, wsb, wbuf, woff, dst_c, bias_off in (
                ("q", 8, "wc1q", "c1wq", wsl * WQ, c1q_c, 0),
                ("k", 2, "wc1k", "c1wk", wsl * WK, c1k_c, 16)):
            for c in range(nch_s):
                bank = ps[c % 2]
                if ("c1", c % 2) in pv_guard:
                    b.wait("T", pv_guard[("c1", c % 2)])
                b.wait("T", tok_wc1)
                b.wait("T", win_tok)
                for tap in range(3):
                    ins = nc.tensor.matmul(
                        bank[:, 0:TC1],
                        sb[wsb][:, (tap * nch_s + c) * 128:(tap * nch_s + c + 1) * 128],
                        sb[wbuf][:, woff + c * (TC1 + 2) + tap: woff + c * (TC1 + 2) + tap + TC1],
                        start=(tap == 0), stop=(tap == 2))
                tmm = b.inc("T", ins)
                b.waits("V", [tmm, tzp])
                ta = b.inc("V", nc.vector.tensor_scalar(
                    dst_c[:, c, 2 + t0: 2 + t0 + TC1], bank[:, 0:TC1],
                    sb["biascols"][:, bias_off + c: bias_off + c + 1], None, ALU.add))
                pv_guard[("c1", c % 2)] = ta
        pv_guard[("win", wsl)] = tmm
    for g in range(4):
        bpv = 64 * (g % 2)
        for j in range(NJ):
            if ("vt",) in pv_guard:
                b.wait("T", pv_guard[("vt",)])
            tt = b.inc("T", nc.tensor.transpose(
                psb[:, 0:64],
                sb["vfm"][bpv:bpv + 64, (g // 2) * T + 128 * j: (g // 2) * T + 128 * (j + 1)],
                sb["eyeb"][bpv:bpv + 64, bpv:bpv + 64]))
            b.wait("V", tt)
            tv = b.inc("V", nc.vector.tensor_copy(
                sb["vtm"][:, g * NJ * VW + j * VW: g * NJ * VW + j * VW + 64],
                psb[:, 0:64]))
            pv_guard[("vt",)] = tv
    b.drain_dma()
    b.barrier()
    es_c1.close()
    es_raw.close()
    if upto == "p1b_c1":
        b.barrier()
        es2.close()
        es_tabs.close()
        es_qkv.close()
        es_qm.close()
        es_psb.close()
        es_cst.close()
        es.close()
        return nc

    es3 = ExitStack()
    for nm, shp, dt_ in (("wc2q", [128, 24 * 1024], BF16), ("wc2k", [128, 6 * 256], BF16),
                         ("qpost", [128, 8 * TN2], BF16), ("kpost", [128, 2 * TN2], BF16),
                         ("sqb", [128, 2 * TN2], BF16), ("mm2b", [128, 2 * TN2], BF16),
                         ("rqf", [128, TN2], F32), ("rqb", [128, TN2], BF16),
                         ("rsw", [128, TN2], BF16), ("rt1", [128, TN2], BF16),
                         ("rt2", [128, TN2], BF16)):
        sb[nm] = es3.enter_context(nc.sbuf_tensor(nm, shp, dt_, side="right"))
    wq_toks2 = []
    for tap in range(3):
        prs = [(sb["wc2q"][:, (tap * 8 + kc) * 1024:(tap * 8 + kc + 1) * 1024],
                dr["qc2T"][tap, 128 * kc:128 * kc + 128, :]) for kc in range(8)]
        wq_toks2.append(b.dma("G", prs))
    tok_wc2k = b.dma("G", [(sb["wc2k"][:, (tap * 2 + kc) * 256:(tap * 2 + kc + 1) * 256],
                            dr["kc2T"][tap, 128 * kc:128 * kc + 128, :])
                           for tap in range(3) for kc in range(2)])

    PAIRSWAP = []
    for i in range(16):
        PAIRSWAP += [2 * i + 1, 2 * i]

    guard = {}
    SIDES = [("q", 8, "wc2q", c1q_c, 8, 8, slice(0, TN2), "qpost", ps[3], "sel16b", 16),
             ("k", 2, "wc2k", c1k_c, 2, 18, slice(TN2, 2 * TN2), "kpost", ps[4], "sel4b", 4)]
    for ti in range(nt2):
        t0 = ti * TN2
        mdeps = [guard.get("mm2b")]
        b.waits("V", [d for d in mdeps if d])
        tv = b.inc("V", nc.vector.tensor_copy(sb["mm2b"][0:64, 0:TN2], sb["qmkm"][64:128, t0:t0 + TN2]))
        tv = b.inc("V", nc.vector.tensor_copy(sb["mm2b"][64:128, 0:TN2], sb["qmkm"][64:128, t0:t0 + TN2]))
        tv = b.inc("V", nc.vector.tensor_copy(sb["mm2b"][0:64, TN2:2 * TN2], sb["qmkm"][0:64, t0:t0 + TN2]))
        t_mm2b = b.inc("V", nc.vector.tensor_copy(sb["mm2b"][64:128, TN2:2 * TN2], sb["qmkm"][0:64, t0:t0 + TN2]))

        pending_ssq = None

        def emit_ssq(p):
            side_i, oc, tsq = p
            _, nch_s, _, _, _, _, _, _, ps_ssq, selb, selw = SIDES[side_i]
            b.waits("T", [tsq, guard.get("ssq_read")])
            ins = nc.tensor.matmul(
                ps_ssq[0:selw, 0:TN2], sb[selb][:, oc * selw:(oc + 1) * selw],
                sb["sqb"][:, ((side_i * 8 + oc) % 2) * TN2:((side_i * 8 + oc) % 2 + 1) * TN2],
                start=(oc == 0), stop=(oc == nch_s - 1))
            return b.inc("T", ins)

        for side_i, (side, nch_s, wsb, src_c, nkc, bias_off, post_sl, dst_post,
                     ps_ssq, selb, selw) in enumerate(SIDES):
            for oc in range(nch_s):
                gi = side_i * 8 + oc
                bank = ps[gi % 3]
                if ("c2", gi % 3) in guard:
                    b.wait("T", guard[("c2", gi % 3)])
                for tok in (wq_toks2 if side == "q" else [tok_wc2k]):
                    b.wait("T", tok)
                first = True
                for tap in range(3):
                    for kc in range(nkc):
                        if side == "q":
                            wsl = sb[wsb][:, (tap * 8 + kc) * 1024 + oc * 128:
                                          (tap * 8 + kc) * 1024 + (oc + 1) * 128]
                        else:
                            wsl = sb[wsb][:, (tap * 2 + kc) * 256 + oc * 128:
                                          (tap * 2 + kc) * 256 + (oc + 1) * 128]
                        ins = nc.tensor.matmul(
                            bank[:, 0:TN2], wsl, src_c[:, kc, t0 + tap: t0 + tap + TN2],
                            start=first, stop=(tap == 2 and kc == nkc - 1))
                        first = False
                tmm = b.inc("T", ins)
                if pending_ssq is not None:
                    tprev = emit_ssq(pending_ssq)
                    guard[("sq", (pending_ssq[0] * 8 + pending_ssq[1]) % 2)] = tprev
                b.waits("V", [tmm, t_mm2b, guard.get(("rope", side, oc))])
                tpost = b.inc("V", nc.vector.scalar_tensor_tensor(
                    sb[dst_post][:, oc * TN2:(oc + 1) * TN2], bank[:, 0:TN2],
                    sb["biascols"][:, bias_off + oc: bias_off + oc + 1],
                    sb["mm2b"][:, post_sl], ALU.add, ALU.add))
                guard[("c2", gi % 3)] = tpost
                guard[("post", side, oc)] = tpost
                b.waits("A", [tpost, guard.get(("sq", gi % 2))])
                tsq = b.inc("A", nc.scalar.activation(
                    sb["sqb"][:, (gi % 2) * TN2:(gi % 2 + 1) * TN2],
                    sb[dst_post][:, oc * TN2:(oc + 1) * TN2], AF.Square))
                pending_ssq = (side_i, oc, tsq)
        tssq_last = emit_ssq(pending_ssq)
        guard[("ssq_done", "k")] = tssq_last
        pending_ssq = None

        b.waits("A", [tssq_last, guard.get("rq_used")])
        ta = b.inc("A", nc.scalar.activation(sb["rqf"][0:16, :], ps[3][0:16, 0:TN2], AF.Sqrt))
        ta2 = b.inc("A", nc.scalar.activation(sb["rqf"][32:36, :], ps[4][0:4, 0:TN2], AF.Sqrt))
        b.waits("V", [ta, ta2])
        tvq = b.inc("V", nc.vector.reciprocal(sb["rqf"][0:16, :], sb["rqf"][0:16, :]))
        tvk = b.inc("V", nc.vector.reciprocal(sb["rqf"][32:36, :], sb["rqf"][32:36, :]))
        b.waits("V", [tvq, tvk, tok_r2])
        tvk = b.inc("V", nc.vector.tensor_scalar(sb["rqf"][32:36, :], sb["rqf"][32:36, :],
                                                 sb["ktcol"][32:36, 0:1], None, ALU.mult))
        b.waits("V", [tvk, guard.get("rqb_used")])
        tvq = b.inc("V", nc.vector.tensor_copy(sb["rqb"][0:16, :], sb["rqf"][0:16, :]))
        tvk = b.inc("V", nc.vector.tensor_copy(sb["rqb"][32:36, :], sb["rqf"][32:36, :]))
        guard["ssq_read"] = tvk
        guard["rqb_ready"] = tvk

        for side, nch_s, dst_post in (("q", 8, "qpost"), ("k", 2, "kpost")):
            for oc in range(nch_s):
                post_ap = sb[dst_post][:, oc * TN2:(oc + 1) * TN2]
                bankR = ps[5 + oc % 2]
                if ("bR", oc % 2) in guard:
                    b.wait("T", guard[("bR", oc % 2)])
                b.wait("T", guard["rqb_ready"])
                if side == "q":
                    tbc = b.inc("T", nc.tensor.matmul(
                        bankR[:, 0:TN2], sb["bc16b"][:, oc * 128:(oc + 1) * 128],
                        sb["rqb"][0:16, :], start=True, stop=True))
                else:
                    tbc = b.inc("T", nc.tensor.matmul(
                        bankR[:, 0:TN2], sb["bc4b"][32:36, oc * 128:(oc + 1) * 128],
                        sb["rqb"][32:36, :], start=True, stop=True))
                deps = [guard.get(("rope", side, oc)), guard.get("rope_bufs")]
                b.waits("V", [d for d in deps if d] + [guard[("post", side, oc)]])
                s1 = b.inc("V", nc.vector.stream_shuffle(sb["rsw"][:, :], post_ap, PAIRSWAP))
                s2 = b.inc("V", nc.vector.tensor_mul(sb["rt1"][:, :], post_ap,
                                                     sb["tabs"][:, t0:t0 + TN2]))
                b.waits("V", [s1])
                s3 = b.inc("V", nc.vector.tensor_mul(sb["rt2"][:, :], sb["rsw"][:, :],
                                                     sb["tabs"][:, T + t0:T + t0 + TN2]))
                b.waits("V", [s2, s3])
                s4 = b.inc("V", nc.vector.tensor_add(sb["rt1"][:, :], sb["rt1"][:, :],
                                                     sb["rt2"][:, :]))
                b.waits("V", [s4, tbc])
                if side == "q":
                    nqt2 = TN2 // 128
                    for half in range(2):
                        h = 2 * oc + half
                        g, l = h // 4, h % 4
                        q4t = sb["q4a"] if g < 2 else sb["q4b"]
                        bpq = 64 * (g % 2)
                        qt0 = t0 // 128
                        dst = q4t.ap()[bpq:bpq + 64].rearrange(
                            "p (qt l c) -> p qt l c", l=4, c=128)[:, qt0:qt0 + nqt2, l, :]
                        sv = b.inc("V", nc.vector.tensor_mul(
                            dst,
                            sb["rt1"][64 * half:64 * half + 64, :].rearrange(
                                "p (qt c) -> p qt c", c=128),
                            bankR[64 * half:64 * half + 64, 0:TN2].rearrange(
                                "p (qt c) -> p qt c", c=128)))
                else:
                    sv = b.inc("V", nc.vector.tensor_mul(
                        sb["kfin"][:, oc * T + t0: oc * T + t0 + TN2],
                        sb["rt1"][:, :], bankR[:, 0:TN2]))
                guard[("rope", side, oc)] = sv
                guard[("bR", oc % 2)] = sv
                guard["rope_bufs"] = sv
                guard["rqb_used"] = tbc
        guard["mm2b"] = tpost
    b.drain_dma()
    b.barrier()
    es3.close()
    es_qm.close()
    es2.close()
    es_tabs.close()
    if upto == "p1b":
        b.drain_dma()
        b.barrier()
        es_psb.close()
        es_qkv.close()
        es_cst.close()
        es.close()
        return nc
    es_psb.close()

    es_oan = ExitStack()
    sb["oan"] = es_oan.enter_context(nc.sbuf_tensor("oan", [128, 8 * T], BF16, side="right"))
    es_p2a = ExitStack()
    sb["pt"] = es_p2a.enter_context(nc.sbuf_tensor("pt", [128, 3 * 512], BF16, side="right"))
    sb["o4"] = es_p2a.enter_context(nc.sbuf_tensor("o4", [128, 2 * 512], F32, side="right"))
    sb["lrb"] = es_p2a.enter_context(nc.sbuf_tensor("lrb", [128, 2 * 512], BF16, side="right"))
    sb["maskTs"] = es_p2a.enter_context(nc.sbuf_tensor("maskTs", [128, 512], F32, side="right"))
    tok_mT = b.dma("S", [(sb["maskTs"][:, :], dr["maskT"][:, :])])
    b.wait("V", tok_mT)
    NQT = T // 128
    ag = {}
    b.waits("A", [tok_r2])

    def p2a_scores(k, i, g):
        gp = 64 * (g % 2)
        q4t = sb["q4a"] if g < 2 else sb["q4b"]
        psO = ps[3 + k % 2]
        b.wait("T", ag.get(("psO", k % 2)))
        texp = {}
        tpv = None

        def emit_s(j):
            psS = ps[j % 3]
            b.wait("T", ag.get(("psS", j % 3)))
            tmm = b.inc("T", nc.tensor.matmul(
                psS[:, 0:512],
                sb["kfin"][gp:gp + 64, (g // 2) * T + 128 * j: (g // 2) * T + 128 * (j + 1)],
                q4t[gp:gp + 64, 512 * i: 512 * (i + 1)],
                start=True, stop=True))
            expdep = tmm
            if j == i:
                b.waits("V", [tmm])
                expdep = b.inc("V", nc.vector.tensor_add(psS[:, 0:512], psS[:, 0:512],
                                                         sb["maskTs"][:, :]))
            b.waits("A", [expdep, ag.get(("pt", j % 3))])
            texp[j] = b.inc("A", nc.scalar.activation(
                sb["pt"][:, (j % 3) * 512:(j % 3 + 1) * 512], psS[:, 0:512],
                AF.Exp, scale=0.125, bias=sb["negm"][:, 0:1]))
            ag[("psS", j % 3)] = texp[j]

        def emit_pv(j):
            nonlocal tpv
            b.waits("T", [texp[j]])
            tpv = b.inc("T", nc.tensor.matmul(
                psO[0:65, 0:512],
                sb["vtm"][:, g * NJ * VW + j * VW: g * NJ * VW + (j + 1) * VW],
                sb["pt"][:, (j % 3) * 512:(j % 3 + 1) * 512],
                start=(j == 0), stop=(j == i)))
            ag[("pt", j % 3)] = tpv

        for j in range(i + 1):
            emit_s(j)
            if j >= 2:
                emit_pv(j - 2)
        for j in range(max(0, i - 1), i + 1):
            emit_pv(j)
        return tpv

    def p2a_norm(k, i, g, tpv):
        psO = ps[3 + k % 2]
        osl = (k % 2) * 512
        b.waits("V", [tpv, ag.get(("o4", k % 2))])
        tev = b.inc("V", nc.vector.tensor_copy(sb["o4"][0:65, osl:osl + 512],
                                               psO[0:65, 0:512]))
        ag[("psO", k % 2)] = tev
        b.waits("V", [tev])
        trc = b.inc("V", nc.vector.reciprocal(sb["o4"][64:65, osl:osl + 512],
                                              sb["o4"][64:65, osl:osl + 512]))
        b.waits("V", [trc, ag.get(("lrb", k % 2))])
        tcb = b.inc("V", nc.vector.tensor_copy(sb["lrb"][64:65, osl:osl + 512],
                                               sb["o4"][64:65, osl:osl + 512]))
        b.waits("T", [tcb, ag.get(("psR", k % 2))])
        tbr = b.inc("T", nc.tensor.matmul(ps[5 + k % 2][0:64, 0:512],
                                          sb["onesrow"][64:65, 0:64],
                                          sb["lrb"][64:65, osl:osl + 512],
                                          start=True, stop=True))
        b.waits("V", [tbr])
        for l in range(4):
            tm = b.inc("V", nc.vector.tensor_mul(
                sb["oan"][64 * (l % 2):64 * (l % 2) + 64,
                          (2 * g + l // 2) * T + 128 * i: (2 * g + l // 2) * T + 128 * (i + 1)],
                sb["o4"][0:64, osl + 128 * l: osl + 128 * (l + 1)],
                ps[5 + k % 2][0:64, 128 * l: 128 * (l + 1)]))
        ag[("psR", k % 2)] = tm
        ag[("o4", k % 2)] = tm
        ag[("lrb", k % 2)] = tbr

    pending = None
    k = 0
    for i in range(NQT):
        for g in range(4):
            tpv = p2a_scores(k, i, g)
            if pending is not None:
                p2a_norm(*pending)
            pending = (k, i, g, tpv)
            k += 1
    p2a_norm(*pending)
    b.drain_dma()
    b.barrier()
    es_p2a.close()
    es_qkv.close()
    es_cst.close()
    if upto == "p2a":
        b.drain_dma()
        b.barrier()
        es_oan.close()
        es.close()
        return nc

    TNB = min(512, T)
    ntb = T // TNB
    es_p2bl = ExitStack()
    sb["woTb"] = es_p2bl.enter_context(nc.sbuf_tensor("woTb", [128, 8 * D], BF16))
    sb["ztile"] = es_p2bl.enter_context(nc.sbuf_tensor("ztile", [128, 2 * NCH * TNB], BF16))
    es_p2bt = ExitStack()
    for nm, shp, dt_ in (("xc", [128, 3 * TNB], F32), ("x1b", [128, 2 * TNB], F32),
                         ("zsq", [128, 2 * TNB], BF16), ("x1q", [128, 2 * TNB], BF16),
                         ("za", [128, TNB], F32), ("bc2t", [128, 2 * TNB], BF16),
                         ("wocsb", [128, 8], BF16), ("wocsf", [128, 8], F32)):
        sb[nm] = es_p2bt.enter_context(nc.sbuf_tensor(nm, shp, dt_, side="right"))
    wo_toks = []
    for g4 in range(2):
        prs = [(sb["woTb"][:, kc * D:(kc + 1) * D], dr["woT"][128 * kc:128 * kc + 128, :])
               for kc in range(4 * g4, 4 * g4 + 4)]
        wo_toks.append(b.dma("G", prs))
    for kc in range(8):
        b.wait("V", wo_toks[kc // 4])
        tw = b.inc("V", nc.vector.reduce_sum(sb["wocsf"][:, kc:kc + 1],
                                             sb["woTb"][:, kc * D:(kc + 1) * D],
                                             mybir.AxisListType.X))
    b.wait("V", tw)
    twb = b.inc("V", nc.vector.tensor_copy(sb["wocsb"][:, :], sb["wocsf"][:, :]))

    bg = {}

    def emit_bmm2(k):
        b.waits("T", [bg[("rt2b", k)], bg.get("ps3_tok")])
        tb1 = b.inc("T", nc.tensor.matmul(ps[3][:, 0:TNB], sb["onesrow"][0:1, :],
                                          rlbs(0, TNB), start=True, stop=True))
        b.waits("A", [tb1, bg.get("bc2_read")])
        e1 = b.inc("A", nc.scalar.activation(sb["bc2t"][:, 0:TNB], ps[3][:, 0:TNB], AF.Copy))
        b.waits("T", [e1])
        tb2 = b.inc("T", nc.tensor.matmul(ps[3][:, 0:TNB], sb["onesrow"][0:1, :],
                                          rlbs(1, TNB), start=True, stop=True))
        b.waits("A", [tb2, bg.get("bc2_read")])
        e2 = b.inc("A", nc.scalar.activation(sb["bc2t"][:, TNB:2 * TNB], ps[3][:, 0:TNB], AF.Copy))
        bg["ps3_tok"] = e2
        bg["rlb_used"] = tb2
        bg["ebc2"] = e2

    def zsweep_gen(k):
        t0 = k * TNB
        zof = (k % 2) * NCH * TNB
        b.waits("T", [twb, bg.get("sz_read")])
        for kc in range(8):
            ins = nc.tensor.matmul(ps[5][0:1, 0:TNB], sb["wocsb"][:, kc:kc + 1],
                                   sb["oan"][:, kc * T + t0: kc * T + t0 + TNB],
                                   start=(kc == 0), stop=(kc == 7))
        bg["tsz"] = b.inc("T", ins)
        pending_q2 = None

        def emit_q2(p):
            oc, tzq = p
            b.waits("T", [tzq] + ([bg.get("sz_read")] if oc == 0 else []))
            ins = nc.tensor.matmul(ps[4][0:1, 0:TNB], sb["onescol"][:, :],
                                   sb["zsq"][:, (oc % 2) * TNB:(oc % 2 + 1) * TNB],
                                   start=(oc == 0), stop=(oc == NCH - 1))
            return b.inc("T", ins)

        for oc in range(NCH):
            bank = ps[oc % 3]
            b.wait("T", bg.get(("pz", oc % 3)))
            for kc in range(8):
                ins = nc.tensor.matmul(
                    bank[:, 0:TNB],
                    sb["woTb"][:, kc * D + oc * 128: kc * D + (oc + 1) * 128],
                    sb["oan"][:, kc * T + t0: kc * T + t0 + TNB],
                    start=(kc == 0), stop=(kc == 7))
            tmm = b.inc("T", ins)
            if pending_q2 is not None:
                bg[("zsq_read", pending_q2[0] % 2)] = emit_q2(pending_q2)
            b.waits("A", [tmm, bg.get(("zt", k % 2, oc))])
            tze = b.inc("A", nc.scalar.activation(sb["ztile"][:, zof + oc * TNB:zof + (oc + 1) * TNB],
                                                  bank[:, 0:TNB], AF.Copy))
            b.waits("A", [bg.get(("zsq_read", oc % 2))])
            tzq = b.inc("A", nc.scalar.activation(sb["zsq"][:, (oc % 2) * TNB:(oc % 2 + 1) * TNB],
                                                  bank[:, 0:TNB], AF.Square))
            bg[("pz", oc % 3)] = tzq
            pending_q2 = (oc, tzq)
            bg[("ze", k % 2, oc)] = tze
            yield
        bg["tq2"] = emit_q2(pending_q2)
        bg[("zsq_read", pending_q2[0] % 2)] = bg["tq2"]

    def apply_gen(k):
        t0 = k * TNB
        zof = (k % 2) * NCH * TNB
        ebc2 = bg["ebc2"]
        xtoks = {}
        def x_load(oc):
            xtoks[oc] = b.dma("S", [(sb["xc"][:, (oc % 3) * TNB:(oc % 3 + 1) * TNB],
                                     dr["x_in"][128 * oc:128 * (oc + 1), t0:t0 + TNB])],
                              deps=[bg.get(("xc", oc % 3))])
        for oc in range(3):
            x_load(oc)
        pending_x2 = None

        def emit_x2(p):
            oc, txq = p
            b.waits("T", [txq])
            ins = nc.tensor.matmul(ps[6][0:1, 0:TNB], sb["onescol"][:, :],
                                   sb["x1q"][:, (oc % 2) * TNB:(oc % 2 + 1) * TNB],
                                   start=(oc == 0), stop=(oc == NCH - 1))
            return b.inc("T", ins)

        for oc in range(NCH):
            b.waits("V", [ebc2, bg.get(("ze", k % 2, oc)), bg.get("za_free")])
            tu = b.inc("V", nc.vector.tensor_mul(sb["za"][:, :],
                                                 sb["ztile"][:, zof + oc * TNB:zof + (oc + 1) * TNB],
                                                 sb["bc2t"][:, 0:TNB]))
            bg[("zt", k % 2, oc)] = tu
            b.waits("V", [tu])
            tu2 = b.inc("V", nc.vector.tensor_sub(sb["za"][:, :], sb["za"][:, :],
                                                  sb["bc2t"][:, TNB:2 * TNB]))
            x1sl = oc % 2
            b.waits("V", [tu2, xtoks[oc], bg.get(("x1b", x1sl)), bg.get(("x1ba", x1sl))])
            tx1 = b.inc("V", nc.vector.tensor_add(sb["x1b"][:, x1sl * TNB:(x1sl + 1) * TNB],
                                                  sb["za"][:, :],
                                                  sb["xc"][:, (oc % 3) * TNB:(oc % 3 + 1) * TNB]))
            bg["za_free"] = tx1
            bg[("xc", oc % 3)] = tx1
            if oc + 3 < NCH:
                x_load(oc + 3)
            tws = b.dma("S", [(dr["x1s"][128 * oc:128 * (oc + 1), t0:t0 + TNB],
                               sb["x1b"][:, x1sl * TNB:(x1sl + 1) * TNB])], deps=[tx1])
            bg[("x1b", x1sl)] = tws
            b.waits("A", [tx1, bg.get(("x1q_read", oc % 2))])
            txq = b.inc("A", nc.scalar.activation(sb["x1q"][:, (oc % 2) * TNB:(oc % 2 + 1) * TNB],
                                                  sb["x1b"][:, x1sl * TNB:(x1sl + 1) * TNB],
                                                  AF.Square))
            bg[("x1ba", x1sl)] = txq
            if pending_x2 is not None:
                bg[("x1q_read", pending_x2[0] % 2)] = emit_x2(pending_x2)
            pending_x2 = (oc, txq)
            yield
        tx2 = emit_x2(pending_x2)
        bg[("x1q_read", pending_x2[0] % 2)] = tx2
        bg[("tx2", k)] = tx2
        bg["bc2_read"] = tx1

    def emit_ln2(k):
        b.waits("A", [bg["tsz"], bg["tq2"]])
        b.waits("V", [bg["tsz"], bg["tq2"]])
        rt2b = ln_rows(TNB, ps[5][0:1, 0:TNB], ps[4][0:1, 0:TNB],
                       sfs(6, TNB), rlbs(0, TNB), rlbs(1, TNB),
                       dep=bg.get("rows_dep"), rbdep=bg.get("rlb_used"))
        bg["rows_dep"] = rt2b
        bg[("rt2b", k)] = rt2b
        bg["sz_read"] = rt2b

    def emit_ln3(k):
        t0 = k * TNB
        tx2 = bg[("tx2", k)]
        b.waits("A", [tx2])
        b.waits("V", [tx2])
        rt3 = ln_rows(TNB, None, ps[6][0:1, 0:TNB],
                      m1row(t0, TNB), r3row(t0, TNB), c3row(t0, TNB),
                      dep=bg.get("rows_dep"))
        bg["rows_dep"] = rt3

    for k in range(ntb):
        if k >= 1:
            emit_bmm2(k - 1)
        gz = zsweep_gen(k)
        ga = apply_gen(k - 1) if k >= 1 else None
        for _ in range(NCH):
            next(gz)
            if ga is not None:
                next(ga)
        for _ in gz:
            pass
        if ga is not None:
            for _ in ga:
                pass
        emit_ln2(k)
        if k >= 1:
            emit_ln3(k - 1)
    emit_bmm2(ntb - 1)
    for _ in apply_gen(ntb - 1):
        pass
    emit_ln3(ntb - 1)
    b.drain_dma()
    b.barrier()
    es_p2bt.close()
    es_oan.close()
    es_p2bl.close()
    if upto == "p2b":
        b.drain_dma()
        b.barrier()
        es.close()
        return nc

    TN3 = min(256, T)
    nt3 = T // TN3
    es_p3ps = ExitStack()
    ps7 = es_p3ps.enter_context(nc.psum_tensor("pb7", [128, 512], F32))
    es_p3l = ExitStack()
    sb["upb"] = es_p3l.enter_context(nc.sbuf_tensor("upb", [128, 32 * 1024], BF16))
    sb["downb"] = es_p3l.enter_context(nc.sbuf_tensor("downb", [128, 128 * 256], BF16))
    es_p3r = ExitStack()
    for nm, shp, dt_ in (("x1r", [128, 6 * TN3], F32), ("resr", [128, 4 * TN3], F32),
                         ("ftile", [128, NCH * TN3], BF16),
                         ("hidb", [128, 2 * 8 * TN3], BF16), ("yh", [128, 2 * 2 * TN3], BF16),
                         ("ysc", [128, TN3], F32), ("rsc", [128, TN3], F32),
                         ("fsq", [128, 4 * TN3], BF16),
                         ("bc3t", [128, 2 * TN3], F32), ("bc4t", [128, 2 * TN3], F32),
                         ("outb", [128, 2 * TN3], F32)):
        sb[nm] = es_p3r.enter_context(nc.sbuf_tensor(nm, shp, dt_, side="right"))
    up_toks = []
    for h4 in range(4):
        prs = []
        for h in range(4 * h4, 4 * h4 + 4):
            for kc in range(2):
                prs.append((sb["upb"][:, (2 * h + kc) * 1024:(2 * h + kc + 1) * 1024],
                            dr["up_w"][h, 128 * kc:128 * (kc + 1), :]))
        up_toks.append(b.dma("G", prs))
    dn_toks = []
    for h4 in range(4):
        prs = []
        for h in range(4 * h4, 4 * h4 + 4):
            for mc in range(8):
                prs.append((sb["downb"][:, (8 * h + mc) * 256:(8 * h + mc) * 256 + 256],
                            dr["down_w"][h, 128 * mc:128 * (mc + 1), :]))
        dn_toks.append(b.dma("G", prs))

    fg = {}
    out_toks = []
    res_pend = [None]

    def make_residual(ti_, tevac4_):
        t0r = ti_ * TN3
        rr = {"tok": {}, "cons": fg.get("rr_cons", {})}
        def r_load(li):
            sl = li % 4
            rr["tok"][li] = b.dma("S", [(sb["resr"][:, sl * TN3:(sl + 1) * TN3],
                                         dr["x1s"][128 * li:128 * (li + 1), t0r:t0r + TN3])],
                                  deps=[d for d in (rr["cons"].get(sl),) if d])
        for li in range(4):
            r_load(li)
        fg["rr_cons"] = rr["cons"]
        def step(hs):
            tu2 = None
            for oc in (2 * hs, 2 * hs + 1):
                osl2 = oc % 2
                b.waits("V", [tevac4_, fg[("fe", oc // 2)], fg.get("rscf"),
                              fg.get(("outb", osl2))])
                tu = b.inc("V", nc.vector.tensor_mul(sb["rsc"][:, :],
                                                     sb["ftile"][:, oc * TN3:(oc + 1) * TN3],
                                                     sb["bc4t"][:, 0:TN3]))
                b.waits("V", [tu])
                tu2 = b.inc("V", nc.vector.tensor_sub(sb["rsc"][:, :], sb["rsc"][:, :],
                                                      sb["bc4t"][:, TN3:2 * TN3]))
                b.waits("V", [tu2, rr["tok"][oc]])
                to = b.inc("V", nc.vector.tensor_add(
                    sb["outb"][:, osl2 * TN3:(osl2 + 1) * TN3], sb["rsc"][:, :],
                    sb["resr"][:, (oc % 4) * TN3:(oc % 4 + 1) * TN3]))
                fg[("ftg", oc // 2)] = tu
                fg["rscf"] = to
                rr["cons"][oc % 4] = to
                if oc + 4 < NCH:
                    r_load(oc + 4)
                tso = b.dma("S", [(dr["y_out"][128 * oc:128 * (oc + 1), t0r:t0r + TN3],
                                   sb["outb"][:, osl2 * TN3:(osl2 + 1) * TN3])], deps=[to])
                fg[("outb", osl2)] = tso
                out_toks.append(tso)
            if hs == NH - 1:
                fg["bc4_read"] = tu2
        return step

    for ti in range(nt3):
        t0 = ti * TN3
        xr = {"tok": {}, "cons": fg.get("xr_cons", {})}
        def x1_load(li):
            sl = li % 6
            xr["tok"][li] = b.dma("S", [(sb["x1r"][:, sl * TN3:(sl + 1) * TN3],
                                         dr["x1s"][128 * li:128 * (li + 1), t0:t0 + TN3])],
                                  deps=[d for d in (xr["cons"].get(sl),) if d])
        def x1_slot(li):
            return sb["x1r"][:, (li % 6) * TN3:(li % 6 + 1) * TN3]
        def x1_done(li, tok):
            xr["cons"][li % 6] = tok
            if li + 6 < NCH:
                x1_load(li + 6)
        for li in range(6):
            x1_load(li)
        fg["xr_cons"] = xr["cons"]
        b.waits("T", [fg.get(("hid", 1, 0)), fg.get("bc3_read")])
        tb3 = b.inc("T", nc.tensor.matmul(ps[2][:, 0:TN3], sb["onesrow"][0:1, :],
                                          r3row(t0, TN3), start=True, stop=False))
        tb3 = b.inc("T", nc.tensor.matmul(ps[2][:, TN3:2 * TN3], sb["onesrow"][0:1, :],
                                          c3row(t0, TN3), start=False, stop=True))
        b.waits("V", [tb3, fg.get("bc3_read")])
        tevac3 = b.inc("V", nc.vector.tensor_copy(sb["bc3t"][:, :], ps[2][:, 0:2 * TN3]))

        pending_sum = [None]

        def emit_fsums(p):
            hs, tfe_, tfq_ = p
            extra = [fg.get("ps6_tok"), fg.get("ps7_tok")] if hs == 0 else []
            b.waits("T", [tfe_, tfq_] + extra)
            nc.tensor.matmul(ps[6][0:1, 0:2 * TN3], sb["onescol"][:, :],
                             sb["ftile"][:, 2 * hs * TN3:(2 * hs + 2) * TN3],
                             start=(hs == 0), stop=(hs == NH - 1))
            ins = nc.tensor.matmul(ps7[0:1, 0:2 * TN3], sb["onescol"][:, :],
                                   sb["fsq"][:, (hs % 2) * 2 * TN3:(hs % 2 + 1) * 2 * TN3],
                                   start=(hs == 0), stop=(hs == NH - 1))
            t = b.inc("T", ins)
            fg[("fsq_read", hs % 2)] = t
            fg[("fsum", hs)] = t
            return t

        def emit_down(hd):
            pf = ps[4 + hd % 2]
            b.waits("T", [dn_toks[hd // 4]] + (fg.get(("psFr", hd % 2)) or []))
            first = True
            ins = None
            for dc in range(2):
                for mc in range(8):
                    b.wait("T", fg[("gel", hd % 2, mc // 2)])
                    ins = nc.tensor.matmul(
                        pf[:, dc * TN3:(dc + 1) * TN3],
                        sb["downb"][:, (8 * hd + mc) * 256 + dc * 128:(8 * hd + mc) * 256 + (dc + 1) * 128],
                        sb["hidb"][:, (hd % 2) * 8 * TN3 + mc * TN3:(hd % 2) * 8 * TN3 + (mc + 1) * TN3],
                        start=first, stop=(dc == 1 and mc == 7))
                    first = False
            tdn = b.inc("T", ins)
            fg[("hidbG", hd % 2)] = tdn
            b.waits("A", [tdn, fg.get(("ftg", hd)), fg.get(("fsum", hd))])
            tfe = b.inc("A", nc.scalar.activation(sb["ftile"][:, 2 * hd * TN3:(2 * hd + 2) * TN3],
                                                  pf[:, 0:2 * TN3], AF.Copy))
            b.waits("A", [fg.get(("fsq_read", hd % 2))])
            tfq = b.inc("A", nc.scalar.activation(
                sb["fsq"][:, (hd % 2) * 2 * TN3:(hd % 2 + 1) * 2 * TN3],
                pf[:, 0:2 * TN3], AF.Square))
            fg[("psFr", hd % 2)] = [tfe, tfq]
            fg[("fe", hd)] = tfe
            return (hd, tfe, tfq)

        for h in range(NH):
            ysl = h % 2
            ty = None
            for ci, c in enumerate((2 * h, 2 * h + 1)):
                b.waits("V", [tevac3, xr["tok"][c], fg.get("yscf"), fg.get(("yh", ysl))])
                tyu = b.inc("V", nc.vector.tensor_mul(sb["ysc"][:, :], x1_slot(c),
                                                      sb["bc3t"][:, 0:TN3]))
                b.waits("V", [tyu])
                ty = b.inc("V", nc.vector.tensor_sub(
                    sb["yh"][:, (2 * ysl + ci) * TN3:(2 * ysl + ci + 1) * TN3],
                    sb["ysc"][:, :], sb["bc3t"][:, TN3:2 * TN3]))
                fg["yscf"] = ty
                x1_done(c, tyu)
            hb = (ps[2 * (h % 2)], ps[2 * (h % 2) + 1])
            b.wait("T", up_toks[h // 4])
            b.wait("T", ty)
            if h == 1:
                b.wait("T", tevac3)
            tup = None
            for mc in range(8):
                bank = hb[mc // 2 % 2]
                half = mc % 2
                if half == 0:
                    b.wait("T", fg.get(("hid", h % 2, mc // 2 % 2)))
                for kc in range(2):
                    ins = nc.tensor.matmul(
                        bank[:, half * TN3:(half + 1) * TN3],
                        sb["upb"][:, (2 * h + kc) * 1024 + mc * 128:(2 * h + kc) * 1024 + (mc + 1) * 128],
                        sb["yh"][:, (2 * ysl + kc) * TN3:(2 * ysl + kc + 1) * TN3],
                        start=(half == 0 and kc == 0), stop=(half == 1 and kc == 1))
                if half == 1:
                    tup = b.inc("T", ins)
                    b.waits("A", [tup, fg.get(("hidbG", h % 2))])
                    tg_ = b.inc("A", nc.scalar.activation(
                        sb["hidb"][:, (h % 2) * 8 * TN3 + (mc - 1) * TN3:(h % 2) * 8 * TN3 + (mc + 1) * TN3],
                        bank[:, 0:2 * TN3], AF.Gelu))
                    fg[("hid", h % 2, mc // 2 % 2)] = tg_
                    fg[("gel", h % 2, mc // 2)] = tg_
            fg[("yh", ysl)] = tup
            if h > 0:
                pdn = emit_down(h - 1)
                if pending_sum[0] is not None:
                    emit_fsums(pending_sum[0])
                pending_sum[0] = pdn
            if res_pend[0] is not None:
                res_pend[0](h)
        pdn = emit_down(NH - 1)
        if pending_sum[0] is not None:
            emit_fsums(pending_sum[0])
        tS2 = emit_fsums(pdn)
        b.waits("A", [tS2, fg.get("rows_dep4")])
        SF4, SF5 = 4 * 512, 5 * 512
        th1 = b.inc("A", nc.scalar.activation(sb["smallf"][0:1, SF4:SF4 + 2 * TN3],
                                              ps[6][0:1, 0:2 * TN3], AF.Copy))
        th2 = b.inc("A", nc.scalar.activation(sb["smallf"][0:1, SF5:SF5 + 2 * TN3],
                                              ps7[0:1, 0:2 * TN3], AF.Copy))
        fg["ps7_tok"] = th2
        b.waits("V", [th1, th2, fg.get("rows_dep4")])
        th3 = b.inc("V", nc.vector.tensor_add(sb["smallf"][0:1, SF4:SF4 + TN3],
                                              sb["smallf"][0:1, SF4:SF4 + TN3],
                                              sb["smallf"][0:1, SF4 + TN3:SF4 + 2 * TN3]))
        th4 = b.inc("V", nc.vector.tensor_add(sb["smallf"][0:1, SF5:SF5 + TN3],
                                              sb["smallf"][0:1, SF5:SF5 + TN3],
                                              sb["smallf"][0:1, SF5 + TN3:SF5 + 2 * TN3]))
        b.waits("A", [th3, th4])
        b.waits("V", [th3, th4])
        rt4 = ln_rows(TN3, sb["smallf"][0:1, SF4:SF4 + TN3], sb["smallf"][0:1, SF5:SF5 + TN3],
                      sfs(6, TN3), rlbs(0, TN3), rlbs(1, TN3),
                      dep=fg.get("rows_dep4"), rbdep=fg.get("rlb_used"))
        fg["rows_dep4"] = rt4
        b.waits("T", [rt4, th1])
        tb4 = b.inc("T", nc.tensor.matmul(ps[6][:, 0:TN3], sb["onesrow"][0:1, :],
                                          rlbs(0, TN3), start=True, stop=False))
        tb4 = b.inc("T", nc.tensor.matmul(ps[6][:, TN3:2 * TN3], sb["onesrow"][0:1, :],
                                          rlbs(1, TN3), start=False, stop=True))
        fg["rlb_used"] = tb4
        b.waits("V", [tb4, fg.get("bc4_read")])
        tevac4 = b.inc("V", nc.vector.tensor_copy(sb["bc4t"][:, :], ps[6][:, 0:2 * TN3]))
        fg["ps6_tok"] = tevac4
        fg["bc3_read"] = ty
        res_pend[0] = make_residual(ti, tevac4)
    for hs in range(NH):
        res_pend[0](hs)
    b.drain_dma()
    b.barrier()
    es_p3r.close()
    es_p3l.close()
    es_p3ps.close()
    es.close()
    return nc


def kernel(**inputs):
    in_maps, perm, iperm, B, T = host_stage(inputs)
    nc = build_program(T)
    res = run_bass_kernel_spmd(nc, in_maps, core_ids=list(range(B)))
    return assemble_output(res.results, perm, B, T)
